# version 02
# baseline (speedup 1.0000x reference)
import numpy as np
from contextlib import ExitStack

import concourse.bass as bass
import concourse.mybir as mybir
from concourse.bass_utils import run_bass_kernel_spmd
from concourse.bass import IndirectOffsetOnAxis

F32 = mybir.dt.float32
BF16 = mybir.dt.bfloat16
I32 = mybir.dt.int32
AF = mybir.ActivationFunctionType
ALU = mybir.AluOpType
AX = mybir.AxisListType

NCORES = 8
TC = 4096
D = 1024
E = 64
CAP = 1024
TOPK = 8
NG = 8
EPC = E // NCORES
FEXP = 512
FSH = 1024
HF = 4096
LN_EPS = 1e-5
SCALE = 2.5
NSLOT = E * CAP
ZROW = NSLOT


class Em:

    def __init__(self, nc, es):
        self.nc = nc
        self.es = es
        self.ops = {e: [] for e in ("sync", "scalar", "tensor", "vector", "gpsimd")}
        self.sems = {}
        self.counts = {}
        self.chan("cv")
        self.chan("cs")
        self.chan("ct")
        self.chan("cg")
        self.dma_chans = []

    def chan(self, name):
        self.sems[name] = self.es.enter_context(self.nc.semaphore(name))
        self.counts[name] = 0
        return name

    def _emit(self, eng, fn, waits, inc, n, serial):
        wl = []
        for w in waits:
            if isinstance(w, str):
                wl.append((w, self.counts[w]))
            else:
                if w is not None and w[1] > 0:
                    wl.append(w)
        if serial and self.counts[inc] > 0:
            wl.append((inc, self.counts[inc]))
        self.counts[inc] += n
        self.ops[eng].append((fn, wl, (inc, n)))
        return (inc, self.counts[inc])

    def v(self, fn, waits=()):
        return self._emit("vector", fn, waits, "cv", 1, True)

    def s(self, fn, waits=()):
        return self._emit("scalar", fn, waits, "cs", 1, True)

    def t(self, fn, waits=()):
        return self._emit("tensor", fn, waits, "ct", 1, False)

    def g(self, fn, waits=()):
        return self._emit("gpsimd", fn, waits, "cg", 1, True)

    def dma(self, chan, eng, fn, waits=()):
        if chan not in self.sems:
            self.chan(chan)
            self.dma_chans.append(chan)
        return self._emit(eng, fn, waits, chan, 16, False)

    def now(self, name):
        return (name, self.counts[name])

    def barrier_all(self):
        for ch in self.dma_chans:
            c = self.counts[ch]
            if c > 0:
                self.ops["gpsimd"].append(
                    (lambda g: None, [(ch, c)], ("cg", 0)))

    def build(self, block):
        for eng in ("sync", "scalar", "tensor", "vector", "gpsimd"):
            lst = self.ops[eng]
            if not lst:
                continue

            def mk(lst):
                def body(e):
                    for fn, wl, tgt in lst:
                        for s, c in wl:
                            if c > 0:
                                e.wait_ge(self.sems[s], c)
                        inst = fn(e)
                        if tgt is not None and tgt[1] > 0:
                            if inst is None:
                                raise RuntimeError("inc on None instruction")
                            inst.then_inc(self.sems[tgt[0]], tgt[1])
                return body

            getattr(block, eng)(mk(lst))


def chunked_load(em, dst_sb, src_ap, nchunk, inner, chan="dpre"):
    fn = lambda e: e.dma_start(
        out=dst_sb[:, :].rearrange("p (c f) -> p c f", c=nchunk),
        in_=src_ap.rearrange("(c p) f -> p c f", p=128))
    eng = "gpsimd" if src_ap.dtype != dst_sb.dtype else "sync"
    return em.dma(chan if eng == "sync" else chan + "_g", eng, fn)


def emit_identity(em, ident):
    em.g(lambda g: g.memset(ident[:, :], 0.0))
    return em.g(lambda g: g.affine_select(
        out=ident[:, :], in_=ident[:, :], compare_op=ALU.not_equal,
        fill=1.0, base=0, pattern=[[-1, ident.shape[0]]], channel_multiplier=1))


def emit_transpose8(em, psT, src, ident, waits):
    n = src.shape[1] // 128

    def fn(t):
        inst = None
        for c in range(n):
            inst = t.transpose(psT[:, c * 128:(c + 1) * 128],
                               src[:, c * 128:(c + 1) * 128], ident[:, :])
        return inst
    return em.t(fn, waits=waits)


def emit_ln(em, x_tile, xln_tile, stats, mv, col, rr, waits):
    nh = x_tile.shape[1] // 512
    for h in range(nh):
        em.v(lambda v, h=h: v.bn_stats(stats[:, 6 * h:6 * h + 6],
                                       x_tile[:, 512 * h:512 * h + 512]),
             waits=(waits if h == 0 else ()))
    em.v(lambda v, nh=nh: v.bn_aggr(mv[:, :], stats[:, 0:6 * nh]),
         waits=[em.now("cs")])
    em.v(lambda v: v.tensor_scalar(col[:, :], mv[:, 1:2], float(LN_EPS), None,
                                   ALU.add))
    m = em.now("cv")
    em.s(lambda s: s.activation(col[:, :], col[:, :], AF.Sqrt), waits=[m])
    em.v(lambda v: v.reciprocal(rr[:, :], col[:, :]), waits=[em.now("cs")])
    return em.v(lambda v: v.tensor_scalar(
        xln_tile, x_tile, mv[:, 0:1], rr[:, 0:1], ALU.subtract, ALU.mult))


def build_launch_A(nt=TC // 128, with_shared=True):
    nc = bass.Bass("TRN2", target_bir_lowering=False)
    TCl = nt * 128
    x_in = nc.dram_tensor("x", [TCl, D], F32, kind="ExternalInput").ap()
    gwT = nc.dram_tensor("gwT", [D, E], F32, kind="ExternalInput").ap()
    ebias_in = nc.dram_tensor("e_bias", [1, E], F32, kind="ExternalInput").ap()
    sgT = nc.dram_tensor("sgT", [D, FSH], F32, kind="ExternalInput").ap()
    suT = nc.dram_tensor("suT", [D, FSH], F32, kind="ExternalInput").ap()
    sdT = nc.dram_tensor("sdT", [FSH, D], F32, kind="ExternalInput").ap()
    gid_in = nc.dram_tensor("gid", [128, nt], F32, kind="ExternalInput").ap()
    cst_up = nc.dram_tensor("cst_up", [128, 128], F32, kind="ExternalInput").ap()
    cst_iota = nc.dram_tensor("cst_iota", [128, E], F32, kind="ExternalInput").ap()

    o_xln = nc.dram_tensor("xln", [TCl, D], BF16, kind="ExternalOutput").ap()
    o_counts = nc.dram_tensor("counts", [1, E], F32, kind="ExternalOutput").ap()
    o_contrib = nc.dram_tensor("contrib", [NSLOT, 2], F32, kind="ExternalOutput").ap()
    o_dest = nc.dram_tensor("dest", [TCl, TOPK], F32, kind="ExternalOutput").ap()
    o_shared = nc.dram_tensor("shared", [TCl, D], F32, kind="ExternalOutput").ap()

    es = ExitStack()
    sb = lambda name, shape, dt: es.enter_context(nc.sbuf_tensor(name, shape, dt))
    psum = lambda name, shape, dt: es.enter_context(nc.psum_tensor(name, shape, dt))

    gwTb = sb("gwTb", [128, 8 * E], BF16)
    ebias_b = sb("ebias_b", [128, E], F32)
    eb_row = sb("eb_row", [1, E], F32)
    ones128_b = sb("ones128_b", [128, 1], BF16)
    ones1_f = sb("ones1_f", [1, 128], F32)
    upb = sb("upb", [128, 128], BF16)
    iota64 = sb("iota64", [128, E], F32)
    ident = sb("ident", [128, 128], BF16)
    if with_shared:
        sgTb = sb("sgTb", [128, 8 * FSH], BF16)
        suTb = sb("suTb", [128, 8 * FSH], BF16)
        sdTb = sb("sdTb", [128, 8 * D], BF16)
    gidb = sb("gidb", [128, nt], F32)

    xt = sb("xt", [128, 2 * D], F32)
    stats = sb("stats", [128, 12], F32)
    mv = sb("mv", [128, 2], F32)
    rr = sb("rr", [128, 1], F32)
    xlnt = sb("xlnt", [128, 2 * D], BF16)
    xlnT = sb("xlnT", [128, 2 * D], BF16)
    psT = psum("psT", [128, D], BF16)
    ps_log = psum("ps_log", [128, E], F32)
    ps_cnt = psum("ps_cnt", [1, E], F32)
    ps_pos = psum("ps_pos", [128, E], F32)
    scores = sb("scores", [128, E], F32)
    sfc = sb("sfc", [128, E], F32)
    gsc = sb("gsc", [128, NG], F32)
    g8 = sb("g8", [128, NG], F32)
    scr1 = sb("scr1", [128, E], F32)
    scr2 = sb("scr2", [128, E], F32)
    scr3 = sb("scr3", [128, E], F32)
    cums = sb("cums", [128, E], F32)
    cum2 = sb("cum2", [128, E], F32)
    col1 = sb("col1", [128, 1], F32)
    col2 = sb("col2", [128, 1], F32)
    col3 = sb("col3", [128, 1], F32)
    sel = sb("sel", [128, E], F32)
    selb = sb("selb", [128, E], BF16)
    wsel = sb("wsel", [128, E], F32)
    msk = sb("msk", [128, E], F32)
    posd = sb("posd", [128, E], F32)
    destf = sb("destf", [128, E], F32)
    desti = sb("desti", [128, TOPK], I32)
    sval = sb("sval", [128, 2 * TOPK], F32)
    dest_out = sb("dest_out", [128, TOPK], F32)
    pct = sb("pct", [1, E], F32)
    runc = sb("runc", [1, E], F32)
    if with_shared:
        ps_sh = psum("ps_sh", [128, 512], F32)
        ps_sh2 = psum("ps_sh2", [128, 512], F32)
        gfm = sb("gfm", [128, 512], BF16)
        prod = sb("prod", [128, FSH], BF16)
        prodT = sb("prodT", [128, FSH], BF16)
        shout = sb("shout", [128, D], F32)

    with nc.Block() as block:
        em = Em(nc, es)

        chunked_load(em, gwTb, gwT, 8, E)
        if with_shared:
            chunked_load(em, sgTb, sgT, 8, FSH)
            chunked_load(em, suTb, suT, 8, FSH)
            chunked_load(em, sdTb, sdT, 8, D)
        em.dma("dpre_g", "gpsimd", lambda g: g.dma_start(out=upb[:, :], in_=cst_up[:, :]))
        em.dma("dpre", "sync", lambda s: s.dma_start(out=iota64[:, :], in_=cst_iota[:, :]))
        em.dma("dpre", "sync", lambda s: s.dma_start(out=gidb[:, :], in_=gid_in[:, :]))
        em.dma("dpre", "sync", lambda s: s.dma_start(out=eb_row[:, :], in_=ebias_in[:, :]))
        em.g(lambda g: g.memset(ones128_b[:, :], 1.0))
        em.g(lambda g: g.memset(ones1_f[:, :], 1.0))
        em.g(lambda g: g.memset(runc[:, :], 0.0))
        emit_identity(em, ident)
        ld_all = em.now("dpre")
        gp_all = em.now("dpre_g")
        g_all = em.now("cg")

        em.t(lambda t: t.matmul(ps_log[:, :], ones1_f[:, :], eb_row[:, :],
                                start=True, stop=True),
             waits=[ld_all, gp_all, g_all])
        ev_eb = em.v(lambda v: v.tensor_copy(ebias_b[:, :], ps_log[:, :]),
                     waits=[em.now("ct")])

        regs = {}

        def init_reg(g):
            regs["bc"] = g.register("bc_sc").__enter__()
            g.reg_mov(regs["bc"], NSLOT - 1)
            return None
        em.ops["gpsimd"].append((init_reg, [], ("cg", 0)))

        marks = {"xt_free": [None, None], "xln_free": [[], []],
                 "xT_free": [None, None], "xlnT_free": [None, None],
                 "sc_prev": None, "dest_free": None, "sh_free": None}

        for i in range(nt):
            pb = i % 2
            xti = xt[:, pb * D:(pb + 1) * D]
            xlni = xlnt[:, pb * D:(pb + 1) * D]
            xlnTi = xlnT[:, pb * D:(pb + 1) * D]

            w_ld = [w for w in [marks["xt_free"][pb]] if w]
            ld_x = em.dma("dx%d" % pb, "sync", lambda s, i=i, xti=xti: s.dma_start(
                out=xti, in_=x_in[i * 128:(i + 1) * 128, :]), waits=w_ld)

            w_x = [ld_x] + list(marks["xln_free"][pb])
            ln_done = emit_ln(em, xti, xlni, stats, mv, col1, rr, w_x)
            marks["xt_free"][pb] = ln_done
            st_x = em.dma("sxln%d" % pb, "sync",
                          lambda s, i=i, xlni=xlni: s.dma_start(
                out=o_xln[i * 128:(i + 1) * 128, :], in_=xlni), waits=[ln_done])

            w_tr = [ln_done] + [w for w in [marks["xT_free"][pb]] if w]
            tr = emit_transpose8(em, psT, xlni, ident, w_tr + [g_all])
            w_evxT = [tr] + [w for w in [marks["xlnT_free"][pb]] if w]
            ev_xT = em.v(lambda v, xlnTi=xlnTi: v.tensor_copy(xlnTi, psT[:, :]),
                         waits=w_evxT)
            marks["xln_free"][pb] = [st_x, tr]

            def mm_logits(t, xlnTi=xlnTi):
                inst = None
                for c in range(8):
                    inst = t.matmul(ps_log[:, :], xlnTi[:, c * 128:(c + 1) * 128],
                                    gwTb[:, c * E:(c + 1) * E],
                                    start=(c == 0), stop=(c == 7))
                return inst
            mm_log = em.t(mm_logits, waits=[ev_xT, ev_eb, gp_all, em.now("cs")])
            sig = em.s(lambda s: s.activation(scores[:, :], ps_log[:, :],
                                              AF.Sigmoid),
                       waits=[mm_log, em.now("cv")])
            em.v(lambda v: v.tensor_add(sfc[:, :], scores[:, :], ebias_b[:, :]),
                 waits=[sig])

            for gi in range(NG):
                em.v(lambda v, gi=gi: v.max(out=g8[:, :],
                                            in_=sfc[:, gi * 8:(gi + 1) * 8]))
                em.v(lambda v, gi=gi: v.tensor_add(gsc[:, gi:gi + 1],
                                                   g8[:, 0:1], g8[:, 1:2]))
            em.v(lambda v: v.max(out=g8[:, :], in_=gsc[:, :]))
            em.v(lambda v: v.tensor_scalar(scr1[:, 0:NG], gsc[:, :], g8[:, 3:4],
                                           None, ALU.is_gt))
            em.v(lambda v: v.reduce_sum(col1[:, :], scr1[:, 0:NG], axis=AX.X))
            em.v(lambda v: v.tensor_scalar(scr2[:, 0:NG], gsc[:, :], g8[:, 3:4],
                                           None, ALU.is_equal))
            em.v(lambda v: v.tensor_copy(scr3[:, 0:1], scr2[:, 0:1]))
            em.v(lambda v: v.tensor_add(scr3[:, 1:NG], scr2[:, 1:NG],
                                        scr2[:, 0:NG - 1]))
            em.v(lambda v: v.tensor_copy(cum2[:, 0:2], scr3[:, 0:2]))
            em.v(lambda v: v.tensor_add(cum2[:, 2:NG], scr3[:, 2:NG],
                                        scr3[:, 0:NG - 2]))
            em.v(lambda v: v.tensor_copy(cums[:, 0:4], cum2[:, 0:4]))
            em.v(lambda v: v.tensor_add(cums[:, 4:NG], cum2[:, 4:NG],
                                        cum2[:, 0:NG - 4]))
            em.v(lambda v: v.tensor_scalar(col2[:, :], col1[:, :], -4.0, -1.0,
                                           ALU.add, ALU.mult))
            em.v(lambda v: v.tensor_scalar(cum2[:, 0:NG], cums[:, 0:NG],
                                           col2[:, 0:1], None, ALU.is_le))
            em.v(lambda v: v.tensor_mul(cum2[:, 0:NG], cum2[:, 0:NG],
                                        scr2[:, 0:NG]))
            em.v(lambda v: v.tensor_max(g8[:, :], cum2[:, 0:NG], scr1[:, 0:NG]))
            em.v(lambda v: v.tensor_copy(
                scr1[:, :], bass.AP(g8, 0, [[NG, 128], [1, NG], [0, 8]])))
            em.v(lambda v: v.tensor_mul(scr2[:, :], sfc[:, :], scr1[:, :]))
            em.v(lambda v: v.max(out=g8[:, :], in_=scr2[:, :]))
            em.v(lambda v: v.tensor_scalar(scr1[:, :], scr2[:, :], g8[:, 7:8],
                                           None, ALU.is_gt))
            em.v(lambda v: v.reduce_sum(col1[:, :], scr1[:, :], axis=AX.X))
            em.v(lambda v: v.tensor_scalar(scr3[:, :], scr2[:, :], g8[:, 7:8],
                                           None, ALU.is_equal))
            a, b = scr3, cums
            for sh in (1, 2, 4, 8, 16, 32):
                em.v(lambda v, a=a, b=b, sh=sh: v.tensor_copy(b[:, 0:sh],
                                                              a[:, 0:sh]))
                em.v(lambda v, a=a, b=b, sh=sh: v.tensor_add(
                    b[:, sh:E], a[:, sh:E], a[:, 0:E - sh]))
                a, b = b, a
            em.v(lambda v: v.tensor_scalar(col2[:, :], col1[:, :], -8.0, -1.0,
                                           ALU.add, ALU.mult))
            em.v(lambda v: v.tensor_scalar(cum2[:, :], scr3[:, :], col2[:, 0:1],
                                           None, ALU.is_le))
            em.v(lambda v: v.tensor_scalar(msk[:, :], scr2[:, :], g8[:, 7:8],
                                           None, ALU.is_equal))
            em.v(lambda v: v.tensor_mul(cum2[:, :], cum2[:, :], msk[:, :]))
            em.v(lambda v: v.tensor_scalar(msk[:, :], scr2[:, :], g8[:, 7:8],
                                           None, ALU.is_gt))
            sel_done = em.v(lambda v: v.tensor_max(sel[:, :], cum2[:, :],
                                                   msk[:, :]))
            em.v(lambda v: v.tensor_mul(wsel[:, :], sel[:, :], scores[:, :]))
            em.v(lambda v: v.reduce_sum(col1[:, :], wsel[:, :], axis=AX.X))
            em.v(lambda v: v.tensor_scalar(col2[:, :], col1[:, :], 1e-20, None,
                                           ALU.add))
            em.v(lambda v: v.reciprocal(col3[:, :], col2[:, :]))
            em.v(lambda v: v.tensor_scalar(wsel[:, :], wsel[:, :], col3[:, 0:1],
                                           float(SCALE), ALU.mult, ALU.mult))
            selb_done = em.v(lambda v: v.tensor_copy(selb[:, :], sel[:, :]))

            mm_cnt = em.t(lambda t: t.matmul(ps_cnt[:, :], ones128_b[:, :],
                                             selb[:, :], start=True, stop=True),
                          waits=[selb_done, g_all, em.now("cv")])
            pct_ev = em.v(lambda v: v.tensor_copy(pct[:, :], ps_cnt[:, :]),
                          waits=[mm_cnt])

            def mm_pos(t):
                t.matmul(ps_pos[:, :], upb[:, :], selb[:, :],
                         start=True, stop=False)
                return t.matmul(ps_pos[:, :], ones1_f[:, :], runc[:, :],
                                start=False, stop=True)
            mm_pos_c = em.t(mm_pos, waits=[selb_done, gp_all, g_all,
                                           em.now("cv")])
            pos_ev = em.v(lambda v: v.tensor_copy(posd[:, :], ps_pos[:, :]),
                          waits=[mm_pos_c])
            em.v(lambda v: v.tensor_add(runc[:, :], runc[:, :], pct[:, :]),
                 waits=[pct_ev, mm_pos_c])

            em.v(lambda v: v.scalar_tensor_tensor(destf[:, :], iota64[:, :],
                                                  1024.0, posd[:, :],
                                                  ALU.mult, ALU.add))
            em.v(lambda v: v.tensor_scalar(scr1[:, :], posd[:, :],
                                           float(CAP) - 0.5, 1e9,
                                           ALU.is_ge, ALU.mult))
            em.v(lambda v: v.tensor_add(destf[:, :], destf[:, :], scr1[:, :]))

            em.v(lambda v: v.tensor_copy(scr3[:, :], sel[:, :]))
            a, b = scr3, cums
            for sh in (1, 2, 4, 8, 16, 32):
                em.v(lambda v, a=a, b=b, sh=sh: v.tensor_copy(b[:, 0:sh],
                                                              a[:, 0:sh]))
                em.v(lambda v, a=a, b=b, sh=sh: v.tensor_add(
                    b[:, sh:E], a[:, sh:E], a[:, 0:E - sh]))
                a, b = b, a

            for k in range(TOPK):
                em.v(lambda v, k=k: v.tensor_scalar(scr1[:, :], scr3[:, :],
                                                    float(k + 1), None,
                                                    ALU.is_equal))
                em.v(lambda v: v.tensor_mul(scr1[:, :], scr1[:, :], sel[:, :]))
                em.v(lambda v: v.tensor_mul(scr2[:, :], scr1[:, :], destf[:, :]))
                w_dk = [w for w in [marks["dest_free"]] if w] if k == 0 else []
                em.v(lambda v, k=k: v.reduce_sum(dest_out[:, k:k + 1],
                                                 scr2[:, :], axis=AX.X),
                     waits=w_dk)
                em.v(lambda v: v.tensor_mul(scr2[:, :], scr1[:, :], wsel[:, :]))
                w_sc = [w for w in [marks["sc_prev"]] if w] if k == 0 else []
                em.v(lambda v, k=k: v.reduce_sum(
                    sval[:, 2 * k + 1:2 * k + 2], scr2[:, :], axis=AX.X),
                    waits=w_sc)
            em.v(lambda v, i=i: v.tensor_copy(
                bass.AP(sval, 0, [[2 * TOPK, 128], [2, TOPK]]),
                bass.AP(gidb, i, [[nt, 128], [0, TOPK]])))
            em.v(lambda v: v.tensor_copy(desti[:, :], dest_out[:, :]))
            stg_done = em.now("cv")
            for k in range(TOPK):
                sc = em.dma("dsc", "gpsimd", lambda g, k=k: g.indirect_dma_start(
                    out=o_contrib[:, :],
                    out_offset=IndirectOffsetOnAxis(desti[:, k:k + 1], 0),
                    in_=sval[:, 2 * k:2 * k + 2], in_offset=None,
                    bounds_check=regs["bc"], oob_is_err=False),
                    waits=[stg_done])
            marks["sc_prev"] = sc

            st_d = em.dma("sdst", "sync", lambda s, i=i: s.dma_start(
                out=o_dest[i * 128:(i + 1) * 128, :], in_=dest_out[:, :]),
                waits=[em.now("cv")])
            marks["dest_free"] = st_d

            if with_shared:
                for half in range(2):
                    def mm_g(t, h=half, xlnTi=xlnTi):
                        inst = None
                        for c in range(8):
                            inst = t.matmul(
                                ps_sh[:, :], xlnTi[:, c * 128:(c + 1) * 128],
                                sgTb[:, c * FSH + h * 512:c * FSH + h * 512 + 512],
                                start=(c == 0), stop=(c == 7))
                        return inst
                    mmg = em.t(mm_g, waits=[ev_xT, gp_all, em.now("cs")])
                    silu = em.s(lambda s: s.activation(gfm[:, :], ps_sh[:, :],
                                                       AF.Silu),
                                waits=[mmg, em.now("cv")])

                    def mm_u(t, h=half, xlnTi=xlnTi):
                        inst = None
                        for c in range(8):
                            inst = t.matmul(
                                ps_sh2[:, :], xlnTi[:, c * 128:(c + 1) * 128],
                                suTb[:, c * FSH + h * 512:c * FSH + h * 512 + 512],
                                start=(c == 0), stop=(c == 7))
                        return inst
                    mmu = em.t(mm_u, waits=[ev_xT, em.now("cv")])
                    em.v(lambda v, h=half: v.tensor_mul(
                        prod[:, h * 512:(h + 1) * 512], ps_sh2[:, :], gfm[:, :]),
                        waits=[mmu, silu])
                marks["xlnT_free"][pb] = em.now("ct")
                prod_done = em.now("cv")
                trp = emit_transpose8(em, psT, prod[:, :], ident,
                                      [prod_done, ev_xT])
                prT = em.v(lambda v: v.tensor_copy(prodT[:, :], psT[:, :]),
                           waits=[trp])
                marks["xT_free"][pb] = prT
                for half in range(2):
                    def mm_sd(t, h=half):
                        inst = None
                        for c in range(8):
                            inst = t.matmul(
                                ps_sh[:, :], prodT[:, c * 128:(c + 1) * 128],
                                sdTb[:, c * D + h * 512:c * D + h * 512 + 512],
                                start=(c == 0), stop=(c == 7))
                        return inst
                    mmsd = em.t(mm_sd, waits=[prT, em.now("cs"), em.now("cv")])
                    w_sh = [mmsd] + ([w for w in [marks["sh_free"]] if w]
                                     if half == 0 else [])
                    em.v(lambda v, h=half: v.tensor_copy(
                        shout[:, h * 512:(h + 1) * 512], ps_sh[:, :]),
                        waits=w_sh)
                st_sh = em.dma("ssh", "sync", lambda s, i=i: s.dma_start(
                    out=o_shared[i * 128:(i + 1) * 128, :], in_=shout[:, :]),
                    waits=[em.now("cv")])
                marks["sh_free"] = st_sh
            else:
                marks["xT_free"][pb] = ev_xT
                marks["xlnT_free"][pb] = em.now("ct")

        em.dma("scnt", "sync", lambda s: s.dma_start(
            out=o_counts[:, :], in_=runc[:, :]), waits=[em.now("cv")])
        em.barrier_all()
        em.build(block)
    es.close()
    return nc


def host_consts():
    return {
        "cst_up": np.triu(np.ones((128, 128), np.float32), 1),
        "cst_iota": np.broadcast_to(
            np.arange(E, dtype=np.float32), (128, E)).copy(),
    }


def launch_A_inputs(x_full, gate_weight, e_bias, shared_gate, shared_up,
                    shared_down):
    cs = host_consts()
    gwT = np.ascontiguousarray(gate_weight.T)
    sgT = np.ascontiguousarray(shared_gate.T)
    suT = np.ascontiguousarray(shared_up.T)
    sdT = np.ascontiguousarray(shared_down.T)
    maps = []
    for c in range(NCORES):
        gid = (np.arange(TC, dtype=np.float32) + c * TC).reshape(
            TC // 128, 128).T.copy()
        maps.append({
            "x": np.ascontiguousarray(x_full[c * TC:(c + 1) * TC]),
            "gwT": gwT, "e_bias": e_bias.reshape(1, E).astype(np.float32),
            "sgT": sgT, "suT": suT, "sdT": sdT,
            "gid": gid, **cs,
        })
    return maps


def build_launch_B():
    nc = bass.Bass("TRN2", target_bir_lowering=False)
    xln_all = nc.dram_tensor("xln_all", [NCORES * TC, D], BF16,
                             kind="ExternalInput").ap()
    counts_all = nc.dram_tensor("counts_all", [1, NCORES * E], F32,
                                kind="ExternalInput").ap()
    contrib_all = nc.dram_tensor("contrib_all", [NCORES * NSLOT, 2], F32,
                                 kind="ExternalInput").ap()
    w1T_in = nc.dram_tensor("w1T", [EPC * D, FEXP], F32, kind="ExternalInput").ap()
    w3T_in = nc.dram_tensor("w3T", [EPC * D, FEXP], F32, kind="ExternalInput").ap()
    w2T_in = nc.dram_tensor("w2T", [EPC * FEXP, D], F32, kind="ExternalInput").ap()
    slotbase_in = nc.dram_tensor("slotbase", [128, E], F32,
                                 kind="ExternalInput").ap()
    prow_in = nc.dram_tensor("prow", [128, 8], F32, kind="ExternalInput").ap()
    eofs_in = nc.dram_tensor("eofs", [1, 1], F32, kind="ExternalInput").ap()
    o_ye = nc.dram_tensor("ye", [NSLOT, D], BF16, kind="ExternalOutput").ap()

    es = ExitStack()
    sb = lambda name, shape, dt: es.enter_context(nc.sbuf_tensor(name, shape, dt))
    psum = lambda name, shape, dt: es.enter_context(nc.psum_tensor(name, shape, dt))

    cntb = sb("cntb", [1, NCORES * E], F32)
    bases = sb("bases", [1, NCORES * E], F32)
    slotbase = sb("slotbase_sb", [128, E], F32)
    prow = sb("prow_sb", [128, 8], F32)
    eofs = sb("eofs_sb", [1, 1], F32)
    ones1_f = sb("ones1_f", [1, 128], F32)
    ident = sb("ident", [128, 128], BF16)
    Bb = sb("Bb", [128, 8], F32)
    gemask = sb("gemask", [128, 8], F32)
    cofp = sb("cofp", [128, 8], F32)
    bsel = sb("bsel", [128, 8], F32)
    loff = sb("loff", [128, 8], F32)
    gidxf = sb("gidxf", [128, 8], F32)
    gidxi = sb("gidxi", [128, 8], I32)
    dyef = sb("dyef", [128, 8], F32)
    dyei = sb("dyei", [128, 8], I32)
    sm = sb("sm", [128, 16], F32)
    tokf = sb("tokf", [128, 8], F32)
    toki = sb("toki", [128, 8], I32)
    wcol = sb("wcol", [128, 8], F32)
    xe = sb("xe", [128, 8 * D], BF16)
    xeT = sb("xeT", [128, 8 * D], BF16)
    w1b = sb("w1b", [128, 2 * 8 * FEXP], BF16)
    w3b = sb("w3b", [128, 2 * 8 * FEXP], BF16)
    w2b = sb("w2b", [128, 2 * 4 * D], BF16)
    hg = sb("hg", [128, 4 * CAP], BF16)
    hu = sb("hu", [128, 4 * CAP], BF16)
    prodE = sb("prodE", [128, 4 * CAP], BF16)
    yeb = sb("yeb", [128, 8 * D], BF16)
    ps_b = psum("ps_b", [128, 8], F32)
    psT = psum("psT", [128, D], BF16)
    ps_h = psum("ps_h", [128, 512], F32)
    ps_u = psum("ps_u", [128, 512], F32)

    with nc.Block() as block:
        em = Em(nc, es)
        em.dma("dpre", "sync", lambda s: s.dma_start(out=cntb[:, :], in_=counts_all[:, :]))
        em.dma("dpre", "sync", lambda s: s.dma_start(out=slotbase[:, :], in_=slotbase_in[:, :]))
        em.dma("dpre", "sync", lambda s: s.dma_start(out=prow[:, :], in_=prow_in[:, :]))
        em.dma("dpre", "sync", lambda s: s.dma_start(out=eofs[:, :], in_=eofs_in[:, :]))
        em.g(lambda g: g.memset(ones1_f[:, :], 1.0))
        emit_identity(em, ident)
        ld0 = em.now("dpre")
        em.v(lambda v: v.memset(bases[0:1, 0:E], 0.0), waits=[ld0])
        for c in range(1, NCORES):
            em.v(lambda v, c=c: v.tensor_add(
                bases[0:1, c * E:(c + 1) * E],
                bases[0:1, (c - 1) * E:c * E],
                cntb[0:1, (c - 1) * E:c * E]))
        bases_done = em.now("cv")
        g_all = em.now("cg")

        regs = {}

        def init_regs(g):
            regs["bc_c"] = g.register("bc_c").__enter__()
            g.reg_mov(regs["bc_c"], NCORES * NSLOT - 1)
            regs["bc_x"] = g.register("bc_x").__enter__()
            g.reg_mov(regs["bc_x"], NCORES * TC - 1)
            regs["bc_y"] = g.register("bc_y").__enter__()
            g.reg_mov(regs["bc_y"], NSLOT - 1)
            return None
        em.ops["gpsimd"].append((init_regs, [], ("cg", 0)))

        marks = {"w_free": [None, None], "ye_free": None, "sm_free": None,
                 "xe_free": None}

        for el in range(EPC):
            pb = el % 2
            w_w = [w for w in [marks["w_free"][pb]] if w]
            lw1 = em.dma("dw%d" % pb, "gpsimd", lambda g, el=el, pb=pb: g.dma_start(
                out=w1b[:, pb * 8 * FEXP:(pb + 1) * 8 * FEXP].rearrange(
                    "p (c f) -> p c f", c=8),
                in_=w1T_in[el * D:(el + 1) * D, :].rearrange(
                    "(c p) f -> p c f", p=128)), waits=w_w)
            lw3 = em.dma("dw%d" % pb, "gpsimd", lambda g, el=el, pb=pb: g.dma_start(
                out=w3b[:, pb * 8 * FEXP:(pb + 1) * 8 * FEXP].rearrange(
                    "p (c f) -> p c f", c=8),
                in_=w3T_in[el * D:(el + 1) * D, :].rearrange(
                    "(c p) f -> p c f", p=128)))
            lw2 = em.dma("dw%d" % pb, "gpsimd", lambda g, el=el, pb=pb: g.dma_start(
                out=w2b[:, pb * 4 * D:(pb + 1) * 4 * D].rearrange(
                    "p (c f) -> p c f", c=4),
                in_=w2T_in[el * FEXP:(el + 1) * FEXP, :].rearrange(
                    "(c p) f -> p c f", p=128)))

            def mm_B(t, el=el):
                return t.matmul(ps_b[:, :], ones1_f[:, :],
                                bass.AP(bases, 8 * 0 + el, [[NCORES * E, 1], [E, NCORES]]),
                                start=True, stop=True)
            mmB = em.t(mm_B, waits=[bases_done, g_all])
            em.v(lambda v: v.tensor_copy(Bb[:, :], ps_b[:, :]), waits=[mmB])
            for s in range(8):
                em.v(lambda v, s=s: v.tensor_tensor(
                    gemask[:, :],
                    bass.AP(prow, s, [[8, 128], [0, 8]]),
                    Bb[:, :], ALU.is_ge))
                em.v(lambda v, s=s: v.reduce_sum(cofp[:, s:s + 1], gemask[:, :],
                                                 axis=AX.X))
                em.v(lambda v: v.tensor_mul(gemask[:, :], gemask[:, :], Bb[:, :]))
                em.v(lambda v, s=s: v.reduce_max(bsel[:, s:s + 1], gemask[:, :],
                                                 axis=AX.X))
            em.v(lambda v: v.tensor_scalar(cofp[:, :], cofp[:, :], -1.0, None,
                                           ALU.add))
            em.v(lambda v: v.tensor_sub(loff[:, :], prow[:, :], bsel[:, :]))
            em.v(lambda v, el=el: v.scalar_tensor_tensor(
                gidxf[:, :], cofp[:, :], float(NSLOT),
                slotbase[:, el * 8:(el + 1) * 8], ALU.mult, ALU.add))
            em.v(lambda v: v.tensor_sub(gidxf[:, :], gidxf[:, :], bsel[:, :]))
            em.v(lambda v, el=el: v.scalar_tensor_tensor(
                dyef[:, :], cofp[:, :], float(EPC * CAP), loff[:, :],
                ALU.mult, ALU.add))
            em.v(lambda v, el=el: v.tensor_scalar(
                dyef[:, :], dyef[:, :], float(el * CAP), None, ALU.add))
            ye_guard = [w for w in [marks["ye_free"]] if w]
            em.v(lambda v: v.tensor_copy(gidxi[:, :], gidxf[:, :]),
                 waits=ye_guard)
            em.v(lambda v: v.tensor_copy(dyei[:, :], dyef[:, :]))
            stg = em.now("cv")

            sm_guard = [w for w in [marks["sm_free"]] if w]
            for s in range(8):
                sm_done = em.dma("dsm", "gpsimd", lambda g, s=s: g.indirect_dma_start(
                    out=sm[:, 2 * s:2 * s + 2], out_offset=None,
                    in_=contrib_all[:, :],
                    in_offset=IndirectOffsetOnAxis(gidxi[:, s:s + 1], 0),
                    bounds_check=regs["bc_c"], oob_is_err=False),
                    waits=[stg] + sm_guard)
            em.v(lambda v: v.tensor_copy(
                tokf[:, :], bass.AP(sm, 0, [[16, 128], [2, 8]])),
                waits=[sm_done])
            em.v(lambda v: v.tensor_copy(
                wcol[:, :], bass.AP(sm, 1, [[16, 128], [2, 8]])))
            em.v(lambda v: v.tensor_copy(toki[:, :], tokf[:, :]))
            tok_done = em.now("cv")
            marks["sm_free"] = tok_done

            xe_guard = [w for w in [marks["xe_free"]] if w]
            for s in range(8):
                xe_done = em.dma("dxe", "gpsimd", lambda g, s=s: g.indirect_dma_start(
                    out=xe[:, s * D:(s + 1) * D], out_offset=None,
                    in_=xln_all[:, :],
                    in_offset=IndirectOffsetOnAxis(toki[:, s:s + 1], 0),
                    bounds_check=regs["bc_x"], oob_is_err=False),
                    waits=[tok_done] + xe_guard)

            for s in range(8):
                tr = emit_transpose8(em, psT, xe[:, s * D:(s + 1) * D], ident,
                                     [xe_done, g_all, em.now("cv")])
                em.v(lambda v, s=s: v.tensor_copy(
                    bass.AP(xeT, s * 128, [[8 * D, 128], [1024, 8], [1, 128]]),
                    psT[:, :]), waits=[tr])
            xeT_done = em.now("cv")
            marks["xe_free"] = xeT_done

            wv = [lw2]
            for fc in range(4):
                for half in range(2):
                    def mm_h(t, fc=fc, half=half, pb=pb):
                        inst = None
                        for c in range(8):
                            inst = t.matmul(
                                ps_h[:, :],
                                w1b[:, pb * 8 * FEXP + c * FEXP + fc * 128:
                                    pb * 8 * FEXP + c * FEXP + fc * 128 + 128],
                                bass.AP(xeT, c * CAP + half * 512,
                                        [[8 * D, 128], [1, 512]]),
                                start=(c == 0), stop=(c == 7))
                        return inst
                    mmh = em.t(mm_h, waits=[xeT_done, em.now("cs")] + wv)
                    em.s(lambda s, fc=fc, half=half: s.activation(
                        hg[:, fc * CAP + half * 512:fc * CAP + half * 512 + 512],
                        ps_h[:, :], AF.Silu), waits=[mmh])

                    def mm_u(t, fc=fc, half=half, pb=pb):
                        inst = None
                        for c in range(8):
                            inst = t.matmul(
                                ps_u[:, :],
                                w3b[:, pb * 8 * FEXP + c * FEXP + fc * 128:
                                    pb * 8 * FEXP + c * FEXP + fc * 128 + 128],
                                bass.AP(xeT, c * CAP + half * 512,
                                        [[8 * D, 128], [1, 512]]),
                                start=(c == 0), stop=(c == 7))
                        return inst
                    mmu = em.t(mm_u, waits=[xeT_done, em.now("cv")] + wv)
                    em.v(lambda v, fc=fc, half=half: v.tensor_mul(
                        prodE[:, fc * CAP + half * 512:fc * CAP + half * 512 + 512],
                        ps_u[:, :],
                        hg[:, fc * CAP + half * 512:fc * CAP + half * 512 + 512]),
                        waits=[mmu, em.now("cs")])
            prod_done = em.now("cv")

            for s in range(8):
                for half in range(2):
                    def mm_y(t, s=s, half=half, pb=pb):
                        inst = None
                        for fc in range(4):
                            inst = t.matmul(
                                ps_h[:, :],
                                prodE[:, fc * CAP + s * 128:fc * CAP + s * 128 + 128],
                                w2b[:, pb * 4 * D + fc * D + half * 512:
                                    pb * 4 * D + fc * D + half * 512 + 512],
                                start=(fc == 0), stop=(fc == 3))
                        return inst
                    mmy = em.t(mm_y, waits=[prod_done, lw2, em.now("cs"),
                                            em.now("cv")])
                    em.v(lambda v, s=s, half=half: v.tensor_scalar(
                        yeb[:, s * D + half * 512:s * D + half * 512 + 512],
                        ps_h[:, :], wcol[:, s:s + 1], None, ALU.mult),
                        waits=[mmy])
            ye_done = em.now("cv")
            marks["w_free"][pb] = ye_done

            for s in range(8):
                yemark = em.dma("dye", "gpsimd", lambda g, s=s: g.indirect_dma_start(
                    out=o_ye[:, :],
                    out_offset=IndirectOffsetOnAxis(dyei[:, s:s + 1], 0),
                    in_=yeb[:, s * D:(s + 1) * D], in_offset=None,
                    bounds_check=regs["bc_y"], oob_is_err=False),
                    waits=[ye_done])
            marks["ye_free"] = yemark

        em.barrier_all()
        em.build(block)
    es.close()
    return nc


def launch_B_inputs(core, xln_list, counts_list, contrib_list, w1, w2, w3):
    xln_all = np.concatenate(xln_list, 0)
    contrib_all = np.concatenate(contrib_list, 0)
    cnt = np.stack([c.reshape(E) for c in counts_list], 0)
    e0 = core * EPC
    cnt_rot = np.roll(cnt, -e0, axis=1)
    counts_all = cnt_rot.reshape(1, NCORES * E).astype(np.float32)
    sl = slice(e0, e0 + EPC)
    w1T = np.ascontiguousarray(w1[sl].transpose(0, 2, 1)).reshape(EPC * D, FEXP)
    w3T = np.ascontiguousarray(w3[sl].transpose(0, 2, 1)).reshape(EPC * D, FEXP)
    w2T = np.ascontiguousarray(w2[sl].transpose(0, 2, 1)).reshape(EPC * FEXP, D)
    rows = np.arange(128, dtype=np.float32)
    slotbase = np.zeros((128, E), np.float32)
    for el in range(EPC):
        for s in range(8):
            slotbase[:, el * 8 + s] = (e0 + el) * CAP + s * 128 + rows
    prow = np.zeros((128, 8), np.float32)
    for s in range(8):
        prow[:, s] = s * 128 + rows
    return {
        "xln_all": xln_all, "counts_all": counts_all,
        "contrib_all": contrib_all,
        "w1T": w1T.astype(np.float32), "w3T": w3T.astype(np.float32),
        "w2T": w2T.astype(np.float32),
        "slotbase": slotbase, "prow": prow,
        "eofs": np.array([[float(e0)]], np.float32),
    }


def build_launch_C(nst=TC // 512, upto="all"):
    nc = bass.Bass("TRN2", target_bir_lowering=False)
    TCl = nst * 512
    x_in = nc.dram_tensor("x", [TCl, D], F32, kind="ExternalInput").ap()
    sh_in = nc.dram_tensor("shared", [TCl, D], F32, kind="ExternalInput").ap()
    dest_in = nc.dram_tensor("dest", [TCl, TOPK], F32, kind="ExternalInput").ap()
    yeR = nc.dram_tensor("yeR", [NSLOT + 1, D], BF16, kind="ExternalInput").ap()
    WqT_in = nc.dram_tensor("WqT", [D, D], F32, kind="ExternalInput").ap()
    WkT_in = nc.dram_tensor("WkT", [D, D], F32, kind="ExternalInput").ap()
    WvT_in = nc.dram_tensor("WvT", [D, D], F32, kind="ExternalInput").ap()
    bqc_in = nc.dram_tensor("bqc", [128, 8], F32, kind="ExternalInput").ap()
    bkc_in = nc.dram_tensor("bkc", [128, 8], F32, kind="ExternalInput").ap()
    bv_in = nc.dram_tensor("bv", [1, D], F32, kind="ExternalInput").ap()
    tri_in = nc.dram_tensor("tri", [128, 128], F32, kind="ExternalInput").ap()
    bdm_in = nc.dram_tensor("bdm", [128, 128], F32, kind="ExternalInput").ap()
    hmask_in = nc.dram_tensor("hmask", [128, 2], F32, kind="ExternalInput").ap()

    o_x1 = nc.dram_tensor("x1", [TCl, D], F32, kind="ExternalOutput").ap()
    o_op = nc.dram_tensor("opart", [TCl, D], F32, kind="ExternalOutput").ap()
    o_qf = nc.dram_tensor("qfm", [nst * 128, 8 * 512], BF16,
                          kind="ExternalOutput").ap()
    o_st = nc.dram_tensor("state", [128, 8 * 128], F32,
                          kind="ExternalOutput").ap()

    es = ExitStack()
    sb = lambda name, shape, dt: es.enter_context(nc.sbuf_tensor(name, shape, dt))
    psum = lambda name, shape, dt: es.enter_context(nc.psum_tensor(name, shape, dt))

    WqTb = sb("WqTb", [128, 8 * D], BF16)
    WkTb = sb("WkTb", [128, 8 * D], BF16)
    WvTb = sb("WvTb", [128, 8 * D], BF16)
    bqc = sb("bqc_sb", [128, 8], F32)
    bkc = sb("bkc_sb", [128, 8], F32)
    bvb = sb("bvb", [1, D], BF16)
    trim = sb("trim", [128, 128], F32)
    bdm = sb("bdm_sb", [128, 128], F32)
    hmask = sb("hmask_sb", [128, 2], F32)
    qzb = sb("qzb", [128, 2 * 8 * 512], BF16)
    ident = sb("ident", [128, 128], BF16)
    ones1_b = sb("ones1_b", [1, 128], BF16)

    xt = sb("xt", [128, D], F32)
    sht = sb("sht", [128, D], F32)
    x1t = sb("x1t", [128, D], F32)
    destf = sb("destf_c", [128, TOPK], F32)
    desti = sb("desti_c", [128, TOPK], I32)
    moe8 = sb("moe8", [128, TOPK * D], BF16)
    stats = sb("stats_c", [128, 12], F32)
    mv = sb("mv_c", [128, 2], F32)
    rr = sb("rr_c", [128, 1], F32)
    col = sb("col_c", [128, 1], F32)
    xlnt = sb("xlnt_c", [128, D], BF16)
    xlnT = sb("xlnT_c", [128, 4 * 8 * 128], BF16)
    qfm = sb("qfm_sb", [128, 8 * 512], BF16)
    kfm = sb("kfm_sb", [128, 8 * 512], BF16)
    ktm = sb("ktm_sb", [128, 4 * D], BF16)
    vtm = sb("vtm_sb", [128, 4 * D], BF16)
    act1 = sb("act1", [128, 512], F32)
    act2 = sb("act2", [128, 512], F32)
    sT = sb("sT_sb", [128, 4 * 128], BF16)
    stbf = sb("stbf", [128, 8 * 128], BF16)
    state = sb("state_sb", [128, 8 * 128], F32)
    sustage = sb("sustage", [128, 8 * 128], F32)
    oev = sb("oev", [128, D], F32)

    psT = psum("psT_c", [128, D], BF16)
    ps_a = psum("ps_a", [128, 512], F32)
    ps_b = psum("ps_b", [128, 512], F32)
    ps_o = psum("ps_o", [128, D], F32)
    ps_su = psum("ps_su", [128, 8 * 128], F32)

    with nc.Block() as block:
        em = Em(nc, es)
        chunked_load(em, WqTb, WqT_in, 8, D)
        chunked_load(em, WkTb, WkT_in, 8, D)
        chunked_load(em, WvTb, WvT_in, 8, D)
        em.dma("dpre", "sync", lambda s: s.dma_start(out=bqc[:, :], in_=bqc_in[:, :]))
        em.dma("dpre", "sync", lambda s: s.dma_start(out=bkc[:, :], in_=bkc_in[:, :]))
        em.dma("dpre_g", "gpsimd", lambda g: g.dma_start(out=bvb[:, :], in_=bv_in[:, :]))
        em.dma("dpre", "sync", lambda s: s.dma_start(out=trim[:, :], in_=tri_in[:, :]))
        em.dma("dpre", "sync", lambda s: s.dma_start(out=bdm[:, :], in_=bdm_in[:, :]))
        em.dma("dpre", "sync", lambda s: s.dma_start(out=hmask[:, :], in_=hmask_in[:, :]))
        em.g(lambda g: g.memset(ones1_b[:, :], 1.0))
        em.v(lambda v: v.memset(state[:, :], 0.0))
        emit_identity(em, ident)
        ld_all = em.now("dpre")
        gp_all = em.now("dpre_g")
        g_all = em.now("cg")

        regs = {}

        def init_regs(g):
            regs["bc"] = g.register("bc_ye").__enter__()
            g.reg_mov(regs["bc"], NSLOT)
            return None
        em.ops["gpsimd"].append((init_regs, [], ("cg", 0)))

        marks = {"moe_free": None, "x1_free": None, "xlnT_free": None,
                 "q_free": None, "k_free": None, "v_free": None,
                 "o_free": None, "sT_free": None, "d_free": None,
                 "xln_free": None}

        def W(key):
            return [w for w in [marks[key]] if w]

        for st in range(nst):
            for q in range(4):
                i = st * 4 + q
                ld_x = em.dma("dxx", "sync", lambda s, i=i: s.dma_start(
                    out=xt[:, :], in_=x_in[i * 128:(i + 1) * 128, :]),
                    waits=[em.now("cv")])
                ld_sh = em.dma("dsh", "sync", lambda s, i=i: s.dma_start(
                    out=sht[:, :], in_=sh_in[i * 128:(i + 1) * 128, :]),
                    waits=[em.now("cv")])
                ld_d = em.dma("ddst", "sync", lambda s, i=i: s.dma_start(
                    out=destf[:, :], in_=dest_in[i * 128:(i + 1) * 128, :]),
                    waits=[em.now("cv")])
                em.v(lambda v: v.tensor_scalar(destf[:, :], destf[:, :],
                                               float(NSLOT), None, ALU.min),
                     waits=[ld_d])
                cast = em.v(lambda v: v.tensor_copy(desti[:, :], destf[:, :]),
                            waits=W("d_free"))
                for k in range(TOPK):
                    moe_done = em.dma("dmoe", "gpsimd",
                                      lambda g, k=k: g.indirect_dma_start(
                        out=moe8[:, k * D:(k + 1) * D], out_offset=None,
                        in_=yeR[:, :],
                        in_offset=IndirectOffsetOnAxis(desti[:, k:k + 1], 0),
                        bounds_check=regs["bc"], oob_is_err=False),
                        waits=[cast] + W("moe_free"))
                marks["d_free"] = moe_done
                for half in range(2):
                    def mm_sum(t, half=half):
                        inst = None
                        for k in range(TOPK):
                            inst = t.matmul(
                                ps_a[:, :] if half == 0 else ps_b[:, :],
                                ident[:, :],
                                moe8[:, k * D + half * 512:k * D + half * 512 + 512],
                                start=(k == 0), stop=(k == TOPK - 1))
                        return inst
                    mms = em.t(mm_sum, waits=[moe_done, g_all, em.now("cv")])
                    pshalf = ps_a if half == 0 else ps_b
                    em.v(lambda v, half=half, pshalf=pshalf: v.tensor_add(
                        x1t[:, half * 512:half * 512 + 512], pshalf[:, :],
                        sht[:, half * 512:half * 512 + 512]),
                        waits=[mms, ld_sh] + (W("x1_free") if half == 0 else []))
                    em.v(lambda v, half=half: v.scalar_tensor_tensor(
                        x1t[:, half * 512:half * 512 + 512],
                        x1t[:, half * 512:half * 512 + 512], 0.5,
                        xt[:, half * 512:half * 512 + 512],
                        ALU.mult, ALU.add), waits=[ld_x])
                marks["moe_free"] = em.now("ct")
                x1_done = em.now("cv")
                st_x1 = em.dma("sx1", "sync", lambda s, i=i: s.dma_start(
                    out=o_x1[i * 128:(i + 1) * 128, :], in_=x1t[:, :]),
                    waits=[x1_done])
                marks["x1_free"] = st_x1
                ln_done = emit_ln(em, x1t[:, :], xlnt[:, :], stats, mv, col, rr,
                                  [x1_done] + W("xln_free"))
                tr = emit_transpose8(em, psT, xlnt[:, :], ident,
                                     [ln_done, g_all] + W("xlnT_free"))
                marks["xln_free"] = tr
                em.v(lambda v, q=q: v.tensor_copy(
                    bass.AP(xlnT, q * 128, [[4 * 8 * 128, 128], [512, 8], [1, 128]]),
                    psT[:, :]), waits=[tr])
            xlnT_done = em.now("cv")
            if upto == "combine":
                continue

            for which, Wb, bcol, dst in (("q", WqTb, bqc, qfm), ("k", WkTb, bkc, kfm)):
                for c in range(8):
                    def mm_qk(t, Wb=Wb, c=c):
                        inst = None
                        for dc in range(8):
                            inst = t.matmul(
                                ps_a[:, :],
                                Wb[:, dc * D + c * 128:dc * D + c * 128 + 128],
                                bass.AP(xlnT, dc * 512, [[4096, 128], [1, 512]]),
                                start=(dc == 0), stop=(dc == 7))
                        return inst
                    mm1 = em.t(mm_qk, waits=[xlnT_done, gp_all, ld_all,
                                             em.now("cv"), em.now("cs")])
                    em.v(lambda v, bcol=bcol, c=c: v.tensor_scalar(
                        act1[:, :], ps_a[:, :], bcol[:, c:c + 1], 0.0,
                        ALU.add, ALU.max), waits=[mm1] + (W("q_free") if which == "q" else W("k_free")))
                    em.v(lambda v, bcol=bcol, c=c: v.scalar_tensor_tensor(
                        act2[:, :], ps_a[:, :], bcol[:, c:c + 1], act1[:, :],
                        ALU.add, ALU.subtract))
                    sc_exp = em.s(lambda s: s.activation(act2[:, :], act2[:, :],
                                                         AF.Exp),
                                  waits=[em.now("cv")])
                    em.v(lambda v, dst=dst, c=c: v.tensor_add(
                        dst[:, c * 512:(c + 1) * 512], act1[:, :], act2[:, :]),
                        waits=[sc_exp])
            for c in range(8):
                for par in range(2):
                    em.v(lambda v, c=c, par=par: v.tensor_scalar(
                        qzb[:, c * 1024 + par * 512:c * 1024 + par * 512 + 512],
                        qfm[:, c * 512:(c + 1) * 512],
                        hmask[:, par:par + 1], None, ALU.mult))
            q_done = em.now("cv")
            st_qf = em.dma("sqf", "sync", lambda s, st=st: s.dma_start(
                out=o_qf[st * 128:(st + 1) * 128, :], in_=qfm[:, :]),
                waits=[q_done])
            for q in range(4):
                for half in range(2):
                    def mm_v(t, q=q, half=half):
                        for dc in range(8):
                            t.matmul(ps_b[:, :],
                                     bass.AP(xlnT, dc * 512 + q * 128,
                                             [[4096, 128], [1, 128]]),
                                     WvTb[:, dc * D + half * 512:
                                          dc * D + half * 512 + 512],
                                     start=(dc == 0), stop=False)
                        return t.matmul(ps_b[:, :], ones1_b[:, :],
                                        bvb[:, half * 512:half * 512 + 512],
                                        start=False, stop=True)
                    mmv = em.t(mm_v, waits=[xlnT_done, gp_all, g_all,
                                            em.now("cv")])
                    em.v(lambda v, q=q, half=half: v.tensor_copy(
                        vtm[:, q * D + half * 512:q * D + half * 512 + 512],
                        ps_b[:, :]), waits=[mmv] + (W("v_free") if q == 0 and half == 0 else []))
            v_done = em.now("cv")
            for q in range(4):
                def tr_k(t, q=q):
                    inst = None
                    for c in range(8):
                        inst = t.transpose(
                            psT[:, c * 128:(c + 1) * 128],
                            kfm[:, c * 512 + q * 128:c * 512 + q * 128 + 128],
                            ident[:, :])
                    return inst
                trk = em.t(tr_k, waits=[q_done, em.now("cv")])
                em.v(lambda v, q=q: v.tensor_copy(ktm[:, q * D:(q + 1) * D],
                                                  psT[:, :]), waits=[trk])
            k_done = em.now("cv")
            marks["xlnT_free"] = em.now("ct")
            if upto == "qkv":
                continue

            for q in range(4):
                i = st * 4 + q
                em.v(lambda v: v.tensor_copy(stbf[:, :], state[:, :]),
                     waits=[em.now("ct")])
                stbf_done = em.now("cv")
                for hg in range(4):
                    def mm_sT2(t, hg=hg, q=q):
                        inst = None
                        for hh in range(4):
                            h = hg * 4 + hh
                            c = h // 2
                            par = h % 2
                            lhs = kfm[:, c * 512 + q * 128:
                                      c * 512 + q * 128 + 128]
                            rhs = qzb[:, c * 1024 + par * 512 + q * 128:
                                      c * 1024 + par * 512 + q * 128 + 128]
                            inst = t.matmul(ps_a[:, hh * 128:(hh + 1) * 128],
                                            lhs, rhs, start=True, stop=True)
                        return inst
                    mmst = em.t(mm_sT2, waits=[q_done, k_done, em.now("cv")])
                    em.v(lambda v, hg=hg: v.tensor_tensor(
                        sT[:, :], ps_a[:, :],
                        bass.AP(trim, 0, [[128, 128], [0, 4], [1, 128]]),
                        ALU.mult), waits=[mmst] + W("sT_free"))
                    sT_done = em.now("cv")
                    def mm_o(t, hg=hg, q=q):
                        inst = None
                        for hh in range(4):
                            h = hg * 4 + hh
                            inst = t.matmul(
                                ps_o[:, h * 64:(h + 1) * 64],
                                sT[:, hh * 128:(hh + 1) * 128],
                                vtm[:, q * D + h * 64:q * D + h * 64 + 64],
                                start=(h % 8 == 0), stop=False)
                        return inst
                    if upto in ("all", "o"):
                        em.t(mm_o, waits=[sT_done, v_done])
                    marks["sT_free"] = em.now("ct")
                def mm_sr(t, q=q):
                    inst = None
                    for c in range(8):
                        inst = t.matmul(
                            ps_o[:, c * 128:(c + 1) * 128],
                            qfm[:, c * 512 + q * 128:c * 512 + q * 128 + 128],
                            stbf[:, c * 128:(c + 1) * 128],
                            start=False, stop=(c == 3 or c == 7))
                    return inst
                if upto in ("all", "o"):
                    mmsr = em.t(mm_sr, waits=[stbf_done, q_done])
                    ev1 = em.v(lambda v: v.tensor_copy(oev[:, 0:512],
                                                       ps_o[:, 0:512]),
                               waits=[mmsr] + W("o_free"))
                    ev2 = em.v(lambda v: v.tensor_copy(oev[:, 512:1024],
                                                       ps_o[:, 512:1024]))
                    st_o = em.dma("sop", "sync", lambda s, i=i: s.dma_start(
                        out=o_op[i * 128:(i + 1) * 128, :], in_=oev[:, :]),
                        waits=[em.now("cv")])
                    marks["o_free"] = st_o
                def mm_su(t, q=q):
                    inst = None
                    for c in range(8):
                        inst = t.matmul(
                            ps_su[:, c * 128:(c + 1) * 128],
                            ktm[:, q * D + c * 128:q * D + c * 128 + 128],
                            vtm[:, q * D + c * 128:q * D + c * 128 + 128],
                            start=True, stop=True)
                    return inst
                if upto == "all":
                    mmsu = em.t(mm_su, waits=[k_done, v_done, em.now("cv")])
                    em.v(lambda v: v.tensor_tensor(
                        sustage[:, :], ps_su[:, :],
                        bass.AP(bdm, 0, [[128, 128], [0, 8], [1, 128]]),
                        ALU.mult), waits=[mmsu])
                    em.v(lambda v: v.tensor_add(state[:, :], state[:, :],
                                                sustage[:, :]))
            marks["q_free"] = em.now("ct")
            marks["k_free"] = em.now("ct")
            marks["v_free"] = em.now("ct")

        em.dma("sst", "sync", lambda s: s.dma_start(
            out=o_st[:, :], in_=state[:, :]), waits=[em.now("cv")])
        em.barrier_all()
        em.build(block)
    es.close()
    return nc


def build_launch_D(nst=TC // 512):
    nc = bass.Bass("TRN2", target_bir_lowering=False)
    TCl = nst * 512
    x1_in = nc.dram_tensor("x1", [TCl, D], F32, kind="ExternalInput").ap()
    op_in = nc.dram_tensor("opart", [TCl, D], F32, kind="ExternalInput").ap()
    qf_in = nc.dram_tensor("qfm", [nst * 128, 8 * 512], BF16,
                           kind="ExternalInput").ap()
    sp_in = nc.dram_tensor("Spart", [128, 8 * 128], F32,
                           kind="ExternalInput").ap()
    lng_in = nc.dram_tensor("ln_g", [128, 8], F32, kind="ExternalInput").ap()
    lnb_in = nc.dram_tensor("ln_b", [128, 8], F32, kind="ExternalInput").ap()
    WoT_in = nc.dram_tensor("WoT", [D, D], F32, kind="ExternalInput").ap()
    bo_in = nc.dram_tensor("bo", [1, D], F32, kind="ExternalInput").ap()
    o_x2 = nc.dram_tensor("x2", [TCl, D], F32, kind="ExternalOutput").ap()

    es = ExitStack()
    sb = lambda name, shape, dt: es.enter_context(nc.sbuf_tensor(name, shape, dt))
    psum = lambda name, shape, dt: es.enter_context(nc.psum_tensor(name, shape, dt))

    WoTb = sb("WoTb", [128, 8 * D], BF16)
    lngc = sb("lngc", [128, 8], F32)
    lnbc = sb("lnbc", [128, 8], BF16)
    bWo = sb("bWo", [1, D], BF16)
    bob = sb("bob", [1, D], BF16)
    ones1_b = sb("ones1_b_d", [1, 128], BF16)
    ident = sb("ident_d", [128, 128], BF16)
    spb = sb("spb", [128, 8 * 128], BF16)
    qfb = sb("qfb", [128, 8 * 512], BF16)
    x1t = sb("x1t_d", [128, D], F32)
    opt = sb("opt_d", [128, D], F32)
    ot = sb("ot_d", [128, D], F32)
    olnt = sb("olnt_d", [128, D], BF16)
    olnT = sb("olnT_d", [128, D], BF16)
    stats = sb("stats_d", [128, 12], F32)
    mv = sb("mv_d", [128, 2], F32)
    rr = sb("rr_d", [128, 1], F32)
    col = sb("col_d", [128, 1], F32)
    x2t = sb("x2t", [128, D], F32)
    psT = psum("psT_d", [128, D], BF16)
    ps_o = psum("ps_o_d", [128, D], F32)
    ps_p = psum("ps_p_d", [128, 512], F32)
    ps_r = psum("ps_r_d", [1, D], F32)

    with nc.Block() as block:
        em = Em(nc, es)
        chunked_load(em, WoTb, WoT_in, 8, D)
        em.dma("dpre", "sync", lambda s: s.dma_start(
            out=lngc[:, :], in_=lng_in[:, :]))
        em.dma("dpre_g", "gpsimd", lambda g: g.dma_start(
            out=lnbc[:, :], in_=lnb_in[:, :]))
        em.dma("dpre_g", "gpsimd", lambda g: g.dma_start(
            out=spb[:, :], in_=sp_in[:, :]))
        em.dma("dpre_g", "gpsimd", lambda g: g.dma_start(
            out=bob[:, :], in_=bo_in[:, :]))
        em.g(lambda g: g.memset(ones1_b[:, :], 1.0))
        emit_identity(em, ident)
        ld_all = em.now("dpre")
        gp_all = em.now("dpre_g")
        g_all = em.now("cg")

        def mm_bw(t):
            inst = None
            for half in range(2):
                for c in range(8):
                    inst = t.matmul(ps_r[:, half * 512:half * 512 + 512],
                                    lnbc[:, c:c + 1],
                                    WoTb[:, c * D + half * 512:
                                         c * D + half * 512 + 512],
                                    start=(c == 0), stop=(c == 7))
            return inst
        mmbw = em.t(mm_bw, waits=[ld_all, gp_all, g_all])
        em.v(lambda v: v.tensor_copy(bWo[:, :], ps_r[:, :]), waits=[mmbw])
        for c in range(8):
            em.s(lambda s, c=c: s.mul(WoTb[:, c * D:(c + 1) * D],
                                      WoTb[:, c * D:(c + 1) * D],
                                      lngc[:, c:c + 1]),
                 waits=([mmbw] if c == 0 else []))
        wos_done = em.now("cs")

        marks = {"x2_free": None, "oln_free": None, "q_free": None}

        def W(key):
            return [w for w in [marks[key]] if w]

        for st in range(nst):
            ld_q = em.dma("dqf", "sync", lambda s, st=st: s.dma_start(
                out=qfb[:, :], in_=qf_in[st * 128:(st + 1) * 128, :]),
                waits=W("q_free"))
            for q in range(4):
                i = st * 4 + q
                ld_x1 = em.dma("dx1", "sync", lambda s, i=i: s.dma_start(
                    out=x1t[:, :], in_=x1_in[i * 128:(i + 1) * 128, :]),
                    waits=[em.now("cv")])
                ld_op = em.dma("dop", "sync", lambda s, i=i: s.dma_start(
                    out=opt[:, :], in_=op_in[i * 128:(i + 1) * 128, :]),
                    waits=[em.now("cv")])
                def mm_oc(t, q=q):
                    inst = None
                    for c in range(8):
                        inst = t.matmul(
                            ps_o[:, c * 128:(c + 1) * 128],
                            qfb[:, c * 512 + q * 128:c * 512 + q * 128 + 128],
                            spb[:, c * 128:(c + 1) * 128],
                            start=(c % 4 == 0), stop=(c == 3 or c == 7))
                    return inst
                mmoc = em.t(mm_oc, waits=[ld_q, gp_all, em.now("cv")])
                em.v(lambda v: v.tensor_add(ot[:, 0:512], ps_o[:, 0:512],
                                            opt[:, 0:512]),
                     waits=[mmoc, ld_op])
                em.v(lambda v: v.tensor_add(ot[:, 512:1024], ps_o[:, 512:1024],
                                            opt[:, 512:1024]))
                o_done = em.now("cv")
                ln_done = emit_ln(em, ot[:, :], olnt[:, :], stats, mv, col, rr,
                                  [o_done] + W("oln_free"))
                tr = emit_transpose8(em, psT, olnt[:, :], ident,
                                     [ln_done, g_all])
                marks["oln_free"] = tr
                evT = em.v(lambda v: v.tensor_copy(olnT[:, :], psT[:, :]),
                           waits=[tr])
                for half in range(2):
                    def mm_p(t, half=half):
                        for c in range(8):
                            t.matmul(ps_p[:, :],
                                     olnT[:, c * 128:(c + 1) * 128],
                                     WoTb[:, c * D + half * 512:
                                          c * D + half * 512 + 512],
                                     start=(c == 0), stop=False)
                        t.matmul(ps_p[:, :], ones1_b[:, :],
                                 bWo[:, half * 512:half * 512 + 512],
                                 start=False, stop=False)
                        return t.matmul(ps_p[:, :], ones1_b[:, :],
                                        bob[:, half * 512:half * 512 + 512],
                                        start=False, stop=True)
                    mmp = em.t(mm_p, waits=[evT, wos_done, em.now("cv")])
                    em.v(lambda v, half=half: v.tensor_add(
                        x2t[:, half * 512:half * 512 + 512], ps_p[:, :],
                        x1t[:, half * 512:half * 512 + 512]),
                        waits=[mmp, ld_x1] + (W("x2_free") if half == 0 else []))
                st_x2 = em.dma("sx2", "sync", lambda s, i=i: s.dma_start(
                    out=o_x2[i * 128:(i + 1) * 128, :], in_=x2t[:, :]),
                    waits=[em.now("cv")])
                marks["x2_free"] = st_x2
            marks["q_free"] = em.now("ct")

        em.barrier_all()
        em.build(block)
    es.close()
    return nc


def emit_combine_tile(em, i, x_src, sh_src, dest_src, yeR, bufs, regs, marks, W,
                      out_x=None, coef=0.5):
    xt, sht, destf, desti = bufs["xt"], bufs["sht"], bufs["destf"], bufs["desti"]
    moe8, x1t, ident = bufs["moe8"], bufs["x1t"], bufs["ident"]
    ps_a, ps_b = bufs["ps_a"], bufs["ps_b"]
    ld_x = em.dma("dxx", "sync", lambda s: s.dma_start(
        out=xt[:, :], in_=x_src[i * 128:(i + 1) * 128, :]),
        waits=[em.now("cv")])
    ld_sh = em.dma("dsh", "sync", lambda s: s.dma_start(
        out=sht[:, :], in_=sh_src[i * 128:(i + 1) * 128, :]),
        waits=[em.now("cv")])
    ld_d = em.dma("ddst", "sync", lambda s: s.dma_start(
        out=destf[:, :], in_=dest_src[i * 128:(i + 1) * 128, :]),
        waits=[em.now("cv")])
    em.v(lambda v: v.tensor_scalar(destf[:, :], destf[:, :], float(NSLOT),
                                   None, ALU.min), waits=[ld_d])
    cast = em.v(lambda v: v.tensor_copy(desti[:, :], destf[:, :]),
                waits=W("d_free"))
    for k in range(TOPK):
        moe_done = em.dma("dmoe", "gpsimd", lambda g, k=k: g.indirect_dma_start(
            out=moe8[:, k * D:(k + 1) * D], out_offset=None,
            in_=yeR[:, :],
            in_offset=IndirectOffsetOnAxis(desti[:, k:k + 1], 0),
            bounds_check=regs["bc"], oob_is_err=False),
            waits=[cast] + W("moe_free"))
    marks["d_free"] = moe_done
    for half in range(2):
        def mm_sum(t, half=half):
            inst = None
            for k in range(TOPK):
                inst = t.matmul(
                    ps_a[:, :] if half == 0 else ps_b[:, :], ident[:, :],
                    moe8[:, k * D + half * 512:k * D + half * 512 + 512],
                    start=(k == 0), stop=(k == TOPK - 1))
            return inst
        mms = em.t(mm_sum, waits=[moe_done, em.now("cg"), em.now("cv")])
        pshalf = ps_a if half == 0 else ps_b
        em.v(lambda v, half=half, pshalf=pshalf: v.tensor_add(
            x1t[:, half * 512:half * 512 + 512], pshalf[:, :],
            sht[:, half * 512:half * 512 + 512]),
            waits=[mms, ld_sh] + (W("x1_free") if half == 0 else []))
        em.v(lambda v, half=half: v.scalar_tensor_tensor(
            x1t[:, half * 512:half * 512 + 512],
            x1t[:, half * 512:half * 512 + 512], coef,
            xt[:, half * 512:half * 512 + 512],
            ALU.mult, ALU.add), waits=[ld_x])
    marks["moe_free"] = em.now("ct")
    x1_done = em.now("cv")
    if out_x is not None:
        st_x1 = em.dma("sx1", "sync", lambda s: s.dma_start(
            out=out_x[i * 128:(i + 1) * 128, :], in_=x1t[:, :]),
            waits=[x1_done])
        marks["x1_free"] = st_x1
    return x1_done


def build_launch_FFN(nst=TC // 512, final=False):
    nc = bass.Bass("TRN2", target_bir_lowering=False)
    TCl = nst * 512
    x_in = nc.dram_tensor("x", [TCl, D], F32, kind="ExternalInput").ap()
    w1_in = nc.dram_tensor("w1", [HF, D], F32, kind="ExternalInput").ap()
    w1T_in = nc.dram_tensor("w1T", [D, HF], F32, kind="ExternalInput").ap()
    w2_in = nc.dram_tensor("w2", [D, HF], F32, kind="ExternalInput").ap()
    w2T_in = nc.dram_tensor("w2T", [HF, D], F32, kind="ExternalInput").ap()
    b1c_in = nc.dram_tensor("b1c", [128, HF // 128], F32,
                            kind="ExternalInput").ap()
    b2c_in = nc.dram_tensor("b2c", [128, 8], F32, kind="ExternalInput").ap()
    o_x = nc.dram_tensor("xo", [TCl, D], F32, kind="ExternalOutput").ap()

    es = ExitStack()
    sb = lambda name, shape, dt: es.enter_context(nc.sbuf_tensor(name, shape, dt))
    psum = lambda name, shape, dt: es.enter_context(nc.psum_tensor(name, shape, dt))

    NH = HF // 128
    ident = sb("ident_n", [128, 128], BF16)
    s1Tb = sb("s1Tb", [128, 8 * HF], BF16)
    s2Tb = sb("s2Tb", [128, NH * D], BF16)
    al1 = sb("al1", [128, NH], F32)
    al2 = sb("al2", [128, 8], F32)
    b1c = sb("b1c_sb", [128, NH], F32)
    b2c = sb("b2c_sb", [128, 8], F32)
    xA = sb("xA", [128, 4 * D], F32)
    wscr = xA
    stats = sb("stats_n", [128, 12], F32)
    mv = sb("mv_n", [128, 2], F32)
    rr = sb("rr_n", [128, 1], F32)
    col = sb("col_n", [128, 1], F32)
    xlnt = sb("xlnt_n", [128, D], BF16)
    xlnT = sb("xlnT_n", [128, 4096], BF16)
    hfm = sb("hfm", [128, NH * 256], BF16)
    outfm = sb("outfm", [128, 8 * 256], BF16)
    x3ff = sb("x3ff", [128, D], BF16)
    xot = sb("xot", [128, D], F32)
    psT = psum("psT_n", [128, D], BF16)
    ps_h = psum("ps_h_n", [128, 512], F32)

    with nc.Block() as block:
        em = Em(nc, es)
        em.dma("dpre", "sync", lambda s: s.dma_start(out=b1c[:, :],
                                                     in_=b1c_in[:, :]))
        em.dma("dpre", "sync", lambda s: s.dma_start(out=b2c[:, :],
                                                     in_=b2c_in[:, :]))
        emit_identity(em, ident)
        g_all = em.now("cg")

        for ti in range(NH):
            ldw = em.dma("dwn%d" % (ti % 2), "sync", lambda s, ti=ti: s.dma_start(
                out=wscr[:, (ti % 2) * D:(ti % 2 + 1) * D],
                in_=w1_in[ti * 128:(ti + 1) * 128, :]), waits=[em.now("cs")])
            em.s(lambda s, ti=ti: s.activation(
                wscr[:, (ti % 2) * D:(ti % 2 + 1) * D],
                wscr[:, (ti % 2) * D:(ti % 2 + 1) * D],
                AF.Abs, accum_out=al1[:, ti:ti + 1]), waits=[ldw])
        em.v(lambda v: v.tensor_scalar(al1[:, :], al1[:, :], 1.0 / D, None,
                                       ALU.mult), waits=[em.now("cs")])
        for c in range(8):
            ldt = em.dma("dwt", "sync", lambda s, c=c: s.dma_start(
                out=wscr[:, :], in_=w1T_in[c * 128:(c + 1) * 128, :]),
                waits=[em.now("cs")])
            em.s(lambda s, c=c: s.activation(
                s1Tb[:, c * HF:(c + 1) * HF], wscr[:, :], AF.Sign),
                waits=[ldt])
        for ti in range(8):
            ldw = em.dma("dwt", "sync", lambda s, ti=ti: s.dma_start(
                out=wscr[:, :], in_=w2_in[ti * 128:(ti + 1) * 128, :]),
                waits=[em.now("cs")])
            em.s(lambda s, ti=ti: s.activation(
                wscr[:, :], wscr[:, :],
                AF.Abs, accum_out=al2[:, ti:ti + 1]), waits=[ldw])
        em.v(lambda v: v.tensor_scalar(al2[:, :], al2[:, :], 1.0 / HF, None,
                                       ALU.mult), waits=[em.now("cs")])
        for c in range(NH):
            ldt = em.dma("dwn%d" % (c % 2), "sync", lambda s, c=c: s.dma_start(
                out=wscr[:, (c % 2) * D:(c % 2 + 1) * D],
                in_=w2T_in[c * 128:(c + 1) * 128, :]), waits=[em.now("cs")])
            em.s(lambda s, c=c: s.activation(
                s2Tb[:, c * D:(c + 1) * D],
                wscr[:, (c % 2) * D:(c % 2 + 1) * D], AF.Sign), waits=[ldt])
        prep_done = em.now("cs")
        al_done = em.now("cv")

        marks = {"xlnT_free": None, "xln_free": None, "h_free": None,
                 "out_free": None, "xo_free": None, "xA_free": None}

        def W(key):
            return [w for w in [marks[key]] if w]

        for st in range(nst):
            for q in range(4):
                i = st * 4 + q
                ld_x = em.dma("dxx", "sync", lambda s, i=i, q=q: s.dma_start(
                    out=xA[:, q * D:(q + 1) * D],
                    in_=x_in[i * 128:(i + 1) * 128, :]),
                    waits=[em.now("cv"), em.now("ct")] + W("xA_free"))
                ln_done = emit_ln(em, xA[:, q * D:(q + 1) * D], xlnt[:, :],
                                  stats, mv, col, rr,
                                  [ld_x, prep_done] + W("xln_free"))
                tr = emit_transpose8(em, psT, xlnt[:, :], ident,
                                     [ln_done, g_all] + W("xlnT_free"))
                marks["xln_free"] = tr
                em.v(lambda v, q=q: v.tensor_copy(
                    bass.AP(xlnT, q * 128, [[4096, 128], [512, 8], [1, 128]]),
                    psT[:, :]), waits=[tr])
            xlnT_done = em.now("cv")

            for hs in range(2):
                for hc in range(NH):
                    def mm_h(t, hc=hc, hs=hs):
                        inst = None
                        for dc in range(8):
                            inst = t.matmul(
                                ps_h[:, 0:256],
                                s1Tb[:, dc * HF + hc * 128:dc * HF + hc * 128 + 128],
                                bass.AP(xlnT, dc * 512 + hs * 256,
                                        [[4096, 128], [1, 256]]),
                                start=(dc == 0), stop=(dc == 7))
                        return inst
                    mmh = em.t(mm_h, waits=[xlnT_done, em.now("cs")])
                    em.s(lambda s, hc=hc: s.activation(
                        hfm[:, hc * 256:(hc + 1) * 256], ps_h[:, 0:256],
                        AF.Gelu, bias=b1c[:, hc:hc + 1],
                        scale=al1[:, hc:hc + 1]),
                        waits=[mmh, al_done] + (W("h_free") if hc == 0 else []))
                h_done = em.now("cs")

                for dc in range(8):
                    def mm_o(t, dc=dc):
                        inst = None
                        for hc in range(NH):
                            inst = t.matmul(
                                ps_h[:, 0:256],
                                s2Tb[:, hc * D + dc * 128:hc * D + dc * 128 + 128],
                                hfm[:, hc * 256:(hc + 1) * 256],
                                start=(hc == 0), stop=(hc == NH - 1))
                        return inst
                    mmo = em.t(mm_o, waits=[h_done, em.now("cv")])
                    em.v(lambda v, dc=dc: v.tensor_scalar(
                        outfm[:, dc * 256:(dc + 1) * 256], ps_h[:, 0:256],
                        al2[:, dc:dc + 1], b2c[:, dc:dc + 1],
                        ALU.mult, ALU.add),
                        waits=[mmo] + (W("out_free") if dc == 0 else []))
                out_done = em.now("cv")
                marks["h_free"] = em.now("ct")

                for qq in range(2):
                    q = hs * 2 + qq
                    i = st * 4 + q
                    def tr_o(t, qq=qq):
                        inst = None
                        for dc in range(8):
                            inst = t.transpose(
                                psT[:, dc * 128:(dc + 1) * 128],
                                outfm[:, dc * 256 + qq * 128:
                                      dc * 256 + qq * 128 + 128],
                                ident[:, :])
                        return inst
                    tro = em.t(tr_o, waits=[out_done, em.now("cv")])
                    em.v(lambda v: v.tensor_copy(x3ff[:, :], psT[:, :]),
                         waits=[tro])
                    em.v(lambda v, q=q: v.scalar_tensor_tensor(
                        xot[:, :], x3ff[:, :], 0.5, xA[:, q * D:(q + 1) * D],
                        ALU.mult, ALU.add), waits=W("xo_free"))
                    st_xo = em.dma("sxo", "sync", lambda s, i=i: s.dma_start(
                        out=o_x[i * 128:(i + 1) * 128, :], in_=xot[:, :]),
                        waits=[em.now("cv")])
                    marks["xo_free"] = st_xo
                marks["out_free"] = em.now("ct")
            marks["xA_free"] = em.now("cv")

        em.barrier_all()
        em.build(block)
    es.close()
    return nc


def build_launch_WATT(nst=TC // 512):
    nc = bass.Bass("TRN2", target_bir_lowering=False)
    TCl = nst * 512
    x_in = nc.dram_tensor("x", [TCl, D], F32, kind="ExternalInput").ap()
    wqkv_in = nc.dram_tensor("w_qkv", [D, 3 * D], F32, kind="ExternalInput").ap()
    wout_in = nc.dram_tensor("w_out", [D, D], F32, kind="ExternalInput").ap()
    bout_in = nc.dram_tensor("b_out", [1, D], F32, kind="ExternalInput").ap()
    wm_in = nc.dram_tensor("wmask", [128, 256], F32, kind="ExternalInput").ap()
    o_x4 = nc.dram_tensor("x4p", [TCl, D], F32, kind="ExternalOutput").ap()
    o_q0 = nc.dram_tensor("q0fm", [128, 8 * 128], BF16, kind="ExternalOutput").ap()
    o_k0 = nc.dram_tensor("k0fm", [128, 8 * 128], BF16, kind="ExternalOutput").ap()
    o_v0 = nc.dram_tensor("v0tm", [128, D], BF16, kind="ExternalOutput").ap()
    o_kL = nc.dram_tensor("kLfm", [128, 8 * 128], BF16, kind="ExternalOutput").ap()
    o_vL = nc.dram_tensor("vLtm", [128, D], BF16, kind="ExternalOutput").ap()

    es = ExitStack()
    sb = lambda name, shape, dt: es.enter_context(nc.sbuf_tensor(name, shape, dt))
    psum = lambda name, shape, dt: es.enter_context(nc.psum_tensor(name, shape, dt))

    ident = sb("ident_w", [128, 128], BF16)
    ones1_b = sb("ones1_w", [1, 128], BF16)
    wqkvb = sb("wqkvb", [128, 8 * 3 * D], BF16)
    woutb = sb("woutb", [128, 8 * D], BF16)
    boutb = sb("boutb", [1, D], BF16)
    wmb = sb("wmb", [128, 256], F32)
    xA = sb("xA_w", [128, 4 * D], F32)
    stats = sb("stats_w", [128, 12], F32)
    mv = sb("mv_w", [128, 2], F32)
    rr = sb("rr_w", [128, 1], F32)
    col = sb("col_w", [128, 1], F32)
    negmax = sb("negmax_w", [128, 1], F32)
    rden = sb("rden_w", [128, 1], F32)
    xlnt = sb("xlnt_w", [128, D], BF16)
    xlnT = sb("xlnT_w", [128, 4096], BF16)
    qfm = sb("qfm_w", [128, 8 * 512], BF16)
    kfm = sb("kfm_w", [128, 8 * 512], BF16)
    kring = sb("kring", [128, 2 * 8 * 128], BF16)
    vring = sb("vring", [128, 2 * D], BF16)
    ssb = sb("ssb_w", [128, 256], F32)
    eS = sb("eS_w", [128, 256], BF16)
    eST = sb("eST_w", [128, 256], BF16)
    attn = sb("attn_w", [128, D], BF16)
    attnT = sb("attnT_w", [128, D], BF16)
    ebuf = sb("ebuf_w", [128, D], BF16)
    x4t = sb("x4t_w", [128, D], F32)
    psT = psum("psT_w", [128, D], BF16)
    ps_s = psum("ps_s_w", [128, 512], F32)
    ps_a = psum("ps_a_w", [128, 512], F32)
    ps_b = psum("ps_b_w", [128, 512], F32)

    with nc.Block() as block:
        em = Em(nc, es)
        chunked_load(em, wqkvb, wqkv_in, 8, 3 * D)
        chunked_load(em, woutb, wout_in, 8, D)
        em.dma("dpre_g", "gpsimd", lambda g: g.dma_start(out=boutb[:, :],
                                                         in_=bout_in[:, :]))
        em.dma("dpre", "sync", lambda s: s.dma_start(out=wmb[:, :],
                                                     in_=wm_in[:, :]))
        em.g(lambda g: g.memset(ones1_b[:, :], 1.0))
        emit_identity(em, ident)
        ld_all = em.now("dpre")
        gp_all = em.now("dpre_g")
        g_all = em.now("cg")

        marks = {"xlnT_free": None, "xln_free": None, "q_free": None,
                 "k_free": None, "ring_free": None, "x4_free": None,
                 "xA_free": None, "eS_free": None, "attn_free": None}

        def W(key):
            return [w for w in [marks[key]] if w]

        for st in range(nst):
            for q in range(4):
                i = st * 4 + q
                ld_x = em.dma("dxx", "sync", lambda s, i=i, q=q: s.dma_start(
                    out=xA[:, q * D:(q + 1) * D],
                    in_=x_in[i * 128:(i + 1) * 128, :]),
                    waits=[em.now("cv")] + W("xA_free"))
                ln_done = emit_ln(em, xA[:, q * D:(q + 1) * D], xlnt[:, :],
                                  stats, mv, col, rr,
                                  [ld_x] + W("xln_free"))
                tr = emit_transpose8(em, psT, xlnt[:, :], ident,
                                     [ln_done, g_all] + W("xlnT_free"))
                marks["xln_free"] = tr
                em.v(lambda v, q=q: v.tensor_copy(
                    bass.AP(xlnT, q * 128, [[4096, 128], [512, 8], [1, 128]]),
                    psT[:, :]), waits=[tr])
            xlnT_done = em.now("cv")

            for which, off, dst in (("q", 0, qfm), ("k", D, kfm)):
                for c in range(8):
                    def mm_qk(t, off=off, c=c):
                        inst = None
                        for dc in range(8):
                            inst = t.matmul(
                                ps_s[:, :],
                                wqkvb[:, dc * 3 * D + off + c * 128:
                                      dc * 3 * D + off + c * 128 + 128],
                                bass.AP(xlnT, dc * 512, [[4096, 128], [1, 512]]),
                                start=(dc == 0), stop=(dc == 7))
                        return inst
                    guard = W("q_free") if which == "q" else W("k_free")
                    mm1 = em.t(mm_qk, waits=[xlnT_done, gp_all,
                                             em.now("cv")])
                    em.v(lambda v, dst=dst, c=c: v.tensor_copy(
                        dst[:, c * 512:(c + 1) * 512], ps_s[:, :]),
                        waits=[mm1] + (guard if c == 0 else []))
            qk_done = em.now("cv")

            for q in range(4):
                i = st * 4 + q
                rp = i % 2
                for half in range(2):
                    def mm_v(t, q=q, half=half):
                        inst = None
                        for dc in range(8):
                            inst = t.matmul(
                                ps_b[:, :],
                                bass.AP(xlnT, dc * 512 + q * 128,
                                        [[4096, 128], [1, 128]]),
                                wqkvb[:, dc * 3 * D + D + half * 512:
                                      dc * 3 * D + D + half * 512 + 512],
                                start=(dc == 0), stop=(dc == 7))
                        return inst
                    mmv = em.t(mm_v, waits=[xlnT_done, gp_all, em.now("cv")])
                    em.v(lambda v, rp=rp, half=half: v.tensor_copy(
                        vring[:, rp * D + half * 512:rp * D + half * 512 + 512],
                        ps_b[:, :]),
                        waits=[mmv] + (W("ring_free") if half == 0 else []))
                em.v(lambda v, rp=rp, q=q: v.tensor_copy(
                    bass.AP(kring, rp * 1024, [[2 * 1024, 128], [128, 8], [1, 128]]),
                    bass.AP(kfm, q * 128, [[8 * 512, 128], [512, 8], [1, 128]])),
                    waits=[qk_done])
                ring_done = em.now("cv")

                first = (i == 0)
                if not first:
                    def mm_s_fix(t, q=q, rp=rp):
                        inst = None
                        for dc in range(8):
                            t.matmul(ps_s[:, 0:128],
                                     qfm[:, dc * 512 + q * 128:
                                         dc * 512 + q * 128 + 128],
                                     kring[:, (1 - rp) * 1024 + dc * 128:
                                           (1 - rp) * 1024 + dc * 128 + 128],
                                     start=(dc == 0), stop=(dc == 7))
                        for dc in range(8):
                            inst = t.matmul(ps_s[:, 128:256],
                                            qfm[:, dc * 512 + q * 128:
                                                dc * 512 + q * 128 + 128],
                                            kring[:, rp * 1024 + dc * 128:
                                                  rp * 1024 + dc * 128 + 128],
                                            start=(dc == 0),
                                            stop=(dc == 7))
                        return inst
                    mms = em.t(mm_s_fix, waits=[qk_done, ring_done,
                                                em.now("cv"), em.now("cs")])
                    em.v(lambda v: v.scalar_tensor_tensor(
                        ssb[:, :], ps_s[:, 0:256], float(D) ** -0.5, wmb[:, :],
                        ALU.mult, ALU.add), waits=[mms, ld_all])
                    em.v(lambda v: v.reduce_max(col[:, :], ssb[:, :], axis=AX.X))
                    em.v(lambda v: v.tensor_scalar(negmax[:, :], col[:, :],
                                                   -1.0, None, ALU.mult))
                    se = em.s(lambda s: s.activation(eS[:, :], ssb[:, :],
                                                     AF.Exp,
                                                     bias=negmax[:, 0:1]),
                              waits=[em.now("cv")] + W("eS_free"))
                    em.v(lambda v: v.reduce_sum(col[:, :], eS[:, :], axis=AX.X),
                         waits=[se])
                    em.v(lambda v: v.reciprocal(rden[:, :], col[:, :]))
                    sm_done = em.now("cv")
                    def tr_e(t):
                        t.transpose(psT[:, 0:128], eS[:, 0:128], ident[:, :])
                        return t.transpose(psT[:, 128:256], eS[:, 128:256],
                                           ident[:, :])
                    tre = em.t(tr_e, waits=[sm_done, em.now("cv")])
                    marks["eS_free"] = em.now("ct")
                    em.v(lambda v: v.tensor_copy(eST[:, :], psT[:, 0:256]),
                         waits=[tre])
                    est_done = em.now("cv")
                    for half in range(2):
                        def mm_a(t, half=half, rp=rp):
                            t.matmul(ps_a[:, :], eST[:, 0:128],
                                     vring[:, (1 - rp) * D + half * 512:
                                           (1 - rp) * D + half * 512 + 512],
                                     start=True, stop=False)
                            return t.matmul(ps_a[:, :], eST[:, 128:256],
                                            vring[:, rp * D + half * 512:
                                                  rp * D + half * 512 + 512],
                                            start=False, stop=True)
                        mma = em.t(mm_a, waits=[est_done, em.now("cv"),
                                                em.now("cs")])
                        em.v(lambda v, half=half: v.tensor_scalar(
                            attn[:, half * 512:half * 512 + 512], ps_a[:, :],
                            rden[:, 0:1], None, ALU.mult),
                            waits=[mma] + (W("attn_free") if half == 0 else []))
                    at_done = em.now("cv")
                else:
                    at_done = em.v(lambda v: v.memset(attn[:, :], 0.0),
                                   waits=W("attn_free"))
                tra = emit_transpose8(em, psT, attn[:, :], ident,
                                      [at_done, em.now("cv")])
                marks["attn_free"] = tra
                em.v(lambda v: v.tensor_copy(attnT[:, :], psT[:, :]),
                     waits=[tra])
                aT_done = em.now("cv")
                for half in range(2):
                    def mm_p(t, half=half):
                        for dc in range(8):
                            t.matmul(ps_b[:, :],
                                     attnT[:, dc * 128:(dc + 1) * 128],
                                     woutb[:, dc * D + half * 512:
                                           dc * D + half * 512 + 512],
                                     start=(dc == 0), stop=False)
                        return t.matmul(ps_b[:, :], ones1_b[:, :],
                                        boutb[:, half * 512:half * 512 + 512],
                                        start=False, stop=True)
                    mmp = em.t(mm_p, waits=[aT_done, gp_all, g_all,
                                            em.now("cv")])
                    em.v(lambda v, half=half, q=q: v.tensor_add(
                        x4t[:, half * 512:half * 512 + 512], ps_b[:, :],
                        xA[:, q * D + half * 512:q * D + half * 512 + 512]),
                        waits=[mmp] + (W("x4_free") if half == 0 else []))
                st_x4 = em.dma("sx4", "sync", lambda s, i=i: s.dma_start(
                    out=o_x4[i * 128:(i + 1) * 128, :], in_=x4t[:, :]),
                    waits=[em.now("cv")])
                marks["x4_free"] = st_x4
                marks["ring_free"] = em.now("ct")

                if i == 0:
                    em.v(lambda v: v.tensor_copy(
                        ebuf[:, :],
                        bass.AP(qfm, 0, [[8 * 512, 128], [512, 8], [1, 128]])),
                        waits=[qk_done])
                    em.dma("sq0", "sync", lambda s: s.dma_start(
                        out=o_q0[:, :], in_=ebuf[:, 0:1024]),
                        waits=[em.now("cv")])
                    em.dma("sk0", "sync", lambda s, rp=rp: s.dma_start(
                        out=o_k0[:, :], in_=kring[:, rp * 1024:(rp + 1) * 1024]),
                        waits=[ring_done])
                    em.dma("sv0", "sync", lambda s, rp=rp: s.dma_start(
                        out=o_v0[:, :], in_=vring[:, rp * D:(rp + 1) * D]),
                        waits=[ring_done])
                if i == nst * 4 - 1:
                    em.dma("skL", "sync", lambda s, rp=rp: s.dma_start(
                        out=o_kL[:, :], in_=kring[:, rp * 1024:(rp + 1) * 1024]),
                        waits=[ring_done])
                    em.dma("svL", "sync", lambda s, rp=rp: s.dma_start(
                        out=o_vL[:, :], in_=vring[:, rp * D:(rp + 1) * D]),
                        waits=[ring_done])
            marks["q_free"] = em.now("ct")
            marks["k_free"] = em.now("ct")
            marks["xlnT_free"] = em.now("ct")
            marks["xA_free"] = em.now("cv")

        em.barrier_all()
        em.build(block)
    es.close()
    return nc


def build_launch_W0():
    nc = bass.Bass("TRN2", target_bir_lowering=False)
    x4_in = nc.dram_tensor("x4p0", [128, D], F32, kind="ExternalInput").ap()
    q0_in = nc.dram_tensor("q0fm", [128, 8 * 128], BF16, kind="ExternalInput").ap()
    k0_in = nc.dram_tensor("k0fm", [128, 8 * 128], BF16, kind="ExternalInput").ap()
    v0_in = nc.dram_tensor("v0tm", [128, D], BF16, kind="ExternalInput").ap()
    kP_in = nc.dram_tensor("kPfm", [128, 8 * 128], BF16, kind="ExternalInput").ap()
    vP_in = nc.dram_tensor("vPtm", [128, D], BF16, kind="ExternalInput").ap()
    wout_in = nc.dram_tensor("w_out", [D, D], F32, kind="ExternalInput").ap()
    wm_in = nc.dram_tensor("wmask", [128, 256], F32, kind="ExternalInput").ap()
    o_x4 = nc.dram_tensor("x4t0", [128, D], F32, kind="ExternalOutput").ap()

    es = ExitStack()
    sb = lambda name, shape, dt: es.enter_context(nc.sbuf_tensor(name, shape, dt))
    psum = lambda name, shape, dt: es.enter_context(nc.psum_tensor(name, shape, dt))
    ident = sb("ident_0", [128, 128], BF16)
    woutb = sb("woutb_0", [128, 8 * D], BF16)
    wmb = sb("wmb_0", [128, 256], F32)
    q0 = sb("q0_0", [128, 1024], BF16)
    k0 = sb("k0_0", [128, 1024], BF16)
    v0 = sb("v0_0", [128, D], BF16)
    kP = sb("kP_0", [128, 1024], BF16)
    vP = sb("vP_0", [128, D], BF16)
    x4t = sb("x4t_0", [128, D], F32)
    ssb = sb("ssb_0", [128, 256], F32)
    eS = sb("eS_0", [128, 256], BF16)
    eST = sb("eST_0", [128, 256], BF16)
    attn = sb("attn_0", [128, D], BF16)
    attnT = sb("attnT_0", [128, D], BF16)
    col = sb("col_0", [128, 1], F32)
    negmax = sb("negmax_0", [128, 1], F32)
    rden = sb("rden_0", [128, 1], F32)
    psT = psum("psT_0", [128, D], BF16)
    ps_s = psum("ps_s_0", [128, 512], F32)
    ps_b = psum("ps_b_0", [128, 512], F32)

    with nc.Block() as block:
        em = Em(nc, es)
        chunked_load(em, woutb, wout_in, 8, D)
        em.dma("dpre", "sync", lambda s: s.dma_start(out=wmb[:, :], in_=wm_in[:, :]))
        em.dma("dpre", "sync", lambda s: s.dma_start(out=q0[:, :], in_=q0_in[:, :]))
        em.dma("dpre", "sync", lambda s: s.dma_start(out=k0[:, :], in_=k0_in[:, :]))
        em.dma("dpre", "sync", lambda s: s.dma_start(out=v0[:, :], in_=v0_in[:, :]))
        em.dma("dpre", "sync", lambda s: s.dma_start(out=kP[:, :], in_=kP_in[:, :]))
        em.dma("dpre", "sync", lambda s: s.dma_start(out=vP[:, :], in_=vP_in[:, :]))
        em.dma("dpre", "sync", lambda s: s.dma_start(out=x4t[:, :], in_=x4_in[:, :]))
        emit_identity(em, ident)
        ld_all = em.now("dpre")
        gp_all = em.now("dpre_g")
        g_all = em.now("cg")

        def mm_s(t):
            inst = None
            for dc in range(8):
                t.matmul(ps_s[:, 0:128], q0[:, dc * 128:(dc + 1) * 128],
                         kP[:, dc * 128:(dc + 1) * 128],
                         start=(dc == 0), stop=(dc == 7))
            for dc in range(8):
                inst = t.matmul(ps_s[:, 128:256], q0[:, dc * 128:(dc + 1) * 128],
                                k0[:, dc * 128:(dc + 1) * 128],
                                start=(dc == 0), stop=(dc == 7))
            return inst
        mms = em.t(mm_s, waits=[ld_all, gp_all, g_all])
        em.v(lambda v: v.scalar_tensor_tensor(
            ssb[:, :], ps_s[:, 0:256], float(D) ** -0.5, wmb[:, :],
            ALU.mult, ALU.add), waits=[mms])
        em.v(lambda v: v.reduce_max(col[:, :], ssb[:, :], axis=AX.X))
        em.v(lambda v: v.tensor_scalar(negmax[:, :], col[:, :], -1.0, None,
                                       ALU.mult))
        se = em.s(lambda s: s.activation(eS[:, :], ssb[:, :], AF.Exp,
                                         bias=negmax[:, 0:1]),
                  waits=[em.now("cv")])
        em.v(lambda v: v.reduce_sum(col[:, :], eS[:, :], axis=AX.X), waits=[se])
        em.v(lambda v: v.reciprocal(rden[:, :], col[:, :]))
        sm_done = em.now("cv")

        def tr_e(t):
            t.transpose(psT[:, 0:128], eS[:, 0:128], ident[:, :])
            return t.transpose(psT[:, 128:256], eS[:, 128:256], ident[:, :])
        tre = em.t(tr_e, waits=[sm_done])
        em.v(lambda v: v.tensor_copy(eST[:, :], psT[:, 0:256]), waits=[tre])
        est_done = em.now("cv")
        for half in range(2):
            def mm_a(t, half=half):
                t.matmul(ps_b[:, :], eST[:, 0:128],
                         vP[:, half * 512:half * 512 + 512],
                         start=True, stop=False)
                return t.matmul(ps_b[:, :], eST[:, 128:256],
                                v0[:, half * 512:half * 512 + 512],
                                start=False, stop=True)
            mma = em.t(mm_a, waits=[est_done, em.now("cv")])
            em.v(lambda v, half=half: v.tensor_scalar(
                attn[:, half * 512:half * 512 + 512], ps_b[:, :],
                rden[:, 0:1], None, ALU.mult), waits=[mma])
        at_done = em.now("cv")
        tra = emit_transpose8(em, psT, attn[:, :], ident,
                              [at_done, em.now("ct")])
        em.v(lambda v: v.tensor_copy(attnT[:, :], psT[:, :]), waits=[tra])
        aT_done = em.now("cv")
        for half in range(2):
            def mm_p(t, half=half):
                inst = None
                for dc in range(8):
                    inst = t.matmul(ps_b[:, :],
                                    attnT[:, dc * 128:(dc + 1) * 128],
                                    woutb[:, dc * D + half * 512:
                                          dc * D + half * 512 + 512],
                                    start=(dc == 0), stop=(dc == 7))
                return inst
            mmp = em.t(mm_p, waits=[aT_done, em.now("cv")])
            em.v(lambda v, half=half: v.tensor_add(
                x4t[:, half * 512:half * 512 + 512], ps_b[:, :],
                x4t[:, half * 512:half * 512 + 512]), waits=[mmp])
        em.dma("sx40", "sync", lambda s: s.dma_start(
            out=o_x4[:, :], in_=x4t[:, :]), waits=[em.now("cv")])
        em.barrier_all()
        em.build(block)
    es.close()
    return nc


_BUILD_CACHE = {}


def _prog(name, fn, **kw):
    key = (name, tuple(sorted(kw.items())))
    if key not in _BUILD_CACHE:
        _BUILD_CACHE[key] = fn(**kw)
    return _BUILD_CACHE[key]


def _run(nc, maps):
    return run_bass_kernel_spmd(nc, maps, core_ids=list(range(NCORES)))


def _moe_round(x_flat, gate_weight, e_bias, shared_gate, shared_up, shared_down,
               ffn_w1, ffn_w2, ffn_w3):
    import ml_dtypes
    ncA = _prog("A", build_launch_A)
    mapsA = launch_A_inputs(x_flat, gate_weight, e_bias, shared_gate,
                            shared_up, shared_down)
    resA = _run(ncA, mapsA).results
    xln_list = [r["xln"] for r in resA]
    counts_list = [r["counts"] for r in resA]
    contrib_list = [r["contrib"] for r in resA]
    dest = [r["dest"] for r in resA]
    shared = [r["shared"] for r in resA]

    ncB = _prog("B", build_launch_B)
    mapsB = [launch_B_inputs(c, xln_list, counts_list, contrib_list,
                             ffn_w1, ffn_w2, ffn_w3) for c in range(NCORES)]
    resB = _run(ncB, mapsB).results
    ye = [r["ye"] for r in resB]
    zero_row = np.zeros((1, D), ml_dtypes.bfloat16)
    yeR = []
    for j in range(NCORES):
        parts = [ye[i][j * EPC * CAP:(j + 1) * EPC * CAP] for i in range(NCORES)]
        yeR.append(np.concatenate(parts + [zero_row], 0))
    return dest, shared, yeR


def _c_consts():
    return {
        "tri": np.triu(np.ones((128, 128), np.float32)),
        "bdm": np.kron(np.eye(2, dtype=np.float32),
                       np.ones((64, 64), np.float32)),
        "hmask": np.concatenate(
            [np.repeat([[1.0], [0.0]], 64, 0),
             np.repeat([[0.0], [1.0]], 64, 0)], 1).astype(np.float32),
    }


def kernel(**inputs):
    import ml_dtypes
    g = lambda k: np.asarray(inputs[k], dtype=np.float32)
    x0 = g("L0_attn_query").reshape(NCORES * TC, D)

    dest1, shared1, yeR1 = _moe_round(
        x0, g("L0_ffn_gate_weight"), g("L0_ffn_e_bias"),
        g("L0_ffn_shared_gate"), g("L0_ffn_shared_up"),
        g("L0_ffn_shared_down"),
        g("L0_ffn_w1"), g("L0_ffn_w2"), g("L0_ffn_w3"))

    ncC = _prog("C", build_launch_C)
    cc = _c_consts()
    Wq, Wk, Wv = g("L0_attn_Wq"), g("L0_attn_Wk"), g("L0_attn_Wv")
    bq, bk, bv = g("L0_attn_bq"), g("L0_attn_bk"), g("L0_attn_bv")
    mapsC = []
    for c in range(NCORES):
        mapsC.append({
            "x": np.ascontiguousarray(x0[c * TC:(c + 1) * TC]),
            "shared": shared1[c], "dest": dest1[c], "yeR": yeR1[c],
            "WqT": np.ascontiguousarray(Wq.T),
            "WkT": np.ascontiguousarray(Wk.T),
            "WvT": np.ascontiguousarray(Wv.T),
            "bqc": bq.reshape(8, 128).T.copy(),
            "bkc": bk.reshape(8, 128).T.copy(),
            "bv": bv.reshape(1, D), **cc,
        })
    resC = _run(ncC, mapsC).results

    ncD = _prog("D", build_launch_D)
    zst = np.zeros((128, 8 * 128), np.float32)
    mapsD = []
    for c in range(NCORES):
        sp = resC[c - 1]["state"] if c % 2 == 1 else zst
        mapsD.append({
            "x1": resC[c]["x1"], "opart": resC[c]["opart"],
            "qfm": resC[c]["qfm"], "Spart": np.asarray(sp, np.float32),
            "ln_g": g("L0_attn_ln_g").reshape(8, 128).T.copy(),
            "ln_b": g("L0_attn_ln_b").reshape(8, 128).T.copy(),
            "WoT": np.ascontiguousarray(g("L0_attn_Wo").T),
            "bo": g("L0_attn_bo").reshape(1, D),
        })
    resD = _run(ncD, mapsD).results
    x2 = np.concatenate([r["x2"] for r in resD], 0)

    dest2, shared2, yeR2 = _moe_round(
        x2, g("L0_ffn_gate_weight"), g("L0_ffn_e_bias"),
        g("L0_ffn_shared_gate"), g("L0_ffn_shared_up"),
        g("L0_ffn_shared_down"),
        g("L0_ffn_w1"), g("L0_ffn_w2"), g("L0_ffn_w3"))

    ncC2 = _prog("Cc", build_launch_C, upto="combine")
    mapsC2 = []
    for c in range(NCORES):
        m = dict(mapsC[c])
        m["x"] = np.ascontiguousarray(x2[c * TC:(c + 1) * TC])
        m["shared"] = shared2[c]
        m["dest"] = dest2[c]
        m["yeR"] = yeR2[c]
        mapsC2.append(m)
    resC2 = _run(ncC2, mapsC2).results
    x3moe = [r["x1"] for r in resC2]

    ncF = _prog("FFN", build_launch_FFN)
    w1, w2 = g("L1_ffn_w1"), g("L1_ffn_w2")
    ffnw = {
        "w1": w1, "w1T": np.ascontiguousarray(w1.T),
        "w2": w2, "w2T": np.ascontiguousarray(w2.T),
        "b1c": g("L1_ffn_b1").reshape(HF // 128, 128).T.copy(),
        "b2c": g("L1_ffn_b2").reshape(8, 128).T.copy(),
    }
    mapsF = [{"x": np.asarray(x3moe[c], np.float32), **ffnw}
             for c in range(NCORES)]
    resF = _run(ncF, mapsF).results
    x3 = [r["xo"] for r in resF]

    ncW = _prog("WATT", build_launch_WATT)
    wmask = np.where(np.triu(np.ones((128, 256), bool), 129), -3.4e38,
                     0).astype(np.float32)
    wq_nat = g("L1_attn_w_qkv")
    wo_nat = g("L1_attn_w_out")
    mapsW = [{"x": np.asarray(x3[c], np.float32), "w_qkv": wq_nat,
              "w_out": wo_nat, "b_out": g("L1_attn_b_out").reshape(1, D),
              "wmask": wmask} for c in range(NCORES)]
    resW = _run(ncW, mapsW).results

    ncW0 = _prog("W0", build_launch_W0)
    zbf = np.zeros((128, 1024), ml_dtypes.bfloat16)
    maps0 = []
    for c in range(NCORES):
        kP = resW[c - 1]["kLfm"] if c % 2 == 1 else zbf
        vP = resW[c - 1]["vLtm"] if c % 2 == 1 else zbf
        maps0.append({
            "x4p0": np.ascontiguousarray(resW[c]["x4p"][0:128]),
            "q0fm": resW[c]["q0fm"], "k0fm": resW[c]["k0fm"],
            "v0tm": resW[c]["v0tm"], "kPfm": np.asarray(kP),
            "vPtm": np.asarray(vP), "w_out": wo_nat, "wmask": wmask,
        })
    res0 = _run(ncW0, maps0).results

    mapsF2 = []
    for c in range(NCORES):
        x4 = np.array(resW[c]["x4p"], np.float32)
        x4[0:128] = res0[c]["x4t0"]
        mapsF2.append({"x": x4, **ffnw})
    resF2 = _run(ncF, mapsF2).results

    out = np.concatenate([r["xo"] for r in resF2], 0)
    return out.reshape(4, 8192, D).astype(np.float32)


# version 03
# speedup vs baseline: 1.0024x; 1.0024x over previous
import numpy as np
from contextlib import ExitStack

import concourse.bass as bass
import concourse.mybir as mybir
from concourse.bass_utils import run_bass_kernel_spmd
from concourse.bass import IndirectOffsetOnAxis

F32 = mybir.dt.float32
BF16 = mybir.dt.bfloat16
I32 = mybir.dt.int32
AF = mybir.ActivationFunctionType
ALU = mybir.AluOpType
AX = mybir.AxisListType

NCORES = 8
TC = 4096
D = 1024
E = 64
CAP = 1024
TOPK = 8
NG = 8
EPC = E // NCORES
FEXP = 512
FSH = 1024
HF = 4096
LN_EPS = 1e-5
SCALE = 2.5
NSLOT = E * CAP
ZROW = NSLOT


class Em:

    def __init__(self, nc, es):
        self.nc = nc
        self.es = es
        self.ops = {e: [] for e in ("sync", "scalar", "tensor", "vector", "gpsimd")}
        self.sems = {}
        self.counts = {}
        self.chan("cv")
        self.chan("cs")
        self.chan("ct")
        self.chan("cg")
        self.dma_chans = []

    def chan(self, name):
        self.sems[name] = self.es.enter_context(self.nc.semaphore(name))
        self.counts[name] = 0
        return name

    def _emit(self, eng, fn, waits, inc, n, serial):
        wl = []
        for w in waits:
            if isinstance(w, str):
                wl.append((w, self.counts[w]))
            else:
                if w is not None and w[1] > 0:
                    wl.append(w)
        if serial and self.counts[inc] > 0:
            wl.append((inc, self.counts[inc]))
        self.counts[inc] += n
        self.ops[eng].append((fn, wl, (inc, n)))
        return (inc, self.counts[inc])

    def v(self, fn, waits=()):
        return self._emit("vector", fn, waits, "cv", 1, True)

    def s(self, fn, waits=()):
        return self._emit("scalar", fn, waits, "cs", 1, True)

    def t(self, fn, waits=()):
        return self._emit("tensor", fn, waits, "ct", 1, False)

    def g(self, fn, waits=()):
        return self._emit("gpsimd", fn, waits, "cg", 1, True)

    def dma(self, chan, eng, fn, waits=()):
        if chan not in self.sems:
            self.chan(chan)
            self.dma_chans.append(chan)
        return self._emit(eng, fn, waits, chan, 16, False)

    def now(self, name):
        return (name, self.counts[name])

    def barrier_all(self):
        for ch in self.dma_chans:
            c = self.counts[ch]
            if c > 0:
                self.ops["gpsimd"].append(
                    (lambda g: None, [(ch, c)], ("cg", 0)))

    def build(self, block):
        for eng in ("sync", "scalar", "tensor", "vector", "gpsimd"):
            lst = self.ops[eng]
            if not lst:
                continue

            def mk(lst):
                def body(e):
                    for fn, wl, tgt in lst:
                        for s, c in wl:
                            if c > 0:
                                e.wait_ge(self.sems[s], c)
                        inst = fn(e)
                        if tgt is not None and tgt[1] > 0:
                            if inst is None:
                                raise RuntimeError("inc on None instruction")
                            inst.then_inc(self.sems[tgt[0]], tgt[1])
                return body

            getattr(block, eng)(mk(lst))


def chunked_load(em, dst_sb, src_ap, nchunk, inner, chan="dpre"):
    fn = lambda e: e.dma_start(
        out=dst_sb[:, :].rearrange("p (c f) -> p c f", c=nchunk),
        in_=src_ap.rearrange("(c p) f -> p c f", p=128))
    eng = "gpsimd" if src_ap.dtype != dst_sb.dtype else "sync"
    return em.dma(chan if eng == "sync" else chan + "_g", eng, fn)


def emit_identity(em, ident):
    em.g(lambda g: g.memset(ident[:, :], 0.0))
    return em.g(lambda g: g.affine_select(
        out=ident[:, :], in_=ident[:, :], compare_op=ALU.not_equal,
        fill=1.0, base=0, pattern=[[-1, ident.shape[0]]], channel_multiplier=1))


def emit_transpose8(em, psT, src, ident, waits):
    n = src.shape[1] // 128

    def fn(t):
        inst = None
        for c in range(n):
            inst = t.transpose(psT[:, c * 128:(c + 1) * 128],
                               src[:, c * 128:(c + 1) * 128], ident[:, :])
        return inst
    return em.t(fn, waits=waits)


def emit_ln(em, x_tile, xln_tile, stats, mv, col, rr, waits):
    nh = x_tile.shape[1] // 512
    for h in range(nh):
        em.v(lambda v, h=h: v.bn_stats(stats[:, 6 * h:6 * h + 6],
                                       x_tile[:, 512 * h:512 * h + 512]),
             waits=(waits if h == 0 else ()))
    em.v(lambda v, nh=nh: v.bn_aggr(mv[:, :], stats[:, 0:6 * nh]),
         waits=[em.now("cs")])
    em.v(lambda v: v.tensor_scalar(col[:, :], mv[:, 1:2], float(LN_EPS), None,
                                   ALU.add))
    m = em.now("cv")
    em.s(lambda s: s.activation(col[:, :], col[:, :], AF.Sqrt), waits=[m])
    em.v(lambda v: v.reciprocal(rr[:, :], col[:, :]), waits=[em.now("cs")])
    return em.v(lambda v: v.tensor_scalar(
        xln_tile, x_tile, mv[:, 0:1], rr[:, 0:1], ALU.subtract, ALU.mult))


def build_launch_A(nt=TC // 128, with_shared=True):
    nc = bass.Bass("TRN2", target_bir_lowering=False)
    TCl = nt * 128
    x_in = nc.dram_tensor("x", [TCl, D], F32, kind="ExternalInput").ap()
    gwT = nc.dram_tensor("gwT", [D, E], F32, kind="ExternalInput").ap()
    ebias_in = nc.dram_tensor("e_bias", [1, E], F32, kind="ExternalInput").ap()
    sgT = nc.dram_tensor("sgT", [D, FSH], F32, kind="ExternalInput").ap()
    suT = nc.dram_tensor("suT", [D, FSH], F32, kind="ExternalInput").ap()
    sdT = nc.dram_tensor("sdT", [FSH, D], F32, kind="ExternalInput").ap()
    gid_in = nc.dram_tensor("gid", [128, nt], F32, kind="ExternalInput").ap()
    cst_up = nc.dram_tensor("cst_up", [128, 128], F32, kind="ExternalInput").ap()
    cst_iota = nc.dram_tensor("cst_iota", [128, E], F32, kind="ExternalInput").ap()

    o_xln = nc.dram_tensor("xln", [TCl, D], BF16, kind="ExternalOutput").ap()
    o_counts = nc.dram_tensor("counts", [1, E], F32, kind="ExternalOutput").ap()
    o_contrib = nc.dram_tensor("contrib", [NSLOT, 2], F32, kind="ExternalOutput").ap()
    o_dest = nc.dram_tensor("dest", [TCl, TOPK], F32, kind="ExternalOutput").ap()
    o_shared = nc.dram_tensor("shared", [TCl, D], F32, kind="ExternalOutput").ap()

    es = ExitStack()
    sb = lambda name, shape, dt: es.enter_context(nc.sbuf_tensor(name, shape, dt))
    psum = lambda name, shape, dt: es.enter_context(nc.psum_tensor(name, shape, dt))

    gwTb = sb("gwTb", [128, 8 * E], BF16)
    ebias_b = sb("ebias_b", [128, E], F32)
    eb_row = sb("eb_row", [1, E], F32)
    ones128_b = sb("ones128_b", [128, 1], BF16)
    ones1_f = sb("ones1_f", [1, 128], F32)
    upb = sb("upb", [128, 128], BF16)
    iota64 = sb("iota64", [128, E], F32)
    ident = sb("ident", [128, 128], BF16)
    if with_shared:
        sgTb = sb("sgTb", [128, 8 * FSH], BF16)
        suTb = sb("suTb", [128, 8 * FSH], BF16)
        sdTb = sb("sdTb", [128, 8 * D], BF16)
    gidb = sb("gidb", [128, nt], F32)

    xt = sb("xt", [128, 2 * D], F32)
    stats = sb("stats", [128, 12], F32)
    mv = sb("mv", [128, 2], F32)
    rr = sb("rr", [128, 1], F32)
    xlnt = sb("xlnt", [128, 2 * D], BF16)
    xlnT = sb("xlnT", [128, 2 * D], BF16)
    psT = psum("psT", [128, D], BF16)
    ps_log = psum("ps_log", [128, E], F32)
    ps_cnt = psum("ps_cnt", [1, E], F32)
    ps_pos = psum("ps_pos", [128, E], F32)
    scores = sb("scores", [128, E], F32)
    sfc = sb("sfc", [128, E], F32)
    gsc = sb("gsc", [128, NG], F32)
    g8 = sb("g8", [128, NG], F32)
    scr1 = sb("scr1", [128, E], F32)
    scr2 = sb("scr2", [128, E], F32)
    scr3 = sb("scr3", [128, E], F32)
    cums = sb("cums", [128, E], F32)
    cum2 = sb("cum2", [128, E], F32)
    col1 = sb("col1", [128, 1], F32)
    col2 = sb("col2", [128, 1], F32)
    col3 = sb("col3", [128, 1], F32)
    sel = sb("sel", [128, E], F32)
    selb = sb("selb", [128, E], BF16)
    wsel = sb("wsel", [128, E], F32)
    msk = sb("msk", [128, E], F32)
    posd = sb("posd", [128, E], F32)
    destf = sb("destf", [128, E], F32)
    desti = sb("desti", [128, TOPK], I32)
    sval = sb("sval", [128, 2 * TOPK], F32)
    dest_out = sb("dest_out", [128, TOPK], F32)
    pct = sb("pct", [1, E], F32)
    runc = sb("runc", [1, E], F32)
    if with_shared:
        ps_sh = psum("ps_sh", [128, 512], F32)
        ps_sh2 = psum("ps_sh2", [128, 512], F32)
        gfm = sb("gfm", [128, 512], BF16)
        prod = sb("prod", [128, FSH], BF16)
        prodT = sb("prodT", [128, FSH], BF16)
        shout = sb("shout", [128, D], F32)

    with nc.Block() as block:
        em = Em(nc, es)

        chunked_load(em, gwTb, gwT, 8, E)
        if with_shared:
            chunked_load(em, sgTb, sgT, 8, FSH)
            chunked_load(em, suTb, suT, 8, FSH)
            chunked_load(em, sdTb, sdT, 8, D)
        em.dma("dpre_g", "gpsimd", lambda g: g.dma_start(out=upb[:, :], in_=cst_up[:, :]))
        em.dma("dpre", "sync", lambda s: s.dma_start(out=iota64[:, :], in_=cst_iota[:, :]))
        em.dma("dpre", "sync", lambda s: s.dma_start(out=gidb[:, :], in_=gid_in[:, :]))
        em.dma("dpre", "sync", lambda s: s.dma_start(out=eb_row[:, :], in_=ebias_in[:, :]))
        em.g(lambda g: g.memset(ones128_b[:, :], 1.0))
        em.g(lambda g: g.memset(ones1_f[:, :], 1.0))
        em.g(lambda g: g.memset(runc[:, :], 0.0))
        emit_identity(em, ident)
        ld_all = em.now("dpre")
        gp_all = em.now("dpre_g")
        g_all = em.now("cg")

        em.t(lambda t: t.matmul(ps_log[:, :], ones1_f[:, :], eb_row[:, :],
                                start=True, stop=True),
             waits=[ld_all, gp_all, g_all])
        ev_eb = em.v(lambda v: v.tensor_copy(ebias_b[:, :], ps_log[:, :]),
                     waits=[em.now("ct")])

        regs = {}

        def init_reg(g):
            regs["bc"] = g.register("bc_sc").__enter__()
            g.reg_mov(regs["bc"], NSLOT - 1)
            return None
        em.ops["gpsimd"].append((init_reg, [], ("cg", 0)))

        marks = {"xt_free": [None, None], "xln_free": [[], []],
                 "xT_free": [None, None], "xlnT_free": [None, None],
                 "sc_prev": None, "dest_free": None, "sh_free": None}

        for i in range(nt):
            pb = i % 2
            xti = xt[:, pb * D:(pb + 1) * D]
            xlni = xlnt[:, pb * D:(pb + 1) * D]
            xlnTi = xlnT[:, pb * D:(pb + 1) * D]

            w_ld = [w for w in [marks["xt_free"][pb]] if w]
            ld_x = em.dma("dx%d" % pb, "sync", lambda s, i=i, xti=xti: s.dma_start(
                out=xti, in_=x_in[i * 128:(i + 1) * 128, :]), waits=w_ld)

            w_x = [ld_x] + list(marks["xln_free"][pb])
            ln_done = emit_ln(em, xti, xlni, stats, mv, col1, rr, w_x)
            marks["xt_free"][pb] = ln_done
            st_x = em.dma("sxln%d" % pb, "sync",
                          lambda s, i=i, xlni=xlni: s.dma_start(
                out=o_xln[i * 128:(i + 1) * 128, :], in_=xlni), waits=[ln_done])

            w_tr = [ln_done] + [w for w in [marks["xT_free"][pb]] if w]
            tr = emit_transpose8(em, psT, xlni, ident, w_tr + [g_all])
            w_evxT = [tr] + [w for w in [marks["xlnT_free"][pb]] if w]
            ev_xT = em.v(lambda v, xlnTi=xlnTi: v.tensor_copy(xlnTi, psT[:, :]),
                         waits=w_evxT)
            marks["xln_free"][pb] = [st_x, tr]

            if with_shared:
                for half in range(2):
                    def mm_g(t, h=half, xlnTi=xlnTi):
                        inst = None
                        for c in range(8):
                            inst = t.matmul(
                                ps_sh[:, :], xlnTi[:, c * 128:(c + 1) * 128],
                                sgTb[:, c * FSH + h * 512:c * FSH + h * 512 + 512],
                                start=(c == 0), stop=(c == 7))
                        return inst
                    mmg = em.t(mm_g, waits=[ev_xT, gp_all, em.now("cs")])
                    silu = em.s(lambda s: s.activation(gfm[:, :], ps_sh[:, :],
                                                       AF.Silu),
                                waits=[mmg, em.now("cv")])

                    def mm_u(t, h=half, xlnTi=xlnTi):
                        inst = None
                        for c in range(8):
                            inst = t.matmul(
                                ps_sh2[:, :], xlnTi[:, c * 128:(c + 1) * 128],
                                suTb[:, c * FSH + h * 512:c * FSH + h * 512 + 512],
                                start=(c == 0), stop=(c == 7))
                        return inst
                    mmu = em.t(mm_u, waits=[ev_xT, em.now("cv")])
                    em.v(lambda v, h=half: v.tensor_mul(
                        prod[:, h * 512:(h + 1) * 512], ps_sh2[:, :], gfm[:, :]),
                        waits=[mmu, silu])
                marks["xlnT_free"][pb] = em.now("ct")
                prod_done = em.now("cv")
                trp = emit_transpose8(em, psT, prod[:, :], ident,
                                      [prod_done, ev_xT])
                prT = em.v(lambda v: v.tensor_copy(prodT[:, :], psT[:, :]),
                           waits=[trp])
                marks["xT_free"][pb] = prT
                for half in range(2):
                    def mm_sd(t, h=half):
                        inst = None
                        for c in range(8):
                            inst = t.matmul(
                                ps_sh[:, :], prodT[:, c * 128:(c + 1) * 128],
                                sdTb[:, c * D + h * 512:c * D + h * 512 + 512],
                                start=(c == 0), stop=(c == 7))
                        return inst
                    mmsd = em.t(mm_sd, waits=[prT, em.now("cs"), em.now("cv")])
                    w_sh = [mmsd] + ([w for w in [marks["sh_free"]] if w]
                                     if half == 0 else [])
                    em.v(lambda v, h=half: v.tensor_copy(
                        shout[:, h * 512:(h + 1) * 512], ps_sh[:, :]),
                        waits=w_sh)
                st_sh = em.dma("ssh", "sync", lambda s, i=i: s.dma_start(
                    out=o_shared[i * 128:(i + 1) * 128, :], in_=shout[:, :]),
                    waits=[em.now("cv")])
                marks["sh_free"] = st_sh
            else:
                marks["xT_free"][pb] = ev_xT
                marks["xlnT_free"][pb] = em.now("ct")

            def mm_logits(t, xlnTi=xlnTi):
                inst = None
                for c in range(8):
                    inst = t.matmul(ps_log[:, :], xlnTi[:, c * 128:(c + 1) * 128],
                                    gwTb[:, c * E:(c + 1) * E],
                                    start=(c == 0), stop=(c == 7))
                return inst
            mm_log = em.t(mm_logits, waits=[ev_xT, ev_eb, gp_all, em.now("cs")])
            sig = em.s(lambda s: s.activation(scores[:, :], ps_log[:, :],
                                              AF.Sigmoid),
                       waits=[mm_log, em.now("cv")])
            em.v(lambda v: v.tensor_add(sfc[:, :], scores[:, :], ebias_b[:, :]),
                 waits=[sig])

            for gi in range(NG):
                em.v(lambda v, gi=gi: v.max(out=g8[:, :],
                                            in_=sfc[:, gi * 8:(gi + 1) * 8]))
                em.v(lambda v, gi=gi: v.tensor_add(gsc[:, gi:gi + 1],
                                                   g8[:, 0:1], g8[:, 1:2]))
            em.v(lambda v: v.max(out=g8[:, :], in_=gsc[:, :]))
            em.v(lambda v: v.tensor_scalar(scr1[:, 0:NG], gsc[:, :], g8[:, 3:4],
                                           None, ALU.is_gt))
            em.v(lambda v: v.reduce_sum(col1[:, :], scr1[:, 0:NG], axis=AX.X))
            em.v(lambda v: v.tensor_scalar(scr2[:, 0:NG], gsc[:, :], g8[:, 3:4],
                                           None, ALU.is_equal))
            em.v(lambda v: v.tensor_copy(scr3[:, 0:1], scr2[:, 0:1]))
            em.v(lambda v: v.tensor_add(scr3[:, 1:NG], scr2[:, 1:NG],
                                        scr2[:, 0:NG - 1]))
            em.v(lambda v: v.tensor_copy(cum2[:, 0:2], scr3[:, 0:2]))
            em.v(lambda v: v.tensor_add(cum2[:, 2:NG], scr3[:, 2:NG],
                                        scr3[:, 0:NG - 2]))
            em.v(lambda v: v.tensor_copy(cums[:, 0:4], cum2[:, 0:4]))
            em.v(lambda v: v.tensor_add(cums[:, 4:NG], cum2[:, 4:NG],
                                        cum2[:, 0:NG - 4]))
            em.v(lambda v: v.tensor_scalar(col2[:, :], col1[:, :], -4.0, -1.0,
                                           ALU.add, ALU.mult))
            em.v(lambda v: v.tensor_scalar(cum2[:, 0:NG], cums[:, 0:NG],
                                           col2[:, 0:1], None, ALU.is_le))
            em.v(lambda v: v.tensor_mul(cum2[:, 0:NG], cum2[:, 0:NG],
                                        scr2[:, 0:NG]))
            em.v(lambda v: v.tensor_max(g8[:, :], cum2[:, 0:NG], scr1[:, 0:NG]))
            em.v(lambda v: v.tensor_copy(
                scr1[:, :], bass.AP(g8, 0, [[NG, 128], [1, NG], [0, 8]])))
            em.v(lambda v: v.tensor_mul(scr2[:, :], sfc[:, :], scr1[:, :]))
            em.v(lambda v: v.max(out=g8[:, :], in_=scr2[:, :]))
            em.v(lambda v: v.tensor_scalar(scr1[:, :], scr2[:, :], g8[:, 7:8],
                                           None, ALU.is_gt))
            em.v(lambda v: v.reduce_sum(col1[:, :], scr1[:, :], axis=AX.X))
            em.v(lambda v: v.tensor_scalar(scr3[:, :], scr2[:, :], g8[:, 7:8],
                                           None, ALU.is_equal))
            a, b = scr3, cums
            for sh in (1, 2, 4, 8, 16, 32):
                em.v(lambda v, a=a, b=b, sh=sh: v.tensor_copy(b[:, 0:sh],
                                                              a[:, 0:sh]))
                em.v(lambda v, a=a, b=b, sh=sh: v.tensor_add(
                    b[:, sh:E], a[:, sh:E], a[:, 0:E - sh]))
                a, b = b, a
            em.v(lambda v: v.tensor_scalar(col2[:, :], col1[:, :], -8.0, -1.0,
                                           ALU.add, ALU.mult))
            em.v(lambda v: v.tensor_scalar(cum2[:, :], scr3[:, :], col2[:, 0:1],
                                           None, ALU.is_le))
            em.v(lambda v: v.tensor_scalar(msk[:, :], scr2[:, :], g8[:, 7:8],
                                           None, ALU.is_equal))
            em.v(lambda v: v.tensor_mul(cum2[:, :], cum2[:, :], msk[:, :]))
            em.v(lambda v: v.tensor_scalar(msk[:, :], scr2[:, :], g8[:, 7:8],
                                           None, ALU.is_gt))
            sel_done = em.v(lambda v: v.tensor_max(sel[:, :], cum2[:, :],
                                                   msk[:, :]))
            em.v(lambda v: v.tensor_mul(wsel[:, :], sel[:, :], scores[:, :]))
            em.v(lambda v: v.reduce_sum(col1[:, :], wsel[:, :], axis=AX.X))
            em.v(lambda v: v.tensor_scalar(col2[:, :], col1[:, :], 1e-20, None,
                                           ALU.add))
            em.v(lambda v: v.reciprocal(col3[:, :], col2[:, :]))
            em.v(lambda v: v.tensor_scalar(wsel[:, :], wsel[:, :], col3[:, 0:1],
                                           float(SCALE), ALU.mult, ALU.mult))
            selb_done = em.v(lambda v: v.tensor_copy(selb[:, :], sel[:, :]))

            mm_cnt = em.t(lambda t: t.matmul(ps_cnt[:, :], ones128_b[:, :],
                                             selb[:, :], start=True, stop=True),
                          waits=[selb_done, g_all, em.now("cv")])
            pct_ev = em.v(lambda v: v.tensor_copy(pct[:, :], ps_cnt[:, :]),
                          waits=[mm_cnt])

            def mm_pos(t):
                t.matmul(ps_pos[:, :], upb[:, :], selb[:, :],
                         start=True, stop=False)
                return t.matmul(ps_pos[:, :], ones1_f[:, :], runc[:, :],
                                start=False, stop=True)
            mm_pos_c = em.t(mm_pos, waits=[selb_done, gp_all, g_all,
                                           em.now("cv")])
            pos_ev = em.v(lambda v: v.tensor_copy(posd[:, :], ps_pos[:, :]),
                          waits=[mm_pos_c])
            em.v(lambda v: v.tensor_add(runc[:, :], runc[:, :], pct[:, :]),
                 waits=[pct_ev, mm_pos_c])

            em.v(lambda v: v.scalar_tensor_tensor(destf[:, :], iota64[:, :],
                                                  1024.0, posd[:, :],
                                                  ALU.mult, ALU.add))
            em.v(lambda v: v.tensor_scalar(scr1[:, :], posd[:, :],
                                           float(CAP) - 0.5, 1e9,
                                           ALU.is_ge, ALU.mult))
            em.v(lambda v: v.tensor_add(destf[:, :], destf[:, :], scr1[:, :]))

            em.v(lambda v: v.tensor_copy(scr3[:, :], sel[:, :]))
            a, b = scr3, cums
            for sh in (1, 2, 4, 8, 16, 32):
                em.v(lambda v, a=a, b=b, sh=sh: v.tensor_copy(b[:, 0:sh],
                                                              a[:, 0:sh]))
                em.v(lambda v, a=a, b=b, sh=sh: v.tensor_add(
                    b[:, sh:E], a[:, sh:E], a[:, 0:E - sh]))
                a, b = b, a

            for k in range(TOPK):
                em.v(lambda v, k=k: v.tensor_scalar(scr1[:, :], scr3[:, :],
                                                    float(k + 1), None,
                                                    ALU.is_equal))
                em.v(lambda v: v.tensor_mul(scr1[:, :], scr1[:, :], sel[:, :]))
                em.v(lambda v: v.tensor_mul(scr2[:, :], scr1[:, :], destf[:, :]))
                w_dk = [w for w in [marks["dest_free"]] if w] if k == 0 else []
                em.v(lambda v, k=k: v.reduce_sum(dest_out[:, k:k + 1],
                                                 scr2[:, :], axis=AX.X),
                     waits=w_dk)
                em.v(lambda v: v.tensor_mul(scr2[:, :], scr1[:, :], wsel[:, :]))
                w_sc = [w for w in [marks["sc_prev"]] if w] if k == 0 else []
                em.v(lambda v, k=k: v.reduce_sum(
                    sval[:, 2 * k + 1:2 * k + 2], scr2[:, :], axis=AX.X),
                    waits=w_sc)
            em.v(lambda v, i=i: v.tensor_copy(
                bass.AP(sval, 0, [[2 * TOPK, 128], [2, TOPK]]),
                bass.AP(gidb, i, [[nt, 128], [0, TOPK]])))
            em.v(lambda v: v.tensor_copy(desti[:, :], dest_out[:, :]))
            stg_done = em.now("cv")
            for k in range(TOPK):
                sc = em.dma("dsc", "gpsimd", lambda g, k=k: g.indirect_dma_start(
                    out=o_contrib[:, :],
                    out_offset=IndirectOffsetOnAxis(desti[:, k:k + 1], 0),
                    in_=sval[:, 2 * k:2 * k + 2], in_offset=None,
                    bounds_check=regs["bc"], oob_is_err=False),
                    waits=[stg_done])
            marks["sc_prev"] = sc

            st_d = em.dma("sdst", "sync", lambda s, i=i: s.dma_start(
                out=o_dest[i * 128:(i + 1) * 128, :], in_=dest_out[:, :]),
                waits=[em.now("cv")])
            marks["dest_free"] = st_d

        em.dma("scnt", "sync", lambda s: s.dma_start(
            out=o_counts[:, :], in_=runc[:, :]), waits=[em.now("cv")])
        em.barrier_all()
        em.build(block)
    es.close()
    return nc


def host_consts():
    return {
        "cst_up": np.triu(np.ones((128, 128), np.float32), 1),
        "cst_iota": np.broadcast_to(
            np.arange(E, dtype=np.float32), (128, E)).copy(),
    }


def launch_A_inputs(x_full, gate_weight, e_bias, shared_gate, shared_up,
                    shared_down):
    cs = host_consts()
    gwT = np.ascontiguousarray(gate_weight.T)
    sgT = np.ascontiguousarray(shared_gate.T)
    suT = np.ascontiguousarray(shared_up.T)
    sdT = np.ascontiguousarray(shared_down.T)
    maps = []
    for c in range(NCORES):
        gid = (np.arange(TC, dtype=np.float32) + c * TC).reshape(
            TC // 128, 128).T.copy()
        maps.append({
            "x": np.ascontiguousarray(x_full[c * TC:(c + 1) * TC]),
            "gwT": gwT, "e_bias": e_bias.reshape(1, E).astype(np.float32),
            "sgT": sgT, "suT": suT, "sdT": sdT,
            "gid": gid, **cs,
        })
    return maps


def build_launch_B():
    nc = bass.Bass("TRN2", target_bir_lowering=False)
    xln_all = nc.dram_tensor("xln_all", [NCORES * TC, D], BF16,
                             kind="ExternalInput").ap()
    counts_all = nc.dram_tensor("counts_all", [1, NCORES * E], F32,
                                kind="ExternalInput").ap()
    contrib_all = nc.dram_tensor("contrib_all", [NCORES * NSLOT, 2], F32,
                                 kind="ExternalInput").ap()
    w1T_in = nc.dram_tensor("w1T", [EPC * D, FEXP], F32, kind="ExternalInput").ap()
    w3T_in = nc.dram_tensor("w3T", [EPC * D, FEXP], F32, kind="ExternalInput").ap()
    w2T_in = nc.dram_tensor("w2T", [EPC * FEXP, D], F32, kind="ExternalInput").ap()
    slotbase_in = nc.dram_tensor("slotbase", [128, E], F32,
                                 kind="ExternalInput").ap()
    prow_in = nc.dram_tensor("prow", [128, 8], F32, kind="ExternalInput").ap()
    eofs_in = nc.dram_tensor("eofs", [1, 1], F32, kind="ExternalInput").ap()
    o_ye = nc.dram_tensor("ye", [NSLOT, D], BF16, kind="ExternalOutput").ap()

    es = ExitStack()
    sb = lambda name, shape, dt: es.enter_context(nc.sbuf_tensor(name, shape, dt))
    psum = lambda name, shape, dt: es.enter_context(nc.psum_tensor(name, shape, dt))

    cntb = sb("cntb", [1, NCORES * E], F32)
    bases = sb("bases", [1, NCORES * E], F32)
    slotbase = sb("slotbase_sb", [128, E], F32)
    prow = sb("prow_sb", [128, 8], F32)
    eofs = sb("eofs_sb", [1, 1], F32)
    ones1_f = sb("ones1_f", [1, 128], F32)
    ident = sb("ident", [128, 128], BF16)
    Bb = sb("Bb", [128, 8], F32)
    gemask = sb("gemask", [128, 8], F32)
    cofp = sb("cofp", [128, 8], F32)
    bsel = sb("bsel", [128, 8], F32)
    loff = sb("loff", [128, 8], F32)
    gidxf = sb("gidxf", [128, 8], F32)
    gidxi = sb("gidxi", [128, 8], I32)
    dyef = sb("dyef", [128, 8], F32)
    dyei = sb("dyei", [128, 8], I32)
    sm = sb("sm", [128, 16], F32)
    tokf = sb("tokf", [128, 8], F32)
    toki = sb("toki", [128, 8], I32)
    wcol = sb("wcol", [128, 8], F32)
    xe = sb("xe", [128, 8 * D], BF16)
    xeT = sb("xeT", [128, 8 * D], BF16)
    w1b = sb("w1b", [128, 2 * 8 * FEXP], BF16)
    w3b = sb("w3b", [128, 2 * 8 * FEXP], BF16)
    w2b = sb("w2b", [128, 2 * 4 * D], BF16)
    hg = sb("hg", [128, 4 * CAP], BF16)
    hu = sb("hu", [128, 4 * CAP], BF16)
    prodE = sb("prodE", [128, 4 * CAP], BF16)
    yeb = sb("yeb", [128, 8 * D], BF16)
    ps_b = psum("ps_b", [128, 8], F32)
    psT = psum("psT", [128, D], BF16)
    ps_h = psum("ps_h", [128, 512], F32)
    ps_u = psum("ps_u", [128, 512], F32)

    with nc.Block() as block:
        em = Em(nc, es)
        em.dma("dpre", "sync", lambda s: s.dma_start(out=cntb[:, :], in_=counts_all[:, :]))
        em.dma("dpre", "sync", lambda s: s.dma_start(out=slotbase[:, :], in_=slotbase_in[:, :]))
        em.dma("dpre", "sync", lambda s: s.dma_start(out=prow[:, :], in_=prow_in[:, :]))
        em.dma("dpre", "sync", lambda s: s.dma_start(out=eofs[:, :], in_=eofs_in[:, :]))
        em.g(lambda g: g.memset(ones1_f[:, :], 1.0))
        emit_identity(em, ident)
        ld0 = em.now("dpre")
        em.v(lambda v: v.memset(bases[0:1, 0:E], 0.0), waits=[ld0])
        for c in range(1, NCORES):
            em.v(lambda v, c=c: v.tensor_add(
                bases[0:1, c * E:(c + 1) * E],
                bases[0:1, (c - 1) * E:c * E],
                cntb[0:1, (c - 1) * E:c * E]))
        bases_done = em.now("cv")
        g_all = em.now("cg")

        regs = {}

        def init_regs(g):
            regs["bc_c"] = g.register("bc_c").__enter__()
            g.reg_mov(regs["bc_c"], NCORES * NSLOT - 1)
            regs["bc_x"] = g.register("bc_x").__enter__()
            g.reg_mov(regs["bc_x"], NCORES * TC - 1)
            regs["bc_y"] = g.register("bc_y").__enter__()
            g.reg_mov(regs["bc_y"], NSLOT - 1)
            return None
        em.ops["gpsimd"].append((init_regs, [], ("cg", 0)))

        marks = {"w_free": [None, None], "ye_free": None, "sm_free": None,
                 "xe_free": None}

        for el in range(EPC):
            pb = el % 2
            w_w = [w for w in [marks["w_free"][pb]] if w]
            lw1 = em.dma("dw%d" % pb, "gpsimd", lambda g, el=el, pb=pb: g.dma_start(
                out=w1b[:, pb * 8 * FEXP:(pb + 1) * 8 * FEXP].rearrange(
                    "p (c f) -> p c f", c=8),
                in_=w1T_in[el * D:(el + 1) * D, :].rearrange(
                    "(c p) f -> p c f", p=128)), waits=w_w)
            lw3 = em.dma("dw%d" % pb, "gpsimd", lambda g, el=el, pb=pb: g.dma_start(
                out=w3b[:, pb * 8 * FEXP:(pb + 1) * 8 * FEXP].rearrange(
                    "p (c f) -> p c f", c=8),
                in_=w3T_in[el * D:(el + 1) * D, :].rearrange(
                    "(c p) f -> p c f", p=128)))
            lw2 = em.dma("dw%d" % pb, "gpsimd", lambda g, el=el, pb=pb: g.dma_start(
                out=w2b[:, pb * 4 * D:(pb + 1) * 4 * D].rearrange(
                    "p (c f) -> p c f", c=4),
                in_=w2T_in[el * FEXP:(el + 1) * FEXP, :].rearrange(
                    "(c p) f -> p c f", p=128)))

            def mm_B(t, el=el):
                return t.matmul(ps_b[:, :], ones1_f[:, :],
                                bass.AP(bases, 8 * 0 + el, [[NCORES * E, 1], [E, NCORES]]),
                                start=True, stop=True)
            mmB = em.t(mm_B, waits=[bases_done, g_all])
            em.v(lambda v: v.tensor_copy(Bb[:, :], ps_b[:, :]), waits=[mmB])
            for s in range(8):
                em.v(lambda v, s=s: v.tensor_tensor(
                    gemask[:, :],
                    bass.AP(prow, s, [[8, 128], [0, 8]]),
                    Bb[:, :], ALU.is_ge))
                em.v(lambda v, s=s: v.reduce_sum(cofp[:, s:s + 1], gemask[:, :],
                                                 axis=AX.X))
                em.v(lambda v: v.tensor_mul(gemask[:, :], gemask[:, :], Bb[:, :]))
                em.v(lambda v, s=s: v.reduce_max(bsel[:, s:s + 1], gemask[:, :],
                                                 axis=AX.X))
            em.v(lambda v: v.tensor_scalar(cofp[:, :], cofp[:, :], -1.0, None,
                                           ALU.add))
            em.v(lambda v: v.tensor_sub(loff[:, :], prow[:, :], bsel[:, :]))
            em.v(lambda v, el=el: v.scalar_tensor_tensor(
                gidxf[:, :], cofp[:, :], float(NSLOT),
                slotbase[:, el * 8:(el + 1) * 8], ALU.mult, ALU.add))
            em.v(lambda v: v.tensor_sub(gidxf[:, :], gidxf[:, :], bsel[:, :]))
            em.v(lambda v, el=el: v.scalar_tensor_tensor(
                dyef[:, :], cofp[:, :], float(EPC * CAP), loff[:, :],
                ALU.mult, ALU.add))
            em.v(lambda v, el=el: v.tensor_scalar(
                dyef[:, :], dyef[:, :], float(el * CAP), None, ALU.add))
            ye_guard = [w for w in [marks["ye_free"]] if w]
            em.v(lambda v: v.tensor_copy(gidxi[:, :], gidxf[:, :]),
                 waits=ye_guard)
            em.v(lambda v: v.tensor_copy(dyei[:, :], dyef[:, :]))
            stg = em.now("cv")

            sm_guard = [w for w in [marks["sm_free"]] if w]
            for s in range(8):
                sm_done = em.dma("dsm", "gpsimd", lambda g, s=s: g.indirect_dma_start(
                    out=sm[:, 2 * s:2 * s + 2], out_offset=None,
                    in_=contrib_all[:, :],
                    in_offset=IndirectOffsetOnAxis(gidxi[:, s:s + 1], 0),
                    bounds_check=regs["bc_c"], oob_is_err=False),
                    waits=[stg] + sm_guard)
            em.v(lambda v: v.tensor_copy(
                tokf[:, :], bass.AP(sm, 0, [[16, 128], [2, 8]])),
                waits=[sm_done])
            em.v(lambda v: v.tensor_copy(
                wcol[:, :], bass.AP(sm, 1, [[16, 128], [2, 8]])))
            em.v(lambda v: v.tensor_copy(toki[:, :], tokf[:, :]))
            tok_done = em.now("cv")
            marks["sm_free"] = tok_done

            xe_guard = [w for w in [marks["xe_free"]] if w]
            for s in range(8):
                xe_done = em.dma("dxe", "gpsimd", lambda g, s=s: g.indirect_dma_start(
                    out=xe[:, s * D:(s + 1) * D], out_offset=None,
                    in_=xln_all[:, :],
                    in_offset=IndirectOffsetOnAxis(toki[:, s:s + 1], 0),
                    bounds_check=regs["bc_x"], oob_is_err=False),
                    waits=[tok_done] + xe_guard)

            for s in range(8):
                tr = emit_transpose8(em, psT, xe[:, s * D:(s + 1) * D], ident,
                                     [xe_done, g_all, em.now("cv")])
                em.v(lambda v, s=s: v.tensor_copy(
                    bass.AP(xeT, s * 128, [[8 * D, 128], [1024, 8], [1, 128]]),
                    psT[:, :]), waits=[tr])
            xeT_done = em.now("cv")
            marks["xe_free"] = xeT_done

            wv = [lw2]
            for fc in range(4):
                for half in range(2):
                    def mm_h(t, fc=fc, half=half, pb=pb):
                        inst = None
                        for c in range(8):
                            inst = t.matmul(
                                ps_h[:, :],
                                w1b[:, pb * 8 * FEXP + c * FEXP + fc * 128:
                                    pb * 8 * FEXP + c * FEXP + fc * 128 + 128],
                                bass.AP(xeT, c * CAP + half * 512,
                                        [[8 * D, 128], [1, 512]]),
                                start=(c == 0), stop=(c == 7))
                        return inst
                    mmh = em.t(mm_h, waits=[xeT_done, em.now("cs")] + wv)
                    em.s(lambda s, fc=fc, half=half: s.activation(
                        hg[:, fc * CAP + half * 512:fc * CAP + half * 512 + 512],
                        ps_h[:, :], AF.Silu), waits=[mmh])

                    def mm_u(t, fc=fc, half=half, pb=pb):
                        inst = None
                        for c in range(8):
                            inst = t.matmul(
                                ps_u[:, :],
                                w3b[:, pb * 8 * FEXP + c * FEXP + fc * 128:
                                    pb * 8 * FEXP + c * FEXP + fc * 128 + 128],
                                bass.AP(xeT, c * CAP + half * 512,
                                        [[8 * D, 128], [1, 512]]),
                                start=(c == 0), stop=(c == 7))
                        return inst
                    mmu = em.t(mm_u, waits=[xeT_done, em.now("cv")] + wv)
                    em.v(lambda v, fc=fc, half=half: v.tensor_mul(
                        prodE[:, fc * CAP + half * 512:fc * CAP + half * 512 + 512],
                        ps_u[:, :],
                        hg[:, fc * CAP + half * 512:fc * CAP + half * 512 + 512]),
                        waits=[mmu, em.now("cs")])
            prod_done = em.now("cv")

            for s in range(8):
                for half in range(2):
                    def mm_y(t, s=s, half=half, pb=pb):
                        inst = None
                        for fc in range(4):
                            inst = t.matmul(
                                ps_h[:, :],
                                prodE[:, fc * CAP + s * 128:fc * CAP + s * 128 + 128],
                                w2b[:, pb * 4 * D + fc * D + half * 512:
                                    pb * 4 * D + fc * D + half * 512 + 512],
                                start=(fc == 0), stop=(fc == 3))
                        return inst
                    mmy = em.t(mm_y, waits=[prod_done, lw2, em.now("cs"),
                                            em.now("cv")])
                    em.v(lambda v, s=s, half=half: v.tensor_scalar(
                        yeb[:, s * D + half * 512:s * D + half * 512 + 512],
                        ps_h[:, :], wcol[:, s:s + 1], None, ALU.mult),
                        waits=[mmy])
            ye_done = em.now("cv")
            marks["w_free"][pb] = ye_done

            for s in range(8):
                yemark = em.dma("dye", "gpsimd", lambda g, s=s: g.indirect_dma_start(
                    out=o_ye[:, :],
                    out_offset=IndirectOffsetOnAxis(dyei[:, s:s + 1], 0),
                    in_=yeb[:, s * D:(s + 1) * D], in_offset=None,
                    bounds_check=regs["bc_y"], oob_is_err=False),
                    waits=[ye_done])
            marks["ye_free"] = yemark

        em.barrier_all()
        em.build(block)
    es.close()
    return nc


def launch_B_inputs(core, xln_list, counts_list, contrib_list, w1, w2, w3):
    xln_all = np.concatenate(xln_list, 0)
    contrib_all = np.concatenate(contrib_list, 0)
    cnt = np.stack([c.reshape(E) for c in counts_list], 0)
    e0 = core * EPC
    cnt_rot = np.roll(cnt, -e0, axis=1)
    counts_all = cnt_rot.reshape(1, NCORES * E).astype(np.float32)
    sl = slice(e0, e0 + EPC)
    w1T = np.ascontiguousarray(w1[sl].transpose(0, 2, 1)).reshape(EPC * D, FEXP)
    w3T = np.ascontiguousarray(w3[sl].transpose(0, 2, 1)).reshape(EPC * D, FEXP)
    w2T = np.ascontiguousarray(w2[sl].transpose(0, 2, 1)).reshape(EPC * FEXP, D)
    rows = np.arange(128, dtype=np.float32)
    slotbase = np.zeros((128, E), np.float32)
    for el in range(EPC):
        for s in range(8):
            slotbase[:, el * 8 + s] = (e0 + el) * CAP + s * 128 + rows
    prow = np.zeros((128, 8), np.float32)
    for s in range(8):
        prow[:, s] = s * 128 + rows
    return {
        "xln_all": xln_all, "counts_all": counts_all,
        "contrib_all": contrib_all,
        "w1T": w1T.astype(np.float32), "w3T": w3T.astype(np.float32),
        "w2T": w2T.astype(np.float32),
        "slotbase": slotbase, "prow": prow,
        "eofs": np.array([[float(e0)]], np.float32),
    }


def build_launch_C(nst=TC // 512, upto="all"):
    nc = bass.Bass("TRN2", target_bir_lowering=False)
    TCl = nst * 512
    x_in = nc.dram_tensor("x", [TCl, D], F32, kind="ExternalInput").ap()
    sh_in = nc.dram_tensor("shared", [TCl, D], F32, kind="ExternalInput").ap()
    dest_in = nc.dram_tensor("dest", [TCl, TOPK], F32, kind="ExternalInput").ap()
    yeR = nc.dram_tensor("yeR", [NSLOT + 1, D], BF16, kind="ExternalInput").ap()
    WqT_in = nc.dram_tensor("WqT", [D, D], F32, kind="ExternalInput").ap()
    WkT_in = nc.dram_tensor("WkT", [D, D], F32, kind="ExternalInput").ap()
    WvT_in = nc.dram_tensor("WvT", [D, D], F32, kind="ExternalInput").ap()
    bqc_in = nc.dram_tensor("bqc", [128, 8], F32, kind="ExternalInput").ap()
    bkc_in = nc.dram_tensor("bkc", [128, 8], F32, kind="ExternalInput").ap()
    bv_in = nc.dram_tensor("bv", [1, D], F32, kind="ExternalInput").ap()
    tri_in = nc.dram_tensor("tri", [128, 128], F32, kind="ExternalInput").ap()
    bdm_in = nc.dram_tensor("bdm", [128, 128], F32, kind="ExternalInput").ap()
    hmask_in = nc.dram_tensor("hmask", [128, 2], F32, kind="ExternalInput").ap()

    o_x1 = nc.dram_tensor("x1", [TCl, D], F32, kind="ExternalOutput").ap()
    o_op = nc.dram_tensor("opart", [TCl, D], F32, kind="ExternalOutput").ap()
    o_qf = nc.dram_tensor("qfm", [nst * 128, 8 * 512], BF16,
                          kind="ExternalOutput").ap()
    o_st = nc.dram_tensor("state", [128, 8 * 128], F32,
                          kind="ExternalOutput").ap()

    es = ExitStack()
    sb = lambda name, shape, dt: es.enter_context(nc.sbuf_tensor(name, shape, dt))
    psum = lambda name, shape, dt: es.enter_context(nc.psum_tensor(name, shape, dt))

    WqTb = sb("WqTb", [128, 8 * D], BF16)
    WkTb = sb("WkTb", [128, 8 * D], BF16)
    WvTb = sb("WvTb", [128, 8 * D], BF16)
    bqc = sb("bqc_sb", [128, 8], F32)
    bkc = sb("bkc_sb", [128, 8], F32)
    bvb = sb("bvb", [1, D], BF16)
    trim = sb("trim", [128, 128], F32)
    bdm = sb("bdm_sb", [128, 128], F32)
    hmask = sb("hmask_sb", [128, 2], F32)
    qzb = sb("qzb", [128, 2 * 8 * 512], BF16)
    ident = sb("ident", [128, 128], BF16)
    ones1_b = sb("ones1_b", [1, 128], BF16)

    xt = sb("xt", [128, D], F32)
    sht = sb("sht", [128, D], F32)
    x1t = sb("x1t", [128, D], F32)
    destf = sb("destf_c", [128, TOPK], F32)
    desti = sb("desti_c", [128, TOPK], I32)
    moe8 = sb("moe8", [128, TOPK * D], BF16)
    stats = sb("stats_c", [128, 12], F32)
    mv = sb("mv_c", [128, 2], F32)
    rr = sb("rr_c", [128, 1], F32)
    col = sb("col_c", [128, 1], F32)
    xlnt = sb("xlnt_c", [128, D], BF16)
    xlnT = sb("xlnT_c", [128, 4 * 8 * 128], BF16)
    qfm = sb("qfm_sb", [128, 8 * 512], BF16)
    kfm = sb("kfm_sb", [128, 8 * 512], BF16)
    ktm = sb("ktm_sb", [128, 4 * D], BF16)
    vtm = sb("vtm_sb", [128, 4 * D], BF16)
    act1 = sb("act1", [128, 512], F32)
    act2 = sb("act2", [128, 512], F32)
    sT = sb("sT_sb", [128, 4 * 128], BF16)
    stbf = sb("stbf", [128, 8 * 128], BF16)
    state = sb("state_sb", [128, 8 * 128], F32)
    sustage = sb("sustage", [128, 8 * 128], F32)
    oev = sb("oev", [128, D], F32)

    psT = psum("psT_c", [128, D], BF16)
    ps_a = psum("ps_a", [128, 512], F32)
    ps_b = psum("ps_b", [128, 512], F32)
    ps_o = psum("ps_o", [128, D], F32)
    ps_su = psum("ps_su", [128, 8 * 128], F32)

    with nc.Block() as block:
        em = Em(nc, es)
        chunked_load(em, WqTb, WqT_in, 8, D)
        chunked_load(em, WkTb, WkT_in, 8, D)
        chunked_load(em, WvTb, WvT_in, 8, D)
        em.dma("dpre", "sync", lambda s: s.dma_start(out=bqc[:, :], in_=bqc_in[:, :]))
        em.dma("dpre", "sync", lambda s: s.dma_start(out=bkc[:, :], in_=bkc_in[:, :]))
        em.dma("dpre_g", "gpsimd", lambda g: g.dma_start(out=bvb[:, :], in_=bv_in[:, :]))
        em.dma("dpre", "sync", lambda s: s.dma_start(out=trim[:, :], in_=tri_in[:, :]))
        em.dma("dpre", "sync", lambda s: s.dma_start(out=bdm[:, :], in_=bdm_in[:, :]))
        em.dma("dpre", "sync", lambda s: s.dma_start(out=hmask[:, :], in_=hmask_in[:, :]))
        em.g(lambda g: g.memset(ones1_b[:, :], 1.0))
        em.v(lambda v: v.memset(state[:, :], 0.0))
        emit_identity(em, ident)
        ld_all = em.now("dpre")
        gp_all = em.now("dpre_g")
        g_all = em.now("cg")

        regs = {}

        def init_regs(g):
            regs["bc"] = g.register("bc_ye").__enter__()
            g.reg_mov(regs["bc"], NSLOT)
            return None
        em.ops["gpsimd"].append((init_regs, [], ("cg", 0)))

        marks = {"moe_free": None, "x1_free": None, "xlnT_free": None,
                 "q_free": None, "k_free": None, "v_free": None,
                 "o_free": None, "sT_free": None, "d_free": None,
                 "xln_free": None}

        def W(key):
            return [w for w in [marks[key]] if w]

        for st in range(nst):
            for q in range(4):
                i = st * 4 + q
                ld_x = em.dma("dxx", "sync", lambda s, i=i: s.dma_start(
                    out=xt[:, :], in_=x_in[i * 128:(i + 1) * 128, :]),
                    waits=[em.now("cv")])
                ld_sh = em.dma("dsh", "sync", lambda s, i=i: s.dma_start(
                    out=sht[:, :], in_=sh_in[i * 128:(i + 1) * 128, :]),
                    waits=[em.now("cv")])
                ld_d = em.dma("ddst", "sync", lambda s, i=i: s.dma_start(
                    out=destf[:, :], in_=dest_in[i * 128:(i + 1) * 128, :]),
                    waits=[em.now("cv")])
                em.v(lambda v: v.tensor_scalar(destf[:, :], destf[:, :],
                                               float(NSLOT), None, ALU.min),
                     waits=[ld_d])
                cast = em.v(lambda v: v.tensor_copy(desti[:, :], destf[:, :]),
                            waits=W("d_free"))
                for k in range(TOPK):
                    moe_done = em.dma("dmoe", "gpsimd",
                                      lambda g, k=k: g.indirect_dma_start(
                        out=moe8[:, k * D:(k + 1) * D], out_offset=None,
                        in_=yeR[:, :],
                        in_offset=IndirectOffsetOnAxis(desti[:, k:k + 1], 0),
                        bounds_check=regs["bc"], oob_is_err=False),
                        waits=[cast] + W("moe_free"))
                marks["d_free"] = moe_done
                for half in range(2):
                    def mm_sum(t, half=half):
                        inst = None
                        for k in range(TOPK):
                            inst = t.matmul(
                                ps_a[:, :] if half == 0 else ps_b[:, :],
                                ident[:, :],
                                moe8[:, k * D + half * 512:k * D + half * 512 + 512],
                                start=(k == 0), stop=(k == TOPK - 1))
                        return inst
                    mms = em.t(mm_sum, waits=[moe_done, g_all, em.now("cv")])
                    pshalf = ps_a if half == 0 else ps_b
                    em.v(lambda v, half=half, pshalf=pshalf: v.tensor_add(
                        x1t[:, half * 512:half * 512 + 512], pshalf[:, :],
                        sht[:, half * 512:half * 512 + 512]),
                        waits=[mms, ld_sh] + (W("x1_free") if half == 0 else []))
                    em.v(lambda v, half=half: v.scalar_tensor_tensor(
                        x1t[:, half * 512:half * 512 + 512],
                        x1t[:, half * 512:half * 512 + 512], 0.5,
                        xt[:, half * 512:half * 512 + 512],
                        ALU.mult, ALU.add), waits=[ld_x])
                marks["moe_free"] = em.now("ct")
                x1_done = em.now("cv")
                st_x1 = em.dma("sx1", "sync", lambda s, i=i: s.dma_start(
                    out=o_x1[i * 128:(i + 1) * 128, :], in_=x1t[:, :]),
                    waits=[x1_done])
                marks["x1_free"] = st_x1
                if upto == "combine":
                    continue
                ln_done = emit_ln(em, x1t[:, :], xlnt[:, :], stats, mv, col, rr,
                                  [x1_done] + W("xln_free"))
                tr = emit_transpose8(em, psT, xlnt[:, :], ident,
                                     [ln_done, g_all] + W("xlnT_free"))
                marks["xln_free"] = tr
                em.v(lambda v, q=q: v.tensor_copy(
                    bass.AP(xlnT, q * 128, [[4 * 8 * 128, 128], [512, 8], [1, 128]]),
                    psT[:, :]), waits=[tr])
            xlnT_done = em.now("cv")
            if upto == "combine":
                continue

            for which, Wb, bcol, dst in (("q", WqTb, bqc, qfm), ("k", WkTb, bkc, kfm)):
                for c in range(8):
                    def mm_qk(t, Wb=Wb, c=c):
                        inst = None
                        for dc in range(8):
                            inst = t.matmul(
                                ps_a[:, :],
                                Wb[:, dc * D + c * 128:dc * D + c * 128 + 128],
                                bass.AP(xlnT, dc * 512, [[4096, 128], [1, 512]]),
                                start=(dc == 0), stop=(dc == 7))
                        return inst
                    mm1 = em.t(mm_qk, waits=[xlnT_done, gp_all, ld_all,
                                             em.now("cv"), em.now("cs")])
                    em.v(lambda v, bcol=bcol, c=c: v.tensor_scalar(
                        act1[:, :], ps_a[:, :], bcol[:, c:c + 1], 0.0,
                        ALU.add, ALU.max), waits=[mm1] + (W("q_free") if which == "q" else W("k_free")))
                    em.v(lambda v, bcol=bcol, c=c: v.scalar_tensor_tensor(
                        act2[:, :], ps_a[:, :], bcol[:, c:c + 1], act1[:, :],
                        ALU.add, ALU.subtract))
                    sc_exp = em.s(lambda s: s.activation(act2[:, :], act2[:, :],
                                                         AF.Exp),
                                  waits=[em.now("cv")])
                    em.v(lambda v, dst=dst, c=c: v.tensor_add(
                        dst[:, c * 512:(c + 1) * 512], act1[:, :], act2[:, :]),
                        waits=[sc_exp])
            for c in range(8):
                for par in range(2):
                    em.v(lambda v, c=c, par=par: v.tensor_scalar(
                        qzb[:, c * 1024 + par * 512:c * 1024 + par * 512 + 512],
                        qfm[:, c * 512:(c + 1) * 512],
                        hmask[:, par:par + 1], None, ALU.mult))
            q_done = em.now("cv")
            st_qf = em.dma("sqf", "sync", lambda s, st=st: s.dma_start(
                out=o_qf[st * 128:(st + 1) * 128, :], in_=qfm[:, :]),
                waits=[q_done])
            for q in range(4):
                for half in range(2):
                    def mm_v(t, q=q, half=half):
                        for dc in range(8):
                            t.matmul(ps_b[:, :],
                                     bass.AP(xlnT, dc * 512 + q * 128,
                                             [[4096, 128], [1, 128]]),
                                     WvTb[:, dc * D + half * 512:
                                          dc * D + half * 512 + 512],
                                     start=(dc == 0), stop=False)
                        return t.matmul(ps_b[:, :], ones1_b[:, :],
                                        bvb[:, half * 512:half * 512 + 512],
                                        start=False, stop=True)
                    mmv = em.t(mm_v, waits=[xlnT_done, gp_all, g_all,
                                            em.now("cv")])
                    em.v(lambda v, q=q, half=half: v.tensor_copy(
                        vtm[:, q * D + half * 512:q * D + half * 512 + 512],
                        ps_b[:, :]), waits=[mmv] + (W("v_free") if q == 0 and half == 0 else []))
            v_done = em.now("cv")
            for q in range(4):
                def tr_k(t, q=q):
                    inst = None
                    for c in range(8):
                        inst = t.transpose(
                            psT[:, c * 128:(c + 1) * 128],
                            kfm[:, c * 512 + q * 128:c * 512 + q * 128 + 128],
                            ident[:, :])
                    return inst
                trk = em.t(tr_k, waits=[q_done, em.now("cv")])
                em.v(lambda v, q=q: v.tensor_copy(ktm[:, q * D:(q + 1) * D],
                                                  psT[:, :]), waits=[trk])
            k_done = em.now("cv")
            marks["xlnT_free"] = em.now("ct")
            if upto == "qkv":
                continue

            for q in range(4):
                i = st * 4 + q
                em.v(lambda v: v.tensor_copy(stbf[:, :], state[:, :]),
                     waits=[em.now("ct")])
                stbf_done = em.now("cv")
                for hg in range(4):
                    def mm_sT2(t, hg=hg, q=q):
                        inst = None
                        for hh in range(4):
                            h = hg * 4 + hh
                            c = h // 2
                            par = h % 2
                            lhs = kfm[:, c * 512 + q * 128:
                                      c * 512 + q * 128 + 128]
                            rhs = qzb[:, c * 1024 + par * 512 + q * 128:
                                      c * 1024 + par * 512 + q * 128 + 128]
                            inst = t.matmul(ps_a[:, hh * 128:(hh + 1) * 128],
                                            lhs, rhs, start=True, stop=True)
                        return inst
                    mmst = em.t(mm_sT2, waits=[q_done, k_done, em.now("cv")])
                    em.v(lambda v, hg=hg: v.tensor_tensor(
                        sT[:, :], ps_a[:, :],
                        bass.AP(trim, 0, [[128, 128], [0, 4], [1, 128]]),
                        ALU.mult), waits=[mmst] + W("sT_free"))
                    sT_done = em.now("cv")
                    def mm_o(t, hg=hg, q=q):
                        inst = None
                        for hh in range(4):
                            h = hg * 4 + hh
                            inst = t.matmul(
                                ps_o[:, h * 64:(h + 1) * 64],
                                sT[:, hh * 128:(hh + 1) * 128],
                                vtm[:, q * D + h * 64:q * D + h * 64 + 64],
                                start=(h % 8 == 0), stop=False)
                        return inst
                    if upto in ("all", "o"):
                        em.t(mm_o, waits=[sT_done, v_done])
                    marks["sT_free"] = em.now("ct")
                def mm_sr(t, q=q):
                    inst = None
                    for c in range(8):
                        inst = t.matmul(
                            ps_o[:, c * 128:(c + 1) * 128],
                            qfm[:, c * 512 + q * 128:c * 512 + q * 128 + 128],
                            stbf[:, c * 128:(c + 1) * 128],
                            start=False, stop=(c == 3 or c == 7))
                    return inst
                if upto in ("all", "o"):
                    mmsr = em.t(mm_sr, waits=[stbf_done, q_done])
                    ev1 = em.v(lambda v: v.tensor_copy(oev[:, 0:512],
                                                       ps_o[:, 0:512]),
                               waits=[mmsr] + W("o_free"))
                    ev2 = em.v(lambda v: v.tensor_copy(oev[:, 512:1024],
                                                       ps_o[:, 512:1024]))
                    st_o = em.dma("sop", "sync", lambda s, i=i: s.dma_start(
                        out=o_op[i * 128:(i + 1) * 128, :], in_=oev[:, :]),
                        waits=[em.now("cv")])
                    marks["o_free"] = st_o
                def mm_su(t, q=q):
                    inst = None
                    for c in range(8):
                        inst = t.matmul(
                            ps_su[:, c * 128:(c + 1) * 128],
                            ktm[:, q * D + c * 128:q * D + c * 128 + 128],
                            vtm[:, q * D + c * 128:q * D + c * 128 + 128],
                            start=True, stop=True)
                    return inst
                if upto == "all":
                    mmsu = em.t(mm_su, waits=[k_done, v_done, em.now("cv")])
                    em.v(lambda v: v.tensor_tensor(
                        sustage[:, :], ps_su[:, :],
                        bass.AP(bdm, 0, [[128, 128], [0, 8], [1, 128]]),
                        ALU.mult), waits=[mmsu])
                    em.v(lambda v: v.tensor_add(state[:, :], state[:, :],
                                                sustage[:, :]))
            marks["q_free"] = em.now("ct")
            marks["k_free"] = em.now("ct")
            marks["v_free"] = em.now("ct")

        em.dma("sst", "sync", lambda s: s.dma_start(
            out=o_st[:, :], in_=state[:, :]), waits=[em.now("cv")])
        em.barrier_all()
        em.build(block)
    es.close()
    return nc


def build_launch_D(nst=TC // 512):
    nc = bass.Bass("TRN2", target_bir_lowering=False)
    TCl = nst * 512
    x1_in = nc.dram_tensor("x1", [TCl, D], F32, kind="ExternalInput").ap()
    op_in = nc.dram_tensor("opart", [TCl, D], F32, kind="ExternalInput").ap()
    qf_in = nc.dram_tensor("qfm", [nst * 128, 8 * 512], BF16,
                           kind="ExternalInput").ap()
    sp_in = nc.dram_tensor("Spart", [128, 8 * 128], F32,
                           kind="ExternalInput").ap()
    lng_in = nc.dram_tensor("ln_g", [128, 8], F32, kind="ExternalInput").ap()
    lnb_in = nc.dram_tensor("ln_b", [128, 8], F32, kind="ExternalInput").ap()
    WoT_in = nc.dram_tensor("WoT", [D, D], F32, kind="ExternalInput").ap()
    bo_in = nc.dram_tensor("bo", [1, D], F32, kind="ExternalInput").ap()
    o_x2 = nc.dram_tensor("x2", [TCl, D], F32, kind="ExternalOutput").ap()

    es = ExitStack()
    sb = lambda name, shape, dt: es.enter_context(nc.sbuf_tensor(name, shape, dt))
    psum = lambda name, shape, dt: es.enter_context(nc.psum_tensor(name, shape, dt))

    WoTb = sb("WoTb", [128, 8 * D], BF16)
    lngc = sb("lngc", [128, 8], F32)
    lnbc = sb("lnbc", [128, 8], BF16)
    bWo = sb("bWo", [1, D], BF16)
    bob = sb("bob", [1, D], BF16)
    ones1_b = sb("ones1_b_d", [1, 128], BF16)
    ident = sb("ident_d", [128, 128], BF16)
    spb = sb("spb", [128, 8 * 128], BF16)
    qfb = sb("qfb", [128, 8 * 512], BF16)
    x1t = sb("x1t_d", [128, D], F32)
    opt = sb("opt_d", [128, D], F32)
    ot = sb("ot_d", [128, D], F32)
    olnt = sb("olnt_d", [128, D], BF16)
    olnT = sb("olnT_d", [128, D], BF16)
    stats = sb("stats_d", [128, 12], F32)
    mv = sb("mv_d", [128, 2], F32)
    rr = sb("rr_d", [128, 1], F32)
    col = sb("col_d", [128, 1], F32)
    x2t = sb("x2t", [128, D], F32)
    psT = psum("psT_d", [128, D], BF16)
    ps_o = psum("ps_o_d", [128, D], F32)
    ps_p = psum("ps_p_d", [128, 512], F32)
    ps_r = psum("ps_r_d", [1, D], F32)

    with nc.Block() as block:
        em = Em(nc, es)
        chunked_load(em, WoTb, WoT_in, 8, D)
        em.dma("dpre", "sync", lambda s: s.dma_start(
            out=lngc[:, :], in_=lng_in[:, :]))
        em.dma("dpre_g", "gpsimd", lambda g: g.dma_start(
            out=lnbc[:, :], in_=lnb_in[:, :]))
        em.dma("dpre_g", "gpsimd", lambda g: g.dma_start(
            out=spb[:, :], in_=sp_in[:, :]))
        em.dma("dpre_g", "gpsimd", lambda g: g.dma_start(
            out=bob[:, :], in_=bo_in[:, :]))
        em.g(lambda g: g.memset(ones1_b[:, :], 1.0))
        emit_identity(em, ident)
        ld_all = em.now("dpre")
        gp_all = em.now("dpre_g")
        g_all = em.now("cg")

        def mm_bw(t):
            inst = None
            for half in range(2):
                for c in range(8):
                    inst = t.matmul(ps_r[:, half * 512:half * 512 + 512],
                                    lnbc[:, c:c + 1],
                                    WoTb[:, c * D + half * 512:
                                         c * D + half * 512 + 512],
                                    start=(c == 0), stop=(c == 7))
            return inst
        mmbw = em.t(mm_bw, waits=[ld_all, gp_all, g_all])
        em.v(lambda v: v.tensor_copy(bWo[:, :], ps_r[:, :]), waits=[mmbw])
        for c in range(8):
            em.s(lambda s, c=c: s.mul(WoTb[:, c * D:(c + 1) * D],
                                      WoTb[:, c * D:(c + 1) * D],
                                      lngc[:, c:c + 1]),
                 waits=([mmbw] if c == 0 else []))
        wos_done = em.now("cs")

        marks = {"x2_free": None, "oln_free": None, "q_free": None}

        def W(key):
            return [w for w in [marks[key]] if w]

        for st in range(nst):
            ld_q = em.dma("dqf", "sync", lambda s, st=st: s.dma_start(
                out=qfb[:, :], in_=qf_in[st * 128:(st + 1) * 128, :]),
                waits=W("q_free"))
            for q in range(4):
                i = st * 4 + q
                ld_x1 = em.dma("dx1", "sync", lambda s, i=i: s.dma_start(
                    out=x1t[:, :], in_=x1_in[i * 128:(i + 1) * 128, :]),
                    waits=[em.now("cv")])
                ld_op = em.dma("dop", "sync", lambda s, i=i: s.dma_start(
                    out=opt[:, :], in_=op_in[i * 128:(i + 1) * 128, :]),
                    waits=[em.now("cv")])
                def mm_oc(t, q=q):
                    inst = None
                    for c in range(8):
                        inst = t.matmul(
                            ps_o[:, c * 128:(c + 1) * 128],
                            qfb[:, c * 512 + q * 128:c * 512 + q * 128 + 128],
                            spb[:, c * 128:(c + 1) * 128],
                            start=(c % 4 == 0), stop=(c == 3 or c == 7))
                    return inst
                mmoc = em.t(mm_oc, waits=[ld_q, gp_all, em.now("cv")])
                em.v(lambda v: v.tensor_add(ot[:, 0:512], ps_o[:, 0:512],
                                            opt[:, 0:512]),
                     waits=[mmoc, ld_op])
                em.v(lambda v: v.tensor_add(ot[:, 512:1024], ps_o[:, 512:1024],
                                            opt[:, 512:1024]))
                o_done = em.now("cv")
                ln_done = emit_ln(em, ot[:, :], olnt[:, :], stats, mv, col, rr,
                                  [o_done] + W("oln_free"))
                tr = emit_transpose8(em, psT, olnt[:, :], ident,
                                     [ln_done, g_all])
                marks["oln_free"] = tr
                evT = em.v(lambda v: v.tensor_copy(olnT[:, :], psT[:, :]),
                           waits=[tr])
                for half in range(2):
                    def mm_p(t, half=half):
                        for c in range(8):
                            t.matmul(ps_p[:, :],
                                     olnT[:, c * 128:(c + 1) * 128],
                                     WoTb[:, c * D + half * 512:
                                          c * D + half * 512 + 512],
                                     start=(c == 0), stop=False)
                        t.matmul(ps_p[:, :], ones1_b[:, :],
                                 bWo[:, half * 512:half * 512 + 512],
                                 start=False, stop=False)
                        return t.matmul(ps_p[:, :], ones1_b[:, :],
                                        bob[:, half * 512:half * 512 + 512],
                                        start=False, stop=True)
                    mmp = em.t(mm_p, waits=[evT, wos_done, em.now("cv")])
                    em.v(lambda v, half=half: v.tensor_add(
                        x2t[:, half * 512:half * 512 + 512], ps_p[:, :],
                        x1t[:, half * 512:half * 512 + 512]),
                        waits=[mmp, ld_x1] + (W("x2_free") if half == 0 else []))
                st_x2 = em.dma("sx2", "sync", lambda s, i=i: s.dma_start(
                    out=o_x2[i * 128:(i + 1) * 128, :], in_=x2t[:, :]),
                    waits=[em.now("cv")])
                marks["x2_free"] = st_x2
            marks["q_free"] = em.now("ct")

        em.barrier_all()
        em.build(block)
    es.close()
    return nc


def emit_combine_tile(em, i, x_src, sh_src, dest_src, yeR, bufs, regs, marks, W,
                      out_x=None, coef=0.5):
    xt, sht, destf, desti = bufs["xt"], bufs["sht"], bufs["destf"], bufs["desti"]
    moe8, x1t, ident = bufs["moe8"], bufs["x1t"], bufs["ident"]
    ps_a, ps_b = bufs["ps_a"], bufs["ps_b"]
    ld_x = em.dma("dxx", "sync", lambda s: s.dma_start(
        out=xt[:, :], in_=x_src[i * 128:(i + 1) * 128, :]),
        waits=[em.now("cv")])
    ld_sh = em.dma("dsh", "sync", lambda s: s.dma_start(
        out=sht[:, :], in_=sh_src[i * 128:(i + 1) * 128, :]),
        waits=[em.now("cv")])
    ld_d = em.dma("ddst", "sync", lambda s: s.dma_start(
        out=destf[:, :], in_=dest_src[i * 128:(i + 1) * 128, :]),
        waits=[em.now("cv")])
    em.v(lambda v: v.tensor_scalar(destf[:, :], destf[:, :], float(NSLOT),
                                   None, ALU.min), waits=[ld_d])
    cast = em.v(lambda v: v.tensor_copy(desti[:, :], destf[:, :]),
                waits=W("d_free"))
    for k in range(TOPK):
        moe_done = em.dma("dmoe", "gpsimd", lambda g, k=k: g.indirect_dma_start(
            out=moe8[:, k * D:(k + 1) * D], out_offset=None,
            in_=yeR[:, :],
            in_offset=IndirectOffsetOnAxis(desti[:, k:k + 1], 0),
            bounds_check=regs["bc"], oob_is_err=False),
            waits=[cast] + W("moe_free"))
    marks["d_free"] = moe_done
    for half in range(2):
        def mm_sum(t, half=half):
            inst = None
            for k in range(TOPK):
                inst = t.matmul(
                    ps_a[:, :] if half == 0 else ps_b[:, :], ident[:, :],
                    moe8[:, k * D + half * 512:k * D + half * 512 + 512],
                    start=(k == 0), stop=(k == TOPK - 1))
            return inst
        mms = em.t(mm_sum, waits=[moe_done, em.now("cg"), em.now("cv")])
        pshalf = ps_a if half == 0 else ps_b
        em.v(lambda v, half=half, pshalf=pshalf: v.tensor_add(
            x1t[:, half * 512:half * 512 + 512], pshalf[:, :],
            sht[:, half * 512:half * 512 + 512]),
            waits=[mms, ld_sh] + (W("x1_free") if half == 0 else []))
        em.v(lambda v, half=half: v.scalar_tensor_tensor(
            x1t[:, half * 512:half * 512 + 512],
            x1t[:, half * 512:half * 512 + 512], coef,
            xt[:, half * 512:half * 512 + 512],
            ALU.mult, ALU.add), waits=[ld_x])
    marks["moe_free"] = em.now("ct")
    x1_done = em.now("cv")
    if out_x is not None:
        st_x1 = em.dma("sx1", "sync", lambda s: s.dma_start(
            out=out_x[i * 128:(i + 1) * 128, :], in_=x1t[:, :]),
            waits=[x1_done])
        marks["x1_free"] = st_x1
    return x1_done


def build_launch_FFN(nst=TC // 512, final=False):
    nc = bass.Bass("TRN2", target_bir_lowering=False)
    TCl = nst * 512
    x_in = nc.dram_tensor("x", [TCl, D], F32, kind="ExternalInput").ap()
    w1_in = nc.dram_tensor("w1", [HF, D], F32, kind="ExternalInput").ap()
    w1T_in = nc.dram_tensor("w1T", [D, HF], F32, kind="ExternalInput").ap()
    w2_in = nc.dram_tensor("w2", [D, HF], F32, kind="ExternalInput").ap()
    w2T_in = nc.dram_tensor("w2T", [HF, D], F32, kind="ExternalInput").ap()
    b1c_in = nc.dram_tensor("b1c", [128, HF // 128], F32,
                            kind="ExternalInput").ap()
    b2c_in = nc.dram_tensor("b2c", [128, 8], F32, kind="ExternalInput").ap()
    o_x = nc.dram_tensor("xo", [TCl, D], F32, kind="ExternalOutput").ap()

    es = ExitStack()
    sb = lambda name, shape, dt: es.enter_context(nc.sbuf_tensor(name, shape, dt))
    psum = lambda name, shape, dt: es.enter_context(nc.psum_tensor(name, shape, dt))

    NH = HF // 128
    ident = sb("ident_n", [128, 128], BF16)
    s1Tb = sb("s1Tb", [128, 8 * HF], BF16)
    s2Tb = sb("s2Tb", [128, NH * D], BF16)
    al1 = sb("al1", [128, NH], F32)
    al2 = sb("al2", [128, 8], F32)
    b1c = sb("b1c_sb", [128, NH], F32)
    b2c = sb("b2c_sb", [128, 8], F32)
    xA = sb("xA", [128, 4 * D], F32)
    wscr = xA
    stats = sb("stats_n", [128, 12], F32)
    mv = sb("mv_n", [128, 2], F32)
    rr = sb("rr_n", [128, 1], F32)
    col = sb("col_n", [128, 1], F32)
    xlnt = sb("xlnt_n", [128, D], BF16)
    xlnT = sb("xlnT_n", [128, 4096], BF16)
    hfm = sb("hfm", [128, NH * 256], BF16)
    outfm = sb("outfm", [128, 8 * 256], BF16)
    x3ff = sb("x3ff", [128, D], BF16)
    xot = sb("xot", [128, D], F32)
    psT = psum("psT_n", [128, D], BF16)
    ps_h = psum("ps_h_n", [128, 512], F32)
    ps_h2 = psum("ps_h2_n", [128, 512], F32)

    with nc.Block() as block:
        em = Em(nc, es)
        em.dma("dpre", "sync", lambda s: s.dma_start(out=b1c[:, :],
                                                     in_=b1c_in[:, :]))
        em.dma("dpre", "sync", lambda s: s.dma_start(out=b2c[:, :],
                                                     in_=b2c_in[:, :]))
        emit_identity(em, ident)
        g_all = em.now("cg")

        for ti in range(NH):
            ldw = em.dma("dwn%d" % (ti % 2), "sync", lambda s, ti=ti: s.dma_start(
                out=wscr[:, (ti % 2) * D:(ti % 2 + 1) * D],
                in_=w1_in[ti * 128:(ti + 1) * 128, :]), waits=[em.now("cs")])
            em.s(lambda s, ti=ti: s.activation(
                wscr[:, (ti % 2) * D:(ti % 2 + 1) * D],
                wscr[:, (ti % 2) * D:(ti % 2 + 1) * D],
                AF.Abs, accum_out=al1[:, ti:ti + 1]), waits=[ldw])
        em.v(lambda v: v.tensor_scalar(al1[:, :], al1[:, :], 1.0 / D, None,
                                       ALU.mult), waits=[em.now("cs")])
        for c in range(8):
            ldt = em.dma("dwt", "sync", lambda s, c=c: s.dma_start(
                out=wscr[:, :], in_=w1T_in[c * 128:(c + 1) * 128, :]),
                waits=[em.now("cs")])
            em.s(lambda s, c=c: s.activation(
                s1Tb[:, c * HF:(c + 1) * HF], wscr[:, :], AF.Sign),
                waits=[ldt])
        for ti in range(8):
            ldw = em.dma("dwt", "sync", lambda s, ti=ti: s.dma_start(
                out=wscr[:, :], in_=w2_in[ti * 128:(ti + 1) * 128, :]),
                waits=[em.now("cs")])
            em.s(lambda s, ti=ti: s.activation(
                wscr[:, :], wscr[:, :],
                AF.Abs, accum_out=al2[:, ti:ti + 1]), waits=[ldw])
        em.v(lambda v: v.tensor_scalar(al2[:, :], al2[:, :], 1.0 / HF, None,
                                       ALU.mult), waits=[em.now("cs")])
        for c in range(NH):
            ldt = em.dma("dwn%d" % (c % 2), "sync", lambda s, c=c: s.dma_start(
                out=wscr[:, (c % 2) * D:(c % 2 + 1) * D],
                in_=w2T_in[c * 128:(c + 1) * 128, :]), waits=[em.now("cs")])
            em.s(lambda s, c=c: s.activation(
                s2Tb[:, c * D:(c + 1) * D],
                wscr[:, (c % 2) * D:(c % 2 + 1) * D], AF.Sign), waits=[ldt])
        prep_done = em.now("cs")
        al_done = em.now("cv")

        marks = {"xlnT_free": None, "xln_free": None, "h_free": None,
                 "out_free": None, "xo_free": None, "xA_free": None}

        def W(key):
            return [w for w in [marks[key]] if w]

        for st in range(nst):
            for q in range(4):
                i = st * 4 + q
                ld_x = em.dma("dxx", "sync", lambda s, i=i, q=q: s.dma_start(
                    out=xA[:, q * D:(q + 1) * D],
                    in_=x_in[i * 128:(i + 1) * 128, :]),
                    waits=[em.now("cv"), em.now("ct")] + W("xA_free"))
                ln_done = emit_ln(em, xA[:, q * D:(q + 1) * D], xlnt[:, :],
                                  stats, mv, col, rr,
                                  [ld_x, prep_done] + W("xln_free"))
                tr = emit_transpose8(em, psT, xlnt[:, :], ident,
                                     [ln_done, g_all] + W("xlnT_free"))
                marks["xln_free"] = tr
                em.v(lambda v, q=q: v.tensor_copy(
                    bass.AP(xlnT, q * 128, [[4096, 128], [512, 8], [1, 128]]),
                    psT[:, :]), waits=[tr])
            xlnT_done = em.now("cv")

            for hs in range(2):
                for hc in range(NH):
                    def mm_h(t, hc=hc, hs=hs):
                        inst = None
                        for dc in range(8):
                            inst = t.matmul(
                                ps_h[:, 0:256],
                                s1Tb[:, dc * HF + hc * 128:dc * HF + hc * 128 + 128],
                                bass.AP(xlnT, dc * 512 + hs * 256,
                                        [[4096, 128], [1, 256]]),
                                start=(dc == 0), stop=(dc == 7))
                        return inst
                    mmh = em.t(mm_h, waits=[xlnT_done, em.now("cs")])
                    em.s(lambda s, hc=hc: s.activation(
                        hfm[:, hc * 256:(hc + 1) * 256], ps_h[:, 0:256],
                        AF.Gelu, bias=b1c[:, hc:hc + 1],
                        scale=al1[:, hc:hc + 1]),
                        waits=[mmh, al_done] + (W("h_free") if hc == 0 else []))
                h_done = em.now("cs")

                for dc in range(8):
                    def mm_o(t, dc=dc):
                        inst = None
                        for hc in range(NH):
                            inst = t.matmul(
                                ps_h[:, 0:256],
                                s2Tb[:, hc * D + dc * 128:hc * D + dc * 128 + 128],
                                hfm[:, hc * 256:(hc + 1) * 256],
                                start=(hc == 0), stop=(hc == NH - 1))
                        return inst
                    mmo = em.t(mm_o, waits=[h_done, em.now("cv")])
                    em.v(lambda v, dc=dc: v.tensor_scalar(
                        outfm[:, dc * 256:(dc + 1) * 256], ps_h[:, 0:256],
                        al2[:, dc:dc + 1], b2c[:, dc:dc + 1],
                        ALU.mult, ALU.add),
                        waits=[mmo] + (W("out_free") if dc == 0 else []))
                out_done = em.now("cv")
                marks["h_free"] = em.now("ct")

                for qq in range(2):
                    q = hs * 2 + qq
                    i = st * 4 + q
                    def tr_o(t, qq=qq):
                        inst = None
                        for dc in range(8):
                            inst = t.transpose(
                                psT[:, dc * 128:(dc + 1) * 128],
                                outfm[:, dc * 256 + qq * 128:
                                      dc * 256 + qq * 128 + 128],
                                ident[:, :])
                        return inst
                    tro = em.t(tr_o, waits=[out_done, em.now("cv")])
                    em.v(lambda v: v.tensor_copy(x3ff[:, :], psT[:, :]),
                         waits=[tro])
                    em.v(lambda v, q=q: v.scalar_tensor_tensor(
                        xot[:, :], x3ff[:, :], 0.5, xA[:, q * D:(q + 1) * D],
                        ALU.mult, ALU.add), waits=W("xo_free"))
                    st_xo = em.dma("sxo", "sync", lambda s, i=i: s.dma_start(
                        out=o_x[i * 128:(i + 1) * 128, :], in_=xot[:, :]),
                        waits=[em.now("cv")])
                    marks["xo_free"] = st_xo
                marks["out_free"] = em.now("ct")
            marks["xA_free"] = em.now("cv")

        em.barrier_all()
        em.build(block)
    es.close()
    return nc


def build_launch_WATT(nst=TC // 512):
    nc = bass.Bass("TRN2", target_bir_lowering=False)
    TCl = nst * 512
    x_in = nc.dram_tensor("x", [TCl, D], F32, kind="ExternalInput").ap()
    wqkv_in = nc.dram_tensor("w_qkv", [D, 3 * D], F32, kind="ExternalInput").ap()
    wout_in = nc.dram_tensor("w_out", [D, D], F32, kind="ExternalInput").ap()
    bout_in = nc.dram_tensor("b_out", [1, D], F32, kind="ExternalInput").ap()
    wm_in = nc.dram_tensor("wmask", [128, 256], F32, kind="ExternalInput").ap()
    o_x4 = nc.dram_tensor("x4p", [TCl, D], F32, kind="ExternalOutput").ap()
    o_q0 = nc.dram_tensor("q0fm", [128, 8 * 128], BF16, kind="ExternalOutput").ap()
    o_k0 = nc.dram_tensor("k0fm", [128, 8 * 128], BF16, kind="ExternalOutput").ap()
    o_v0 = nc.dram_tensor("v0tm", [128, D], BF16, kind="ExternalOutput").ap()
    o_kL = nc.dram_tensor("kLfm", [128, 8 * 128], BF16, kind="ExternalOutput").ap()
    o_vL = nc.dram_tensor("vLtm", [128, D], BF16, kind="ExternalOutput").ap()

    es = ExitStack()
    sb = lambda name, shape, dt: es.enter_context(nc.sbuf_tensor(name, shape, dt))
    psum = lambda name, shape, dt: es.enter_context(nc.psum_tensor(name, shape, dt))

    ident = sb("ident_w", [128, 128], BF16)
    ones1_b = sb("ones1_w", [1, 128], BF16)
    wqkvb = sb("wqkvb", [128, 8 * 3 * D], BF16)
    woutb = sb("woutb", [128, 8 * D], BF16)
    boutb = sb("boutb", [1, D], BF16)
    wmb = sb("wmb", [128, 256], F32)
    xA = sb("xA_w", [128, 4 * D], F32)
    stats = sb("stats_w", [128, 12], F32)
    mv = sb("mv_w", [128, 2], F32)
    rr = sb("rr_w", [128, 1], F32)
    col = sb("col_w", [128, 1], F32)
    negmax = sb("negmax_w", [128, 1], F32)
    rden = sb("rden_w", [128, 1], F32)
    xlnt = sb("xlnt_w", [128, D], BF16)
    xlnT = sb("xlnT_w", [128, 4096], BF16)
    qfm = sb("qfm_w", [128, 8 * 512], BF16)
    kfm = sb("kfm_w", [128, 8 * 512], BF16)
    kring = sb("kring", [128, 2 * 8 * 128], BF16)
    vring = sb("vring", [128, 2 * D], BF16)
    ssb = sb("ssb_w", [128, 256], F32)
    eS = sb("eS_w", [128, 256], BF16)
    eST = sb("eST_w", [128, 256], BF16)
    attn = sb("attn_w", [128, D], BF16)
    attnT = sb("attnT_w", [128, D], BF16)
    ebuf = sb("ebuf_w", [128, D], BF16)
    x4t = sb("x4t_w", [128, D], F32)
    psT = psum("psT_w", [128, D], BF16)
    ps_s = psum("ps_s_w", [128, 512], F32)
    ps_a = psum("ps_a_w", [128, 512], F32)
    ps_b = psum("ps_b_w", [128, 512], F32)

    with nc.Block() as block:
        em = Em(nc, es)
        chunked_load(em, wqkvb, wqkv_in, 8, 3 * D)
        chunked_load(em, woutb, wout_in, 8, D)
        em.dma("dpre_g", "gpsimd", lambda g: g.dma_start(out=boutb[:, :],
                                                         in_=bout_in[:, :]))
        em.dma("dpre", "sync", lambda s: s.dma_start(out=wmb[:, :],
                                                     in_=wm_in[:, :]))
        em.g(lambda g: g.memset(ones1_b[:, :], 1.0))
        emit_identity(em, ident)
        ld_all = em.now("dpre")
        gp_all = em.now("dpre_g")
        g_all = em.now("cg")

        marks = {"xlnT_free": None, "xln_free": None, "q_free": None,
                 "k_free": None, "ring_free": None, "x4_free": None,
                 "xA_free": None, "eS_free": None, "attn_free": None}

        def W(key):
            return [w for w in [marks[key]] if w]

        for st in range(nst):
            for q in range(4):
                i = st * 4 + q
                ld_x = em.dma("dxx", "sync", lambda s, i=i, q=q: s.dma_start(
                    out=xA[:, q * D:(q + 1) * D],
                    in_=x_in[i * 128:(i + 1) * 128, :]),
                    waits=[em.now("cv")] + W("xA_free"))
                ln_done = emit_ln(em, xA[:, q * D:(q + 1) * D], xlnt[:, :],
                                  stats, mv, col, rr,
                                  [ld_x] + W("xln_free"))
                tr = emit_transpose8(em, psT, xlnt[:, :], ident,
                                     [ln_done, g_all] + W("xlnT_free"))
                marks["xln_free"] = tr
                em.v(lambda v, q=q: v.tensor_copy(
                    bass.AP(xlnT, q * 128, [[4096, 128], [512, 8], [1, 128]]),
                    psT[:, :]), waits=[tr])
            xlnT_done = em.now("cv")

            for which, off, dst in (("q", 0, qfm), ("k", D, kfm)):
                for c in range(8):
                    def mm_qk(t, off=off, c=c):
                        inst = None
                        for dc in range(8):
                            inst = t.matmul(
                                ps_s[:, :],
                                wqkvb[:, dc * 3 * D + off + c * 128:
                                      dc * 3 * D + off + c * 128 + 128],
                                bass.AP(xlnT, dc * 512, [[4096, 128], [1, 512]]),
                                start=(dc == 0), stop=(dc == 7))
                        return inst
                    guard = W("q_free") if which == "q" else W("k_free")
                    mm1 = em.t(mm_qk, waits=[xlnT_done, gp_all,
                                             em.now("cv")])
                    em.v(lambda v, dst=dst, c=c: v.tensor_copy(
                        dst[:, c * 512:(c + 1) * 512], ps_s[:, :]),
                        waits=[mm1] + (guard if c == 0 else []))
            qk_done = em.now("cv")

            for q in range(4):
                i = st * 4 + q
                rp = i % 2
                for half in range(2):
                    def mm_v(t, q=q, half=half):
                        inst = None
                        for dc in range(8):
                            inst = t.matmul(
                                ps_b[:, :],
                                bass.AP(xlnT, dc * 512 + q * 128,
                                        [[4096, 128], [1, 128]]),
                                wqkvb[:, dc * 3 * D + D + half * 512:
                                      dc * 3 * D + D + half * 512 + 512],
                                start=(dc == 0), stop=(dc == 7))
                        return inst
                    mmv = em.t(mm_v, waits=[xlnT_done, gp_all, em.now("cv")])
                    em.v(lambda v, rp=rp, half=half: v.tensor_copy(
                        vring[:, rp * D + half * 512:rp * D + half * 512 + 512],
                        ps_b[:, :]),
                        waits=[mmv] + (W("ring_free") if half == 0 else []))
                em.v(lambda v, rp=rp, q=q: v.tensor_copy(
                    bass.AP(kring, rp * 1024, [[2 * 1024, 128], [128, 8], [1, 128]]),
                    bass.AP(kfm, q * 128, [[8 * 512, 128], [512, 8], [1, 128]])),
                    waits=[qk_done])
                ring_done = em.now("cv")

                first = (i == 0)
                if not first:
                    def mm_s_fix(t, q=q, rp=rp):
                        inst = None
                        for dc in range(8):
                            t.matmul(ps_s[:, 0:128],
                                     qfm[:, dc * 512 + q * 128:
                                         dc * 512 + q * 128 + 128],
                                     kring[:, (1 - rp) * 1024 + dc * 128:
                                           (1 - rp) * 1024 + dc * 128 + 128],
                                     start=(dc == 0), stop=(dc == 7))
                        for dc in range(8):
                            inst = t.matmul(ps_s[:, 128:256],
                                            qfm[:, dc * 512 + q * 128:
                                                dc * 512 + q * 128 + 128],
                                            kring[:, rp * 1024 + dc * 128:
                                                  rp * 1024 + dc * 128 + 128],
                                            start=(dc == 0),
                                            stop=(dc == 7))
                        return inst
                    mms = em.t(mm_s_fix, waits=[qk_done, ring_done,
                                                em.now("cv"), em.now("cs")])
                    em.v(lambda v: v.scalar_tensor_tensor(
                        ssb[:, :], ps_s[:, 0:256], float(D) ** -0.5, wmb[:, :],
                        ALU.mult, ALU.add), waits=[mms, ld_all])
                    em.v(lambda v: v.reduce_max(col[:, :], ssb[:, :], axis=AX.X))
                    em.v(lambda v: v.tensor_scalar(negmax[:, :], col[:, :],
                                                   -1.0, None, ALU.mult))
                    se = em.s(lambda s: s.activation(eS[:, :], ssb[:, :],
                                                     AF.Exp,
                                                     bias=negmax[:, 0:1]),
                              waits=[em.now("cv")] + W("eS_free"))
                    em.v(lambda v: v.reduce_sum(col[:, :], eS[:, :], axis=AX.X),
                         waits=[se])
                    em.v(lambda v: v.reciprocal(rden[:, :], col[:, :]))
                    sm_done = em.now("cv")
                    def tr_e(t):
                        t.transpose(psT[:, 0:128], eS[:, 0:128], ident[:, :])
                        return t.transpose(psT[:, 128:256], eS[:, 128:256],
                                           ident[:, :])
                    tre = em.t(tr_e, waits=[sm_done, em.now("cv")])
                    marks["eS_free"] = em.now("ct")
                    em.v(lambda v: v.tensor_copy(eST[:, :], psT[:, 0:256]),
                         waits=[tre])
                    est_done = em.now("cv")
                    for half in range(2):
                        def mm_a(t, half=half, rp=rp):
                            t.matmul(ps_a[:, :], eST[:, 0:128],
                                     vring[:, (1 - rp) * D + half * 512:
                                           (1 - rp) * D + half * 512 + 512],
                                     start=True, stop=False)
                            return t.matmul(ps_a[:, :], eST[:, 128:256],
                                            vring[:, rp * D + half * 512:
                                                  rp * D + half * 512 + 512],
                                            start=False, stop=True)
                        mma = em.t(mm_a, waits=[est_done, em.now("cv"),
                                                em.now("cs")])
                        em.v(lambda v, half=half: v.tensor_scalar(
                            attn[:, half * 512:half * 512 + 512], ps_a[:, :],
                            rden[:, 0:1], None, ALU.mult),
                            waits=[mma] + (W("attn_free") if half == 0 else []))
                    at_done = em.now("cv")
                else:
                    at_done = em.v(lambda v: v.memset(attn[:, :], 0.0),
                                   waits=W("attn_free"))
                tra = emit_transpose8(em, psT, attn[:, :], ident,
                                      [at_done, em.now("cv")])
                marks["attn_free"] = tra
                em.v(lambda v: v.tensor_copy(attnT[:, :], psT[:, :]),
                     waits=[tra])
                aT_done = em.now("cv")
                for half in range(2):
                    def mm_p(t, half=half):
                        for dc in range(8):
                            t.matmul(ps_b[:, :],
                                     attnT[:, dc * 128:(dc + 1) * 128],
                                     woutb[:, dc * D + half * 512:
                                           dc * D + half * 512 + 512],
                                     start=(dc == 0), stop=False)
                        return t.matmul(ps_b[:, :], ones1_b[:, :],
                                        boutb[:, half * 512:half * 512 + 512],
                                        start=False, stop=True)
                    mmp = em.t(mm_p, waits=[aT_done, gp_all, g_all,
                                            em.now("cv")])
                    em.v(lambda v, half=half, q=q: v.tensor_add(
                        x4t[:, half * 512:half * 512 + 512], ps_b[:, :],
                        xA[:, q * D + half * 512:q * D + half * 512 + 512]),
                        waits=[mmp] + (W("x4_free") if half == 0 else []))
                st_x4 = em.dma("sx4", "sync", lambda s, i=i: s.dma_start(
                    out=o_x4[i * 128:(i + 1) * 128, :], in_=x4t[:, :]),
                    waits=[em.now("cv")])
                marks["x4_free"] = st_x4
                marks["ring_free"] = em.now("ct")

                if i == 0:
                    em.v(lambda v: v.tensor_copy(
                        ebuf[:, :],
                        bass.AP(qfm, 0, [[8 * 512, 128], [512, 8], [1, 128]])),
                        waits=[qk_done])
                    em.dma("sq0", "sync", lambda s: s.dma_start(
                        out=o_q0[:, :], in_=ebuf[:, 0:1024]),
                        waits=[em.now("cv")])
                    em.dma("sk0", "sync", lambda s, rp=rp: s.dma_start(
                        out=o_k0[:, :], in_=kring[:, rp * 1024:(rp + 1) * 1024]),
                        waits=[ring_done])
                    em.dma("sv0", "sync", lambda s, rp=rp: s.dma_start(
                        out=o_v0[:, :], in_=vring[:, rp * D:(rp + 1) * D]),
                        waits=[ring_done])
                if i == nst * 4 - 1:
                    em.dma("skL", "sync", lambda s, rp=rp: s.dma_start(
                        out=o_kL[:, :], in_=kring[:, rp * 1024:(rp + 1) * 1024]),
                        waits=[ring_done])
                    em.dma("svL", "sync", lambda s, rp=rp: s.dma_start(
                        out=o_vL[:, :], in_=vring[:, rp * D:(rp + 1) * D]),
                        waits=[ring_done])
            marks["q_free"] = em.now("ct")
            marks["k_free"] = em.now("ct")
            marks["xlnT_free"] = em.now("ct")
            marks["xA_free"] = em.now("cv")

        em.barrier_all()
        em.build(block)
    es.close()
    return nc


def build_launch_W0():
    nc = bass.Bass("TRN2", target_bir_lowering=False)
    x4_in = nc.dram_tensor("x4p0", [128, D], F32, kind="ExternalInput").ap()
    q0_in = nc.dram_tensor("q0fm", [128, 8 * 128], BF16, kind="ExternalInput").ap()
    k0_in = nc.dram_tensor("k0fm", [128, 8 * 128], BF16, kind="ExternalInput").ap()
    v0_in = nc.dram_tensor("v0tm", [128, D], BF16, kind="ExternalInput").ap()
    kP_in = nc.dram_tensor("kPfm", [128, 8 * 128], BF16, kind="ExternalInput").ap()
    vP_in = nc.dram_tensor("vPtm", [128, D], BF16, kind="ExternalInput").ap()
    wout_in = nc.dram_tensor("w_out", [D, D], F32, kind="ExternalInput").ap()
    wm_in = nc.dram_tensor("wmask", [128, 256], F32, kind="ExternalInput").ap()
    o_x4 = nc.dram_tensor("x4t0", [128, D], F32, kind="ExternalOutput").ap()

    es = ExitStack()
    sb = lambda name, shape, dt: es.enter_context(nc.sbuf_tensor(name, shape, dt))
    psum = lambda name, shape, dt: es.enter_context(nc.psum_tensor(name, shape, dt))
    ident = sb("ident_0", [128, 128], BF16)
    woutb = sb("woutb_0", [128, 8 * D], BF16)
    wmb = sb("wmb_0", [128, 256], F32)
    q0 = sb("q0_0", [128, 1024], BF16)
    k0 = sb("k0_0", [128, 1024], BF16)
    v0 = sb("v0_0", [128, D], BF16)
    kP = sb("kP_0", [128, 1024], BF16)
    vP = sb("vP_0", [128, D], BF16)
    x4t = sb("x4t_0", [128, D], F32)
    ssb = sb("ssb_0", [128, 256], F32)
    eS = sb("eS_0", [128, 256], BF16)
    eST = sb("eST_0", [128, 256], BF16)
    attn = sb("attn_0", [128, D], BF16)
    attnT = sb("attnT_0", [128, D], BF16)
    col = sb("col_0", [128, 1], F32)
    negmax = sb("negmax_0", [128, 1], F32)
    rden = sb("rden_0", [128, 1], F32)
    psT = psum("psT_0", [128, D], BF16)
    ps_s = psum("ps_s_0", [128, 512], F32)
    ps_b = psum("ps_b_0", [128, 512], F32)

    with nc.Block() as block:
        em = Em(nc, es)
        chunked_load(em, woutb, wout_in, 8, D)
        em.dma("dpre", "sync", lambda s: s.dma_start(out=wmb[:, :], in_=wm_in[:, :]))
        em.dma("dpre", "sync", lambda s: s.dma_start(out=q0[:, :], in_=q0_in[:, :]))
        em.dma("dpre", "sync", lambda s: s.dma_start(out=k0[:, :], in_=k0_in[:, :]))
        em.dma("dpre", "sync", lambda s: s.dma_start(out=v0[:, :], in_=v0_in[:, :]))
        em.dma("dpre", "sync", lambda s: s.dma_start(out=kP[:, :], in_=kP_in[:, :]))
        em.dma("dpre", "sync", lambda s: s.dma_start(out=vP[:, :], in_=vP_in[:, :]))
        em.dma("dpre", "sync", lambda s: s.dma_start(out=x4t[:, :], in_=x4_in[:, :]))
        emit_identity(em, ident)
        ld_all = em.now("dpre")
        gp_all = em.now("dpre_g")
        g_all = em.now("cg")

        def mm_s(t):
            inst = None
            for dc in range(8):
                t.matmul(ps_s[:, 0:128], q0[:, dc * 128:(dc + 1) * 128],
                         kP[:, dc * 128:(dc + 1) * 128],
                         start=(dc == 0), stop=(dc == 7))
            for dc in range(8):
                inst = t.matmul(ps_s[:, 128:256], q0[:, dc * 128:(dc + 1) * 128],
                                k0[:, dc * 128:(dc + 1) * 128],
                                start=(dc == 0), stop=(dc == 7))
            return inst
        mms = em.t(mm_s, waits=[ld_all, gp_all, g_all])
        em.v(lambda v: v.scalar_tensor_tensor(
            ssb[:, :], ps_s[:, 0:256], float(D) ** -0.5, wmb[:, :],
            ALU.mult, ALU.add), waits=[mms])
        em.v(lambda v: v.reduce_max(col[:, :], ssb[:, :], axis=AX.X))
        em.v(lambda v: v.tensor_scalar(negmax[:, :], col[:, :], -1.0, None,
                                       ALU.mult))
        se = em.s(lambda s: s.activation(eS[:, :], ssb[:, :], AF.Exp,
                                         bias=negmax[:, 0:1]),
                  waits=[em.now("cv")])
        em.v(lambda v: v.reduce_sum(col[:, :], eS[:, :], axis=AX.X), waits=[se])
        em.v(lambda v: v.reciprocal(rden[:, :], col[:, :]))
        sm_done = em.now("cv")

        def tr_e(t):
            t.transpose(psT[:, 0:128], eS[:, 0:128], ident[:, :])
            return t.transpose(psT[:, 128:256], eS[:, 128:256], ident[:, :])
        tre = em.t(tr_e, waits=[sm_done])
        em.v(lambda v: v.tensor_copy(eST[:, :], psT[:, 0:256]), waits=[tre])
        est_done = em.now("cv")
        for half in range(2):
            def mm_a(t, half=half):
                t.matmul(ps_b[:, :], eST[:, 0:128],
                         vP[:, half * 512:half * 512 + 512],
                         start=True, stop=False)
                return t.matmul(ps_b[:, :], eST[:, 128:256],
                                v0[:, half * 512:half * 512 + 512],
                                start=False, stop=True)
            mma = em.t(mm_a, waits=[est_done, em.now("cv")])
            em.v(lambda v, half=half: v.tensor_scalar(
                attn[:, half * 512:half * 512 + 512], ps_b[:, :],
                rden[:, 0:1], None, ALU.mult), waits=[mma])
        at_done = em.now("cv")
        tra = emit_transpose8(em, psT, attn[:, :], ident,
                              [at_done, em.now("ct")])
        em.v(lambda v: v.tensor_copy(attnT[:, :], psT[:, :]), waits=[tra])
        aT_done = em.now("cv")
        for half in range(2):
            def mm_p(t, half=half):
                inst = None
                for dc in range(8):
                    inst = t.matmul(ps_b[:, :],
                                    attnT[:, dc * 128:(dc + 1) * 128],
                                    woutb[:, dc * D + half * 512:
                                          dc * D + half * 512 + 512],
                                    start=(dc == 0), stop=(dc == 7))
                return inst
            mmp = em.t(mm_p, waits=[aT_done, em.now("cv")])
            em.v(lambda v, half=half: v.tensor_add(
                x4t[:, half * 512:half * 512 + 512], ps_b[:, :],
                x4t[:, half * 512:half * 512 + 512]), waits=[mmp])
        em.dma("sx40", "sync", lambda s: s.dma_start(
            out=o_x4[:, :], in_=x4t[:, :]), waits=[em.now("cv")])
        em.barrier_all()
        em.build(block)
    es.close()
    return nc


_BUILD_CACHE = {}


def _prog(name, fn, **kw):
    key = (name, tuple(sorted(kw.items())))
    if key not in _BUILD_CACHE:
        _BUILD_CACHE[key] = fn(**kw)
    return _BUILD_CACHE[key]


def _run(nc, maps):
    return run_bass_kernel_spmd(nc, maps, core_ids=list(range(NCORES)))


def _moe_round(x_flat, gate_weight, e_bias, shared_gate, shared_up, shared_down,
               ffn_w1, ffn_w2, ffn_w3):
    import ml_dtypes
    ncA = _prog("A", build_launch_A)
    mapsA = launch_A_inputs(x_flat, gate_weight, e_bias, shared_gate,
                            shared_up, shared_down)
    resA = _run(ncA, mapsA).results
    xln_list = [r["xln"] for r in resA]
    counts_list = [r["counts"] for r in resA]
    contrib_list = [r["contrib"] for r in resA]
    dest = [r["dest"] for r in resA]
    shared = [r["shared"] for r in resA]

    ncB = _prog("B", build_launch_B)
    mapsB = [launch_B_inputs(c, xln_list, counts_list, contrib_list,
                             ffn_w1, ffn_w2, ffn_w3) for c in range(NCORES)]
    resB = _run(ncB, mapsB).results
    ye = [r["ye"] for r in resB]
    zero_row = np.zeros((1, D), ml_dtypes.bfloat16)
    yeR = []
    for j in range(NCORES):
        parts = [ye[i][j * EPC * CAP:(j + 1) * EPC * CAP] for i in range(NCORES)]
        yeR.append(np.concatenate(parts + [zero_row], 0))
    return dest, shared, yeR


def _c_consts():
    return {
        "tri": np.triu(np.ones((128, 128), np.float32)),
        "bdm": np.kron(np.eye(2, dtype=np.float32),
                       np.ones((64, 64), np.float32)),
        "hmask": np.concatenate(
            [np.repeat([[1.0], [0.0]], 64, 0),
             np.repeat([[0.0], [1.0]], 64, 0)], 1).astype(np.float32),
    }


def kernel(**inputs):
    import ml_dtypes
    g = lambda k: np.asarray(inputs[k], dtype=np.float32)
    x0 = g("L0_attn_query").reshape(NCORES * TC, D)

    dest1, shared1, yeR1 = _moe_round(
        x0, g("L0_ffn_gate_weight"), g("L0_ffn_e_bias"),
        g("L0_ffn_shared_gate"), g("L0_ffn_shared_up"),
        g("L0_ffn_shared_down"),
        g("L0_ffn_w1"), g("L0_ffn_w2"), g("L0_ffn_w3"))

    ncC = _prog("C", build_launch_C)
    cc = _c_consts()
    Wq, Wk, Wv = g("L0_attn_Wq"), g("L0_attn_Wk"), g("L0_attn_Wv")
    bq, bk, bv = g("L0_attn_bq"), g("L0_attn_bk"), g("L0_attn_bv")
    mapsC = []
    for c in range(NCORES):
        mapsC.append({
            "x": np.ascontiguousarray(x0[c * TC:(c + 1) * TC]),
            "shared": shared1[c], "dest": dest1[c], "yeR": yeR1[c],
            "WqT": np.ascontiguousarray(Wq.T),
            "WkT": np.ascontiguousarray(Wk.T),
            "WvT": np.ascontiguousarray(Wv.T),
            "bqc": bq.reshape(8, 128).T.copy(),
            "bkc": bk.reshape(8, 128).T.copy(),
            "bv": bv.reshape(1, D), **cc,
        })
    resC = _run(ncC, mapsC).results

    ncD = _prog("D", build_launch_D)
    zst = np.zeros((128, 8 * 128), np.float32)
    mapsD = []
    for c in range(NCORES):
        sp = resC[c - 1]["state"] if c % 2 == 1 else zst
        mapsD.append({
            "x1": resC[c]["x1"], "opart": resC[c]["opart"],
            "qfm": resC[c]["qfm"], "Spart": np.asarray(sp, np.float32),
            "ln_g": g("L0_attn_ln_g").reshape(8, 128).T.copy(),
            "ln_b": g("L0_attn_ln_b").reshape(8, 128).T.copy(),
            "WoT": np.ascontiguousarray(g("L0_attn_Wo").T),
            "bo": g("L0_attn_bo").reshape(1, D),
        })
    resD = _run(ncD, mapsD).results
    x2 = np.concatenate([r["x2"] for r in resD], 0)

    dest2, shared2, yeR2 = _moe_round(
        x2, g("L0_ffn_gate_weight"), g("L0_ffn_e_bias"),
        g("L0_ffn_shared_gate"), g("L0_ffn_shared_up"),
        g("L0_ffn_shared_down"),
        g("L0_ffn_w1"), g("L0_ffn_w2"), g("L0_ffn_w3"))

    ncC2 = _prog("Cc", build_launch_C, upto="combine")
    mapsC2 = []
    for c in range(NCORES):
        m = dict(mapsC[c])
        m["x"] = np.ascontiguousarray(x2[c * TC:(c + 1) * TC])
        m["shared"] = shared2[c]
        m["dest"] = dest2[c]
        m["yeR"] = yeR2[c]
        mapsC2.append(m)
    resC2 = _run(ncC2, mapsC2).results
    x3moe = [r["x1"] for r in resC2]

    ncF = _prog("FFN", build_launch_FFN)
    w1, w2 = g("L1_ffn_w1"), g("L1_ffn_w2")
    ffnw = {
        "w1": w1, "w1T": np.ascontiguousarray(w1.T),
        "w2": w2, "w2T": np.ascontiguousarray(w2.T),
        "b1c": g("L1_ffn_b1").reshape(HF // 128, 128).T.copy(),
        "b2c": g("L1_ffn_b2").reshape(8, 128).T.copy(),
    }
    mapsF = [{"x": np.asarray(x3moe[c], np.float32), **ffnw}
             for c in range(NCORES)]
    resF = _run(ncF, mapsF).results
    x3 = [r["xo"] for r in resF]

    ncW = _prog("WATT", build_launch_WATT)
    wmask = np.where(np.triu(np.ones((128, 256), bool), 129), -3.4e38,
                     0).astype(np.float32)
    wq_nat = g("L1_attn_w_qkv")
    wo_nat = g("L1_attn_w_out")
    mapsW = [{"x": np.asarray(x3[c], np.float32), "w_qkv": wq_nat,
              "w_out": wo_nat, "b_out": g("L1_attn_b_out").reshape(1, D),
              "wmask": wmask} for c in range(NCORES)]
    resW = _run(ncW, mapsW).results

    ncW0 = _prog("W0", build_launch_W0)
    zbf = np.zeros((128, 1024), ml_dtypes.bfloat16)
    maps0 = []
    for c in range(NCORES):
        kP = resW[c - 1]["kLfm"] if c % 2 == 1 else zbf
        vP = resW[c - 1]["vLtm"] if c % 2 == 1 else zbf
        maps0.append({
            "x4p0": np.ascontiguousarray(resW[c]["x4p"][0:128]),
            "q0fm": resW[c]["q0fm"], "k0fm": resW[c]["k0fm"],
            "v0tm": resW[c]["v0tm"], "kPfm": np.asarray(kP),
            "vPtm": np.asarray(vP), "w_out": wo_nat, "wmask": wmask,
        })
    res0 = _run(ncW0, maps0).results

    mapsF2 = []
    for c in range(NCORES):
        x4 = np.array(resW[c]["x4p"], np.float32)
        x4[0:128] = res0[c]["x4t0"]
        mapsF2.append({"x": x4, **ffnw})
    resF2 = _run(ncF, mapsF2).results

    out = np.concatenate([r["xo"] for r in resF2], 0)
    return out.reshape(4, 8192, D).astype(np.float32)


# version 04
# speedup vs baseline: 1.1672x; 1.1644x over previous
import numpy as np
from contextlib import ExitStack

import concourse.bass as bass
import concourse.mybir as mybir
from concourse.bass_utils import run_bass_kernel_spmd
from concourse.bass import IndirectOffsetOnAxis

F32 = mybir.dt.float32
BF16 = mybir.dt.bfloat16
I32 = mybir.dt.int32
AF = mybir.ActivationFunctionType
ALU = mybir.AluOpType
AX = mybir.AxisListType

NCORES = 8
TC = 4096
D = 1024
E = 64
CAP = 1024
TOPK = 8
NG = 8
EPC = E // NCORES
FEXP = 512
FSH = 1024
HF = 4096
LN_EPS = 1e-5
SCALE = 2.5
NSLOT = E * CAP
ZROW = NSLOT


class Em:

    def __init__(self, nc, es):
        self.nc = nc
        self.es = es
        self.ops = {e: [] for e in ("sync", "scalar", "tensor", "vector", "gpsimd")}
        self.sems = {}
        self.counts = {}
        self.chan("cv")
        self.chan("cs")
        self.chan("ct")
        self.chan("cg")
        self.dma_chans = []

    def chan(self, name):
        self.sems[name] = self.es.enter_context(self.nc.semaphore(name))
        self.counts[name] = 0
        return name

    def _emit(self, eng, fn, waits, inc, n, serial):
        wl = []
        for w in waits:
            if isinstance(w, str):
                wl.append((w, self.counts[w]))
            else:
                if w is not None and w[1] > 0:
                    wl.append(w)
        if serial and self.counts[inc] > 0:
            wl.append((inc, self.counts[inc]))
        self.counts[inc] += n
        self.ops[eng].append((fn, wl, (inc, n)))
        return (inc, self.counts[inc])

    def v(self, fn, waits=()):
        return self._emit("vector", fn, waits, "cv", 1, True)

    def s(self, fn, waits=()):
        return self._emit("scalar", fn, waits, "cs", 1, True)

    def t(self, fn, waits=()):
        return self._emit("tensor", fn, waits, "ct", 1, False)

    def g(self, fn, waits=()):
        return self._emit("gpsimd", fn, waits, "cg", 1, True)

    def dma(self, chan, eng, fn, waits=()):
        if chan not in self.sems:
            self.chan(chan)
            self.dma_chans.append(chan)
        return self._emit(eng, fn, waits, chan, 16, False)

    def now(self, name):
        return (name, self.counts[name])

    def barrier_all(self):
        for ch in self.dma_chans:
            c = self.counts[ch]
            if c > 0:
                self.ops["gpsimd"].append(
                    (lambda g: None, [(ch, c)], ("cg", 0)))

    def build(self, block):
        for eng in ("sync", "scalar", "tensor", "vector", "gpsimd"):
            lst = self.ops[eng]
            if not lst:
                continue

            def mk(lst):
                def body(e):
                    for fn, wl, tgt in lst:
                        for s, c in wl:
                            if c > 0:
                                e.wait_ge(self.sems[s], c)
                        inst = fn(e)
                        if tgt is not None and tgt[1] > 0:
                            if inst is None:
                                raise RuntimeError("inc on None instruction")
                            inst.then_inc(self.sems[tgt[0]], tgt[1])
                return body

            getattr(block, eng)(mk(lst))


def chunked_load(em, dst_sb, src_ap, nchunk, inner, chan="dpre"):
    fn = lambda e: e.dma_start(
        out=dst_sb[:, :].rearrange("p (c f) -> p c f", c=nchunk),
        in_=src_ap.rearrange("(c p) f -> p c f", p=128))
    eng = "gpsimd" if src_ap.dtype != dst_sb.dtype else "sync"
    return em.dma(chan if eng == "sync" else chan + "_g", eng, fn)


def emit_identity(em, ident):
    em.g(lambda g: g.memset(ident[:, :], 0.0))
    return em.g(lambda g: g.affine_select(
        out=ident[:, :], in_=ident[:, :], compare_op=ALU.not_equal,
        fill=1.0, base=0, pattern=[[-1, ident.shape[0]]], channel_multiplier=1))


def emit_transpose8(em, psT, src, ident, waits):
    n = src.shape[1] // 128

    def fn(t):
        inst = None
        for c in range(n):
            inst = t.transpose(psT[:, c * 128:(c + 1) * 128],
                               src[:, c * 128:(c + 1) * 128], ident[:, :])
        return inst
    return em.t(fn, waits=waits)


def emit_ln(em, x_tile, xln_tile, stats, mv, col, rr, waits):
    nh = x_tile.shape[1] // 512
    for h in range(nh):
        em.v(lambda v, h=h: v.bn_stats(stats[:, 6 * h:6 * h + 6],
                                       x_tile[:, 512 * h:512 * h + 512]),
             waits=(waits if h == 0 else ()))
    em.v(lambda v, nh=nh: v.bn_aggr(mv[:, :], stats[:, 0:6 * nh]),
         waits=[em.now("cs")])
    em.v(lambda v: v.tensor_scalar(col[:, :], mv[:, 1:2], float(LN_EPS), None,
                                   ALU.add))
    m = em.now("cv")
    em.s(lambda s: s.activation(col[:, :], col[:, :], AF.Sqrt), waits=[m])
    em.v(lambda v: v.reciprocal(rr[:, :], col[:, :]), waits=[em.now("cs")])
    return em.v(lambda v: v.tensor_scalar(
        xln_tile, x_tile, mv[:, 0:1], rr[:, 0:1], ALU.subtract, ALU.mult))


def build_launch_A(nt=TC // 128, with_shared=True):
    nc = bass.Bass("TRN2", target_bir_lowering=False)
    TCl = nt * 128
    x_in = nc.dram_tensor("x", [TCl, D], F32, kind="ExternalInput").ap()
    gwT = nc.dram_tensor("gwT", [D, E], F32, kind="ExternalInput").ap()
    ebias_in = nc.dram_tensor("e_bias", [1, E], F32, kind="ExternalInput").ap()
    sgT = nc.dram_tensor("sgT", [D, FSH], F32, kind="ExternalInput").ap()
    suT = nc.dram_tensor("suT", [D, FSH], F32, kind="ExternalInput").ap()
    sdT = nc.dram_tensor("sdT", [FSH, D], F32, kind="ExternalInput").ap()
    gid_in = nc.dram_tensor("gid", [128, nt], F32, kind="ExternalInput").ap()
    cst_up = nc.dram_tensor("cst_up", [128, 128], F32, kind="ExternalInput").ap()
    cst_iota = nc.dram_tensor("cst_iota", [128, E], F32, kind="ExternalInput").ap()

    o_xln = nc.dram_tensor("xln", [TCl, D], BF16, kind="ExternalOutput").ap()
    o_counts = nc.dram_tensor("counts", [1, E], F32, kind="ExternalOutput").ap()
    o_contrib = nc.dram_tensor("contrib", [NSLOT, 2], F32, kind="ExternalOutput").ap()
    o_dest = nc.dram_tensor("dest", [TCl, TOPK], F32, kind="ExternalOutput").ap()
    o_shared = nc.dram_tensor("shared", [TCl, D], F32, kind="ExternalOutput").ap()

    es = ExitStack()
    sb = lambda name, shape, dt: es.enter_context(nc.sbuf_tensor(name, shape, dt))
    psum = lambda name, shape, dt: es.enter_context(nc.psum_tensor(name, shape, dt))

    gwTb = sb("gwTb", [128, 8 * E], BF16)
    ebias_b = sb("ebias_b", [128, E], F32)
    eb_row = sb("eb_row", [1, E], F32)
    ones128_b = sb("ones128_b", [128, 1], BF16)
    ones1_f = sb("ones1_f", [1, 128], F32)
    upb = sb("upb", [128, 128], BF16)
    iota64 = sb("iota64", [128, E], F32)
    ident = sb("ident", [128, 128], BF16)
    if with_shared:
        sgTb = sb("sgTb", [128, 8 * FSH], BF16)
        suTb = sb("suTb", [128, 8 * FSH], BF16)
        sdTb = sb("sdTb", [128, 8 * D], BF16)
    gidb = sb("gidb", [128, nt], F32)

    xt = sb("xt", [128, 2 * D], F32)
    stats = sb("stats", [128, 12], F32)
    mv = sb("mv", [128, 2], F32)
    rr = sb("rr", [128, 1], F32)
    xlnt = sb("xlnt", [128, 2 * D], BF16)
    xlnT = sb("xlnT", [128, 2 * D], BF16)
    psT = psum("psT", [128, D], BF16)
    ps_log = psum("ps_log", [128, E], F32)
    ps_cnt = psum("ps_cnt", [1, E], F32)
    ps_pos = psum("ps_pos", [128, E], F32)
    scores = sb("scores", [128, E], F32)
    sfc = sb("sfc", [128, E], F32)
    gsc = sb("gsc", [128, NG], F32)
    g8 = sb("g8", [128, NG], F32)
    scr1 = sb("scr1", [128, E], F32)
    scr2 = sb("scr2", [128, E], F32)
    scr3 = sb("scr3", [128, E], F32)
    cums = sb("cums", [128, E], F32)
    cum2 = sb("cum2", [128, E], F32)
    col1 = sb("col1", [128, 1], F32)
    col2 = sb("col2", [128, 1], F32)
    col3 = sb("col3", [128, 1], F32)
    sel = sb("sel", [128, E], F32)
    selb = sb("selb", [128, E], BF16)
    wsel = sb("wsel", [128, E], F32)
    msk = sb("msk", [128, E], F32)
    posd = sb("posd", [128, E], F32)
    destfw = sb("destfw", [128, 4 * E], F32)
    wselw = sb("wselw", [128, 4 * E], F32)
    rankw = sb("rankw", [128, 4 * E], F32)
    ohm = sb("ohm", [128, 4 * E], F32)
    ohs = sb("ohs", [128, 4 * E], F32)
    desti = sb("desti", [128, 4 * TOPK], I32)
    dest4 = sb("dest4", [128, 4 * TOPK], F32)
    sval = sb("sval", [128, 8 * TOPK], F32)
    dest_out = sb("dest_out", [128, TOPK], F32)
    pct = sb("pct", [1, E], F32)
    runc = sb("runc", [1, E], F32)
    if with_shared:
        ps_sh = psum("ps_sh", [128, 512], F32)
        ps_sh2 = psum("ps_sh2", [128, 512], F32)
        gfm = sb("gfm", [128, 512], BF16)
        prod = sb("prod", [128, FSH], BF16)
        prodT = sb("prodT", [128, FSH], BF16)
        shout = sb("shout", [128, D], F32)

    with nc.Block() as block:
        em = Em(nc, es)

        chunked_load(em, gwTb, gwT, 8, E)
        if with_shared:
            chunked_load(em, sgTb, sgT, 8, FSH)
            chunked_load(em, suTb, suT, 8, FSH)
            chunked_load(em, sdTb, sdT, 8, D)
        em.dma("dpre_g", "gpsimd", lambda g: g.dma_start(out=upb[:, :], in_=cst_up[:, :]))
        em.dma("dpre", "sync", lambda s: s.dma_start(out=iota64[:, :], in_=cst_iota[:, :]))
        em.dma("dpre", "sync", lambda s: s.dma_start(out=gidb[:, :], in_=gid_in[:, :]))
        em.dma("dpre", "sync", lambda s: s.dma_start(out=eb_row[:, :], in_=ebias_in[:, :]))
        em.g(lambda g: g.memset(ones128_b[:, :], 1.0))
        em.g(lambda g: g.memset(ones1_f[:, :], 1.0))
        em.g(lambda g: g.memset(runc[:, :], 0.0))
        emit_identity(em, ident)
        ld_all = em.now("dpre")
        gp_all = em.now("dpre_g")
        g_all = em.now("cg")

        em.t(lambda t: t.matmul(ps_log[:, :], ones1_f[:, :], eb_row[:, :],
                                start=True, stop=True),
             waits=[ld_all, gp_all, g_all])
        ev_eb = em.v(lambda v: v.tensor_copy(ebias_b[:, :], ps_log[:, :]),
                     waits=[em.now("ct")])

        regs = {}

        def init_reg(g):
            regs["bc"] = g.register("bc_sc").__enter__()
            g.reg_mov(regs["bc"], NSLOT - 1)
            return None
        em.ops["gpsimd"].append((init_reg, [], ("cg", 0)))

        marks = {"xt_free": [None, None], "xln_free": [[], []],
                 "xT_free": [None, None], "xlnT_free": [None, None],
                 "sc_prev": None, "dest_free": None, "sh_free": None}

        def W(key):
            return [w for w in [marks[key]] if w]

        for i in range(nt):
            pb = i % 2
            xti = xt[:, pb * D:(pb + 1) * D]
            xlni = xlnt[:, pb * D:(pb + 1) * D]
            xlnTi = xlnT[:, pb * D:(pb + 1) * D]

            w_ld = [w for w in [marks["xt_free"][pb]] if w]
            ld_x = em.dma("dx%d" % pb, "sync", lambda s, i=i, xti=xti: s.dma_start(
                out=xti, in_=x_in[i * 128:(i + 1) * 128, :]), waits=w_ld)

            w_x = [ld_x] + list(marks["xln_free"][pb])
            ln_done = emit_ln(em, xti, xlni, stats, mv, col1, rr, w_x)
            marks["xt_free"][pb] = ln_done
            st_x = em.dma("sxln%d" % pb, "sync",
                          lambda s, i=i, xlni=xlni: s.dma_start(
                out=o_xln[i * 128:(i + 1) * 128, :], in_=xlni), waits=[ln_done])

            w_tr = [ln_done] + [w for w in [marks["xT_free"][pb]] if w]
            tr = emit_transpose8(em, psT, xlni, ident, w_tr + [g_all])
            w_evxT = [tr] + [w for w in [marks["xlnT_free"][pb]] if w]
            ev_xT = em.v(lambda v, xlnTi=xlnTi: v.tensor_copy(xlnTi, psT[:, :]),
                         waits=w_evxT)
            marks["xln_free"][pb] = [st_x, tr]

            if with_shared:
                for half in range(2):
                    def mm_g(t, h=half, xlnTi=xlnTi):
                        inst = None
                        for c in range(8):
                            inst = t.matmul(
                                ps_sh[:, :], xlnTi[:, c * 128:(c + 1) * 128],
                                sgTb[:, c * FSH + h * 512:c * FSH + h * 512 + 512],
                                start=(c == 0), stop=(c == 7))
                        return inst
                    mmg = em.t(mm_g, waits=[ev_xT, gp_all, em.now("cs")])
                    silu = em.s(lambda s: s.activation(gfm[:, :], ps_sh[:, :],
                                                       AF.Silu),
                                waits=[mmg, em.now("cv")])

                    def mm_u(t, h=half, xlnTi=xlnTi):
                        inst = None
                        for c in range(8):
                            inst = t.matmul(
                                ps_sh2[:, :], xlnTi[:, c * 128:(c + 1) * 128],
                                suTb[:, c * FSH + h * 512:c * FSH + h * 512 + 512],
                                start=(c == 0), stop=(c == 7))
                        return inst
                    mmu = em.t(mm_u, waits=[ev_xT, em.now("cv")])
                    em.v(lambda v, h=half: v.tensor_mul(
                        prod[:, h * 512:(h + 1) * 512], ps_sh2[:, :], gfm[:, :]),
                        waits=[mmu, silu])
                marks["xlnT_free"][pb] = em.now("ct")
                prod_done = em.now("cv")
                trp = emit_transpose8(em, psT, prod[:, :], ident,
                                      [prod_done, ev_xT])
                prT = em.v(lambda v: v.tensor_copy(prodT[:, :], psT[:, :]),
                           waits=[trp])
                marks["xT_free"][pb] = prT
                for half in range(2):
                    def mm_sd(t, h=half):
                        inst = None
                        for c in range(8):
                            inst = t.matmul(
                                ps_sh[:, :], prodT[:, c * 128:(c + 1) * 128],
                                sdTb[:, c * D + h * 512:c * D + h * 512 + 512],
                                start=(c == 0), stop=(c == 7))
                        return inst
                    mmsd = em.t(mm_sd, waits=[prT, em.now("cs"), em.now("cv")])
                    w_sh = [mmsd] + ([w for w in [marks["sh_free"]] if w]
                                     if half == 0 else [])
                    em.v(lambda v, h=half: v.tensor_copy(
                        shout[:, h * 512:(h + 1) * 512], ps_sh[:, :]),
                        waits=w_sh)
                st_sh = em.dma("ssh", "sync", lambda s, i=i: s.dma_start(
                    out=o_shared[i * 128:(i + 1) * 128, :], in_=shout[:, :]),
                    waits=[em.now("cv")])
                marks["sh_free"] = st_sh
            else:
                marks["xT_free"][pb] = ev_xT
                marks["xlnT_free"][pb] = em.now("ct")

            def mm_logits(t, xlnTi=xlnTi):
                inst = None
                for c in range(8):
                    inst = t.matmul(ps_log[:, :], xlnTi[:, c * 128:(c + 1) * 128],
                                    gwTb[:, c * E:(c + 1) * E],
                                    start=(c == 0), stop=(c == 7))
                return inst
            mm_log = em.t(mm_logits, waits=[ev_xT, ev_eb, gp_all, em.now("cs")])
            sig = em.s(lambda s: s.activation(scores[:, :], ps_log[:, :],
                                              AF.Sigmoid),
                       waits=[mm_log, em.now("cv")])
            em.v(lambda v: v.tensor_add(sfc[:, :], scores[:, :], ebias_b[:, :]),
                 waits=[sig])

            for gi in range(NG):
                em.v(lambda v, gi=gi: v.max(out=g8[:, :],
                                            in_=sfc[:, gi * 8:(gi + 1) * 8]))
                em.v(lambda v, gi=gi: v.tensor_add(gsc[:, gi:gi + 1],
                                                   g8[:, 0:1], g8[:, 1:2]))
            em.v(lambda v: v.max(out=g8[:, :], in_=gsc[:, :]))
            em.v(lambda v: v.tensor_scalar(scr1[:, 0:NG], gsc[:, :], g8[:, 3:4],
                                           None, ALU.is_gt))
            em.v(lambda v: v.reduce_sum(col1[:, :], scr1[:, 0:NG], axis=AX.X))
            em.v(lambda v: v.tensor_scalar(scr2[:, 0:NG], gsc[:, :], g8[:, 3:4],
                                           None, ALU.is_equal))
            em.v(lambda v: v.tensor_copy(scr3[:, 0:1], scr2[:, 0:1]))
            em.v(lambda v: v.tensor_add(scr3[:, 1:NG], scr2[:, 1:NG],
                                        scr2[:, 0:NG - 1]))
            em.v(lambda v: v.tensor_copy(cum2[:, 0:2], scr3[:, 0:2]))
            em.v(lambda v: v.tensor_add(cum2[:, 2:NG], scr3[:, 2:NG],
                                        scr3[:, 0:NG - 2]))
            em.v(lambda v: v.tensor_copy(cums[:, 0:4], cum2[:, 0:4]))
            em.v(lambda v: v.tensor_add(cums[:, 4:NG], cum2[:, 4:NG],
                                        cum2[:, 0:NG - 4]))
            em.v(lambda v: v.tensor_scalar(col2[:, :], col1[:, :], -4.0, -1.0,
                                           ALU.add, ALU.mult))
            em.v(lambda v: v.tensor_scalar(cum2[:, 0:NG], cums[:, 0:NG],
                                           col2[:, 0:1], None, ALU.is_le))
            em.v(lambda v: v.tensor_mul(cum2[:, 0:NG], cum2[:, 0:NG],
                                        scr2[:, 0:NG]))
            em.v(lambda v: v.tensor_max(g8[:, :], cum2[:, 0:NG], scr1[:, 0:NG]))
            em.v(lambda v: v.tensor_copy(
                scr1[:, :], bass.AP(g8, 0, [[NG, 128], [1, NG], [0, 8]])))
            em.v(lambda v: v.tensor_mul(scr2[:, :], sfc[:, :], scr1[:, :]))
            em.v(lambda v: v.max(out=g8[:, :], in_=scr2[:, :]))
            em.v(lambda v: v.tensor_scalar(scr1[:, :], scr2[:, :], g8[:, 7:8],
                                           None, ALU.is_gt))
            em.v(lambda v: v.reduce_sum(col1[:, :], scr1[:, :], axis=AX.X))
            em.v(lambda v: v.tensor_scalar(scr3[:, :], scr2[:, :], g8[:, 7:8],
                                           None, ALU.is_equal))
            a, b = scr3, cums
            for sh in (1, 2, 4, 8, 16, 32):
                em.v(lambda v, a=a, b=b, sh=sh: v.tensor_copy(b[:, 0:sh],
                                                              a[:, 0:sh]))
                em.v(lambda v, a=a, b=b, sh=sh: v.tensor_add(
                    b[:, sh:E], a[:, sh:E], a[:, 0:E - sh]))
                a, b = b, a
            em.v(lambda v: v.tensor_scalar(col2[:, :], col1[:, :], -8.0, -1.0,
                                           ALU.add, ALU.mult))
            em.v(lambda v: v.tensor_scalar(cum2[:, :], scr3[:, :], col2[:, 0:1],
                                           None, ALU.is_le))
            em.v(lambda v: v.tensor_scalar(msk[:, :], scr2[:, :], g8[:, 7:8],
                                           None, ALU.is_equal))
            em.v(lambda v: v.tensor_mul(cum2[:, :], cum2[:, :], msk[:, :]))
            em.v(lambda v: v.tensor_scalar(msk[:, :], scr2[:, :], g8[:, 7:8],
                                           None, ALU.is_gt))
            sel_done = em.v(lambda v: v.tensor_max(sel[:, :], cum2[:, :],
                                                   msk[:, :]))
            em.v(lambda v: v.tensor_mul(wsel[:, :], sel[:, :], scores[:, :]))
            em.v(lambda v: v.reduce_sum(col1[:, :], wsel[:, :], axis=AX.X))
            em.v(lambda v: v.tensor_scalar(col2[:, :], col1[:, :], 1e-20, None,
                                           ALU.add))
            em.v(lambda v: v.reciprocal(col3[:, :], col2[:, :]))
            em.v(lambda v: v.tensor_scalar(wsel[:, :], wsel[:, :], col3[:, 0:1],
                                           float(SCALE), ALU.mult, ALU.mult))
            selb_done = em.v(lambda v: v.tensor_copy(selb[:, :], sel[:, :]))

            mm_cnt = em.t(lambda t: t.matmul(ps_cnt[:, :], ones128_b[:, :],
                                             selb[:, :], start=True, stop=True),
                          waits=[selb_done, g_all, em.now("cv")])
            pct_ev = em.v(lambda v: v.tensor_copy(pct[:, :], ps_cnt[:, :]),
                          waits=[mm_cnt])

            def mm_pos(t):
                t.matmul(ps_pos[:, :], upb[:, :], selb[:, :],
                         start=True, stop=False)
                return t.matmul(ps_pos[:, :], ones1_f[:, :], runc[:, :],
                                start=False, stop=True)
            mm_pos_c = em.t(mm_pos, waits=[selb_done, gp_all, g_all,
                                           em.now("cv")])
            pos_ev = em.v(lambda v: v.tensor_copy(posd[:, :], ps_pos[:, :]),
                          waits=[mm_pos_c])
            em.v(lambda v: v.tensor_add(runc[:, :], runc[:, :], pct[:, :]),
                 waits=[pct_ev, mm_pos_c])

            tb = i % 4
            w_stash = [w for w in [marks["sc_prev"]] if w] if tb == 0 else []
            em.v(lambda v, tb=tb: v.scalar_tensor_tensor(
                destfw[:, tb * E:(tb + 1) * E], iota64[:, :], 1024.0,
                posd[:, :], ALU.mult, ALU.add), waits=[pos_ev] + w_stash)
            em.v(lambda v: v.tensor_scalar(scr1[:, :], posd[:, :],
                                           float(CAP) - 0.5, 1e9,
                                           ALU.is_ge, ALU.mult))
            em.v(lambda v, tb=tb: v.tensor_add(destfw[:, tb * E:(tb + 1) * E],
                                               destfw[:, tb * E:(tb + 1) * E],
                                               scr1[:, :]))
            em.v(lambda v, tb=tb: v.tensor_copy(wselw[:, tb * E:(tb + 1) * E],
                                                wsel[:, :]))
            em.v(lambda v: v.tensor_copy(scr3[:, :], sel[:, :]))
            a, b = scr3, cums
            for sh in (1, 2, 4, 8, 16, 32):
                em.v(lambda v, a=a, b=b, sh=sh: v.tensor_copy(b[:, 0:sh],
                                                              a[:, 0:sh]))
                em.v(lambda v, a=a, b=b, sh=sh: v.tensor_add(
                    b[:, sh:E], a[:, sh:E], a[:, 0:E - sh]))
                a, b = b, a
            em.v(lambda v, tb=tb: v.tensor_mul(rankw[:, tb * E:(tb + 1) * E],
                                               scr3[:, :], sel[:, :]))

            if tb == 3:
                for k in range(TOPK):
                    em.v(lambda v, k=k: v.tensor_scalar(
                        ohm[:, :], rankw[:, :], float(k + 1), None,
                        ALU.is_equal))
                    em.v(lambda v: v.tensor_mul(ohs[:, :], ohm[:, :],
                                                destfw[:, :]))
                    em.v(lambda v, k=k: v.reduce_sum(
                        bass.AP(dest4, 4 * k, [[4 * TOPK, 128], [1, 4], [1, 1]]),
                        ohs[:, :].rearrange("p (s e) -> p s e", s=4),
                        axis=AX.X))
                    em.v(lambda v: v.tensor_mul(ohs[:, :], ohm[:, :],
                                                wselw[:, :]))
                    em.v(lambda v, k=k: v.reduce_sum(
                        bass.AP(sval, 8 * k + 1, [[8 * TOPK, 128], [2, 4], [1, 1]]),
                        ohs[:, :].rearrange("p (s e) -> p s e", s=4),
                        axis=AX.X))
                for tb2 in range(4):
                    em.v(lambda v, i=i, tb2=tb2: v.tensor_copy(
                        bass.AP(sval, 2 * tb2, [[8 * TOPK, 128], [8, TOPK]]),
                        bass.AP(gidb, i - 3 + tb2, [[nt, 128], [0, TOPK]])))
                em.v(lambda v: v.tensor_copy(desti[:, :], dest4[:, :]))
                stg_done = em.now("cv")
                for k in range(TOPK):
                    for tb2 in range(4):
                        j = k * 4 + tb2
                        sc = em.dma("dsc", "gpsimd",
                                    lambda g, j=j: g.indirect_dma_start(
                            out=o_contrib[:, :],
                            out_offset=IndirectOffsetOnAxis(desti[:, j:j + 1], 0),
                            in_=sval[:, 2 * j:2 * j + 2], in_offset=None,
                            bounds_check=regs["bc"], oob_is_err=False),
                            waits=[stg_done])
                marks["sc_prev"] = sc
                for tb2 in range(4):
                    em.v(lambda v, tb2=tb2: v.tensor_copy(
                        dest_out[:, :],
                        bass.AP(dest4, tb2, [[4 * TOPK, 128], [4, TOPK]])),
                        waits=W("dest_free"))
                    st_d = em.dma("sdst", "sync",
                                  lambda s, i=i, tb2=tb2: s.dma_start(
                        out=o_dest[(i - 3 + tb2) * 128:(i - 2 + tb2) * 128, :],
                        in_=dest_out[:, :]), waits=[em.now("cv")])
                    marks["dest_free"] = st_d

        em.dma("scnt", "sync", lambda s: s.dma_start(
            out=o_counts[:, :], in_=runc[:, :]), waits=[em.now("cv")])
        em.barrier_all()
        em.build(block)
    es.close()
    return nc


def host_consts():
    return {
        "cst_up": np.triu(np.ones((128, 128), np.float32), 1),
        "cst_iota": np.broadcast_to(
            np.arange(E, dtype=np.float32), (128, E)).copy(),
    }


def launch_A_inputs(x_full, gate_weight, e_bias, shared_gate, shared_up,
                    shared_down):
    cs = host_consts()
    gwT = np.ascontiguousarray(gate_weight.T)
    sgT = np.ascontiguousarray(shared_gate.T)
    suT = np.ascontiguousarray(shared_up.T)
    sdT = np.ascontiguousarray(shared_down.T)
    maps = []
    for c in range(NCORES):
        gid = (np.arange(TC, dtype=np.float32) + c * TC).reshape(
            TC // 128, 128).T.copy()
        maps.append({
            "x": np.ascontiguousarray(x_full[c * TC:(c + 1) * TC]),
            "gwT": gwT, "e_bias": e_bias.reshape(1, E).astype(np.float32),
            "sgT": sgT, "suT": suT, "sdT": sdT,
            "gid": gid, **cs,
        })
    return maps


def build_launch_B():
    nc = bass.Bass("TRN2", target_bir_lowering=False)
    xln_all = nc.dram_tensor("xln_all", [NCORES * TC, D], BF16,
                             kind="ExternalInput").ap()
    counts_all = nc.dram_tensor("counts_all", [1, NCORES * E], F32,
                                kind="ExternalInput").ap()
    contrib_all = nc.dram_tensor("contrib_all", [NCORES * NSLOT, 2], F32,
                                 kind="ExternalInput").ap()
    w1T_in = nc.dram_tensor("w1T", [EPC * D, FEXP], F32, kind="ExternalInput").ap()
    w3T_in = nc.dram_tensor("w3T", [EPC * D, FEXP], F32, kind="ExternalInput").ap()
    w2T_in = nc.dram_tensor("w2T", [EPC * FEXP, D], F32, kind="ExternalInput").ap()
    slotbase_in = nc.dram_tensor("slotbase", [128, E], F32,
                                 kind="ExternalInput").ap()
    prow_in = nc.dram_tensor("prow", [128, 8], F32, kind="ExternalInput").ap()
    eofs_in = nc.dram_tensor("eofs", [1, 1], F32, kind="ExternalInput").ap()
    o_ye = nc.dram_tensor("ye", [NSLOT, D], BF16, kind="ExternalOutput").ap()

    es = ExitStack()
    sb = lambda name, shape, dt: es.enter_context(nc.sbuf_tensor(name, shape, dt))
    psum = lambda name, shape, dt: es.enter_context(nc.psum_tensor(name, shape, dt))

    cntb = sb("cntb", [1, NCORES * E], F32)
    bases = sb("bases", [1, NCORES * E], F32)
    slotbase = sb("slotbase_sb", [128, E], F32)
    prow = sb("prow_sb", [128, 8], F32)
    eofs = sb("eofs_sb", [1, 1], F32)
    ones1_f = sb("ones1_f", [1, 128], F32)
    ident = sb("ident", [128, 128], BF16)
    Bb = sb("Bb", [128, 8], F32)
    gemask = sb("gemask", [128, 8], F32)
    cofp = sb("cofp", [128, 8], F32)
    bsel = sb("bsel", [128, 8], F32)
    loff = sb("loff", [128, 8], F32)
    gidxf = sb("gidxf", [128, 8], F32)
    gidxi = sb("gidxi", [128, 8], I32)
    dyef = sb("dyef", [128, 8], F32)
    dyei = sb("dyei", [128, 8], I32)
    sm = sb("sm", [128, 16], F32)
    tokf = sb("tokf", [128, 8], F32)
    toki = sb("toki", [128, 8], I32)
    wcol = sb("wcol", [128, 8], F32)
    xe = sb("xe", [128, 8 * D], BF16)
    xeT = sb("xeT", [128, 8 * D], BF16)
    w1b = sb("w1b", [128, 2 * 8 * FEXP], BF16)
    w3b = sb("w3b", [128, 2 * 8 * FEXP], BF16)
    w2b = sb("w2b", [128, 2 * 4 * D], BF16)
    hg = sb("hg", [128, 4 * CAP], BF16)
    hu = sb("hu", [128, 4 * CAP], BF16)
    prodE = sb("prodE", [128, 4 * CAP], BF16)
    yeb = sb("yeb", [128, 8 * D], BF16)
    ps_b = psum("ps_b", [128, 8], F32)
    psT = psum("psT", [128, D], BF16)
    ps_h = psum("ps_h", [128, 512], F32)
    ps_u = psum("ps_u", [128, 512], F32)

    with nc.Block() as block:
        em = Em(nc, es)
        em.dma("dpre", "sync", lambda s: s.dma_start(out=cntb[:, :], in_=counts_all[:, :]))
        em.dma("dpre", "sync", lambda s: s.dma_start(out=slotbase[:, :], in_=slotbase_in[:, :]))
        em.dma("dpre", "sync", lambda s: s.dma_start(out=prow[:, :], in_=prow_in[:, :]))
        em.dma("dpre", "sync", lambda s: s.dma_start(out=eofs[:, :], in_=eofs_in[:, :]))
        em.g(lambda g: g.memset(ones1_f[:, :], 1.0))
        emit_identity(em, ident)
        ld0 = em.now("dpre")
        em.v(lambda v: v.memset(bases[0:1, 0:E], 0.0), waits=[ld0])
        for c in range(1, NCORES):
            em.v(lambda v, c=c: v.tensor_add(
                bases[0:1, c * E:(c + 1) * E],
                bases[0:1, (c - 1) * E:c * E],
                cntb[0:1, (c - 1) * E:c * E]))
        bases_done = em.now("cv")
        g_all = em.now("cg")

        regs = {}

        def init_regs(g):
            regs["bc_c"] = g.register("bc_c").__enter__()
            g.reg_mov(regs["bc_c"], NCORES * NSLOT - 1)
            regs["bc_x"] = g.register("bc_x").__enter__()
            g.reg_mov(regs["bc_x"], NCORES * TC - 1)
            regs["bc_y"] = g.register("bc_y").__enter__()
            g.reg_mov(regs["bc_y"], NSLOT - 1)
            return None
        em.ops["gpsimd"].append((init_regs, [], ("cg", 0)))

        marks = {"w_free": [None, None], "ye_free": None, "sm_free": None,
                 "xe_free": None}

        for el in range(EPC):
            pb = el % 2
            w_w = [w for w in [marks["w_free"][pb]] if w]
            lw1 = em.dma("dw%d" % pb, "gpsimd", lambda g, el=el, pb=pb: g.dma_start(
                out=w1b[:, pb * 8 * FEXP:(pb + 1) * 8 * FEXP].rearrange(
                    "p (c f) -> p c f", c=8),
                in_=w1T_in[el * D:(el + 1) * D, :].rearrange(
                    "(c p) f -> p c f", p=128)), waits=w_w)
            lw3 = em.dma("dw%d" % pb, "gpsimd", lambda g, el=el, pb=pb: g.dma_start(
                out=w3b[:, pb * 8 * FEXP:(pb + 1) * 8 * FEXP].rearrange(
                    "p (c f) -> p c f", c=8),
                in_=w3T_in[el * D:(el + 1) * D, :].rearrange(
                    "(c p) f -> p c f", p=128)))
            lw2 = em.dma("dw%d" % pb, "gpsimd", lambda g, el=el, pb=pb: g.dma_start(
                out=w2b[:, pb * 4 * D:(pb + 1) * 4 * D].rearrange(
                    "p (c f) -> p c f", c=4),
                in_=w2T_in[el * FEXP:(el + 1) * FEXP, :].rearrange(
                    "(c p) f -> p c f", p=128)))

            def mm_B(t, el=el):
                return t.matmul(ps_b[:, :], ones1_f[:, :],
                                bass.AP(bases, 8 * 0 + el, [[NCORES * E, 1], [E, NCORES]]),
                                start=True, stop=True)
            mmB = em.t(mm_B, waits=[bases_done, g_all])
            em.v(lambda v: v.tensor_copy(Bb[:, :], ps_b[:, :]), waits=[mmB])
            for s in range(8):
                em.v(lambda v, s=s: v.tensor_tensor(
                    gemask[:, :],
                    bass.AP(prow, s, [[8, 128], [0, 8]]),
                    Bb[:, :], ALU.is_ge))
                em.v(lambda v, s=s: v.reduce_sum(cofp[:, s:s + 1], gemask[:, :],
                                                 axis=AX.X))
                em.v(lambda v: v.tensor_mul(gemask[:, :], gemask[:, :], Bb[:, :]))
                em.v(lambda v, s=s: v.reduce_max(bsel[:, s:s + 1], gemask[:, :],
                                                 axis=AX.X))
            em.v(lambda v: v.tensor_scalar(cofp[:, :], cofp[:, :], -1.0, None,
                                           ALU.add))
            em.v(lambda v: v.tensor_sub(loff[:, :], prow[:, :], bsel[:, :]))
            em.v(lambda v, el=el: v.scalar_tensor_tensor(
                gidxf[:, :], cofp[:, :], float(NSLOT),
                slotbase[:, el * 8:(el + 1) * 8], ALU.mult, ALU.add))
            em.v(lambda v: v.tensor_sub(gidxf[:, :], gidxf[:, :], bsel[:, :]))
            em.v(lambda v, el=el: v.scalar_tensor_tensor(
                dyef[:, :], cofp[:, :], float(EPC * CAP), loff[:, :],
                ALU.mult, ALU.add))
            em.v(lambda v, el=el: v.tensor_scalar(
                dyef[:, :], dyef[:, :], float(el * CAP), None, ALU.add))
            ye_guard = [w for w in [marks["ye_free"]] if w]
            em.v(lambda v: v.tensor_copy(gidxi[:, :], gidxf[:, :]),
                 waits=ye_guard)
            em.v(lambda v: v.tensor_copy(dyei[:, :], dyef[:, :]))
            stg = em.now("cv")

            sm_guard = [w for w in [marks["sm_free"]] if w]
            for s in range(8):
                sm_done = em.dma("dsm", "gpsimd", lambda g, s=s: g.indirect_dma_start(
                    out=sm[:, 2 * s:2 * s + 2], out_offset=None,
                    in_=contrib_all[:, :],
                    in_offset=IndirectOffsetOnAxis(gidxi[:, s:s + 1], 0),
                    bounds_check=regs["bc_c"], oob_is_err=False),
                    waits=[stg] + sm_guard)
            em.v(lambda v: v.tensor_copy(
                tokf[:, :], bass.AP(sm, 0, [[16, 128], [2, 8]])),
                waits=[sm_done])
            em.v(lambda v: v.tensor_copy(
                wcol[:, :], bass.AP(sm, 1, [[16, 128], [2, 8]])))
            em.v(lambda v: v.tensor_copy(toki[:, :], tokf[:, :]))
            tok_done = em.now("cv")
            marks["sm_free"] = tok_done

            xe_guard = [w for w in [marks["xe_free"]] if w]
            for s in range(8):
                xe_done = em.dma("dxe", "gpsimd", lambda g, s=s: g.indirect_dma_start(
                    out=xe[:, s * D:(s + 1) * D], out_offset=None,
                    in_=xln_all[:, :],
                    in_offset=IndirectOffsetOnAxis(toki[:, s:s + 1], 0),
                    bounds_check=regs["bc_x"], oob_is_err=False),
                    waits=[tok_done] + xe_guard)

            for s in range(8):
                tr = emit_transpose8(em, psT, xe[:, s * D:(s + 1) * D], ident,
                                     [xe_done, g_all, em.now("cv")])
                em.v(lambda v, s=s: v.tensor_copy(
                    bass.AP(xeT, s * 128, [[8 * D, 128], [1024, 8], [1, 128]]),
                    psT[:, :]), waits=[tr])
            xeT_done = em.now("cv")
            marks["xe_free"] = xeT_done

            wv = [lw2]
            for fc in range(4):
                for half in range(2):
                    def mm_h(t, fc=fc, half=half, pb=pb):
                        inst = None
                        for c in range(8):
                            inst = t.matmul(
                                ps_h[:, :],
                                w1b[:, pb * 8 * FEXP + c * FEXP + fc * 128:
                                    pb * 8 * FEXP + c * FEXP + fc * 128 + 128],
                                bass.AP(xeT, c * CAP + half * 512,
                                        [[8 * D, 128], [1, 512]]),
                                start=(c == 0), stop=(c == 7))
                        return inst
                    mmh = em.t(mm_h, waits=[xeT_done, em.now("cs")] + wv)
                    em.s(lambda s, fc=fc, half=half: s.activation(
                        hg[:, fc * CAP + half * 512:fc * CAP + half * 512 + 512],
                        ps_h[:, :], AF.Silu), waits=[mmh])

                    def mm_u(t, fc=fc, half=half, pb=pb):
                        inst = None
                        for c in range(8):
                            inst = t.matmul(
                                ps_u[:, :],
                                w3b[:, pb * 8 * FEXP + c * FEXP + fc * 128:
                                    pb * 8 * FEXP + c * FEXP + fc * 128 + 128],
                                bass.AP(xeT, c * CAP + half * 512,
                                        [[8 * D, 128], [1, 512]]),
                                start=(c == 0), stop=(c == 7))
                        return inst
                    mmu = em.t(mm_u, waits=[xeT_done, em.now("cv")] + wv)
                    em.v(lambda v, fc=fc, half=half: v.tensor_mul(
                        prodE[:, fc * CAP + half * 512:fc * CAP + half * 512 + 512],
                        ps_u[:, :],
                        hg[:, fc * CAP + half * 512:fc * CAP + half * 512 + 512]),
                        waits=[mmu, em.now("cs")])
            prod_done = em.now("cv")

            for s in range(8):
                for half in range(2):
                    def mm_y(t, s=s, half=half, pb=pb):
                        inst = None
                        for fc in range(4):
                            inst = t.matmul(
                                ps_h[:, :],
                                prodE[:, fc * CAP + s * 128:fc * CAP + s * 128 + 128],
                                w2b[:, pb * 4 * D + fc * D + half * 512:
                                    pb * 4 * D + fc * D + half * 512 + 512],
                                start=(fc == 0), stop=(fc == 3))
                        return inst
                    mmy = em.t(mm_y, waits=[prod_done, lw2, em.now("cs"),
                                            em.now("cv")])
                    em.v(lambda v, s=s, half=half: v.tensor_scalar(
                        yeb[:, s * D + half * 512:s * D + half * 512 + 512],
                        ps_h[:, :], wcol[:, s:s + 1], None, ALU.mult),
                        waits=[mmy])
            ye_done = em.now("cv")
            marks["w_free"][pb] = ye_done

            for s in range(8):
                yemark = em.dma("dye", "gpsimd", lambda g, s=s: g.indirect_dma_start(
                    out=o_ye[:, :],
                    out_offset=IndirectOffsetOnAxis(dyei[:, s:s + 1], 0),
                    in_=yeb[:, s * D:(s + 1) * D], in_offset=None,
                    bounds_check=regs["bc_y"], oob_is_err=False),
                    waits=[ye_done])
            marks["ye_free"] = yemark

        em.barrier_all()
        em.build(block)
    es.close()
    return nc


def launch_B_inputs(core, xln_list, counts_list, contrib_list, w1, w2, w3):
    xln_all = np.concatenate(xln_list, 0)
    contrib_all = np.concatenate(contrib_list, 0)
    cnt = np.stack([c.reshape(E) for c in counts_list], 0)
    e0 = core * EPC
    cnt_rot = np.roll(cnt, -e0, axis=1)
    counts_all = cnt_rot.reshape(1, NCORES * E).astype(np.float32)
    sl = slice(e0, e0 + EPC)
    w1T = np.ascontiguousarray(w1[sl].transpose(0, 2, 1)).reshape(EPC * D, FEXP)
    w3T = np.ascontiguousarray(w3[sl].transpose(0, 2, 1)).reshape(EPC * D, FEXP)
    w2T = np.ascontiguousarray(w2[sl].transpose(0, 2, 1)).reshape(EPC * FEXP, D)
    rows = np.arange(128, dtype=np.float32)
    slotbase = np.zeros((128, E), np.float32)
    for el in range(EPC):
        for s in range(8):
            slotbase[:, el * 8 + s] = (e0 + el) * CAP + s * 128 + rows
    prow = np.zeros((128, 8), np.float32)
    for s in range(8):
        prow[:, s] = s * 128 + rows
    return {
        "xln_all": xln_all, "counts_all": counts_all,
        "contrib_all": contrib_all,
        "w1T": w1T.astype(np.float32), "w3T": w3T.astype(np.float32),
        "w2T": w2T.astype(np.float32),
        "slotbase": slotbase, "prow": prow,
        "eofs": np.array([[float(e0)]], np.float32),
    }


def build_launch_C(nst=TC // 512, upto="all"):
    nc = bass.Bass("TRN2", target_bir_lowering=False)
    TCl = nst * 512
    x_in = nc.dram_tensor("x", [TCl, D], F32, kind="ExternalInput").ap()
    sh_in = nc.dram_tensor("shared", [TCl, D], F32, kind="ExternalInput").ap()
    dest_in = nc.dram_tensor("dest", [TCl, TOPK], F32, kind="ExternalInput").ap()
    yeR = nc.dram_tensor("yeR", [NSLOT + 1, D], BF16, kind="ExternalInput").ap()
    WqT_in = nc.dram_tensor("WqT", [D, D], F32, kind="ExternalInput").ap()
    WkT_in = nc.dram_tensor("WkT", [D, D], F32, kind="ExternalInput").ap()
    WvT_in = nc.dram_tensor("WvT", [D, D], F32, kind="ExternalInput").ap()
    bqc_in = nc.dram_tensor("bqc", [128, 8], F32, kind="ExternalInput").ap()
    bkc_in = nc.dram_tensor("bkc", [128, 8], F32, kind="ExternalInput").ap()
    bv_in = nc.dram_tensor("bv", [1, D], F32, kind="ExternalInput").ap()
    tri_in = nc.dram_tensor("tri", [128, 128], F32, kind="ExternalInput").ap()
    bdm_in = nc.dram_tensor("bdm", [128, 128], F32, kind="ExternalInput").ap()
    hmask_in = nc.dram_tensor("hmask", [128, 2], F32, kind="ExternalInput").ap()

    o_x1 = nc.dram_tensor("x1", [TCl, D], F32, kind="ExternalOutput").ap()
    o_op = nc.dram_tensor("opart", [TCl, D], F32, kind="ExternalOutput").ap()
    o_qf = nc.dram_tensor("qfm", [nst * 128, 8 * 512], BF16,
                          kind="ExternalOutput").ap()
    o_st = nc.dram_tensor("state", [128, 8 * 128], F32,
                          kind="ExternalOutput").ap()

    es = ExitStack()
    sb = lambda name, shape, dt: es.enter_context(nc.sbuf_tensor(name, shape, dt))
    psum = lambda name, shape, dt: es.enter_context(nc.psum_tensor(name, shape, dt))

    WqTb = sb("WqTb", [128, 8 * D], BF16)
    WkTb = sb("WkTb", [128, 8 * D], BF16)
    WvTb = sb("WvTb", [128, 8 * D], BF16)
    bqc = sb("bqc_sb", [128, 8], F32)
    bkc = sb("bkc_sb", [128, 8], F32)
    bvb = sb("bvb", [1, D], BF16)
    trim = sb("trim", [128, 128], F32)
    bdm = sb("bdm_sb", [128, 128], F32)
    hmask = sb("hmask_sb", [128, 2], F32)
    qzb = sb("qzb", [128, 2 * 8 * 512], BF16)
    ident = sb("ident", [128, 128], BF16)
    ones1_b = sb("ones1_b", [1, 128], BF16)

    xt = sb("xt", [128, D], F32)
    sht = sb("sht", [128, D], F32)
    x1t = sb("x1t", [128, D], F32)
    destf = sb("destf_c", [128, TOPK], F32)
    desti = sb("desti_c", [128, TOPK], I32)
    moe8 = sb("moe8", [128, TOPK * D], BF16)
    stats = sb("stats_c", [128, 12], F32)
    mv = sb("mv_c", [128, 2], F32)
    rr = sb("rr_c", [128, 1], F32)
    col = sb("col_c", [128, 1], F32)
    xlnt = sb("xlnt_c", [128, D], BF16)
    xlnT = sb("xlnT_c", [128, 4 * 8 * 128], BF16)
    qfm = sb("qfm_sb", [128, 8 * 512], BF16)
    kfm = sb("kfm_sb", [128, 8 * 512], BF16)
    ktm = sb("ktm_sb", [128, 4 * D], BF16)
    vtm = sb("vtm_sb", [128, 4 * D], BF16)
    act1 = sb("act1", [128, 512], F32)
    act2 = sb("act2", [128, 512], F32)
    sT = sb("sT_sb", [128, 4 * 128], BF16)
    stbf = sb("stbf", [128, 8 * 128], BF16)
    state = sb("state_sb", [128, 8 * 128], F32)
    sustage = sb("sustage", [128, 8 * 128], F32)
    oev = sb("oev", [128, D], F32)

    psT = psum("psT_c", [128, D], BF16)
    ps_a = psum("ps_a", [128, 512], F32)
    ps_b = psum("ps_b", [128, 512], F32)
    ps_o = psum("ps_o", [128, D], F32)
    ps_su = psum("ps_su", [128, 8 * 128], F32)

    with nc.Block() as block:
        em = Em(nc, es)
        chunked_load(em, WqTb, WqT_in, 8, D)
        chunked_load(em, WkTb, WkT_in, 8, D)
        chunked_load(em, WvTb, WvT_in, 8, D)
        em.dma("dpre", "sync", lambda s: s.dma_start(out=bqc[:, :], in_=bqc_in[:, :]))
        em.dma("dpre", "sync", lambda s: s.dma_start(out=bkc[:, :], in_=bkc_in[:, :]))
        em.dma("dpre_g", "gpsimd", lambda g: g.dma_start(out=bvb[:, :], in_=bv_in[:, :]))
        em.dma("dpre", "sync", lambda s: s.dma_start(out=trim[:, :], in_=tri_in[:, :]))
        em.dma("dpre", "sync", lambda s: s.dma_start(out=bdm[:, :], in_=bdm_in[:, :]))
        em.dma("dpre", "sync", lambda s: s.dma_start(out=hmask[:, :], in_=hmask_in[:, :]))
        em.g(lambda g: g.memset(ones1_b[:, :], 1.0))
        em.v(lambda v: v.memset(state[:, :], 0.0))
        emit_identity(em, ident)
        ld_all = em.now("dpre")
        gp_all = em.now("dpre_g")
        g_all = em.now("cg")

        regs = {}

        def init_regs(g):
            regs["bc"] = g.register("bc_ye").__enter__()
            g.reg_mov(regs["bc"], NSLOT)
            return None
        em.ops["gpsimd"].append((init_regs, [], ("cg", 0)))

        marks = {"moe_free": None, "x1_free": None, "xlnT_free": None,
                 "q_free": None, "k_free": None, "v_free": None,
                 "o_free": None, "sT_free": None, "d_free": None,
                 "xln_free": None}

        def W(key):
            return [w for w in [marks[key]] if w]

        for st in range(nst):
            for q in range(4):
                i = st * 4 + q
                ld_x = em.dma("dxx", "sync", lambda s, i=i: s.dma_start(
                    out=xt[:, :], in_=x_in[i * 128:(i + 1) * 128, :]),
                    waits=[em.now("cv")])
                ld_sh = em.dma("dsh", "sync", lambda s, i=i: s.dma_start(
                    out=sht[:, :], in_=sh_in[i * 128:(i + 1) * 128, :]),
                    waits=[em.now("cv")])
                ld_d = em.dma("ddst", "sync", lambda s, i=i: s.dma_start(
                    out=destf[:, :], in_=dest_in[i * 128:(i + 1) * 128, :]),
                    waits=[em.now("cv")])
                em.v(lambda v: v.tensor_scalar(destf[:, :], destf[:, :],
                                               float(NSLOT), None, ALU.min),
                     waits=[ld_d])
                cast = em.v(lambda v: v.tensor_copy(desti[:, :], destf[:, :]),
                            waits=W("d_free"))
                for k in range(TOPK):
                    moe_done = em.dma("dmoe", "gpsimd",
                                      lambda g, k=k: g.indirect_dma_start(
                        out=moe8[:, k * D:(k + 1) * D], out_offset=None,
                        in_=yeR[:, :],
                        in_offset=IndirectOffsetOnAxis(desti[:, k:k + 1], 0),
                        bounds_check=regs["bc"], oob_is_err=False),
                        waits=[cast] + W("moe_free"))
                marks["d_free"] = moe_done
                for half in range(2):
                    def mm_sum(t, half=half):
                        inst = None
                        for k in range(TOPK):
                            inst = t.matmul(
                                ps_a[:, :] if half == 0 else ps_b[:, :],
                                ident[:, :],
                                moe8[:, k * D + half * 512:k * D + half * 512 + 512],
                                start=(k == 0), stop=(k == TOPK - 1))
                        return inst
                    mms = em.t(mm_sum, waits=[moe_done, g_all, em.now("cv")])
                    pshalf = ps_a if half == 0 else ps_b
                    em.v(lambda v, half=half, pshalf=pshalf: v.tensor_add(
                        x1t[:, half * 512:half * 512 + 512], pshalf[:, :],
                        sht[:, half * 512:half * 512 + 512]),
                        waits=[mms, ld_sh] + (W("x1_free") if half == 0 else []))
                    em.v(lambda v, half=half: v.scalar_tensor_tensor(
                        x1t[:, half * 512:half * 512 + 512],
                        x1t[:, half * 512:half * 512 + 512], 0.5,
                        xt[:, half * 512:half * 512 + 512],
                        ALU.mult, ALU.add), waits=[ld_x])
                marks["moe_free"] = em.now("ct")
                x1_done = em.now("cv")
                st_x1 = em.dma("sx1", "sync", lambda s, i=i: s.dma_start(
                    out=o_x1[i * 128:(i + 1) * 128, :], in_=x1t[:, :]),
                    waits=[x1_done])
                marks["x1_free"] = st_x1
                if upto == "combine":
                    continue
                ln_done = emit_ln(em, x1t[:, :], xlnt[:, :], stats, mv, col, rr,
                                  [x1_done] + W("xln_free"))
                tr = emit_transpose8(em, psT, xlnt[:, :], ident,
                                     [ln_done, g_all] + W("xlnT_free"))
                marks["xln_free"] = tr
                em.v(lambda v, q=q: v.tensor_copy(
                    bass.AP(xlnT, q * 128, [[4 * 8 * 128, 128], [512, 8], [1, 128]]),
                    psT[:, :]), waits=[tr])
            xlnT_done = em.now("cv")
            if upto == "combine":
                continue

            for which, Wb, bcol, dst in (("q", WqTb, bqc, qfm), ("k", WkTb, bkc, kfm)):
                for c in range(8):
                    def mm_qk(t, Wb=Wb, c=c):
                        inst = None
                        for dc in range(8):
                            inst = t.matmul(
                                ps_a[:, :],
                                Wb[:, dc * D + c * 128:dc * D + c * 128 + 128],
                                bass.AP(xlnT, dc * 512, [[4096, 128], [1, 512]]),
                                start=(dc == 0), stop=(dc == 7))
                        return inst
                    mm1 = em.t(mm_qk, waits=[xlnT_done, gp_all, ld_all,
                                             em.now("cv"), em.now("cs")])
                    em.v(lambda v, bcol=bcol, c=c: v.tensor_scalar(
                        act1[:, :], ps_a[:, :], bcol[:, c:c + 1], 0.0,
                        ALU.add, ALU.max), waits=[mm1] + (W("q_free") if which == "q" else W("k_free")))
                    em.v(lambda v, bcol=bcol, c=c: v.scalar_tensor_tensor(
                        act2[:, :], ps_a[:, :], bcol[:, c:c + 1], act1[:, :],
                        ALU.add, ALU.subtract))
                    sc_exp = em.s(lambda s: s.activation(act2[:, :], act2[:, :],
                                                         AF.Exp),
                                  waits=[em.now("cv")])
                    em.v(lambda v, dst=dst, c=c: v.tensor_add(
                        dst[:, c * 512:(c + 1) * 512], act1[:, :], act2[:, :]),
                        waits=[sc_exp])
            for c in range(8):
                for par in range(2):
                    em.v(lambda v, c=c, par=par: v.tensor_scalar(
                        qzb[:, c * 1024 + par * 512:c * 1024 + par * 512 + 512],
                        qfm[:, c * 512:(c + 1) * 512],
                        hmask[:, par:par + 1], None, ALU.mult))
            q_done = em.now("cv")
            st_qf = em.dma("sqf", "sync", lambda s, st=st: s.dma_start(
                out=o_qf[st * 128:(st + 1) * 128, :], in_=qfm[:, :]),
                waits=[q_done])
            for q in range(4):
                for half in range(2):
                    def mm_v(t, q=q, half=half):
                        for dc in range(8):
                            t.matmul(ps_b[:, :],
                                     bass.AP(xlnT, dc * 512 + q * 128,
                                             [[4096, 128], [1, 128]]),
                                     WvTb[:, dc * D + half * 512:
                                          dc * D + half * 512 + 512],
                                     start=(dc == 0), stop=False)
                        return t.matmul(ps_b[:, :], ones1_b[:, :],
                                        bvb[:, half * 512:half * 512 + 512],
                                        start=False, stop=True)
                    mmv = em.t(mm_v, waits=[xlnT_done, gp_all, g_all,
                                            em.now("cv")])
                    em.v(lambda v, q=q, half=half: v.tensor_copy(
                        vtm[:, q * D + half * 512:q * D + half * 512 + 512],
                        ps_b[:, :]), waits=[mmv] + (W("v_free") if q == 0 and half == 0 else []))
            v_done = em.now("cv")
            for q in range(4):
                def tr_k(t, q=q):
                    inst = None
                    for c in range(8):
                        inst = t.transpose(
                            psT[:, c * 128:(c + 1) * 128],
                            kfm[:, c * 512 + q * 128:c * 512 + q * 128 + 128],
                            ident[:, :])
                    return inst
                trk = em.t(tr_k, waits=[q_done, em.now("cv")])
                em.v(lambda v, q=q: v.tensor_copy(ktm[:, q * D:(q + 1) * D],
                                                  psT[:, :]), waits=[trk])
            k_done = em.now("cv")
            marks["xlnT_free"] = em.now("ct")
            if upto == "qkv":
                continue

            for q in range(4):
                i = st * 4 + q
                em.v(lambda v: v.tensor_copy(stbf[:, :], state[:, :]),
                     waits=[em.now("ct")])
                stbf_done = em.now("cv")
                for hg in range(4):
                    def mm_sT2(t, hg=hg, q=q):
                        inst = None
                        for hh in range(4):
                            h = hg * 4 + hh
                            c = h // 2
                            par = h % 2
                            lhs = kfm[:, c * 512 + q * 128:
                                      c * 512 + q * 128 + 128]
                            rhs = qzb[:, c * 1024 + par * 512 + q * 128:
                                      c * 1024 + par * 512 + q * 128 + 128]
                            inst = t.matmul(ps_a[:, hh * 128:(hh + 1) * 128],
                                            lhs, rhs, start=True, stop=True)
                        return inst
                    mmst = em.t(mm_sT2, waits=[q_done, k_done, em.now("cv")])
                    em.v(lambda v, hg=hg: v.tensor_tensor(
                        sT[:, :], ps_a[:, :],
                        bass.AP(trim, 0, [[128, 128], [0, 4], [1, 128]]),
                        ALU.mult), waits=[mmst] + W("sT_free"))
                    sT_done = em.now("cv")
                    def mm_o(t, hg=hg, q=q):
                        inst = None
                        for hh in range(4):
                            h = hg * 4 + hh
                            inst = t.matmul(
                                ps_o[:, h * 64:(h + 1) * 64],
                                sT[:, hh * 128:(hh + 1) * 128],
                                vtm[:, q * D + h * 64:q * D + h * 64 + 64],
                                start=(h % 8 == 0), stop=False)
                        return inst
                    if upto in ("all", "o"):
                        em.t(mm_o, waits=[sT_done, v_done])
                    marks["sT_free"] = em.now("ct")
                def mm_sr(t, q=q):
                    inst = None
                    for c in range(8):
                        inst = t.matmul(
                            ps_o[:, c * 128:(c + 1) * 128],
                            qfm[:, c * 512 + q * 128:c * 512 + q * 128 + 128],
                            stbf[:, c * 128:(c + 1) * 128],
                            start=False, stop=(c == 3 or c == 7))
                    return inst
                if upto in ("all", "o"):
                    mmsr = em.t(mm_sr, waits=[stbf_done, q_done])
                    ev1 = em.v(lambda v: v.tensor_copy(oev[:, 0:512],
                                                       ps_o[:, 0:512]),
                               waits=[mmsr] + W("o_free"))
                    ev2 = em.v(lambda v: v.tensor_copy(oev[:, 512:1024],
                                                       ps_o[:, 512:1024]))
                    st_o = em.dma("sop", "sync", lambda s, i=i: s.dma_start(
                        out=o_op[i * 128:(i + 1) * 128, :], in_=oev[:, :]),
                        waits=[em.now("cv")])
                    marks["o_free"] = st_o
                def mm_su(t, q=q):
                    inst = None
                    for c in range(8):
                        inst = t.matmul(
                            ps_su[:, c * 128:(c + 1) * 128],
                            ktm[:, q * D + c * 128:q * D + c * 128 + 128],
                            vtm[:, q * D + c * 128:q * D + c * 128 + 128],
                            start=True, stop=True)
                    return inst
                if upto == "all":
                    mmsu = em.t(mm_su, waits=[k_done, v_done, em.now("cv")])
                    em.v(lambda v: v.tensor_tensor(
                        sustage[:, :], ps_su[:, :],
                        bass.AP(bdm, 0, [[128, 128], [0, 8], [1, 128]]),
                        ALU.mult), waits=[mmsu])
                    em.v(lambda v: v.tensor_add(state[:, :], state[:, :],
                                                sustage[:, :]))
            marks["q_free"] = em.now("ct")
            marks["k_free"] = em.now("ct")
            marks["v_free"] = em.now("ct")

        em.dma("sst", "sync", lambda s: s.dma_start(
            out=o_st[:, :], in_=state[:, :]), waits=[em.now("cv")])
        em.barrier_all()
        em.build(block)
    es.close()
    return nc


def build_launch_D(nst=TC // 512):
    nc = bass.Bass("TRN2", target_bir_lowering=False)
    TCl = nst * 512
    x1_in = nc.dram_tensor("x1", [TCl, D], F32, kind="ExternalInput").ap()
    op_in = nc.dram_tensor("opart", [TCl, D], F32, kind="ExternalInput").ap()
    qf_in = nc.dram_tensor("qfm", [nst * 128, 8 * 512], BF16,
                           kind="ExternalInput").ap()
    sp_in = nc.dram_tensor("Spart", [128, 8 * 128], F32,
                           kind="ExternalInput").ap()
    lng_in = nc.dram_tensor("ln_g", [128, 8], F32, kind="ExternalInput").ap()
    lnb_in = nc.dram_tensor("ln_b", [128, 8], F32, kind="ExternalInput").ap()
    WoT_in = nc.dram_tensor("WoT", [D, D], F32, kind="ExternalInput").ap()
    bo_in = nc.dram_tensor("bo", [1, D], F32, kind="ExternalInput").ap()
    o_x2 = nc.dram_tensor("x2", [TCl, D], F32, kind="ExternalOutput").ap()

    es = ExitStack()
    sb = lambda name, shape, dt: es.enter_context(nc.sbuf_tensor(name, shape, dt))
    psum = lambda name, shape, dt: es.enter_context(nc.psum_tensor(name, shape, dt))

    WoTb = sb("WoTb", [128, 8 * D], BF16)
    lngc = sb("lngc", [128, 8], F32)
    lnbc = sb("lnbc", [128, 8], BF16)
    bWo = sb("bWo", [1, D], BF16)
    bob = sb("bob", [1, D], BF16)
    ones1_b = sb("ones1_b_d", [1, 128], BF16)
    ident = sb("ident_d", [128, 128], BF16)
    spb = sb("spb", [128, 8 * 128], BF16)
    qfb = sb("qfb", [128, 8 * 512], BF16)
    x1t = sb("x1t_d", [128, D], F32)
    opt = sb("opt_d", [128, D], F32)
    ot = sb("ot_d", [128, D], F32)
    olnt = sb("olnt_d", [128, D], BF16)
    olnT = sb("olnT_d", [128, D], BF16)
    stats = sb("stats_d", [128, 12], F32)
    mv = sb("mv_d", [128, 2], F32)
    rr = sb("rr_d", [128, 1], F32)
    col = sb("col_d", [128, 1], F32)
    x2t = sb("x2t", [128, D], F32)
    psT = psum("psT_d", [128, D], BF16)
    ps_o = psum("ps_o_d", [128, D], F32)
    ps_p = psum("ps_p_d", [128, 512], F32)
    ps_r = psum("ps_r_d", [1, D], F32)

    with nc.Block() as block:
        em = Em(nc, es)
        chunked_load(em, WoTb, WoT_in, 8, D)
        em.dma("dpre", "sync", lambda s: s.dma_start(
            out=lngc[:, :], in_=lng_in[:, :]))
        em.dma("dpre_g", "gpsimd", lambda g: g.dma_start(
            out=lnbc[:, :], in_=lnb_in[:, :]))
        em.dma("dpre_g", "gpsimd", lambda g: g.dma_start(
            out=spb[:, :], in_=sp_in[:, :]))
        em.dma("dpre_g", "gpsimd", lambda g: g.dma_start(
            out=bob[:, :], in_=bo_in[:, :]))
        em.g(lambda g: g.memset(ones1_b[:, :], 1.0))
        emit_identity(em, ident)
        ld_all = em.now("dpre")
        gp_all = em.now("dpre_g")
        g_all = em.now("cg")

        def mm_bw(t):
            inst = None
            for half in range(2):
                for c in range(8):
                    inst = t.matmul(ps_r[:, half * 512:half * 512 + 512],
                                    lnbc[:, c:c + 1],
                                    WoTb[:, c * D + half * 512:
                                         c * D + half * 512 + 512],
                                    start=(c == 0), stop=(c == 7))
            return inst
        mmbw = em.t(mm_bw, waits=[ld_all, gp_all, g_all])
        em.v(lambda v: v.tensor_copy(bWo[:, :], ps_r[:, :]), waits=[mmbw])
        for c in range(8):
            em.s(lambda s, c=c: s.mul(WoTb[:, c * D:(c + 1) * D],
                                      WoTb[:, c * D:(c + 1) * D],
                                      lngc[:, c:c + 1]),
                 waits=([mmbw] if c == 0 else []))
        wos_done = em.now("cs")

        marks = {"x2_free": None, "oln_free": None, "q_free": None}

        def W(key):
            return [w for w in [marks[key]] if w]

        for st in range(nst):
            ld_q = em.dma("dqf", "sync", lambda s, st=st: s.dma_start(
                out=qfb[:, :], in_=qf_in[st * 128:(st + 1) * 128, :]),
                waits=W("q_free"))
            for q in range(4):
                i = st * 4 + q
                ld_x1 = em.dma("dx1", "sync", lambda s, i=i: s.dma_start(
                    out=x1t[:, :], in_=x1_in[i * 128:(i + 1) * 128, :]),
                    waits=[em.now("cv")])
                ld_op = em.dma("dop", "sync", lambda s, i=i: s.dma_start(
                    out=opt[:, :], in_=op_in[i * 128:(i + 1) * 128, :]),
                    waits=[em.now("cv")])
                def mm_oc(t, q=q):
                    inst = None
                    for c in range(8):
                        inst = t.matmul(
                            ps_o[:, c * 128:(c + 1) * 128],
                            qfb[:, c * 512 + q * 128:c * 512 + q * 128 + 128],
                            spb[:, c * 128:(c + 1) * 128],
                            start=(c % 4 == 0), stop=(c == 3 or c == 7))
                    return inst
                mmoc = em.t(mm_oc, waits=[ld_q, gp_all, em.now("cv")])
                em.v(lambda v: v.tensor_add(ot[:, 0:512], ps_o[:, 0:512],
                                            opt[:, 0:512]),
                     waits=[mmoc, ld_op])
                em.v(lambda v: v.tensor_add(ot[:, 512:1024], ps_o[:, 512:1024],
                                            opt[:, 512:1024]))
                o_done = em.now("cv")
                ln_done = emit_ln(em, ot[:, :], olnt[:, :], stats, mv, col, rr,
                                  [o_done] + W("oln_free"))
                tr = emit_transpose8(em, psT, olnt[:, :], ident,
                                     [ln_done, g_all])
                marks["oln_free"] = tr
                evT = em.v(lambda v: v.tensor_copy(olnT[:, :], psT[:, :]),
                           waits=[tr])
                for half in range(2):
                    def mm_p(t, half=half):
                        for c in range(8):
                            t.matmul(ps_p[:, :],
                                     olnT[:, c * 128:(c + 1) * 128],
                                     WoTb[:, c * D + half * 512:
                                          c * D + half * 512 + 512],
                                     start=(c == 0), stop=False)
                        t.matmul(ps_p[:, :], ones1_b[:, :],
                                 bWo[:, half * 512:half * 512 + 512],
                                 start=False, stop=False)
                        return t.matmul(ps_p[:, :], ones1_b[:, :],
                                        bob[:, half * 512:half * 512 + 512],
                                        start=False, stop=True)
                    mmp = em.t(mm_p, waits=[evT, wos_done, em.now("cv")])
                    em.v(lambda v, half=half: v.tensor_add(
                        x2t[:, half * 512:half * 512 + 512], ps_p[:, :],
                        x1t[:, half * 512:half * 512 + 512]),
                        waits=[mmp, ld_x1] + (W("x2_free") if half == 0 else []))
                st_x2 = em.dma("sx2", "sync", lambda s, i=i: s.dma_start(
                    out=o_x2[i * 128:(i + 1) * 128, :], in_=x2t[:, :]),
                    waits=[em.now("cv")])
                marks["x2_free"] = st_x2
            marks["q_free"] = em.now("ct")

        em.barrier_all()
        em.build(block)
    es.close()
    return nc


def emit_combine_tile(em, i, x_src, sh_src, dest_src, yeR, bufs, regs, marks, W,
                      out_x=None, coef=0.5):
    xt, sht, destf, desti = bufs["xt"], bufs["sht"], bufs["destf"], bufs["desti"]
    moe8, x1t, ident = bufs["moe8"], bufs["x1t"], bufs["ident"]
    ps_a, ps_b = bufs["ps_a"], bufs["ps_b"]
    ld_x = em.dma("dxx", "sync", lambda s: s.dma_start(
        out=xt[:, :], in_=x_src[i * 128:(i + 1) * 128, :]),
        waits=[em.now("cv")])
    ld_sh = em.dma("dsh", "sync", lambda s: s.dma_start(
        out=sht[:, :], in_=sh_src[i * 128:(i + 1) * 128, :]),
        waits=[em.now("cv")])
    ld_d = em.dma("ddst", "sync", lambda s: s.dma_start(
        out=destf[:, :], in_=dest_src[i * 128:(i + 1) * 128, :]),
        waits=[em.now("cv")])
    em.v(lambda v: v.tensor_scalar(destf[:, :], destf[:, :], float(NSLOT),
                                   None, ALU.min), waits=[ld_d])
    cast = em.v(lambda v: v.tensor_copy(desti[:, :], destf[:, :]),
                waits=W("d_free"))
    for k in range(TOPK):
        moe_done = em.dma("dmoe", "gpsimd", lambda g, k=k: g.indirect_dma_start(
            out=moe8[:, k * D:(k + 1) * D], out_offset=None,
            in_=yeR[:, :],
            in_offset=IndirectOffsetOnAxis(desti[:, k:k + 1], 0),
            bounds_check=regs["bc"], oob_is_err=False),
            waits=[cast] + W("moe_free"))
    marks["d_free"] = moe_done
    for half in range(2):
        def mm_sum(t, half=half):
            inst = None
            for k in range(TOPK):
                inst = t.matmul(
                    ps_a[:, :] if half == 0 else ps_b[:, :], ident[:, :],
                    moe8[:, k * D + half * 512:k * D + half * 512 + 512],
                    start=(k == 0), stop=(k == TOPK - 1))
            return inst
        mms = em.t(mm_sum, waits=[moe_done, em.now("cg"), em.now("cv")])
        pshalf = ps_a if half == 0 else ps_b
        em.v(lambda v, half=half, pshalf=pshalf: v.tensor_add(
            x1t[:, half * 512:half * 512 + 512], pshalf[:, :],
            sht[:, half * 512:half * 512 + 512]),
            waits=[mms, ld_sh] + (W("x1_free") if half == 0 else []))
        em.v(lambda v, half=half: v.scalar_tensor_tensor(
            x1t[:, half * 512:half * 512 + 512],
            x1t[:, half * 512:half * 512 + 512], coef,
            xt[:, half * 512:half * 512 + 512],
            ALU.mult, ALU.add), waits=[ld_x])
    marks["moe_free"] = em.now("ct")
    x1_done = em.now("cv")
    if out_x is not None:
        st_x1 = em.dma("sx1", "sync", lambda s: s.dma_start(
            out=out_x[i * 128:(i + 1) * 128, :], in_=x1t[:, :]),
            waits=[x1_done])
        marks["x1_free"] = st_x1
    return x1_done


def build_launch_FFN(nst=TC // 512, final=False):
    nc = bass.Bass("TRN2", target_bir_lowering=False,
                   dynamic_dma_scratch_size=2048)
    TCl = nst * 512
    x_in = nc.dram_tensor("x", [TCl, D], F32, kind="ExternalInput").ap()
    w1_in = nc.dram_tensor("w1", [HF, D], F32, kind="ExternalInput").ap()
    w1T_in = nc.dram_tensor("w1T", [D, HF], F32, kind="ExternalInput").ap()
    w2_in = nc.dram_tensor("w2", [D, HF], F32, kind="ExternalInput").ap()
    w2T_in = nc.dram_tensor("w2T", [HF, D], F32, kind="ExternalInput").ap()
    b1c_in = nc.dram_tensor("b1c", [128, HF // 128], F32,
                            kind="ExternalInput").ap()
    b2c_in = nc.dram_tensor("b2c", [128, 8], F32, kind="ExternalInput").ap()
    o_x = nc.dram_tensor("xo", [TCl, D], F32, kind="ExternalOutput").ap()

    es = ExitStack()
    sb = lambda name, shape, dt: es.enter_context(nc.sbuf_tensor(name, shape, dt))
    psum = lambda name, shape, dt: es.enter_context(nc.psum_tensor(name, shape, dt))

    NH = HF // 128
    ident = sb("ident_n", [128, 128], BF16)
    s1Tb = sb("s1Tb", [128, 8 * HF], BF16)
    s2Tb = sb("s2Tb", [128, NH * D], BF16)
    al1 = sb("al1", [128, NH], F32)
    al2 = sb("al2", [128, 8], F32)
    b1c = sb("b1c_sb", [128, NH], F32)
    b2c = sb("b2c_sb", [128, 8], F32)
    xA = sb("xA", [128, 4 * D], F32)
    wscr = xA
    stats = sb("stats_n", [128, 12], F32)
    mv = sb("mv_n", [128, 2], F32)
    rr = sb("rr_n", [128, 1], F32)
    col = sb("col_n", [128, 1], F32)
    xlnt = sb("xlnt_n", [128, D], BF16)
    xlnT = sb("xlnT_n", [128, 4096], BF16)
    hfm = sb("hfm", [128, NH * 512], BF16)
    outfm = sb("outfm", [128, 8 * 512], BF16)
    x3ff = sb("x3ff", [128, D], BF16)
    xot = sb("xot", [128, D], F32)
    psT = psum("psT_n", [128, D], BF16)
    ps_h = psum("ps_h_n", [128, 512], F32)
    ps_h2 = psum("ps_h2_n", [128, 512], F32)

    with nc.Block() as block:
        em = Em(nc, es)
        em.dma("dpre", "sync", lambda s: s.dma_start(out=b1c[:, :],
                                                     in_=b1c_in[:, :]))
        em.dma("dpre", "sync", lambda s: s.dma_start(out=b2c[:, :],
                                                     in_=b2c_in[:, :]))
        emit_identity(em, ident)
        g_all = em.now("cg")

        for ti in range(NH):
            ldw = em.dma("dwn%d" % (ti % 2), "sync", lambda s, ti=ti: s.dma_start(
                out=wscr[:, (ti % 2) * D:(ti % 2 + 1) * D],
                in_=w1_in[ti * 128:(ti + 1) * 128, :]), waits=[em.now("cs")])
            em.s(lambda s, ti=ti: s.activation(
                wscr[:, (ti % 2) * D:(ti % 2 + 1) * D],
                wscr[:, (ti % 2) * D:(ti % 2 + 1) * D],
                AF.Abs, accum_out=al1[:, ti:ti + 1]), waits=[ldw])
        em.v(lambda v: v.tensor_scalar(al1[:, :], al1[:, :], 1.0 / D, None,
                                       ALU.mult), waits=[em.now("cs")])
        for c in range(8):
            ldt = em.dma("dwt", "sync", lambda s, c=c: s.dma_start(
                out=wscr[:, :], in_=w1T_in[c * 128:(c + 1) * 128, :]),
                waits=[em.now("cs")])
            em.s(lambda s, c=c: s.activation(
                s1Tb[:, c * HF:(c + 1) * HF], wscr[:, :], AF.Sign),
                waits=[ldt])
        for ti in range(8):
            ldw = em.dma("dwt", "sync", lambda s, ti=ti: s.dma_start(
                out=wscr[:, :], in_=w2_in[ti * 128:(ti + 1) * 128, :]),
                waits=[em.now("cs")])
            em.s(lambda s, ti=ti: s.activation(
                wscr[:, :], wscr[:, :],
                AF.Abs, accum_out=al2[:, ti:ti + 1]), waits=[ldw])
        em.v(lambda v: v.tensor_scalar(al2[:, :], al2[:, :], 1.0 / HF, None,
                                       ALU.mult), waits=[em.now("cs")])
        for c in range(NH):
            ldt = em.dma("dwn%d" % (c % 2), "sync", lambda s, c=c: s.dma_start(
                out=wscr[:, (c % 2) * D:(c % 2 + 1) * D],
                in_=w2T_in[c * 128:(c + 1) * 128, :]), waits=[em.now("cs")])
            em.s(lambda s, c=c: s.activation(
                s2Tb[:, c * D:(c + 1) * D],
                wscr[:, (c % 2) * D:(c % 2 + 1) * D], AF.Sign), waits=[ldt])
        prep_done = em.now("cs")
        al_done = em.now("cv")

        marks = {"xlnT_free": None, "xln_free": None, "h_free": None,
                 "out_free": None, "xo_free": None, "xA_free": None}

        def W(key):
            return [w for w in [marks[key]] if w]

        for st in range(nst):
            for q in range(4):
                i = st * 4 + q
                ld_x = em.dma("dxx", "sync", lambda s, i=i, q=q: s.dma_start(
                    out=xA[:, q * D:(q + 1) * D],
                    in_=x_in[i * 128:(i + 1) * 128, :]),
                    waits=[em.now("cv"), em.now("ct")] + W("xA_free"))
                ln_done = emit_ln(em, xA[:, q * D:(q + 1) * D], xlnt[:, :],
                                  stats, mv, col, rr,
                                  [ld_x, prep_done] + W("xln_free"))
                tr = emit_transpose8(em, psT, xlnt[:, :], ident,
                                     [ln_done, g_all] + W("xlnT_free"))
                marks["xln_free"] = tr
                em.v(lambda v, q=q: v.tensor_copy(
                    bass.AP(xlnT, q * 128, [[4096, 128], [512, 8], [1, 128]]),
                    psT[:, :]), waits=[tr])
            xlnT_done = em.now("cv")

            ev_marks = [None, None]
            for hc in range(NH):
                bank = ps_h if hc % 2 == 0 else ps_h2
                def mm_h(t, hc=hc, bank=bank):
                    inst = None
                    for dc in range(8):
                        inst = t.matmul(
                            bank[:, :],
                            s1Tb[:, dc * HF + hc * 128:dc * HF + hc * 128 + 128],
                            bass.AP(xlnT, dc * 512, [[4096, 128], [1, 512]]),
                            start=(dc == 0), stop=(dc == 7))
                    return inst
                w_mm = [xlnT_done]
                if ev_marks[hc % 2]:
                    w_mm.append(ev_marks[hc % 2])
                mmh = em.t(mm_h, waits=w_mm)
                ev_marks[hc % 2] = em.s(
                    lambda s, hc=hc, bank=bank: s.activation(
                        hfm[:, hc * 512:(hc + 1) * 512], bank[:, :],
                        AF.Gelu, bias=b1c[:, hc:hc + 1],
                        scale=al1[:, hc:hc + 1]),
                    waits=[mmh, al_done] + (W("h_free") if hc == 0 else []))
            h_done = em.now("cs")
            marks["xlnT_free"] = em.now("ct")

            ev_marks = [None, None]
            for dc in range(8):
                bank = ps_h if dc % 2 == 0 else ps_h2
                def mm_o(t, dc=dc, bank=bank):
                    inst = None
                    for hc in range(NH):
                        inst = t.matmul(
                            bank[:, :],
                            s2Tb[:, hc * D + dc * 128:hc * D + dc * 128 + 128],
                            hfm[:, hc * 512:(hc + 1) * 512],
                            start=(hc == 0), stop=(hc == NH - 1))
                    return inst
                w_mm = [h_done]
                if ev_marks[dc % 2]:
                    w_mm.append(ev_marks[dc % 2])
                mmo = em.t(mm_o, waits=w_mm)
                ev_marks[dc % 2] = em.v(
                    lambda v, dc=dc, bank=bank: v.tensor_scalar(
                        outfm[:, dc * 512:(dc + 1) * 512], bank[:, :],
                        al2[:, dc:dc + 1], b2c[:, dc:dc + 1],
                        ALU.mult, ALU.add),
                    waits=[mmo] + (W("out_free") if dc == 0 else []))
            out_done = em.now("cv")
            marks["h_free"] = em.now("ct")

            for q in range(4):
                i = st * 4 + q
                def tr_o(t, q=q):
                    inst = None
                    for dc in range(8):
                        inst = t.transpose(
                            psT[:, dc * 128:(dc + 1) * 128],
                            outfm[:, dc * 512 + q * 128:dc * 512 + q * 128 + 128],
                            ident[:, :])
                    return inst
                tro = em.t(tr_o, waits=[out_done, em.now("cv")])
                em.v(lambda v: v.tensor_copy(x3ff[:, :], psT[:, :]),
                     waits=[tro])
                em.v(lambda v, q=q: v.scalar_tensor_tensor(
                    xot[:, :], x3ff[:, :], 0.5, xA[:, q * D:(q + 1) * D],
                    ALU.mult, ALU.add), waits=W("xo_free"))
                st_xo = em.dma("sxo", "sync", lambda s, i=i: s.dma_start(
                    out=o_x[i * 128:(i + 1) * 128, :], in_=xot[:, :]),
                    waits=[em.now("cv")])
                marks["xo_free"] = st_xo
            marks["out_free"] = em.now("ct")
            marks["xA_free"] = em.now("cv")

        em.barrier_all()
        em.build(block)
    es.close()
    return nc


def build_launch_WATT(nst=TC // 512):
    nc = bass.Bass("TRN2", target_bir_lowering=False)
    TCl = nst * 512
    x_in = nc.dram_tensor("x", [TCl, D], F32, kind="ExternalInput").ap()
    wqkv_in = nc.dram_tensor("w_qkv", [D, 3 * D], F32, kind="ExternalInput").ap()
    wout_in = nc.dram_tensor("w_out", [D, D], F32, kind="ExternalInput").ap()
    bout_in = nc.dram_tensor("b_out", [1, D], F32, kind="ExternalInput").ap()
    wm_in = nc.dram_tensor("wmask", [128, 256], F32, kind="ExternalInput").ap()
    o_x4 = nc.dram_tensor("x4p", [TCl, D], F32, kind="ExternalOutput").ap()
    o_q0 = nc.dram_tensor("q0fm", [128, 8 * 128], BF16, kind="ExternalOutput").ap()
    o_k0 = nc.dram_tensor("k0fm", [128, 8 * 128], BF16, kind="ExternalOutput").ap()
    o_v0 = nc.dram_tensor("v0tm", [128, D], BF16, kind="ExternalOutput").ap()
    o_kL = nc.dram_tensor("kLfm", [128, 8 * 128], BF16, kind="ExternalOutput").ap()
    o_vL = nc.dram_tensor("vLtm", [128, D], BF16, kind="ExternalOutput").ap()

    es = ExitStack()
    sb = lambda name, shape, dt: es.enter_context(nc.sbuf_tensor(name, shape, dt))
    psum = lambda name, shape, dt: es.enter_context(nc.psum_tensor(name, shape, dt))

    ident = sb("ident_w", [128, 128], BF16)
    ones1_b = sb("ones1_w", [1, 128], BF16)
    wqkvb = sb("wqkvb", [128, 8 * 3 * D], BF16)
    woutb = sb("woutb", [128, 8 * D], BF16)
    boutb = sb("boutb", [1, D], BF16)
    wmb = sb("wmb", [128, 256], F32)
    xA = sb("xA_w", [128, 4 * D], F32)
    stats = sb("stats_w", [128, 12], F32)
    mv = sb("mv_w", [128, 2], F32)
    rr = sb("rr_w", [128, 1], F32)
    col = sb("col_w", [128, 1], F32)
    negmax = sb("negmax_w", [128, 1], F32)
    rden = sb("rden_w", [128, 1], F32)
    xlnt = sb("xlnt_w", [128, D], BF16)
    xlnT = sb("xlnT_w", [128, 4096], BF16)
    qfm = sb("qfm_w", [128, 8 * 512], BF16)
    kfm = sb("kfm_w", [128, 8 * 512], BF16)
    kring = sb("kring", [128, 2 * 8 * 128], BF16)
    vring = sb("vring", [128, 2 * D], BF16)
    ssb = sb("ssb_w", [128, 256], F32)
    eS = sb("eS_w", [128, 256], BF16)
    eST = sb("eST_w", [128, 256], BF16)
    attn = sb("attn_w", [128, D], BF16)
    attnT = sb("attnT_w", [128, D], BF16)
    ebuf = sb("ebuf_w", [128, D], BF16)
    x4t = sb("x4t_w", [128, D], F32)
    psT = psum("psT_w", [128, D], BF16)
    ps_s = psum("ps_s_w", [128, 512], F32)
    ps_a = psum("ps_a_w", [128, 512], F32)
    ps_b = psum("ps_b_w", [128, 512], F32)

    with nc.Block() as block:
        em = Em(nc, es)
        chunked_load(em, wqkvb, wqkv_in, 8, 3 * D)
        chunked_load(em, woutb, wout_in, 8, D)
        em.dma("dpre_g", "gpsimd", lambda g: g.dma_start(out=boutb[:, :],
                                                         in_=bout_in[:, :]))
        em.dma("dpre", "sync", lambda s: s.dma_start(out=wmb[:, :],
                                                     in_=wm_in[:, :]))
        em.g(lambda g: g.memset(ones1_b[:, :], 1.0))
        emit_identity(em, ident)
        ld_all = em.now("dpre")
        gp_all = em.now("dpre_g")
        g_all = em.now("cg")

        marks = {"xlnT_free": None, "xln_free": None, "q_free": None,
                 "k_free": None, "ring_free": None, "x4_free": None,
                 "xA_free": None, "eS_free": None, "attn_free": None}

        def W(key):
            return [w for w in [marks[key]] if w]

        for st in range(nst):
            for q in range(4):
                i = st * 4 + q
                ld_x = em.dma("dxx", "sync", lambda s, i=i, q=q: s.dma_start(
                    out=xA[:, q * D:(q + 1) * D],
                    in_=x_in[i * 128:(i + 1) * 128, :]),
                    waits=[em.now("cv")] + W("xA_free"))
                ln_done = emit_ln(em, xA[:, q * D:(q + 1) * D], xlnt[:, :],
                                  stats, mv, col, rr,
                                  [ld_x] + W("xln_free"))
                tr = emit_transpose8(em, psT, xlnt[:, :], ident,
                                     [ln_done, g_all] + W("xlnT_free"))
                marks["xln_free"] = tr
                em.v(lambda v, q=q: v.tensor_copy(
                    bass.AP(xlnT, q * 128, [[4096, 128], [512, 8], [1, 128]]),
                    psT[:, :]), waits=[tr])
            xlnT_done = em.now("cv")

            for which, off, dst in (("q", 0, qfm), ("k", D, kfm)):
                for c in range(8):
                    def mm_qk(t, off=off, c=c):
                        inst = None
                        for dc in range(8):
                            inst = t.matmul(
                                ps_s[:, :],
                                wqkvb[:, dc * 3 * D + off + c * 128:
                                      dc * 3 * D + off + c * 128 + 128],
                                bass.AP(xlnT, dc * 512, [[4096, 128], [1, 512]]),
                                start=(dc == 0), stop=(dc == 7))
                        return inst
                    guard = W("q_free") if which == "q" else W("k_free")
                    mm1 = em.t(mm_qk, waits=[xlnT_done, gp_all,
                                             em.now("cv")])
                    em.v(lambda v, dst=dst, c=c: v.tensor_copy(
                        dst[:, c * 512:(c + 1) * 512], ps_s[:, :]),
                        waits=[mm1] + (guard if c == 0 else []))
            qk_done = em.now("cv")

            for q in range(4):
                i = st * 4 + q
                rp = i % 2
                for half in range(2):
                    def mm_v(t, q=q, half=half):
                        inst = None
                        for dc in range(8):
                            inst = t.matmul(
                                ps_b[:, :],
                                bass.AP(xlnT, dc * 512 + q * 128,
                                        [[4096, 128], [1, 128]]),
                                wqkvb[:, dc * 3 * D + D + half * 512:
                                      dc * 3 * D + D + half * 512 + 512],
                                start=(dc == 0), stop=(dc == 7))
                        return inst
                    mmv = em.t(mm_v, waits=[xlnT_done, gp_all, em.now("cv")])
                    em.v(lambda v, rp=rp, half=half: v.tensor_copy(
                        vring[:, rp * D + half * 512:rp * D + half * 512 + 512],
                        ps_b[:, :]),
                        waits=[mmv] + (W("ring_free") if half == 0 else []))
                em.v(lambda v, rp=rp, q=q: v.tensor_copy(
                    bass.AP(kring, rp * 1024, [[2 * 1024, 128], [128, 8], [1, 128]]),
                    bass.AP(kfm, q * 128, [[8 * 512, 128], [512, 8], [1, 128]])),
                    waits=[qk_done])
                ring_done = em.now("cv")

                first = (i == 0)
                if not first:
                    def mm_s_fix(t, q=q, rp=rp):
                        inst = None
                        for dc in range(8):
                            t.matmul(ps_s[:, 0:128],
                                     qfm[:, dc * 512 + q * 128:
                                         dc * 512 + q * 128 + 128],
                                     kring[:, (1 - rp) * 1024 + dc * 128:
                                           (1 - rp) * 1024 + dc * 128 + 128],
                                     start=(dc == 0), stop=(dc == 7))
                        for dc in range(8):
                            inst = t.matmul(ps_s[:, 128:256],
                                            qfm[:, dc * 512 + q * 128:
                                                dc * 512 + q * 128 + 128],
                                            kring[:, rp * 1024 + dc * 128:
                                                  rp * 1024 + dc * 128 + 128],
                                            start=(dc == 0),
                                            stop=(dc == 7))
                        return inst
                    mms = em.t(mm_s_fix, waits=[qk_done, ring_done,
                                                em.now("cv"), em.now("cs")])
                    em.v(lambda v: v.scalar_tensor_tensor(
                        ssb[:, :], ps_s[:, 0:256], float(D) ** -0.5, wmb[:, :],
                        ALU.mult, ALU.add), waits=[mms, ld_all])
                    em.v(lambda v: v.reduce_max(col[:, :], ssb[:, :], axis=AX.X))
                    em.v(lambda v: v.tensor_scalar(negmax[:, :], col[:, :],
                                                   -1.0, None, ALU.mult))
                    se = em.s(lambda s: s.activation(eS[:, :], ssb[:, :],
                                                     AF.Exp,
                                                     bias=negmax[:, 0:1]),
                              waits=[em.now("cv")] + W("eS_free"))
                    em.v(lambda v: v.reduce_sum(col[:, :], eS[:, :], axis=AX.X),
                         waits=[se])
                    em.v(lambda v: v.reciprocal(rden[:, :], col[:, :]))
                    sm_done = em.now("cv")
                    def tr_e(t):
                        t.transpose(psT[:, 0:128], eS[:, 0:128], ident[:, :])
                        return t.transpose(psT[:, 128:256], eS[:, 128:256],
                                           ident[:, :])
                    tre = em.t(tr_e, waits=[sm_done, em.now("cv")])
                    marks["eS_free"] = em.now("ct")
                    em.v(lambda v: v.tensor_copy(eST[:, :], psT[:, 0:256]),
                         waits=[tre])
                    est_done = em.now("cv")
                    for half in range(2):
                        def mm_a(t, half=half, rp=rp):
                            t.matmul(ps_a[:, :], eST[:, 0:128],
                                     vring[:, (1 - rp) * D + half * 512:
                                           (1 - rp) * D + half * 512 + 512],
                                     start=True, stop=False)
                            return t.matmul(ps_a[:, :], eST[:, 128:256],
                                            vring[:, rp * D + half * 512:
                                                  rp * D + half * 512 + 512],
                                            start=False, stop=True)
                        mma = em.t(mm_a, waits=[est_done, em.now("cv"),
                                                em.now("cs")])
                        em.v(lambda v, half=half: v.tensor_scalar(
                            attn[:, half * 512:half * 512 + 512], ps_a[:, :],
                            rden[:, 0:1], None, ALU.mult),
                            waits=[mma] + (W("attn_free") if half == 0 else []))
                    at_done = em.now("cv")
                else:
                    at_done = em.v(lambda v: v.memset(attn[:, :], 0.0),
                                   waits=W("attn_free"))
                tra = emit_transpose8(em, psT, attn[:, :], ident,
                                      [at_done, em.now("cv")])
                marks["attn_free"] = tra
                em.v(lambda v: v.tensor_copy(attnT[:, :], psT[:, :]),
                     waits=[tra])
                aT_done = em.now("cv")
                for half in range(2):
                    def mm_p(t, half=half):
                        for dc in range(8):
                            t.matmul(ps_b[:, :],
                                     attnT[:, dc * 128:(dc + 1) * 128],
                                     woutb[:, dc * D + half * 512:
                                           dc * D + half * 512 + 512],
                                     start=(dc == 0), stop=False)
                        return t.matmul(ps_b[:, :], ones1_b[:, :],
                                        boutb[:, half * 512:half * 512 + 512],
                                        start=False, stop=True)
                    mmp = em.t(mm_p, waits=[aT_done, gp_all, g_all,
                                            em.now("cv")])
                    em.v(lambda v, half=half, q=q: v.tensor_add(
                        x4t[:, half * 512:half * 512 + 512], ps_b[:, :],
                        xA[:, q * D + half * 512:q * D + half * 512 + 512]),
                        waits=[mmp] + (W("x4_free") if half == 0 else []))
                st_x4 = em.dma("sx4", "sync", lambda s, i=i: s.dma_start(
                    out=o_x4[i * 128:(i + 1) * 128, :], in_=x4t[:, :]),
                    waits=[em.now("cv")])
                marks["x4_free"] = st_x4
                marks["ring_free"] = em.now("ct")

                if i == 0:
                    em.v(lambda v: v.tensor_copy(
                        ebuf[:, :],
                        bass.AP(qfm, 0, [[8 * 512, 128], [512, 8], [1, 128]])),
                        waits=[qk_done])
                    em.dma("sq0", "sync", lambda s: s.dma_start(
                        out=o_q0[:, :], in_=ebuf[:, 0:1024]),
                        waits=[em.now("cv")])
                    em.dma("sk0", "sync", lambda s, rp=rp: s.dma_start(
                        out=o_k0[:, :], in_=kring[:, rp * 1024:(rp + 1) * 1024]),
                        waits=[ring_done])
                    em.dma("sv0", "sync", lambda s, rp=rp: s.dma_start(
                        out=o_v0[:, :], in_=vring[:, rp * D:(rp + 1) * D]),
                        waits=[ring_done])
                if i == nst * 4 - 1:
                    em.dma("skL", "sync", lambda s, rp=rp: s.dma_start(
                        out=o_kL[:, :], in_=kring[:, rp * 1024:(rp + 1) * 1024]),
                        waits=[ring_done])
                    em.dma("svL", "sync", lambda s, rp=rp: s.dma_start(
                        out=o_vL[:, :], in_=vring[:, rp * D:(rp + 1) * D]),
                        waits=[ring_done])
            marks["q_free"] = em.now("ct")
            marks["k_free"] = em.now("ct")
            marks["xlnT_free"] = em.now("ct")
            marks["xA_free"] = em.now("cv")

        em.barrier_all()
        em.build(block)
    es.close()
    return nc


def build_launch_W0():
    nc = bass.Bass("TRN2", target_bir_lowering=False)
    x4_in = nc.dram_tensor("x4p0", [128, D], F32, kind="ExternalInput").ap()
    q0_in = nc.dram_tensor("q0fm", [128, 8 * 128], BF16, kind="ExternalInput").ap()
    k0_in = nc.dram_tensor("k0fm", [128, 8 * 128], BF16, kind="ExternalInput").ap()
    v0_in = nc.dram_tensor("v0tm", [128, D], BF16, kind="ExternalInput").ap()
    kP_in = nc.dram_tensor("kPfm", [128, 8 * 128], BF16, kind="ExternalInput").ap()
    vP_in = nc.dram_tensor("vPtm", [128, D], BF16, kind="ExternalInput").ap()
    wout_in = nc.dram_tensor("w_out", [D, D], F32, kind="ExternalInput").ap()
    wm_in = nc.dram_tensor("wmask", [128, 256], F32, kind="ExternalInput").ap()
    o_x4 = nc.dram_tensor("x4t0", [128, D], F32, kind="ExternalOutput").ap()

    es = ExitStack()
    sb = lambda name, shape, dt: es.enter_context(nc.sbuf_tensor(name, shape, dt))
    psum = lambda name, shape, dt: es.enter_context(nc.psum_tensor(name, shape, dt))
    ident = sb("ident_0", [128, 128], BF16)
    woutb = sb("woutb_0", [128, 8 * D], BF16)
    wmb = sb("wmb_0", [128, 256], F32)
    q0 = sb("q0_0", [128, 1024], BF16)
    k0 = sb("k0_0", [128, 1024], BF16)
    v0 = sb("v0_0", [128, D], BF16)
    kP = sb("kP_0", [128, 1024], BF16)
    vP = sb("vP_0", [128, D], BF16)
    x4t = sb("x4t_0", [128, D], F32)
    ssb = sb("ssb_0", [128, 256], F32)
    eS = sb("eS_0", [128, 256], BF16)
    eST = sb("eST_0", [128, 256], BF16)
    attn = sb("attn_0", [128, D], BF16)
    attnT = sb("attnT_0", [128, D], BF16)
    col = sb("col_0", [128, 1], F32)
    negmax = sb("negmax_0", [128, 1], F32)
    rden = sb("rden_0", [128, 1], F32)
    psT = psum("psT_0", [128, D], BF16)
    ps_s = psum("ps_s_0", [128, 512], F32)
    ps_b = psum("ps_b_0", [128, 512], F32)

    with nc.Block() as block:
        em = Em(nc, es)
        chunked_load(em, woutb, wout_in, 8, D)
        em.dma("dpre", "sync", lambda s: s.dma_start(out=wmb[:, :], in_=wm_in[:, :]))
        em.dma("dpre", "sync", lambda s: s.dma_start(out=q0[:, :], in_=q0_in[:, :]))
        em.dma("dpre", "sync", lambda s: s.dma_start(out=k0[:, :], in_=k0_in[:, :]))
        em.dma("dpre", "sync", lambda s: s.dma_start(out=v0[:, :], in_=v0_in[:, :]))
        em.dma("dpre", "sync", lambda s: s.dma_start(out=kP[:, :], in_=kP_in[:, :]))
        em.dma("dpre", "sync", lambda s: s.dma_start(out=vP[:, :], in_=vP_in[:, :]))
        em.dma("dpre", "sync", lambda s: s.dma_start(out=x4t[:, :], in_=x4_in[:, :]))
        emit_identity(em, ident)
        ld_all = em.now("dpre")
        gp_all = em.now("dpre_g")
        g_all = em.now("cg")

        def mm_s(t):
            inst = None
            for dc in range(8):
                t.matmul(ps_s[:, 0:128], q0[:, dc * 128:(dc + 1) * 128],
                         kP[:, dc * 128:(dc + 1) * 128],
                         start=(dc == 0), stop=(dc == 7))
            for dc in range(8):
                inst = t.matmul(ps_s[:, 128:256], q0[:, dc * 128:(dc + 1) * 128],
                                k0[:, dc * 128:(dc + 1) * 128],
                                start=(dc == 0), stop=(dc == 7))
            return inst
        mms = em.t(mm_s, waits=[ld_all, gp_all, g_all])
        em.v(lambda v: v.scalar_tensor_tensor(
            ssb[:, :], ps_s[:, 0:256], float(D) ** -0.5, wmb[:, :],
            ALU.mult, ALU.add), waits=[mms])
        em.v(lambda v: v.reduce_max(col[:, :], ssb[:, :], axis=AX.X))
        em.v(lambda v: v.tensor_scalar(negmax[:, :], col[:, :], -1.0, None,
                                       ALU.mult))
        se = em.s(lambda s: s.activation(eS[:, :], ssb[:, :], AF.Exp,
                                         bias=negmax[:, 0:1]),
                  waits=[em.now("cv")])
        em.v(lambda v: v.reduce_sum(col[:, :], eS[:, :], axis=AX.X), waits=[se])
        em.v(lambda v: v.reciprocal(rden[:, :], col[:, :]))
        sm_done = em.now("cv")

        def tr_e(t):
            t.transpose(psT[:, 0:128], eS[:, 0:128], ident[:, :])
            return t.transpose(psT[:, 128:256], eS[:, 128:256], ident[:, :])
        tre = em.t(tr_e, waits=[sm_done])
        em.v(lambda v: v.tensor_copy(eST[:, :], psT[:, 0:256]), waits=[tre])
        est_done = em.now("cv")
        for half in range(2):
            def mm_a(t, half=half):
                t.matmul(ps_b[:, :], eST[:, 0:128],
                         vP[:, half * 512:half * 512 + 512],
                         start=True, stop=False)
                return t.matmul(ps_b[:, :], eST[:, 128:256],
                                v0[:, half * 512:half * 512 + 512],
                                start=False, stop=True)
            mma = em.t(mm_a, waits=[est_done, em.now("cv")])
            em.v(lambda v, half=half: v.tensor_scalar(
                attn[:, half * 512:half * 512 + 512], ps_b[:, :],
                rden[:, 0:1], None, ALU.mult), waits=[mma])
        at_done = em.now("cv")
        tra = emit_transpose8(em, psT, attn[:, :], ident,
                              [at_done, em.now("ct")])
        em.v(lambda v: v.tensor_copy(attnT[:, :], psT[:, :]), waits=[tra])
        aT_done = em.now("cv")
        for half in range(2):
            def mm_p(t, half=half):
                inst = None
                for dc in range(8):
                    inst = t.matmul(ps_b[:, :],
                                    attnT[:, dc * 128:(dc + 1) * 128],
                                    woutb[:, dc * D + half * 512:
                                          dc * D + half * 512 + 512],
                                    start=(dc == 0), stop=(dc == 7))
                return inst
            mmp = em.t(mm_p, waits=[aT_done, em.now("cv")])
            em.v(lambda v, half=half: v.tensor_add(
                x4t[:, half * 512:half * 512 + 512], ps_b[:, :],
                x4t[:, half * 512:half * 512 + 512]), waits=[mmp])
        em.dma("sx40", "sync", lambda s: s.dma_start(
            out=o_x4[:, :], in_=x4t[:, :]), waits=[em.now("cv")])
        em.barrier_all()
        em.build(block)
    es.close()
    return nc


_BUILD_CACHE = {}


def _prog(name, fn, **kw):
    key = (name, tuple(sorted(kw.items())))
    if key not in _BUILD_CACHE:
        _BUILD_CACHE[key] = fn(**kw)
    return _BUILD_CACHE[key]


def _run(nc, maps):
    return run_bass_kernel_spmd(nc, maps, core_ids=list(range(NCORES)))


def _moe_round(x_flat, gate_weight, e_bias, shared_gate, shared_up, shared_down,
               ffn_w1, ffn_w2, ffn_w3):
    import ml_dtypes
    ncA = _prog("A", build_launch_A)
    mapsA = launch_A_inputs(x_flat, gate_weight, e_bias, shared_gate,
                            shared_up, shared_down)
    resA = _run(ncA, mapsA).results
    xln_list = [r["xln"] for r in resA]
    counts_list = [r["counts"] for r in resA]
    contrib_list = [r["contrib"] for r in resA]
    dest = [r["dest"] for r in resA]
    shared = [r["shared"] for r in resA]

    ncB = _prog("B", build_launch_B)
    mapsB = [launch_B_inputs(c, xln_list, counts_list, contrib_list,
                             ffn_w1, ffn_w2, ffn_w3) for c in range(NCORES)]
    resB = _run(ncB, mapsB).results
    ye = [r["ye"] for r in resB]
    zero_row = np.zeros((1, D), ml_dtypes.bfloat16)
    yeR = []
    for j in range(NCORES):
        parts = [ye[i][j * EPC * CAP:(j + 1) * EPC * CAP] for i in range(NCORES)]
        yeR.append(np.concatenate(parts + [zero_row], 0))
    return dest, shared, yeR


def _c_consts():
    return {
        "tri": np.triu(np.ones((128, 128), np.float32)),
        "bdm": np.kron(np.eye(2, dtype=np.float32),
                       np.ones((64, 64), np.float32)),
        "hmask": np.concatenate(
            [np.repeat([[1.0], [0.0]], 64, 0),
             np.repeat([[0.0], [1.0]], 64, 0)], 1).astype(np.float32),
    }


def kernel(**inputs):
    import ml_dtypes
    g = lambda k: np.asarray(inputs[k], dtype=np.float32)
    x0 = g("L0_attn_query").reshape(NCORES * TC, D)

    dest1, shared1, yeR1 = _moe_round(
        x0, g("L0_ffn_gate_weight"), g("L0_ffn_e_bias"),
        g("L0_ffn_shared_gate"), g("L0_ffn_shared_up"),
        g("L0_ffn_shared_down"),
        g("L0_ffn_w1"), g("L0_ffn_w2"), g("L0_ffn_w3"))

    ncC = _prog("C", build_launch_C)
    cc = _c_consts()
    Wq, Wk, Wv = g("L0_attn_Wq"), g("L0_attn_Wk"), g("L0_attn_Wv")
    bq, bk, bv = g("L0_attn_bq"), g("L0_attn_bk"), g("L0_attn_bv")
    mapsC = []
    for c in range(NCORES):
        mapsC.append({
            "x": np.ascontiguousarray(x0[c * TC:(c + 1) * TC]),
            "shared": shared1[c], "dest": dest1[c], "yeR": yeR1[c],
            "WqT": np.ascontiguousarray(Wq.T),
            "WkT": np.ascontiguousarray(Wk.T),
            "WvT": np.ascontiguousarray(Wv.T),
            "bqc": bq.reshape(8, 128).T.copy(),
            "bkc": bk.reshape(8, 128).T.copy(),
            "bv": bv.reshape(1, D), **cc,
        })
    resC = _run(ncC, mapsC).results

    ncD = _prog("D", build_launch_D)
    zst = np.zeros((128, 8 * 128), np.float32)
    mapsD = []
    for c in range(NCORES):
        sp = resC[c - 1]["state"] if c % 2 == 1 else zst
        mapsD.append({
            "x1": resC[c]["x1"], "opart": resC[c]["opart"],
            "qfm": resC[c]["qfm"], "Spart": np.asarray(sp, np.float32),
            "ln_g": g("L0_attn_ln_g").reshape(8, 128).T.copy(),
            "ln_b": g("L0_attn_ln_b").reshape(8, 128).T.copy(),
            "WoT": np.ascontiguousarray(g("L0_attn_Wo").T),
            "bo": g("L0_attn_bo").reshape(1, D),
        })
    resD = _run(ncD, mapsD).results
    x2 = np.concatenate([r["x2"] for r in resD], 0)

    dest2, shared2, yeR2 = _moe_round(
        x2, g("L0_ffn_gate_weight"), g("L0_ffn_e_bias"),
        g("L0_ffn_shared_gate"), g("L0_ffn_shared_up"),
        g("L0_ffn_shared_down"),
        g("L0_ffn_w1"), g("L0_ffn_w2"), g("L0_ffn_w3"))

    ncC2 = _prog("Cc", build_launch_C, upto="combine")
    mapsC2 = []
    for c in range(NCORES):
        m = dict(mapsC[c])
        m["x"] = np.ascontiguousarray(x2[c * TC:(c + 1) * TC])
        m["shared"] = shared2[c]
        m["dest"] = dest2[c]
        m["yeR"] = yeR2[c]
        mapsC2.append(m)
    resC2 = _run(ncC2, mapsC2).results
    x3moe = [r["x1"] for r in resC2]

    ncF = _prog("FFN", build_launch_FFN)
    w1, w2 = g("L1_ffn_w1"), g("L1_ffn_w2")
    ffnw = {
        "w1": w1, "w1T": np.ascontiguousarray(w1.T),
        "w2": w2, "w2T": np.ascontiguousarray(w2.T),
        "b1c": g("L1_ffn_b1").reshape(HF // 128, 128).T.copy(),
        "b2c": g("L1_ffn_b2").reshape(8, 128).T.copy(),
    }
    mapsF = [{"x": np.asarray(x3moe[c], np.float32), **ffnw}
             for c in range(NCORES)]
    resF = _run(ncF, mapsF).results
    x3 = [r["xo"] for r in resF]

    ncW = _prog("WATT", build_launch_WATT)
    wmask = np.where(np.triu(np.ones((128, 256), bool), 129), -3.4e38,
                     0).astype(np.float32)
    wq_nat = g("L1_attn_w_qkv")
    wo_nat = g("L1_attn_w_out")
    mapsW = [{"x": np.asarray(x3[c], np.float32), "w_qkv": wq_nat,
              "w_out": wo_nat, "b_out": g("L1_attn_b_out").reshape(1, D),
              "wmask": wmask} for c in range(NCORES)]
    resW = _run(ncW, mapsW).results

    ncW0 = _prog("W0", build_launch_W0)
    zbf = np.zeros((128, 1024), ml_dtypes.bfloat16)
    maps0 = []
    for c in range(NCORES):
        kP = resW[c - 1]["kLfm"] if c % 2 == 1 else zbf
        vP = resW[c - 1]["vLtm"] if c % 2 == 1 else zbf
        maps0.append({
            "x4p0": np.ascontiguousarray(resW[c]["x4p"][0:128]),
            "q0fm": resW[c]["q0fm"], "k0fm": resW[c]["k0fm"],
            "v0tm": resW[c]["v0tm"], "kPfm": np.asarray(kP),
            "vPtm": np.asarray(vP), "w_out": wo_nat, "wmask": wmask,
        })
    res0 = _run(ncW0, maps0).results

    mapsF2 = []
    for c in range(NCORES):
        x4 = np.array(resW[c]["x4p"], np.float32)
        x4[0:128] = res0[c]["x4t0"]
        mapsF2.append({"x": x4, **ffnw})
    resF2 = _run(ncF, mapsF2).results

    out = np.concatenate([r["xo"] for r in resF2], 0)
    return out.reshape(4, 8192, D).astype(np.float32)


# version 05 (final)
# speedup vs baseline: 1.1702x; 1.0025x over previous
import numpy as np
from contextlib import ExitStack

import concourse.bass as bass
import concourse.mybir as mybir
from concourse.bass_utils import run_bass_kernel_spmd
from concourse.bass import IndirectOffsetOnAxis

F32 = mybir.dt.float32
BF16 = mybir.dt.bfloat16
I32 = mybir.dt.int32
AF = mybir.ActivationFunctionType
ALU = mybir.AluOpType
AX = mybir.AxisListType

NCORES = 8
TC = 4096
D = 1024
E = 64
CAP = 1024
TOPK = 8
NG = 8
EPC = E // NCORES
FEXP = 512
FSH = 1024
HF = 4096
LN_EPS = 1e-5
SCALE = 2.5
NSLOT = E * CAP
ZROW = NSLOT


class Em:

    def __init__(self, nc, es):
        self.nc = nc
        self.es = es
        self.ops = {e: [] for e in ("sync", "scalar", "tensor", "vector", "gpsimd")}
        self.sems = {}
        self.counts = {}
        self.chan("cv")
        self.chan("cs")
        self.chan("ct")
        self.chan("cg")
        self.dma_chans = []

    def chan(self, name):
        self.sems[name] = self.es.enter_context(self.nc.semaphore(name))
        self.counts[name] = 0
        return name

    def _emit(self, eng, fn, waits, inc, n, serial):
        wl = []
        for w in waits:
            if isinstance(w, str):
                wl.append((w, self.counts[w]))
            else:
                if w is not None and w[1] > 0:
                    wl.append(w)
        if serial and self.counts[inc] > 0:
            wl.append((inc, self.counts[inc]))
        self.counts[inc] += n
        self.ops[eng].append((fn, wl, (inc, n)))
        return (inc, self.counts[inc])

    def v(self, fn, waits=()):
        return self._emit("vector", fn, waits, "cv", 1, True)

    def s(self, fn, waits=()):
        return self._emit("scalar", fn, waits, "cs", 1, True)

    def t(self, fn, waits=()):
        return self._emit("tensor", fn, waits, "ct", 1, False)

    def g(self, fn, waits=()):
        return self._emit("gpsimd", fn, waits, "cg", 1, True)

    def dma(self, chan, eng, fn, waits=()):
        if chan not in self.sems:
            self.chan(chan)
            self.dma_chans.append(chan)
        return self._emit(eng, fn, waits, chan, 16, False)

    def now(self, name):
        return (name, self.counts[name])

    def barrier_all(self):
        for ch in self.dma_chans:
            c = self.counts[ch]
            if c > 0:
                self.ops["gpsimd"].append(
                    (lambda g: None, [(ch, c)], ("cg", 0)))

    def build(self, block):
        for eng in ("sync", "scalar", "tensor", "vector", "gpsimd"):
            lst = self.ops[eng]
            if not lst:
                continue

            def mk(lst):
                def body(e):
                    for fn, wl, tgt in lst:
                        for s, c in wl:
                            if c > 0:
                                e.wait_ge(self.sems[s], c)
                        inst = fn(e)
                        if tgt is not None and tgt[1] > 0:
                            if inst is None:
                                raise RuntimeError("inc on None instruction")
                            inst.then_inc(self.sems[tgt[0]], tgt[1])
                return body

            getattr(block, eng)(mk(lst))


def chunked_load(em, dst_sb, src_ap, nchunk, inner, chan="dpre"):
    fn = lambda e: e.dma_start(
        out=dst_sb[:, :].rearrange("p (c f) -> p c f", c=nchunk),
        in_=src_ap.rearrange("(c p) f -> p c f", p=128))
    eng = "gpsimd" if src_ap.dtype != dst_sb.dtype else "sync"
    return em.dma(chan if eng == "sync" else chan + "_g", eng, fn)


def emit_identity(em, ident):
    em.g(lambda g: g.memset(ident[:, :], 0.0))
    return em.g(lambda g: g.affine_select(
        out=ident[:, :], in_=ident[:, :], compare_op=ALU.not_equal,
        fill=1.0, base=0, pattern=[[-1, ident.shape[0]]], channel_multiplier=1))


def emit_transpose8(em, psT, src, ident, waits):
    n = src.shape[1] // 128

    def fn(t):
        inst = None
        for c in range(n):
            inst = t.transpose(psT[:, c * 128:(c + 1) * 128],
                               src[:, c * 128:(c + 1) * 128], ident[:, :])
        return inst
    return em.t(fn, waits=waits)


def emit_ln(em, x_tile, xln_tile, stats, mv, col, rr, waits):
    nh = x_tile.shape[1] // 512
    for h in range(nh):
        em.v(lambda v, h=h: v.bn_stats(stats[:, 6 * h:6 * h + 6],
                                       x_tile[:, 512 * h:512 * h + 512]),
             waits=(waits if h == 0 else ()))
    em.v(lambda v, nh=nh: v.bn_aggr(mv[:, :], stats[:, 0:6 * nh]),
         waits=[em.now("cs")])
    em.v(lambda v: v.tensor_scalar(col[:, :], mv[:, 1:2], float(LN_EPS), None,
                                   ALU.add))
    m = em.now("cv")
    em.s(lambda s: s.activation(col[:, :], col[:, :], AF.Sqrt), waits=[m])
    em.v(lambda v: v.reciprocal(rr[:, :], col[:, :]), waits=[em.now("cs")])
    return em.v(lambda v: v.tensor_scalar(
        xln_tile, x_tile, mv[:, 0:1], rr[:, 0:1], ALU.subtract, ALU.mult))


def build_launch_A(nt=TC // 128, with_shared=True):
    nc = bass.Bass("TRN2", target_bir_lowering=False)
    TCl = nt * 128
    x_in = nc.dram_tensor("x", [TCl, D], F32, kind="ExternalInput").ap()
    gwT = nc.dram_tensor("gwT", [D, E], F32, kind="ExternalInput").ap()
    ebias_in = nc.dram_tensor("e_bias", [1, E], F32, kind="ExternalInput").ap()
    sgT = nc.dram_tensor("sgT", [D, FSH], F32, kind="ExternalInput").ap()
    suT = nc.dram_tensor("suT", [D, FSH], F32, kind="ExternalInput").ap()
    sdT = nc.dram_tensor("sdT", [FSH, D], F32, kind="ExternalInput").ap()
    gid_in = nc.dram_tensor("gid", [128, nt], F32, kind="ExternalInput").ap()
    cst_up = nc.dram_tensor("cst_up", [128, 128], F32, kind="ExternalInput").ap()
    cst_iota = nc.dram_tensor("cst_iota", [128, E], F32, kind="ExternalInput").ap()

    o_xln = nc.dram_tensor("xln", [TCl, D], BF16, kind="ExternalOutput").ap()
    o_counts = nc.dram_tensor("counts", [1, E], F32, kind="ExternalOutput").ap()
    o_contrib = nc.dram_tensor("contrib", [NSLOT, 2], F32, kind="ExternalOutput").ap()
    o_dest = nc.dram_tensor("dest", [TCl, TOPK], F32, kind="ExternalOutput").ap()
    o_shared = nc.dram_tensor("shared", [TCl, D], F32, kind="ExternalOutput").ap()

    es = ExitStack()
    sb = lambda name, shape, dt: es.enter_context(nc.sbuf_tensor(name, shape, dt))
    psum = lambda name, shape, dt: es.enter_context(nc.psum_tensor(name, shape, dt))

    gwTb = sb("gwTb", [128, 8 * E], BF16)
    ebias_b = sb("ebias_b", [128, E], F32)
    eb_row = sb("eb_row", [1, E], F32)
    ones128_b = sb("ones128_b", [128, 1], BF16)
    ones1_f = sb("ones1_f", [1, 128], F32)
    upb = sb("upb", [128, 128], BF16)
    iota64 = sb("iota64", [128, E], F32)
    ident = sb("ident", [128, 128], BF16)
    if with_shared:
        sgTb = sb("sgTb", [128, 8 * FSH], BF16)
        suTb = sb("suTb", [128, 8 * FSH], BF16)
        sdTb = sb("sdTb", [128, 8 * D], BF16)
    gidb = sb("gidb", [128, nt], F32)

    xt = sb("xt", [128, 2 * D], F32)
    stats = sb("stats", [128, 12], F32)
    mv = sb("mv", [128, 2], F32)
    rr = sb("rr", [128, 1], F32)
    xlnt = sb("xlnt", [128, 2 * D], BF16)
    xlnT = sb("xlnT", [128, 2 * D], BF16)
    psT = psum("psT", [128, D], BF16)
    ps_log = psum("ps_log", [128, E], F32)
    ps_cnt = psum("ps_cnt", [1, E], F32)
    ps_pos = psum("ps_pos", [128, E], F32)
    scores = sb("scores", [128, E], F32)
    sfc = sb("sfc", [128, E], F32)
    gsc = sb("gsc", [128, NG], F32)
    g8 = sb("g8", [128, NG], F32)
    scr1 = sb("scr1", [128, E], F32)
    scr2 = sb("scr2", [128, E], F32)
    scr3 = sb("scr3", [128, E], F32)
    cums = sb("cums", [128, E], F32)
    cum2 = sb("cum2", [128, E], F32)
    col1 = sb("col1", [128, 1], F32)
    col2 = sb("col2", [128, 1], F32)
    col3 = sb("col3", [128, 1], F32)
    sel = sb("sel", [128, E], F32)
    selb = sb("selb", [128, E], BF16)
    wsel = sb("wsel", [128, E], F32)
    msk = sb("msk", [128, E], F32)
    posd = sb("posd", [128, E], F32)
    destfw = sb("destfw", [128, 4 * E], F32)
    wselw = sb("wselw", [128, 4 * E], F32)
    rankw = sb("rankw", [128, 4 * E], F32)
    ohm = sb("ohm", [128, 4 * E], F32)
    ohs = sb("ohs", [128, 4 * E], F32)
    desti = sb("desti", [128, 4 * TOPK], I32)
    dest4 = sb("dest4", [128, 4 * TOPK], F32)
    sval = sb("sval", [128, 8 * TOPK], F32)
    dest_out = sb("dest_out", [128, TOPK], F32)
    pct = sb("pct", [1, E], F32)
    runc = sb("runc", [1, E], F32)
    if with_shared:
        ps_sh = psum("ps_sh", [128, 512], F32)
        ps_sh2 = psum("ps_sh2", [128, 512], F32)
        gfm = sb("gfm", [128, 512], BF16)
        prod = sb("prod", [128, FSH], BF16)
        prodT = sb("prodT", [128, FSH], BF16)
        shout = sb("shout", [128, D], F32)

    with nc.Block() as block:
        em = Em(nc, es)

        chunked_load(em, gwTb, gwT, 8, E)
        if with_shared:
            chunked_load(em, sgTb, sgT, 8, FSH)
            chunked_load(em, suTb, suT, 8, FSH)
            chunked_load(em, sdTb, sdT, 8, D)
        em.dma("dpre_g", "gpsimd", lambda g: g.dma_start(out=upb[:, :], in_=cst_up[:, :]))
        em.dma("dpre", "sync", lambda s: s.dma_start(out=iota64[:, :], in_=cst_iota[:, :]))
        em.dma("dpre", "sync", lambda s: s.dma_start(out=gidb[:, :], in_=gid_in[:, :]))
        em.dma("dpre", "sync", lambda s: s.dma_start(out=eb_row[:, :], in_=ebias_in[:, :]))
        em.g(lambda g: g.memset(ones128_b[:, :], 1.0))
        em.g(lambda g: g.memset(ones1_f[:, :], 1.0))
        em.g(lambda g: g.memset(runc[:, :], 0.0))
        emit_identity(em, ident)
        ld_all = em.now("dpre")
        gp_all = em.now("dpre_g")
        g_all = em.now("cg")

        em.t(lambda t: t.matmul(ps_log[:, :], ones1_f[:, :], eb_row[:, :],
                                start=True, stop=True),
             waits=[ld_all, gp_all, g_all])
        ev_eb = em.v(lambda v: v.tensor_copy(ebias_b[:, :], ps_log[:, :]),
                     waits=[em.now("ct")])

        regs = {}

        def init_reg(g):
            regs["bc"] = g.register("bc_sc").__enter__()
            g.reg_mov(regs["bc"], NSLOT - 1)
            return None
        em.ops["gpsimd"].append((init_reg, [], ("cg", 0)))

        marks = {"xt_free": [None, None], "xln_free": [[], []],
                 "xT_free": [None, None], "xlnT_free": [None, None],
                 "sc_prev": None, "dest_free": None, "sh_free": None}

        def W(key):
            return [w for w in [marks[key]] if w]

        for i in range(nt):
            pb = i % 2
            xti = xt[:, pb * D:(pb + 1) * D]
            xlni = xlnt[:, pb * D:(pb + 1) * D]
            xlnTi = xlnT[:, pb * D:(pb + 1) * D]

            w_ld = [w for w in [marks["xt_free"][pb]] if w]
            ld_x = em.dma("dx%d" % pb, "sync", lambda s, i=i, xti=xti: s.dma_start(
                out=xti, in_=x_in[i * 128:(i + 1) * 128, :]), waits=w_ld)

            w_x = [ld_x] + list(marks["xln_free"][pb])
            ln_done = emit_ln(em, xti, xlni, stats, mv, col1, rr, w_x)
            marks["xt_free"][pb] = ln_done
            st_x = em.dma("sxln%d" % pb, "sync",
                          lambda s, i=i, xlni=xlni: s.dma_start(
                out=o_xln[i * 128:(i + 1) * 128, :], in_=xlni), waits=[ln_done])

            w_tr = [ln_done] + [w for w in [marks["xT_free"][pb]] if w]
            tr = emit_transpose8(em, psT, xlni, ident, w_tr + [g_all])
            w_evxT = [tr] + [w for w in [marks["xlnT_free"][pb]] if w]
            ev_xT = em.v(lambda v, xlnTi=xlnTi: v.tensor_copy(xlnTi, psT[:, :]),
                         waits=w_evxT)
            marks["xln_free"][pb] = [st_x, tr]

            if with_shared:
                for half in range(2):
                    def mm_g(t, h=half, xlnTi=xlnTi):
                        inst = None
                        for c in range(8):
                            inst = t.matmul(
                                ps_sh[:, :], xlnTi[:, c * 128:(c + 1) * 128],
                                sgTb[:, c * FSH + h * 512:c * FSH + h * 512 + 512],
                                start=(c == 0), stop=(c == 7))
                        return inst
                    mmg = em.t(mm_g, waits=[ev_xT, gp_all, em.now("cs")])
                    silu = em.s(lambda s: s.activation(gfm[:, :], ps_sh[:, :],
                                                       AF.Silu),
                                waits=[mmg, em.now("cv")])

                    def mm_u(t, h=half, xlnTi=xlnTi):
                        inst = None
                        for c in range(8):
                            inst = t.matmul(
                                ps_sh2[:, :], xlnTi[:, c * 128:(c + 1) * 128],
                                suTb[:, c * FSH + h * 512:c * FSH + h * 512 + 512],
                                start=(c == 0), stop=(c == 7))
                        return inst
                    mmu = em.t(mm_u, waits=[ev_xT, em.now("cv")])
                    em.v(lambda v, h=half: v.tensor_mul(
                        prod[:, h * 512:(h + 1) * 512], ps_sh2[:, :], gfm[:, :]),
                        waits=[mmu, silu])
                marks["xlnT_free"][pb] = em.now("ct")
                prod_done = em.now("cv")
                trp = emit_transpose8(em, psT, prod[:, :], ident,
                                      [prod_done, ev_xT])
                prT = em.v(lambda v: v.tensor_copy(prodT[:, :], psT[:, :]),
                           waits=[trp])
                marks["xT_free"][pb] = prT
                for half in range(2):
                    def mm_sd(t, h=half):
                        inst = None
                        for c in range(8):
                            inst = t.matmul(
                                ps_sh[:, :], prodT[:, c * 128:(c + 1) * 128],
                                sdTb[:, c * D + h * 512:c * D + h * 512 + 512],
                                start=(c == 0), stop=(c == 7))
                        return inst
                    mmsd = em.t(mm_sd, waits=[prT, em.now("cs"), em.now("cv")])
                    w_sh = [mmsd] + ([w for w in [marks["sh_free"]] if w]
                                     if half == 0 else [])
                    em.v(lambda v, h=half: v.tensor_copy(
                        shout[:, h * 512:(h + 1) * 512], ps_sh[:, :]),
                        waits=w_sh)
                st_sh = em.dma("ssh", "sync", lambda s, i=i: s.dma_start(
                    out=o_shared[i * 128:(i + 1) * 128, :], in_=shout[:, :]),
                    waits=[em.now("cv")])
                marks["sh_free"] = st_sh
            else:
                marks["xT_free"][pb] = ev_xT
                marks["xlnT_free"][pb] = em.now("ct")

            def mm_logits(t, xlnTi=xlnTi):
                inst = None
                for c in range(8):
                    inst = t.matmul(ps_log[:, :], xlnTi[:, c * 128:(c + 1) * 128],
                                    gwTb[:, c * E:(c + 1) * E],
                                    start=(c == 0), stop=(c == 7))
                return inst
            mm_log = em.t(mm_logits, waits=[ev_xT, ev_eb, gp_all, em.now("cs")])
            sig = em.s(lambda s: s.activation(scores[:, :], ps_log[:, :],
                                              AF.Sigmoid),
                       waits=[mm_log, em.now("cv")])
            em.v(lambda v: v.tensor_add(sfc[:, :], scores[:, :], ebias_b[:, :]),
                 waits=[sig])

            for gi in range(NG):
                em.v(lambda v, gi=gi: v.max(out=g8[:, :],
                                            in_=sfc[:, gi * 8:(gi + 1) * 8]))
                em.v(lambda v, gi=gi: v.tensor_add(gsc[:, gi:gi + 1],
                                                   g8[:, 0:1], g8[:, 1:2]))
            em.v(lambda v: v.max(out=g8[:, :], in_=gsc[:, :]))
            em.v(lambda v: v.tensor_scalar(scr1[:, 0:NG], gsc[:, :], g8[:, 3:4],
                                           None, ALU.is_gt))
            em.v(lambda v: v.reduce_sum(col1[:, :], scr1[:, 0:NG], axis=AX.X))
            em.v(lambda v: v.tensor_scalar(scr2[:, 0:NG], gsc[:, :], g8[:, 3:4],
                                           None, ALU.is_equal))
            em.v(lambda v: v.tensor_copy(scr3[:, 0:1], scr2[:, 0:1]))
            em.v(lambda v: v.tensor_add(scr3[:, 1:NG], scr2[:, 1:NG],
                                        scr2[:, 0:NG - 1]))
            em.v(lambda v: v.tensor_copy(cum2[:, 0:2], scr3[:, 0:2]))
            em.v(lambda v: v.tensor_add(cum2[:, 2:NG], scr3[:, 2:NG],
                                        scr3[:, 0:NG - 2]))
            em.v(lambda v: v.tensor_copy(cums[:, 0:4], cum2[:, 0:4]))
            em.v(lambda v: v.tensor_add(cums[:, 4:NG], cum2[:, 4:NG],
                                        cum2[:, 0:NG - 4]))
            em.v(lambda v: v.tensor_scalar(col2[:, :], col1[:, :], -4.0, -1.0,
                                           ALU.add, ALU.mult))
            em.v(lambda v: v.tensor_scalar(cum2[:, 0:NG], cums[:, 0:NG],
                                           col2[:, 0:1], None, ALU.is_le))
            em.v(lambda v: v.tensor_mul(cum2[:, 0:NG], cum2[:, 0:NG],
                                        scr2[:, 0:NG]))
            em.v(lambda v: v.tensor_max(g8[:, :], cum2[:, 0:NG], scr1[:, 0:NG]))
            em.v(lambda v: v.tensor_copy(
                scr1[:, :], bass.AP(g8, 0, [[NG, 128], [1, NG], [0, 8]])))
            em.v(lambda v: v.tensor_mul(scr2[:, :], sfc[:, :], scr1[:, :]))
            em.v(lambda v: v.max(out=g8[:, :], in_=scr2[:, :]))
            em.v(lambda v: v.tensor_scalar(scr1[:, :], scr2[:, :], g8[:, 7:8],
                                           None, ALU.is_gt))
            em.v(lambda v: v.reduce_sum(col1[:, :], scr1[:, :], axis=AX.X))
            em.v(lambda v: v.tensor_scalar(scr3[:, :], scr2[:, :], g8[:, 7:8],
                                           None, ALU.is_equal))
            a, b = scr3, cums
            for sh in (1, 2, 4, 8, 16, 32):
                em.v(lambda v, a=a, b=b, sh=sh: v.tensor_copy(b[:, 0:sh],
                                                              a[:, 0:sh]))
                em.v(lambda v, a=a, b=b, sh=sh: v.tensor_add(
                    b[:, sh:E], a[:, sh:E], a[:, 0:E - sh]))
                a, b = b, a
            em.v(lambda v: v.tensor_scalar(col2[:, :], col1[:, :], -8.0, -1.0,
                                           ALU.add, ALU.mult))
            em.v(lambda v: v.tensor_scalar(cum2[:, :], scr3[:, :], col2[:, 0:1],
                                           None, ALU.is_le))
            em.v(lambda v: v.tensor_scalar(msk[:, :], scr2[:, :], g8[:, 7:8],
                                           None, ALU.is_equal))
            em.v(lambda v: v.tensor_mul(cum2[:, :], cum2[:, :], msk[:, :]))
            em.v(lambda v: v.tensor_scalar(msk[:, :], scr2[:, :], g8[:, 7:8],
                                           None, ALU.is_gt))
            sel_done = em.v(lambda v: v.tensor_max(sel[:, :], cum2[:, :],
                                                   msk[:, :]))
            em.v(lambda v: v.tensor_mul(wsel[:, :], sel[:, :], scores[:, :]))
            em.v(lambda v: v.reduce_sum(col1[:, :], wsel[:, :], axis=AX.X))
            em.v(lambda v: v.tensor_scalar(col2[:, :], col1[:, :], 1e-20, None,
                                           ALU.add))
            em.v(lambda v: v.reciprocal(col3[:, :], col2[:, :]))
            em.v(lambda v: v.tensor_scalar(wsel[:, :], wsel[:, :], col3[:, 0:1],
                                           float(SCALE), ALU.mult, ALU.mult))
            selb_done = em.v(lambda v: v.tensor_copy(selb[:, :], sel[:, :]))

            mm_cnt = em.t(lambda t: t.matmul(ps_cnt[:, :], ones128_b[:, :],
                                             selb[:, :], start=True, stop=True),
                          waits=[selb_done, g_all, em.now("cv")])
            pct_ev = em.v(lambda v: v.tensor_copy(pct[:, :], ps_cnt[:, :]),
                          waits=[mm_cnt])

            def mm_pos(t):
                t.matmul(ps_pos[:, :], upb[:, :], selb[:, :],
                         start=True, stop=False)
                return t.matmul(ps_pos[:, :], ones1_f[:, :], runc[:, :],
                                start=False, stop=True)
            mm_pos_c = em.t(mm_pos, waits=[selb_done, gp_all, g_all,
                                           em.now("cv")])
            pos_ev = em.v(lambda v: v.tensor_copy(posd[:, :], ps_pos[:, :]),
                          waits=[mm_pos_c])
            em.v(lambda v: v.tensor_add(runc[:, :], runc[:, :], pct[:, :]),
                 waits=[pct_ev, mm_pos_c])

            tb = i % 4
            w_stash = [w for w in [marks["sc_prev"]] if w] if tb == 0 else []
            em.v(lambda v, tb=tb: v.scalar_tensor_tensor(
                destfw[:, tb * E:(tb + 1) * E], iota64[:, :], 1024.0,
                posd[:, :], ALU.mult, ALU.add), waits=[pos_ev] + w_stash)
            em.v(lambda v: v.tensor_scalar(scr1[:, :], posd[:, :],
                                           float(CAP) - 0.5, 1e9,
                                           ALU.is_ge, ALU.mult))
            em.v(lambda v, tb=tb: v.tensor_add(destfw[:, tb * E:(tb + 1) * E],
                                               destfw[:, tb * E:(tb + 1) * E],
                                               scr1[:, :]))
            em.v(lambda v, tb=tb: v.tensor_copy(wselw[:, tb * E:(tb + 1) * E],
                                                wsel[:, :]))
            em.v(lambda v: v.tensor_copy(scr3[:, :], sel[:, :]))
            a, b = scr3, cums
            for sh in (1, 2, 4, 8, 16, 32):
                em.v(lambda v, a=a, b=b, sh=sh: v.tensor_copy(b[:, 0:sh],
                                                              a[:, 0:sh]))
                em.v(lambda v, a=a, b=b, sh=sh: v.tensor_add(
                    b[:, sh:E], a[:, sh:E], a[:, 0:E - sh]))
                a, b = b, a
            em.v(lambda v, tb=tb: v.tensor_mul(rankw[:, tb * E:(tb + 1) * E],
                                               scr3[:, :], sel[:, :]))

            if tb == 3:
                for k in range(TOPK):
                    em.v(lambda v, k=k: v.tensor_scalar(
                        ohm[:, :], rankw[:, :], float(k + 1), None,
                        ALU.is_equal))
                    em.v(lambda v: v.tensor_mul(ohs[:, :], ohm[:, :],
                                                destfw[:, :]))
                    em.v(lambda v, k=k: v.reduce_sum(
                        bass.AP(dest4, 4 * k, [[4 * TOPK, 128], [1, 4], [1, 1]]),
                        ohs[:, :].rearrange("p (s e) -> p s e", s=4),
                        axis=AX.X))
                    em.v(lambda v: v.tensor_mul(ohs[:, :], ohm[:, :],
                                                wselw[:, :]))
                    em.v(lambda v, k=k: v.reduce_sum(
                        bass.AP(sval, 8 * k + 1, [[8 * TOPK, 128], [2, 4], [1, 1]]),
                        ohs[:, :].rearrange("p (s e) -> p s e", s=4),
                        axis=AX.X))
                for tb2 in range(4):
                    em.v(lambda v, i=i, tb2=tb2: v.tensor_copy(
                        bass.AP(sval, 2 * tb2, [[8 * TOPK, 128], [8, TOPK]]),
                        bass.AP(gidb, i - 3 + tb2, [[nt, 128], [0, TOPK]])))
                em.v(lambda v: v.tensor_copy(desti[:, :], dest4[:, :]))
                stg_done = em.now("cv")
                for k in range(TOPK):
                    for tb2 in range(4):
                        j = k * 4 + tb2
                        sc = em.dma("dsc", "gpsimd",
                                    lambda g, j=j: g.indirect_dma_start(
                            out=o_contrib[:, :],
                            out_offset=IndirectOffsetOnAxis(desti[:, j:j + 1], 0),
                            in_=sval[:, 2 * j:2 * j + 2], in_offset=None,
                            bounds_check=regs["bc"], oob_is_err=False),
                            waits=[stg_done])
                marks["sc_prev"] = sc
                for tb2 in range(4):
                    em.v(lambda v, tb2=tb2: v.tensor_copy(
                        dest_out[:, :],
                        bass.AP(dest4, tb2, [[4 * TOPK, 128], [4, TOPK]])),
                        waits=W("dest_free"))
                    st_d = em.dma("sdst", "sync",
                                  lambda s, i=i, tb2=tb2: s.dma_start(
                        out=o_dest[(i - 3 + tb2) * 128:(i - 2 + tb2) * 128, :],
                        in_=dest_out[:, :]), waits=[em.now("cv")])
                    marks["dest_free"] = st_d

        em.dma("scnt", "sync", lambda s: s.dma_start(
            out=o_counts[:, :], in_=runc[:, :]), waits=[em.now("cv")])
        em.barrier_all()
        em.build(block)
    es.close()
    return nc


def host_consts():
    return {
        "cst_up": np.triu(np.ones((128, 128), np.float32), 1),
        "cst_iota": np.broadcast_to(
            np.arange(E, dtype=np.float32), (128, E)).copy(),
    }


def launch_A_inputs(x_full, gate_weight, e_bias, shared_gate, shared_up,
                    shared_down):
    cs = host_consts()
    gwT = np.ascontiguousarray(gate_weight.T)
    sgT = np.ascontiguousarray(shared_gate.T)
    suT = np.ascontiguousarray(shared_up.T)
    sdT = np.ascontiguousarray(shared_down.T)
    maps = []
    for c in range(NCORES):
        gid = (np.arange(TC, dtype=np.float32) + c * TC).reshape(
            TC // 128, 128).T.copy()
        maps.append({
            "x": np.ascontiguousarray(x_full[c * TC:(c + 1) * TC]),
            "gwT": gwT, "e_bias": e_bias.reshape(1, E).astype(np.float32),
            "sgT": sgT, "suT": suT, "sdT": sdT,
            "gid": gid, **cs,
        })
    return maps


def build_launch_B():
    nc = bass.Bass("TRN2", target_bir_lowering=False)
    xln_all = nc.dram_tensor("xln_all", [NCORES * TC, D], BF16,
                             kind="ExternalInput").ap()
    counts_all = nc.dram_tensor("counts_all", [1, NCORES * E], F32,
                                kind="ExternalInput").ap()
    contrib_all = nc.dram_tensor("contrib_all", [NCORES * NSLOT, 2], F32,
                                 kind="ExternalInput").ap()
    w1T_in = nc.dram_tensor("w1T", [EPC * D, FEXP], F32, kind="ExternalInput").ap()
    w3T_in = nc.dram_tensor("w3T", [EPC * D, FEXP], F32, kind="ExternalInput").ap()
    w2T_in = nc.dram_tensor("w2T", [EPC * FEXP, D], F32, kind="ExternalInput").ap()
    slotbase_in = nc.dram_tensor("slotbase", [128, E], F32,
                                 kind="ExternalInput").ap()
    prow_in = nc.dram_tensor("prow", [128, 8], F32, kind="ExternalInput").ap()
    eofs_in = nc.dram_tensor("eofs", [1, 1], F32, kind="ExternalInput").ap()
    o_ye = nc.dram_tensor("ye", [NSLOT, D], BF16, kind="ExternalOutput").ap()

    es = ExitStack()
    sb = lambda name, shape, dt: es.enter_context(nc.sbuf_tensor(name, shape, dt))
    psum = lambda name, shape, dt: es.enter_context(nc.psum_tensor(name, shape, dt))

    cntb = sb("cntb", [1, NCORES * E], F32)
    bases = sb("bases", [1, NCORES * E], F32)
    slotbase = sb("slotbase_sb", [128, E], F32)
    prow = sb("prow_sb", [128, 8], F32)
    eofs = sb("eofs_sb", [1, 1], F32)
    ones1_f = sb("ones1_f", [1, 128], F32)
    ident = sb("ident", [128, 128], BF16)
    Bb = sb("Bb", [128, 8], F32)
    gemask = sb("gemask", [128, 8], F32)
    cofp = sb("cofp", [128, 8], F32)
    bsel = sb("bsel", [128, 8], F32)
    loff = sb("loff", [128, 8], F32)
    gidxf = sb("gidxf", [128, 8], F32)
    gidxi = sb("gidxi", [128, 8], I32)
    dyef = sb("dyef", [128, 8], F32)
    dyei = sb("dyei", [128, 8], I32)
    sm = sb("sm", [128, 16], F32)
    tokf = sb("tokf", [128, 8], F32)
    toki = sb("toki", [128, 8], I32)
    wcol = sb("wcol", [128, 8], F32)
    xe = sb("xe", [128, 8 * D], BF16)
    xeT = sb("xeT", [128, 8 * D], BF16)
    w1b = sb("w1b", [128, 2 * 8 * FEXP], BF16)
    w3b = sb("w3b", [128, 2 * 8 * FEXP], BF16)
    w2b = sb("w2b", [128, 2 * 4 * D], BF16)
    hg = sb("hg", [128, 4 * CAP], BF16)
    hu = sb("hu", [128, 4 * CAP], BF16)
    prodE = sb("prodE", [128, 4 * CAP], BF16)
    yeb = sb("yeb", [128, 8 * D], BF16)
    ps_b = psum("ps_b", [128, 8], F32)
    psT = psum("psT", [128, D], BF16)
    ps_h = psum("ps_h", [128, 512], F32)
    ps_u = psum("ps_u", [128, 512], F32)

    with nc.Block() as block:
        em = Em(nc, es)
        em.dma("dpre", "sync", lambda s: s.dma_start(out=cntb[:, :], in_=counts_all[:, :]))
        em.dma("dpre", "sync", lambda s: s.dma_start(out=slotbase[:, :], in_=slotbase_in[:, :]))
        em.dma("dpre", "sync", lambda s: s.dma_start(out=prow[:, :], in_=prow_in[:, :]))
        em.dma("dpre", "sync", lambda s: s.dma_start(out=eofs[:, :], in_=eofs_in[:, :]))
        em.g(lambda g: g.memset(ones1_f[:, :], 1.0))
        emit_identity(em, ident)
        ld0 = em.now("dpre")
        em.v(lambda v: v.memset(bases[0:1, 0:E], 0.0), waits=[ld0])
        for c in range(1, NCORES):
            em.v(lambda v, c=c: v.tensor_add(
                bases[0:1, c * E:(c + 1) * E],
                bases[0:1, (c - 1) * E:c * E],
                cntb[0:1, (c - 1) * E:c * E]))
        bases_done = em.now("cv")
        g_all = em.now("cg")

        regs = {}

        def init_regs(g):
            regs["bc_c"] = g.register("bc_c").__enter__()
            g.reg_mov(regs["bc_c"], NCORES * NSLOT - 1)
            regs["bc_x"] = g.register("bc_x").__enter__()
            g.reg_mov(regs["bc_x"], NCORES * TC - 1)
            regs["bc_y"] = g.register("bc_y").__enter__()
            g.reg_mov(regs["bc_y"], NSLOT - 1)
            return None
        em.ops["gpsimd"].append((init_regs, [], ("cg", 0)))

        marks = {"w_free": [None, None], "ye_free": None, "sm_free": None,
                 "xe_free": None}

        for el in range(EPC):
            pb = el % 2
            w_w = [w for w in [marks["w_free"][pb]] if w]
            lw1 = em.dma("dw%d" % pb, "gpsimd", lambda g, el=el, pb=pb: g.dma_start(
                out=w1b[:, pb * 8 * FEXP:(pb + 1) * 8 * FEXP].rearrange(
                    "p (c f) -> p c f", c=8),
                in_=w1T_in[el * D:(el + 1) * D, :].rearrange(
                    "(c p) f -> p c f", p=128)), waits=w_w)
            lw3 = em.dma("dw%d" % pb, "gpsimd", lambda g, el=el, pb=pb: g.dma_start(
                out=w3b[:, pb * 8 * FEXP:(pb + 1) * 8 * FEXP].rearrange(
                    "p (c f) -> p c f", c=8),
                in_=w3T_in[el * D:(el + 1) * D, :].rearrange(
                    "(c p) f -> p c f", p=128)))
            lw2 = em.dma("dw%d" % pb, "gpsimd", lambda g, el=el, pb=pb: g.dma_start(
                out=w2b[:, pb * 4 * D:(pb + 1) * 4 * D].rearrange(
                    "p (c f) -> p c f", c=4),
                in_=w2T_in[el * FEXP:(el + 1) * FEXP, :].rearrange(
                    "(c p) f -> p c f", p=128)))

            def mm_B(t, el=el):
                return t.matmul(ps_b[:, :], ones1_f[:, :],
                                bass.AP(bases, 8 * 0 + el, [[NCORES * E, 1], [E, NCORES]]),
                                start=True, stop=True)
            mmB = em.t(mm_B, waits=[bases_done, g_all])
            em.v(lambda v: v.tensor_copy(Bb[:, :], ps_b[:, :]), waits=[mmB])
            for s in range(8):
                em.v(lambda v, s=s: v.tensor_tensor(
                    gemask[:, :],
                    bass.AP(prow, s, [[8, 128], [0, 8]]),
                    Bb[:, :], ALU.is_ge))
                em.v(lambda v, s=s: v.reduce_sum(cofp[:, s:s + 1], gemask[:, :],
                                                 axis=AX.X))
                em.v(lambda v: v.tensor_mul(gemask[:, :], gemask[:, :], Bb[:, :]))
                em.v(lambda v, s=s: v.reduce_max(bsel[:, s:s + 1], gemask[:, :],
                                                 axis=AX.X))
            em.v(lambda v: v.tensor_scalar(cofp[:, :], cofp[:, :], -1.0, None,
                                           ALU.add))
            em.v(lambda v: v.tensor_sub(loff[:, :], prow[:, :], bsel[:, :]))
            em.v(lambda v, el=el: v.scalar_tensor_tensor(
                gidxf[:, :], cofp[:, :], float(NSLOT),
                slotbase[:, el * 8:(el + 1) * 8], ALU.mult, ALU.add))
            em.v(lambda v: v.tensor_sub(gidxf[:, :], gidxf[:, :], bsel[:, :]))
            em.v(lambda v, el=el: v.scalar_tensor_tensor(
                dyef[:, :], cofp[:, :], float(EPC * CAP), loff[:, :],
                ALU.mult, ALU.add))
            em.v(lambda v, el=el: v.tensor_scalar(
                dyef[:, :], dyef[:, :], float(el * CAP), None, ALU.add))
            ye_guard = [w for w in [marks["ye_free"]] if w]
            em.v(lambda v: v.tensor_copy(gidxi[:, :], gidxf[:, :]),
                 waits=ye_guard)
            em.v(lambda v: v.tensor_copy(dyei[:, :], dyef[:, :]))
            stg = em.now("cv")

            sm_guard = [w for w in [marks["sm_free"]] if w]
            for s in range(8):
                sm_done = em.dma("dsm", "gpsimd", lambda g, s=s: g.indirect_dma_start(
                    out=sm[:, 2 * s:2 * s + 2], out_offset=None,
                    in_=contrib_all[:, :],
                    in_offset=IndirectOffsetOnAxis(gidxi[:, s:s + 1], 0),
                    bounds_check=regs["bc_c"], oob_is_err=False),
                    waits=[stg] + sm_guard)
            em.v(lambda v: v.tensor_copy(
                tokf[:, :], bass.AP(sm, 0, [[16, 128], [2, 8]])),
                waits=[sm_done])
            em.v(lambda v: v.tensor_copy(
                wcol[:, :], bass.AP(sm, 1, [[16, 128], [2, 8]])))
            em.v(lambda v: v.tensor_copy(toki[:, :], tokf[:, :]))
            tok_done = em.now("cv")
            marks["sm_free"] = tok_done

            xe_guard = [w for w in [marks["xe_free"]] if w]
            for s in range(8):
                xe_done = em.dma("dxe", "gpsimd", lambda g, s=s: g.indirect_dma_start(
                    out=xe[:, s * D:(s + 1) * D], out_offset=None,
                    in_=xln_all[:, :],
                    in_offset=IndirectOffsetOnAxis(toki[:, s:s + 1], 0),
                    bounds_check=regs["bc_x"], oob_is_err=False),
                    waits=[tok_done] + xe_guard)

            for s in range(8):
                tr = emit_transpose8(em, psT, xe[:, s * D:(s + 1) * D], ident,
                                     [xe_done, g_all, em.now("cv")])
                em.v(lambda v, s=s: v.tensor_copy(
                    bass.AP(xeT, s * 128, [[8 * D, 128], [1024, 8], [1, 128]]),
                    psT[:, :]), waits=[tr])
            xeT_done = em.now("cv")
            marks["xe_free"] = xeT_done

            wv = [lw2]
            for fc in range(4):
                for half in range(2):
                    def mm_h(t, fc=fc, half=half, pb=pb):
                        inst = None
                        for c in range(8):
                            inst = t.matmul(
                                ps_h[:, :],
                                w1b[:, pb * 8 * FEXP + c * FEXP + fc * 128:
                                    pb * 8 * FEXP + c * FEXP + fc * 128 + 128],
                                bass.AP(xeT, c * CAP + half * 512,
                                        [[8 * D, 128], [1, 512]]),
                                start=(c == 0), stop=(c == 7))
                        return inst
                    mmh = em.t(mm_h, waits=[xeT_done, em.now("cs")] + wv)
                    em.s(lambda s, fc=fc, half=half: s.activation(
                        hg[:, fc * CAP + half * 512:fc * CAP + half * 512 + 512],
                        ps_h[:, :], AF.Silu), waits=[mmh])

                    def mm_u(t, fc=fc, half=half, pb=pb):
                        inst = None
                        for c in range(8):
                            inst = t.matmul(
                                ps_u[:, :],
                                w3b[:, pb * 8 * FEXP + c * FEXP + fc * 128:
                                    pb * 8 * FEXP + c * FEXP + fc * 128 + 128],
                                bass.AP(xeT, c * CAP + half * 512,
                                        [[8 * D, 128], [1, 512]]),
                                start=(c == 0), stop=(c == 7))
                        return inst
                    mmu = em.t(mm_u, waits=[xeT_done, em.now("cv")] + wv)
                    em.v(lambda v, fc=fc, half=half: v.tensor_mul(
                        prodE[:, fc * CAP + half * 512:fc * CAP + half * 512 + 512],
                        ps_u[:, :],
                        hg[:, fc * CAP + half * 512:fc * CAP + half * 512 + 512]),
                        waits=[mmu, em.now("cs")])
            prod_done = em.now("cv")

            for s in range(8):
                for half in range(2):
                    def mm_y(t, s=s, half=half, pb=pb):
                        inst = None
                        for fc in range(4):
                            inst = t.matmul(
                                ps_h[:, :],
                                prodE[:, fc * CAP + s * 128:fc * CAP + s * 128 + 128],
                                w2b[:, pb * 4 * D + fc * D + half * 512:
                                    pb * 4 * D + fc * D + half * 512 + 512],
                                start=(fc == 0), stop=(fc == 3))
                        return inst
                    mmy = em.t(mm_y, waits=[prod_done, lw2, em.now("cs"),
                                            em.now("cv")])
                    em.v(lambda v, s=s, half=half: v.tensor_scalar(
                        yeb[:, s * D + half * 512:s * D + half * 512 + 512],
                        ps_h[:, :], wcol[:, s:s + 1], None, ALU.mult),
                        waits=[mmy])
            ye_done = em.now("cv")
            marks["w_free"][pb] = ye_done

            for s in range(8):
                yemark = em.dma("dye", "gpsimd", lambda g, s=s: g.indirect_dma_start(
                    out=o_ye[:, :],
                    out_offset=IndirectOffsetOnAxis(dyei[:, s:s + 1], 0),
                    in_=yeb[:, s * D:(s + 1) * D], in_offset=None,
                    bounds_check=regs["bc_y"], oob_is_err=False),
                    waits=[ye_done])
            marks["ye_free"] = yemark

        em.barrier_all()
        em.build(block)
    es.close()
    return nc


def launch_B_inputs(core, xln_list, counts_list, contrib_list, w1, w2, w3):
    xln_all = np.concatenate(xln_list, 0)
    contrib_all = np.concatenate(contrib_list, 0)
    cnt = np.stack([c.reshape(E) for c in counts_list], 0)
    e0 = core * EPC
    cnt_rot = np.roll(cnt, -e0, axis=1)
    counts_all = cnt_rot.reshape(1, NCORES * E).astype(np.float32)
    sl = slice(e0, e0 + EPC)
    w1T = np.ascontiguousarray(w1[sl].transpose(0, 2, 1)).reshape(EPC * D, FEXP)
    w3T = np.ascontiguousarray(w3[sl].transpose(0, 2, 1)).reshape(EPC * D, FEXP)
    w2T = np.ascontiguousarray(w2[sl].transpose(0, 2, 1)).reshape(EPC * FEXP, D)
    rows = np.arange(128, dtype=np.float32)
    slotbase = np.zeros((128, E), np.float32)
    for el in range(EPC):
        for s in range(8):
            slotbase[:, el * 8 + s] = (e0 + el) * CAP + s * 128 + rows
    prow = np.zeros((128, 8), np.float32)
    for s in range(8):
        prow[:, s] = s * 128 + rows
    return {
        "xln_all": xln_all, "counts_all": counts_all,
        "contrib_all": contrib_all,
        "w1T": w1T.astype(np.float32), "w3T": w3T.astype(np.float32),
        "w2T": w2T.astype(np.float32),
        "slotbase": slotbase, "prow": prow,
        "eofs": np.array([[float(e0)]], np.float32),
    }


def build_launch_C(nst=TC // 512, upto="all"):
    nc = bass.Bass("TRN2", target_bir_lowering=False)
    TCl = nst * 512
    x_in = nc.dram_tensor("x", [TCl, D], F32, kind="ExternalInput").ap()
    sh_in = nc.dram_tensor("shared", [TCl, D], F32, kind="ExternalInput").ap()
    dest_in = nc.dram_tensor("dest", [TCl, TOPK], F32, kind="ExternalInput").ap()
    yeR = nc.dram_tensor("yeR", [NSLOT + 1, D], BF16, kind="ExternalInput").ap()
    WqT_in = nc.dram_tensor("WqT", [D, D], F32, kind="ExternalInput").ap()
    WkT_in = nc.dram_tensor("WkT", [D, D], F32, kind="ExternalInput").ap()
    WvT_in = nc.dram_tensor("WvT", [D, D], F32, kind="ExternalInput").ap()
    bqc_in = nc.dram_tensor("bqc", [128, 8], F32, kind="ExternalInput").ap()
    bkc_in = nc.dram_tensor("bkc", [128, 8], F32, kind="ExternalInput").ap()
    bv_in = nc.dram_tensor("bv", [1, D], F32, kind="ExternalInput").ap()
    tri_in = nc.dram_tensor("tri", [128, 128], F32, kind="ExternalInput").ap()
    bdm_in = nc.dram_tensor("bdm", [128, 128], F32, kind="ExternalInput").ap()
    hmask_in = nc.dram_tensor("hmask", [128, 2], F32, kind="ExternalInput").ap()

    o_x1 = nc.dram_tensor("x1", [TCl, D], F32, kind="ExternalOutput").ap()
    o_op = nc.dram_tensor("opart", [TCl, D], F32, kind="ExternalOutput").ap()
    o_qf = nc.dram_tensor("qfm", [nst * 128, 8 * 512], BF16,
                          kind="ExternalOutput").ap()
    o_st = nc.dram_tensor("state", [128, 8 * 128], F32,
                          kind="ExternalOutput").ap()

    es = ExitStack()
    sb = lambda name, shape, dt: es.enter_context(nc.sbuf_tensor(name, shape, dt))
    psum = lambda name, shape, dt: es.enter_context(nc.psum_tensor(name, shape, dt))

    WqTb = sb("WqTb", [128, 8 * D], BF16)
    WkTb = sb("WkTb", [128, 8 * D], BF16)
    WvTb = sb("WvTb", [128, 8 * D], BF16)
    bqc = sb("bqc_sb", [128, 8], F32)
    bkc = sb("bkc_sb", [128, 8], F32)
    bvb = sb("bvb", [1, D], BF16)
    trim = sb("trim", [128, 128], F32)
    bdm = sb("bdm_sb", [128, 128], F32)
    hmask = sb("hmask_sb", [128, 2], F32)
    qzb = sb("qzb", [128, 2 * 8 * 512], BF16)
    ident = sb("ident", [128, 128], BF16)
    ones1_b = sb("ones1_b", [1, 128], BF16)

    xt = sb("xt", [128, D], F32)
    sht = sb("sht", [128, D], F32)
    x1t = sb("x1t", [128, D], F32)
    destf = sb("destf_c", [128, 2 * TOPK], F32)
    desti = sb("desti_c", [128, 2 * TOPK], I32)
    moe8 = sb("moe8", [128, 2 * TOPK * D], BF16)
    stats = sb("stats_c", [128, 12], F32)
    mv = sb("mv_c", [128, 2], F32)
    rr = sb("rr_c", [128, 1], F32)
    col = sb("col_c", [128, 1], F32)
    xlnt = sb("xlnt_c", [128, D], BF16)
    xlnT = sb("xlnT_c", [128, 4 * 8 * 128], BF16)
    qfm = sb("qfm_sb", [128, 8 * 512], BF16)
    kfm = sb("kfm_sb", [128, 8 * 512], BF16)
    ktm = sb("ktm_sb", [128, 4 * D], BF16)
    vtm = sb("vtm_sb", [128, 4 * D], BF16)
    act1 = sb("act1", [128, 512], F32)
    act2 = sb("act2", [128, 512], F32)
    sT = sb("sT_sb", [128, 4 * 128], BF16)
    stbf = sb("stbf", [128, 8 * 128], BF16)
    state = sb("state_sb", [128, 8 * 128], F32)
    sustage = sb("sustage", [128, 8 * 128], F32)
    oev = sb("oev", [128, D], F32)

    psT = psum("psT_c", [128, D], BF16)
    ps_a = psum("ps_a", [128, 512], F32)
    ps_b = psum("ps_b", [128, 512], F32)
    ps_o = psum("ps_o", [128, D], F32)
    ps_su = psum("ps_su", [128, 8 * 128], F32)

    with nc.Block() as block:
        em = Em(nc, es)
        chunked_load(em, WqTb, WqT_in, 8, D)
        chunked_load(em, WkTb, WkT_in, 8, D)
        chunked_load(em, WvTb, WvT_in, 8, D)
        em.dma("dpre", "sync", lambda s: s.dma_start(out=bqc[:, :], in_=bqc_in[:, :]))
        em.dma("dpre", "sync", lambda s: s.dma_start(out=bkc[:, :], in_=bkc_in[:, :]))
        em.dma("dpre_g", "gpsimd", lambda g: g.dma_start(out=bvb[:, :], in_=bv_in[:, :]))
        em.dma("dpre", "sync", lambda s: s.dma_start(out=trim[:, :], in_=tri_in[:, :]))
        em.dma("dpre", "sync", lambda s: s.dma_start(out=bdm[:, :], in_=bdm_in[:, :]))
        em.dma("dpre", "sync", lambda s: s.dma_start(out=hmask[:, :], in_=hmask_in[:, :]))
        em.g(lambda g: g.memset(ones1_b[:, :], 1.0))
        em.v(lambda v: v.memset(state[:, :], 0.0))
        emit_identity(em, ident)
        ld_all = em.now("dpre")
        gp_all = em.now("dpre_g")
        g_all = em.now("cg")

        regs = {}

        def init_regs(g):
            regs["bc"] = g.register("bc_ye").__enter__()
            g.reg_mov(regs["bc"], NSLOT)
            return None
        em.ops["gpsimd"].append((init_regs, [], ("cg", 0)))

        marks = {"moe_free": None, "x1_free": None, "xlnT_free": None,
                 "q_free": None, "k_free": None, "v_free": None,
                 "o_free": None, "sT_free": None, "d_free": None,
                 "xln_free": None, "d_free2": [None, None],
                 "moe_free2": [None, None]}

        def W(key):
            return [w for w in [marks[key]] if w]

        cbufs = {"xt": xt, "sht": sht, "destf": destf, "desti": desti,
                 "moe8": moe8, "x1t": x1t, "ident": ident,
                 "ps_a": ps_a, "ps_b": ps_b}

        for st in range(nst):
            for q in range(4):
                i = st * 4 + q
                x1_done = emit_combine_tile(
                    em, i, x_in, sh_in, dest_in, yeR, cbufs, regs, marks, W,
                    out_x=o_x1, coef=0.5)
                if upto == "combine":
                    continue
                ln_done = emit_ln(em, x1t[:, :], xlnt[:, :], stats, mv, col, rr,
                                  [x1_done] + W("xln_free"))
                tr = emit_transpose8(em, psT, xlnt[:, :], ident,
                                     [ln_done, g_all] + W("xlnT_free"))
                marks["xln_free"] = tr
                em.v(lambda v, q=q: v.tensor_copy(
                    bass.AP(xlnT, q * 128, [[4 * 8 * 128, 128], [512, 8], [1, 128]]),
                    psT[:, :]), waits=[tr])
            xlnT_done = em.now("cv")
            if upto == "combine":
                continue

            for which, Wb, bcol, dst in (("q", WqTb, bqc, qfm), ("k", WkTb, bkc, kfm)):
                for c in range(8):
                    def mm_qk(t, Wb=Wb, c=c):
                        inst = None
                        for dc in range(8):
                            inst = t.matmul(
                                ps_a[:, :],
                                Wb[:, dc * D + c * 128:dc * D + c * 128 + 128],
                                bass.AP(xlnT, dc * 512, [[4096, 128], [1, 512]]),
                                start=(dc == 0), stop=(dc == 7))
                        return inst
                    mm1 = em.t(mm_qk, waits=[xlnT_done, gp_all, ld_all,
                                             em.now("cv"), em.now("cs")])
                    em.v(lambda v, bcol=bcol, c=c: v.tensor_scalar(
                        act1[:, :], ps_a[:, :], bcol[:, c:c + 1], 0.0,
                        ALU.add, ALU.max), waits=[mm1] + (W("q_free") if which == "q" else W("k_free")))
                    em.v(lambda v, bcol=bcol, c=c: v.scalar_tensor_tensor(
                        act2[:, :], ps_a[:, :], bcol[:, c:c + 1], act1[:, :],
                        ALU.add, ALU.subtract))
                    sc_exp = em.s(lambda s: s.activation(act2[:, :], act2[:, :],
                                                         AF.Exp),
                                  waits=[em.now("cv")])
                    em.v(lambda v, dst=dst, c=c: v.tensor_add(
                        dst[:, c * 512:(c + 1) * 512], act1[:, :], act2[:, :]),
                        waits=[sc_exp])
            for c in range(8):
                for par in range(2):
                    em.v(lambda v, c=c, par=par: v.tensor_scalar(
                        qzb[:, c * 1024 + par * 512:c * 1024 + par * 512 + 512],
                        qfm[:, c * 512:(c + 1) * 512],
                        hmask[:, par:par + 1], None, ALU.mult))
            q_done = em.now("cv")
            st_qf = em.dma("sqf", "sync", lambda s, st=st: s.dma_start(
                out=o_qf[st * 128:(st + 1) * 128, :], in_=qfm[:, :]),
                waits=[q_done])
            for q in range(4):
                for half in range(2):
                    def mm_v(t, q=q, half=half):
                        for dc in range(8):
                            t.matmul(ps_b[:, :],
                                     bass.AP(xlnT, dc * 512 + q * 128,
                                             [[4096, 128], [1, 128]]),
                                     WvTb[:, dc * D + half * 512:
                                          dc * D + half * 512 + 512],
                                     start=(dc == 0), stop=False)
                        return t.matmul(ps_b[:, :], ones1_b[:, :],
                                        bvb[:, half * 512:half * 512 + 512],
                                        start=False, stop=True)
                    mmv = em.t(mm_v, waits=[xlnT_done, gp_all, g_all,
                                            em.now("cv")])
                    em.v(lambda v, q=q, half=half: v.tensor_copy(
                        vtm[:, q * D + half * 512:q * D + half * 512 + 512],
                        ps_b[:, :]), waits=[mmv] + (W("v_free") if q == 0 and half == 0 else []))
            v_done = em.now("cv")
            for q in range(4):
                def tr_k(t, q=q):
                    inst = None
                    for c in range(8):
                        inst = t.transpose(
                            psT[:, c * 128:(c + 1) * 128],
                            kfm[:, c * 512 + q * 128:c * 512 + q * 128 + 128],
                            ident[:, :])
                    return inst
                trk = em.t(tr_k, waits=[q_done, em.now("cv")])
                em.v(lambda v, q=q: v.tensor_copy(ktm[:, q * D:(q + 1) * D],
                                                  psT[:, :]), waits=[trk])
            k_done = em.now("cv")
            marks["xlnT_free"] = em.now("ct")
            if upto == "qkv":
                continue

            for q in range(4):
                i = st * 4 + q
                em.v(lambda v: v.tensor_copy(stbf[:, :], state[:, :]),
                     waits=[em.now("ct")])
                stbf_done = em.now("cv")
                for hg in range(4):
                    def mm_sT2(t, hg=hg, q=q):
                        inst = None
                        for hh in range(4):
                            h = hg * 4 + hh
                            c = h // 2
                            par = h % 2
                            lhs = kfm[:, c * 512 + q * 128:
                                      c * 512 + q * 128 + 128]
                            rhs = qzb[:, c * 1024 + par * 512 + q * 128:
                                      c * 1024 + par * 512 + q * 128 + 128]
                            inst = t.matmul(ps_a[:, hh * 128:(hh + 1) * 128],
                                            lhs, rhs, start=True, stop=True)
                        return inst
                    mmst = em.t(mm_sT2, waits=[q_done, k_done, em.now("cv")])
                    em.v(lambda v, hg=hg: v.tensor_tensor(
                        sT[:, :], ps_a[:, :],
                        bass.AP(trim, 0, [[128, 128], [0, 4], [1, 128]]),
                        ALU.mult), waits=[mmst] + W("sT_free"))
                    sT_done = em.now("cv")
                    def mm_o(t, hg=hg, q=q):
                        inst = None
                        for hh in range(4):
                            h = hg * 4 + hh
                            inst = t.matmul(
                                ps_o[:, h * 64:(h + 1) * 64],
                                sT[:, hh * 128:(hh + 1) * 128],
                                vtm[:, q * D + h * 64:q * D + h * 64 + 64],
                                start=(h % 8 == 0), stop=False)
                        return inst
                    if upto in ("all", "o"):
                        em.t(mm_o, waits=[sT_done, v_done])
                    marks["sT_free"] = em.now("ct")
                def mm_sr(t, q=q):
                    inst = None
                    for c in range(8):
                        inst = t.matmul(
                            ps_o[:, c * 128:(c + 1) * 128],
                            qfm[:, c * 512 + q * 128:c * 512 + q * 128 + 128],
                            stbf[:, c * 128:(c + 1) * 128],
                            start=False, stop=(c == 3 or c == 7))
                    return inst
                if upto in ("all", "o"):
                    mmsr = em.t(mm_sr, waits=[stbf_done, q_done])
                    ev1 = em.v(lambda v: v.tensor_copy(oev[:, 0:512],
                                                       ps_o[:, 0:512]),
                               waits=[mmsr] + W("o_free"))
                    ev2 = em.v(lambda v: v.tensor_copy(oev[:, 512:1024],
                                                       ps_o[:, 512:1024]))
                    st_o = em.dma("sop", "sync", lambda s, i=i: s.dma_start(
                        out=o_op[i * 128:(i + 1) * 128, :], in_=oev[:, :]),
                        waits=[em.now("cv")])
                    marks["o_free"] = st_o
                def mm_su(t, q=q):
                    inst = None
                    for c in range(8):
                        inst = t.matmul(
                            ps_su[:, c * 128:(c + 1) * 128],
                            ktm[:, q * D + c * 128:q * D + c * 128 + 128],
                            vtm[:, q * D + c * 128:q * D + c * 128 + 128],
                            start=True, stop=True)
                    return inst
                if upto == "all":
                    mmsu = em.t(mm_su, waits=[k_done, v_done, em.now("cv")])
                    em.v(lambda v: v.tensor_tensor(
                        sustage[:, :], ps_su[:, :],
                        bass.AP(bdm, 0, [[128, 128], [0, 8], [1, 128]]),
                        ALU.mult), waits=[mmsu])
                    em.v(lambda v: v.tensor_add(state[:, :], state[:, :],
                                                sustage[:, :]))
            marks["q_free"] = em.now("ct")
            marks["k_free"] = em.now("ct")
            marks["v_free"] = em.now("ct")

        em.dma("sst", "sync", lambda s: s.dma_start(
            out=o_st[:, :], in_=state[:, :]), waits=[em.now("cv")])
        em.barrier_all()
        em.build(block)
    es.close()
    return nc


def build_launch_D(nst=TC // 512):
    nc = bass.Bass("TRN2", target_bir_lowering=False)
    TCl = nst * 512
    x1_in = nc.dram_tensor("x1", [TCl, D], F32, kind="ExternalInput").ap()
    op_in = nc.dram_tensor("opart", [TCl, D], F32, kind="ExternalInput").ap()
    qf_in = nc.dram_tensor("qfm", [nst * 128, 8 * 512], BF16,
                           kind="ExternalInput").ap()
    sp_in = nc.dram_tensor("Spart", [128, 8 * 128], F32,
                           kind="ExternalInput").ap()
    lng_in = nc.dram_tensor("ln_g", [128, 8], F32, kind="ExternalInput").ap()
    lnb_in = nc.dram_tensor("ln_b", [128, 8], F32, kind="ExternalInput").ap()
    WoT_in = nc.dram_tensor("WoT", [D, D], F32, kind="ExternalInput").ap()
    bo_in = nc.dram_tensor("bo", [1, D], F32, kind="ExternalInput").ap()
    o_x2 = nc.dram_tensor("x2", [TCl, D], F32, kind="ExternalOutput").ap()

    es = ExitStack()
    sb = lambda name, shape, dt: es.enter_context(nc.sbuf_tensor(name, shape, dt))
    psum = lambda name, shape, dt: es.enter_context(nc.psum_tensor(name, shape, dt))

    WoTb = sb("WoTb", [128, 8 * D], BF16)
    lngc = sb("lngc", [128, 8], F32)
    lnbc = sb("lnbc", [128, 8], BF16)
    bWo = sb("bWo", [1, D], BF16)
    bob = sb("bob", [1, D], BF16)
    ones1_b = sb("ones1_b_d", [1, 128], BF16)
    ident = sb("ident_d", [128, 128], BF16)
    spb = sb("spb", [128, 8 * 128], BF16)
    qfb = sb("qfb", [128, 8 * 512], BF16)
    x1t = sb("x1t_d", [128, D], F32)
    opt = sb("opt_d", [128, D], F32)
    ot = sb("ot_d", [128, D], F32)
    olnt = sb("olnt_d", [128, D], BF16)
    olnT = sb("olnT_d", [128, D], BF16)
    stats = sb("stats_d", [128, 12], F32)
    mv = sb("mv_d", [128, 2], F32)
    rr = sb("rr_d", [128, 1], F32)
    col = sb("col_d", [128, 1], F32)
    x2t = sb("x2t", [128, D], F32)
    psT = psum("psT_d", [128, D], BF16)
    ps_o = psum("ps_o_d", [128, D], F32)
    ps_p = psum("ps_p_d", [128, 512], F32)
    ps_r = psum("ps_r_d", [1, D], F32)

    with nc.Block() as block:
        em = Em(nc, es)
        chunked_load(em, WoTb, WoT_in, 8, D)
        em.dma("dpre", "sync", lambda s: s.dma_start(
            out=lngc[:, :], in_=lng_in[:, :]))
        em.dma("dpre_g", "gpsimd", lambda g: g.dma_start(
            out=lnbc[:, :], in_=lnb_in[:, :]))
        em.dma("dpre_g", "gpsimd", lambda g: g.dma_start(
            out=spb[:, :], in_=sp_in[:, :]))
        em.dma("dpre_g", "gpsimd", lambda g: g.dma_start(
            out=bob[:, :], in_=bo_in[:, :]))
        em.g(lambda g: g.memset(ones1_b[:, :], 1.0))
        emit_identity(em, ident)
        ld_all = em.now("dpre")
        gp_all = em.now("dpre_g")
        g_all = em.now("cg")

        def mm_bw(t):
            inst = None
            for half in range(2):
                for c in range(8):
                    inst = t.matmul(ps_r[:, half * 512:half * 512 + 512],
                                    lnbc[:, c:c + 1],
                                    WoTb[:, c * D + half * 512:
                                         c * D + half * 512 + 512],
                                    start=(c == 0), stop=(c == 7))
            return inst
        mmbw = em.t(mm_bw, waits=[ld_all, gp_all, g_all])
        em.v(lambda v: v.tensor_copy(bWo[:, :], ps_r[:, :]), waits=[mmbw])
        for c in range(8):
            em.s(lambda s, c=c: s.mul(WoTb[:, c * D:(c + 1) * D],
                                      WoTb[:, c * D:(c + 1) * D],
                                      lngc[:, c:c + 1]),
                 waits=([mmbw] if c == 0 else []))
        wos_done = em.now("cs")

        marks = {"x2_free": None, "oln_free": None, "q_free": None}

        def W(key):
            return [w for w in [marks[key]] if w]

        for st in range(nst):
            ld_q = em.dma("dqf", "sync", lambda s, st=st: s.dma_start(
                out=qfb[:, :], in_=qf_in[st * 128:(st + 1) * 128, :]),
                waits=W("q_free"))
            for q in range(4):
                i = st * 4 + q
                ld_x1 = em.dma("dx1", "sync", lambda s, i=i: s.dma_start(
                    out=x1t[:, :], in_=x1_in[i * 128:(i + 1) * 128, :]),
                    waits=[em.now("cv")])
                ld_op = em.dma("dop", "sync", lambda s, i=i: s.dma_start(
                    out=opt[:, :], in_=op_in[i * 128:(i + 1) * 128, :]),
                    waits=[em.now("cv")])
                def mm_oc(t, q=q):
                    inst = None
                    for c in range(8):
                        inst = t.matmul(
                            ps_o[:, c * 128:(c + 1) * 128],
                            qfb[:, c * 512 + q * 128:c * 512 + q * 128 + 128],
                            spb[:, c * 128:(c + 1) * 128],
                            start=(c % 4 == 0), stop=(c == 3 or c == 7))
                    return inst
                mmoc = em.t(mm_oc, waits=[ld_q, gp_all, em.now("cv")])
                em.v(lambda v: v.tensor_add(ot[:, 0:512], ps_o[:, 0:512],
                                            opt[:, 0:512]),
                     waits=[mmoc, ld_op])
                em.v(lambda v: v.tensor_add(ot[:, 512:1024], ps_o[:, 512:1024],
                                            opt[:, 512:1024]))
                o_done = em.now("cv")
                ln_done = emit_ln(em, ot[:, :], olnt[:, :], stats, mv, col, rr,
                                  [o_done] + W("oln_free"))
                tr = emit_transpose8(em, psT, olnt[:, :], ident,
                                     [ln_done, g_all])
                marks["oln_free"] = tr
                evT = em.v(lambda v: v.tensor_copy(olnT[:, :], psT[:, :]),
                           waits=[tr])
                for half in range(2):
                    def mm_p(t, half=half):
                        for c in range(8):
                            t.matmul(ps_p[:, :],
                                     olnT[:, c * 128:(c + 1) * 128],
                                     WoTb[:, c * D + half * 512:
                                          c * D + half * 512 + 512],
                                     start=(c == 0), stop=False)
                        t.matmul(ps_p[:, :], ones1_b[:, :],
                                 bWo[:, half * 512:half * 512 + 512],
                                 start=False, stop=False)
                        return t.matmul(ps_p[:, :], ones1_b[:, :],
                                        bob[:, half * 512:half * 512 + 512],
                                        start=False, stop=True)
                    mmp = em.t(mm_p, waits=[evT, wos_done, em.now("cv")])
                    em.v(lambda v, half=half: v.tensor_add(
                        x2t[:, half * 512:half * 512 + 512], ps_p[:, :],
                        x1t[:, half * 512:half * 512 + 512]),
                        waits=[mmp, ld_x1] + (W("x2_free") if half == 0 else []))
                st_x2 = em.dma("sx2", "sync", lambda s, i=i: s.dma_start(
                    out=o_x2[i * 128:(i + 1) * 128, :], in_=x2t[:, :]),
                    waits=[em.now("cv")])
                marks["x2_free"] = st_x2
            marks["q_free"] = em.now("ct")

        em.barrier_all()
        em.build(block)
    es.close()
    return nc


def emit_combine_tile(em, i, x_src, sh_src, dest_src, yeR, bufs, regs, marks, W,
                      out_x=None, coef=0.5):
    xt, sht = bufs["xt"], bufs["sht"]
    moe8, x1t, ident = bufs["moe8"], bufs["x1t"], bufs["ident"]
    ps_a, ps_b = bufs["ps_a"], bufs["ps_b"]
    pb2 = i % 2
    destf = bufs["destf"][:, pb2 * TOPK:(pb2 + 1) * TOPK]
    desti = bufs["desti"][:, pb2 * TOPK:(pb2 + 1) * TOPK]
    mslot = lambda k: moe8[:, (pb2 * TOPK + k) * D:(pb2 * TOPK + k + 1) * D]
    marks.setdefault("d_free2", [None, None])
    marks.setdefault("moe_free2", [None, None])
    ld_x = em.dma("dxx", "sync", lambda s: s.dma_start(
        out=xt[:, :], in_=x_src[i * 128:(i + 1) * 128, :]),
        waits=[em.now("cv")])
    ld_sh = em.dma("dsh", "sync", lambda s: s.dma_start(
        out=sht[:, :], in_=sh_src[i * 128:(i + 1) * 128, :]),
        waits=[em.now("cv")])
    w_d = [w for w in [marks["d_free2"][pb2]] if w]
    ld_d = em.dma("ddst%d" % pb2, "sync", lambda s: s.dma_start(
        out=destf, in_=dest_src[i * 128:(i + 1) * 128, :]),
        waits=w_d)
    em.v(lambda v: v.tensor_scalar(destf, destf, float(NSLOT), None, ALU.min),
         waits=[ld_d])
    cast = em.v(lambda v: v.tensor_copy(desti, destf))
    w_moe = [w for w in [marks["moe_free2"][pb2]] if w]
    for k in range(TOPK):
        moe_done = em.dma("dmoe%d" % pb2, "gpsimd",
                          lambda g, k=k: g.indirect_dma_start(
            out=mslot(k), out_offset=None,
            in_=yeR[:, :],
            in_offset=IndirectOffsetOnAxis(desti[:, k:k + 1], 0),
            bounds_check=regs["bc"], oob_is_err=False),
            waits=[cast] + w_moe)
    marks["d_free2"][pb2] = moe_done
    for half in range(2):
        def mm_sum(t, half=half, pb2=pb2):
            inst = None
            for k in range(TOPK):
                inst = t.matmul(
                    ps_a[:, :] if half == 0 else ps_b[:, :], ident[:, :],
                    mslot(k)[:, half * 512:half * 512 + 512],
                    start=(k == 0), stop=(k == TOPK - 1))
            return inst
        mms = em.t(mm_sum, waits=[moe_done, em.now("cg"), em.now("cv")])
        pshalf = ps_a if half == 0 else ps_b
        em.v(lambda v, half=half, pshalf=pshalf: v.tensor_add(
            x1t[:, half * 512:half * 512 + 512], pshalf[:, :],
            sht[:, half * 512:half * 512 + 512]),
            waits=[mms, ld_sh] + (W("x1_free") if half == 0 else []))
        em.v(lambda v, half=half: v.scalar_tensor_tensor(
            x1t[:, half * 512:half * 512 + 512],
            x1t[:, half * 512:half * 512 + 512], coef,
            xt[:, half * 512:half * 512 + 512],
            ALU.mult, ALU.add), waits=[ld_x])
    marks["moe_free2"][pb2] = em.now("ct")
    x1_done = em.now("cv")
    if out_x is not None:
        st_x1 = em.dma("sx1", "sync", lambda s: s.dma_start(
            out=out_x[i * 128:(i + 1) * 128, :], in_=x1t[:, :]),
            waits=[x1_done])
        marks["x1_free"] = st_x1
    return x1_done


def build_launch_FFN(nst=TC // 512, final=False):
    nc = bass.Bass("TRN2", target_bir_lowering=False,
                   dynamic_dma_scratch_size=2048)
    TCl = nst * 512
    x_in = nc.dram_tensor("x", [TCl, D], F32, kind="ExternalInput").ap()
    w1_in = nc.dram_tensor("w1", [HF, D], F32, kind="ExternalInput").ap()
    w1T_in = nc.dram_tensor("w1T", [D, HF], F32, kind="ExternalInput").ap()
    w2_in = nc.dram_tensor("w2", [D, HF], F32, kind="ExternalInput").ap()
    w2T_in = nc.dram_tensor("w2T", [HF, D], F32, kind="ExternalInput").ap()
    b1c_in = nc.dram_tensor("b1c", [128, HF // 128], F32,
                            kind="ExternalInput").ap()
    b2c_in = nc.dram_tensor("b2c", [128, 8], F32, kind="ExternalInput").ap()
    o_x = nc.dram_tensor("xo", [TCl, D], F32, kind="ExternalOutput").ap()

    es = ExitStack()
    sb = lambda name, shape, dt: es.enter_context(nc.sbuf_tensor(name, shape, dt))
    psum = lambda name, shape, dt: es.enter_context(nc.psum_tensor(name, shape, dt))

    NH = HF // 128
    ident = sb("ident_n", [128, 128], BF16)
    s1Tb = sb("s1Tb", [128, 8 * HF], BF16)
    s2Tb = sb("s2Tb", [128, NH * D], BF16)
    al1 = sb("al1", [128, NH], F32)
    al2 = sb("al2", [128, 8], F32)
    b1c = sb("b1c_sb", [128, NH], F32)
    b2c = sb("b2c_sb", [128, 8], F32)
    xA = sb("xA", [128, 4 * D], F32)
    wscr = xA
    stats = sb("stats_n", [128, 12], F32)
    mv = sb("mv_n", [128, 2], F32)
    rr = sb("rr_n", [128, 1], F32)
    col = sb("col_n", [128, 1], F32)
    xlnt = sb("xlnt_n", [128, D], BF16)
    xlnT = sb("xlnT_n", [128, 4096], BF16)
    hfm = sb("hfm", [128, NH * 512], BF16)
    outfm = sb("outfm", [128, 8 * 512], BF16)
    x3ff = sb("x3ff", [128, D], BF16)
    xot = sb("xot", [128, D], F32)
    psT = psum("psT_n", [128, D], BF16)
    ps_h = psum("ps_h_n", [128, 512], F32)
    ps_h2 = psum("ps_h2_n", [128, 512], F32)

    with nc.Block() as block:
        em = Em(nc, es)
        em.dma("dpre", "sync", lambda s: s.dma_start(out=b1c[:, :],
                                                     in_=b1c_in[:, :]))
        em.dma("dpre", "sync", lambda s: s.dma_start(out=b2c[:, :],
                                                     in_=b2c_in[:, :]))
        emit_identity(em, ident)
        g_all = em.now("cg")

        for ti in range(NH):
            ldw = em.dma("dwn%d" % (ti % 2), "sync", lambda s, ti=ti: s.dma_start(
                out=wscr[:, (ti % 2) * D:(ti % 2 + 1) * D],
                in_=w1_in[ti * 128:(ti + 1) * 128, :]), waits=[em.now("cs")])
            em.s(lambda s, ti=ti: s.activation(
                wscr[:, (ti % 2) * D:(ti % 2 + 1) * D],
                wscr[:, (ti % 2) * D:(ti % 2 + 1) * D],
                AF.Abs, accum_out=al1[:, ti:ti + 1]), waits=[ldw])
        em.v(lambda v: v.tensor_scalar(al1[:, :], al1[:, :], 1.0 / D, None,
                                       ALU.mult), waits=[em.now("cs")])
        for c in range(8):
            ldt = em.dma("dwt", "sync", lambda s, c=c: s.dma_start(
                out=wscr[:, :], in_=w1T_in[c * 128:(c + 1) * 128, :]),
                waits=[em.now("cs")])
            em.s(lambda s, c=c: s.activation(
                s1Tb[:, c * HF:(c + 1) * HF], wscr[:, :], AF.Sign),
                waits=[ldt])
        for ti in range(8):
            ldw = em.dma("dwt", "sync", lambda s, ti=ti: s.dma_start(
                out=wscr[:, :], in_=w2_in[ti * 128:(ti + 1) * 128, :]),
                waits=[em.now("cs")])
            em.s(lambda s, ti=ti: s.activation(
                wscr[:, :], wscr[:, :],
                AF.Abs, accum_out=al2[:, ti:ti + 1]), waits=[ldw])
        em.v(lambda v: v.tensor_scalar(al2[:, :], al2[:, :], 1.0 / HF, None,
                                       ALU.mult), waits=[em.now("cs")])
        for c in range(NH):
            ldt = em.dma("dwn%d" % (c % 2), "sync", lambda s, c=c: s.dma_start(
                out=wscr[:, (c % 2) * D:(c % 2 + 1) * D],
                in_=w2T_in[c * 128:(c + 1) * 128, :]), waits=[em.now("cs")])
            em.s(lambda s, c=c: s.activation(
                s2Tb[:, c * D:(c + 1) * D],
                wscr[:, (c % 2) * D:(c % 2 + 1) * D], AF.Sign), waits=[ldt])
        prep_done = em.now("cs")
        al_done = em.now("cv")

        marks = {"xlnT_free": None, "xln_free": None, "h_free": None,
                 "out_free": None, "xo_free": None, "xA_free": None}

        def W(key):
            return [w for w in [marks[key]] if w]

        for st in range(nst):
            for q in range(4):
                i = st * 4 + q
                ld_x = em.dma("dxx", "sync", lambda s, i=i, q=q: s.dma_start(
                    out=xA[:, q * D:(q + 1) * D],
                    in_=x_in[i * 128:(i + 1) * 128, :]),
                    waits=[em.now("cv"), em.now("ct")] + W("xA_free"))
                ln_done = emit_ln(em, xA[:, q * D:(q + 1) * D], xlnt[:, :],
                                  stats, mv, col, rr,
                                  [ld_x, prep_done] + W("xln_free"))
                tr = emit_transpose8(em, psT, xlnt[:, :], ident,
                                     [ln_done, g_all] + W("xlnT_free"))
                marks["xln_free"] = tr
                em.v(lambda v, q=q: v.tensor_copy(
                    bass.AP(xlnT, q * 128, [[4096, 128], [512, 8], [1, 128]]),
                    psT[:, :]), waits=[tr])
            xlnT_done = em.now("cv")

            ev_marks = [None, None]
            for hc in range(NH):
                bank = ps_h if hc % 2 == 0 else ps_h2
                def mm_h(t, hc=hc, bank=bank):
                    inst = None
                    for dc in range(8):
                        inst = t.matmul(
                            bank[:, :],
                            s1Tb[:, dc * HF + hc * 128:dc * HF + hc * 128 + 128],
                            bass.AP(xlnT, dc * 512, [[4096, 128], [1, 512]]),
                            start=(dc == 0), stop=(dc == 7))
                    return inst
                w_mm = [xlnT_done]
                if ev_marks[hc % 2]:
                    w_mm.append(ev_marks[hc % 2])
                mmh = em.t(mm_h, waits=w_mm)
                ev_marks[hc % 2] = em.s(
                    lambda s, hc=hc, bank=bank: s.activation(
                        hfm[:, hc * 512:(hc + 1) * 512], bank[:, :],
                        AF.Gelu, bias=b1c[:, hc:hc + 1],
                        scale=al1[:, hc:hc + 1]),
                    waits=[mmh, al_done] + (W("h_free") if hc == 0 else []))
            h_done = em.now("cs")
            marks["xlnT_free"] = em.now("ct")

            ev_marks = [None, None]
            for dc in range(8):
                bank = ps_h if dc % 2 == 0 else ps_h2
                def mm_o(t, dc=dc, bank=bank):
                    inst = None
                    for hc in range(NH):
                        inst = t.matmul(
                            bank[:, :],
                            s2Tb[:, hc * D + dc * 128:hc * D + dc * 128 + 128],
                            hfm[:, hc * 512:(hc + 1) * 512],
                            start=(hc == 0), stop=(hc == NH - 1))
                    return inst
                w_mm = [h_done]
                if ev_marks[dc % 2]:
                    w_mm.append(ev_marks[dc % 2])
                mmo = em.t(mm_o, waits=w_mm)
                ev_marks[dc % 2] = em.v(
                    lambda v, dc=dc, bank=bank: v.tensor_scalar(
                        outfm[:, dc * 512:(dc + 1) * 512], bank[:, :],
                        al2[:, dc:dc + 1], b2c[:, dc:dc + 1],
                        ALU.mult, ALU.add),
                    waits=[mmo] + (W("out_free") if dc == 0 else []))
            out_done = em.now("cv")
            marks["h_free"] = em.now("ct")

            for q in range(4):
                i = st * 4 + q
                def tr_o(t, q=q):
                    inst = None
                    for dc in range(8):
                        inst = t.transpose(
                            psT[:, dc * 128:(dc + 1) * 128],
                            outfm[:, dc * 512 + q * 128:dc * 512 + q * 128 + 128],
                            ident[:, :])
                    return inst
                tro = em.t(tr_o, waits=[out_done, em.now("cv")])
                em.v(lambda v: v.tensor_copy(x3ff[:, :], psT[:, :]),
                     waits=[tro])
                em.v(lambda v, q=q: v.scalar_tensor_tensor(
                    xot[:, :], x3ff[:, :], 0.5, xA[:, q * D:(q + 1) * D],
                    ALU.mult, ALU.add), waits=W("xo_free"))
                st_xo = em.dma("sxo", "sync", lambda s, i=i: s.dma_start(
                    out=o_x[i * 128:(i + 1) * 128, :], in_=xot[:, :]),
                    waits=[em.now("cv")])
                marks["xo_free"] = st_xo
            marks["out_free"] = em.now("ct")
            marks["xA_free"] = em.now("cv")

        em.barrier_all()
        em.build(block)
    es.close()
    return nc


def build_launch_WATT(nst=TC // 512):
    nc = bass.Bass("TRN2", target_bir_lowering=False)
    TCl = nst * 512
    x_in = nc.dram_tensor("x", [TCl, D], F32, kind="ExternalInput").ap()
    wqkv_in = nc.dram_tensor("w_qkv", [D, 3 * D], F32, kind="ExternalInput").ap()
    wout_in = nc.dram_tensor("w_out", [D, D], F32, kind="ExternalInput").ap()
    bout_in = nc.dram_tensor("b_out", [1, D], F32, kind="ExternalInput").ap()
    wm_in = nc.dram_tensor("wmask", [128, 256], F32, kind="ExternalInput").ap()
    o_x4 = nc.dram_tensor("x4p", [TCl, D], F32, kind="ExternalOutput").ap()
    o_q0 = nc.dram_tensor("q0fm", [128, 8 * 128], BF16, kind="ExternalOutput").ap()
    o_k0 = nc.dram_tensor("k0fm", [128, 8 * 128], BF16, kind="ExternalOutput").ap()
    o_v0 = nc.dram_tensor("v0tm", [128, D], BF16, kind="ExternalOutput").ap()
    o_kL = nc.dram_tensor("kLfm", [128, 8 * 128], BF16, kind="ExternalOutput").ap()
    o_vL = nc.dram_tensor("vLtm", [128, D], BF16, kind="ExternalOutput").ap()

    es = ExitStack()
    sb = lambda name, shape, dt: es.enter_context(nc.sbuf_tensor(name, shape, dt))
    psum = lambda name, shape, dt: es.enter_context(nc.psum_tensor(name, shape, dt))

    ident = sb("ident_w", [128, 128], BF16)
    ones1_b = sb("ones1_w", [1, 128], BF16)
    wqkvb = sb("wqkvb", [128, 8 * 3 * D], BF16)
    woutb = sb("woutb", [128, 8 * D], BF16)
    boutb = sb("boutb", [1, D], BF16)
    wmb = sb("wmb", [128, 256], F32)
    xA = sb("xA_w", [128, 4 * D], F32)
    stats = sb("stats_w", [128, 12], F32)
    mv = sb("mv_w", [128, 2], F32)
    rr = sb("rr_w", [128, 1], F32)
    col = sb("col_w", [128, 1], F32)
    negmax = sb("negmax_w", [128, 1], F32)
    rden = sb("rden_w", [128, 1], F32)
    xlnt = sb("xlnt_w", [128, D], BF16)
    xlnT = sb("xlnT_w", [128, 4096], BF16)
    qfm = sb("qfm_w", [128, 8 * 512], BF16)
    kfm = sb("kfm_w", [128, 8 * 512], BF16)
    kring = sb("kring", [128, 2 * 8 * 128], BF16)
    vring = sb("vring", [128, 2 * D], BF16)
    ssb = sb("ssb_w", [128, 256], F32)
    eS = sb("eS_w", [128, 256], BF16)
    eST = sb("eST_w", [128, 256], BF16)
    attn = sb("attn_w", [128, D], BF16)
    attnT = sb("attnT_w", [128, D], BF16)
    ebuf = sb("ebuf_w", [128, D], BF16)
    x4t = sb("x4t_w", [128, D], F32)
    psT = psum("psT_w", [128, D], BF16)
    ps_s = psum("ps_s_w", [128, 512], F32)
    ps_a = psum("ps_a_w", [128, 512], F32)
    ps_b = psum("ps_b_w", [128, 512], F32)

    with nc.Block() as block:
        em = Em(nc, es)
        chunked_load(em, wqkvb, wqkv_in, 8, 3 * D)
        chunked_load(em, woutb, wout_in, 8, D)
        em.dma("dpre_g", "gpsimd", lambda g: g.dma_start(out=boutb[:, :],
                                                         in_=bout_in[:, :]))
        em.dma("dpre", "sync", lambda s: s.dma_start(out=wmb[:, :],
                                                     in_=wm_in[:, :]))
        em.g(lambda g: g.memset(ones1_b[:, :], 1.0))
        emit_identity(em, ident)
        ld_all = em.now("dpre")
        gp_all = em.now("dpre_g")
        g_all = em.now("cg")

        marks = {"xlnT_free": None, "xln_free": None, "q_free": None,
                 "k_free": None, "ring_free": None, "x4_free": None,
                 "xA_free": None, "eS_free": None, "attn_free": None}

        def W(key):
            return [w for w in [marks[key]] if w]

        for st in range(nst):
            for q in range(4):
                i = st * 4 + q
                ld_x = em.dma("dxx", "sync", lambda s, i=i, q=q: s.dma_start(
                    out=xA[:, q * D:(q + 1) * D],
                    in_=x_in[i * 128:(i + 1) * 128, :]),
                    waits=[em.now("cv")] + W("xA_free"))
                ln_done = emit_ln(em, xA[:, q * D:(q + 1) * D], xlnt[:, :],
                                  stats, mv, col, rr,
                                  [ld_x] + W("xln_free"))
                tr = emit_transpose8(em, psT, xlnt[:, :], ident,
                                     [ln_done, g_all] + W("xlnT_free"))
                marks["xln_free"] = tr
                em.v(lambda v, q=q: v.tensor_copy(
                    bass.AP(xlnT, q * 128, [[4096, 128], [512, 8], [1, 128]]),
                    psT[:, :]), waits=[tr])
            xlnT_done = em.now("cv")

            for which, off, dst in (("q", 0, qfm), ("k", D, kfm)):
                for c in range(8):
                    def mm_qk(t, off=off, c=c):
                        inst = None
                        for dc in range(8):
                            inst = t.matmul(
                                ps_s[:, :],
                                wqkvb[:, dc * 3 * D + off + c * 128:
                                      dc * 3 * D + off + c * 128 + 128],
                                bass.AP(xlnT, dc * 512, [[4096, 128], [1, 512]]),
                                start=(dc == 0), stop=(dc == 7))
                        return inst
                    guard = W("q_free") if which == "q" else W("k_free")
                    mm1 = em.t(mm_qk, waits=[xlnT_done, gp_all,
                                             em.now("cv")])
                    em.v(lambda v, dst=dst, c=c: v.tensor_copy(
                        dst[:, c * 512:(c + 1) * 512], ps_s[:, :]),
                        waits=[mm1] + (guard if c == 0 else []))
            qk_done = em.now("cv")

            for q in range(4):
                i = st * 4 + q
                rp = i % 2
                for half in range(2):
                    def mm_v(t, q=q, half=half):
                        inst = None
                        for dc in range(8):
                            inst = t.matmul(
                                ps_b[:, :],
                                bass.AP(xlnT, dc * 512 + q * 128,
                                        [[4096, 128], [1, 128]]),
                                wqkvb[:, dc * 3 * D + D + half * 512:
                                      dc * 3 * D + D + half * 512 + 512],
                                start=(dc == 0), stop=(dc == 7))
                        return inst
                    mmv = em.t(mm_v, waits=[xlnT_done, gp_all, em.now("cv")])
                    em.v(lambda v, rp=rp, half=half: v.tensor_copy(
                        vring[:, rp * D + half * 512:rp * D + half * 512 + 512],
                        ps_b[:, :]),
                        waits=[mmv] + (W("ring_free") if half == 0 else []))
                em.v(lambda v, rp=rp, q=q: v.tensor_copy(
                    bass.AP(kring, rp * 1024, [[2 * 1024, 128], [128, 8], [1, 128]]),
                    bass.AP(kfm, q * 128, [[8 * 512, 128], [512, 8], [1, 128]])),
                    waits=[qk_done])
                ring_done = em.now("cv")

                first = (i == 0)
                if not first:
                    def mm_s_fix(t, q=q, rp=rp):
                        inst = None
                        for dc in range(8):
                            t.matmul(ps_s[:, 0:128],
                                     qfm[:, dc * 512 + q * 128:
                                         dc * 512 + q * 128 + 128],
                                     kring[:, (1 - rp) * 1024 + dc * 128:
                                           (1 - rp) * 1024 + dc * 128 + 128],
                                     start=(dc == 0), stop=(dc == 7))
                        for dc in range(8):
                            inst = t.matmul(ps_s[:, 128:256],
                                            qfm[:, dc * 512 + q * 128:
                                                dc * 512 + q * 128 + 128],
                                            kring[:, rp * 1024 + dc * 128:
                                                  rp * 1024 + dc * 128 + 128],
                                            start=(dc == 0),
                                            stop=(dc == 7))
                        return inst
                    mms = em.t(mm_s_fix, waits=[qk_done, ring_done,
                                                em.now("cv"), em.now("cs")])
                    em.v(lambda v: v.scalar_tensor_tensor(
                        ssb[:, :], ps_s[:, 0:256], float(D) ** -0.5, wmb[:, :],
                        ALU.mult, ALU.add), waits=[mms, ld_all])
                    em.v(lambda v: v.reduce_max(col[:, :], ssb[:, :], axis=AX.X))
                    em.v(lambda v: v.tensor_scalar(negmax[:, :], col[:, :],
                                                   -1.0, None, ALU.mult))
                    se = em.s(lambda s: s.activation(eS[:, :], ssb[:, :],
                                                     AF.Exp,
                                                     bias=negmax[:, 0:1]),
                              waits=[em.now("cv")] + W("eS_free"))
                    em.v(lambda v: v.reduce_sum(col[:, :], eS[:, :], axis=AX.X),
                         waits=[se])
                    em.v(lambda v: v.reciprocal(rden[:, :], col[:, :]))
                    sm_done = em.now("cv")
                    def tr_e(t):
                        t.transpose(psT[:, 0:128], eS[:, 0:128], ident[:, :])
                        return t.transpose(psT[:, 128:256], eS[:, 128:256],
                                           ident[:, :])
                    tre = em.t(tr_e, waits=[sm_done, em.now("cv")])
                    marks["eS_free"] = em.now("ct")
                    em.v(lambda v: v.tensor_copy(eST[:, :], psT[:, 0:256]),
                         waits=[tre])
                    est_done = em.now("cv")
                    for half in range(2):
                        def mm_a(t, half=half, rp=rp):
                            t.matmul(ps_a[:, :], eST[:, 0:128],
                                     vring[:, (1 - rp) * D + half * 512:
                                           (1 - rp) * D + half * 512 + 512],
                                     start=True, stop=False)
                            return t.matmul(ps_a[:, :], eST[:, 128:256],
                                            vring[:, rp * D + half * 512:
                                                  rp * D + half * 512 + 512],
                                            start=False, stop=True)
                        mma = em.t(mm_a, waits=[est_done, em.now("cv"),
                                                em.now("cs")])
                        em.v(lambda v, half=half: v.tensor_scalar(
                            attn[:, half * 512:half * 512 + 512], ps_a[:, :],
                            rden[:, 0:1], None, ALU.mult),
                            waits=[mma] + (W("attn_free") if half == 0 else []))
                    at_done = em.now("cv")
                else:
                    at_done = em.v(lambda v: v.memset(attn[:, :], 0.0),
                                   waits=W("attn_free"))
                tra = emit_transpose8(em, psT, attn[:, :], ident,
                                      [at_done, em.now("cv")])
                marks["attn_free"] = tra
                em.v(lambda v: v.tensor_copy(attnT[:, :], psT[:, :]),
                     waits=[tra])
                aT_done = em.now("cv")
                for half in range(2):
                    def mm_p(t, half=half):
                        for dc in range(8):
                            t.matmul(ps_b[:, :],
                                     attnT[:, dc * 128:(dc + 1) * 128],
                                     woutb[:, dc * D + half * 512:
                                           dc * D + half * 512 + 512],
                                     start=(dc == 0), stop=False)
                        return t.matmul(ps_b[:, :], ones1_b[:, :],
                                        boutb[:, half * 512:half * 512 + 512],
                                        start=False, stop=True)
                    mmp = em.t(mm_p, waits=[aT_done, gp_all, g_all,
                                            em.now("cv")])
                    em.v(lambda v, half=half, q=q: v.tensor_add(
                        x4t[:, half * 512:half * 512 + 512], ps_b[:, :],
                        xA[:, q * D + half * 512:q * D + half * 512 + 512]),
                        waits=[mmp] + (W("x4_free") if half == 0 else []))
                st_x4 = em.dma("sx4", "sync", lambda s, i=i: s.dma_start(
                    out=o_x4[i * 128:(i + 1) * 128, :], in_=x4t[:, :]),
                    waits=[em.now("cv")])
                marks["x4_free"] = st_x4
                marks["ring_free"] = em.now("ct")

                if i == 0:
                    em.v(lambda v: v.tensor_copy(
                        ebuf[:, :],
                        bass.AP(qfm, 0, [[8 * 512, 128], [512, 8], [1, 128]])),
                        waits=[qk_done])
                    em.dma("sq0", "sync", lambda s: s.dma_start(
                        out=o_q0[:, :], in_=ebuf[:, 0:1024]),
                        waits=[em.now("cv")])
                    em.dma("sk0", "sync", lambda s, rp=rp: s.dma_start(
                        out=o_k0[:, :], in_=kring[:, rp * 1024:(rp + 1) * 1024]),
                        waits=[ring_done])
                    em.dma("sv0", "sync", lambda s, rp=rp: s.dma_start(
                        out=o_v0[:, :], in_=vring[:, rp * D:(rp + 1) * D]),
                        waits=[ring_done])
                if i == nst * 4 - 1:
                    em.dma("skL", "sync", lambda s, rp=rp: s.dma_start(
                        out=o_kL[:, :], in_=kring[:, rp * 1024:(rp + 1) * 1024]),
                        waits=[ring_done])
                    em.dma("svL", "sync", lambda s, rp=rp: s.dma_start(
                        out=o_vL[:, :], in_=vring[:, rp * D:(rp + 1) * D]),
                        waits=[ring_done])
            marks["q_free"] = em.now("ct")
            marks["k_free"] = em.now("ct")
            marks["xlnT_free"] = em.now("ct")
            marks["xA_free"] = em.now("cv")

        em.barrier_all()
        em.build(block)
    es.close()
    return nc


def build_launch_W0():
    nc = bass.Bass("TRN2", target_bir_lowering=False)
    x4_in = nc.dram_tensor("x4p0", [128, D], F32, kind="ExternalInput").ap()
    q0_in = nc.dram_tensor("q0fm", [128, 8 * 128], BF16, kind="ExternalInput").ap()
    k0_in = nc.dram_tensor("k0fm", [128, 8 * 128], BF16, kind="ExternalInput").ap()
    v0_in = nc.dram_tensor("v0tm", [128, D], BF16, kind="ExternalInput").ap()
    kP_in = nc.dram_tensor("kPfm", [128, 8 * 128], BF16, kind="ExternalInput").ap()
    vP_in = nc.dram_tensor("vPtm", [128, D], BF16, kind="ExternalInput").ap()
    wout_in = nc.dram_tensor("w_out", [D, D], F32, kind="ExternalInput").ap()
    wm_in = nc.dram_tensor("wmask", [128, 256], F32, kind="ExternalInput").ap()
    o_x4 = nc.dram_tensor("x4t0", [128, D], F32, kind="ExternalOutput").ap()

    es = ExitStack()
    sb = lambda name, shape, dt: es.enter_context(nc.sbuf_tensor(name, shape, dt))
    psum = lambda name, shape, dt: es.enter_context(nc.psum_tensor(name, shape, dt))
    ident = sb("ident_0", [128, 128], BF16)
    woutb = sb("woutb_0", [128, 8 * D], BF16)
    wmb = sb("wmb_0", [128, 256], F32)
    q0 = sb("q0_0", [128, 1024], BF16)
    k0 = sb("k0_0", [128, 1024], BF16)
    v0 = sb("v0_0", [128, D], BF16)
    kP = sb("kP_0", [128, 1024], BF16)
    vP = sb("vP_0", [128, D], BF16)
    x4t = sb("x4t_0", [128, D], F32)
    ssb = sb("ssb_0", [128, 256], F32)
    eS = sb("eS_0", [128, 256], BF16)
    eST = sb("eST_0", [128, 256], BF16)
    attn = sb("attn_0", [128, D], BF16)
    attnT = sb("attnT_0", [128, D], BF16)
    col = sb("col_0", [128, 1], F32)
    negmax = sb("negmax_0", [128, 1], F32)
    rden = sb("rden_0", [128, 1], F32)
    psT = psum("psT_0", [128, D], BF16)
    ps_s = psum("ps_s_0", [128, 512], F32)
    ps_b = psum("ps_b_0", [128, 512], F32)

    with nc.Block() as block:
        em = Em(nc, es)
        chunked_load(em, woutb, wout_in, 8, D)
        em.dma("dpre", "sync", lambda s: s.dma_start(out=wmb[:, :], in_=wm_in[:, :]))
        em.dma("dpre", "sync", lambda s: s.dma_start(out=q0[:, :], in_=q0_in[:, :]))
        em.dma("dpre", "sync", lambda s: s.dma_start(out=k0[:, :], in_=k0_in[:, :]))
        em.dma("dpre", "sync", lambda s: s.dma_start(out=v0[:, :], in_=v0_in[:, :]))
        em.dma("dpre", "sync", lambda s: s.dma_start(out=kP[:, :], in_=kP_in[:, :]))
        em.dma("dpre", "sync", lambda s: s.dma_start(out=vP[:, :], in_=vP_in[:, :]))
        em.dma("dpre", "sync", lambda s: s.dma_start(out=x4t[:, :], in_=x4_in[:, :]))
        emit_identity(em, ident)
        ld_all = em.now("dpre")
        gp_all = em.now("dpre_g")
        g_all = em.now("cg")

        def mm_s(t):
            inst = None
            for dc in range(8):
                t.matmul(ps_s[:, 0:128], q0[:, dc * 128:(dc + 1) * 128],
                         kP[:, dc * 128:(dc + 1) * 128],
                         start=(dc == 0), stop=(dc == 7))
            for dc in range(8):
                inst = t.matmul(ps_s[:, 128:256], q0[:, dc * 128:(dc + 1) * 128],
                                k0[:, dc * 128:(dc + 1) * 128],
                                start=(dc == 0), stop=(dc == 7))
            return inst
        mms = em.t(mm_s, waits=[ld_all, gp_all, g_all])
        em.v(lambda v: v.scalar_tensor_tensor(
            ssb[:, :], ps_s[:, 0:256], float(D) ** -0.5, wmb[:, :],
            ALU.mult, ALU.add), waits=[mms])
        em.v(lambda v: v.reduce_max(col[:, :], ssb[:, :], axis=AX.X))
        em.v(lambda v: v.tensor_scalar(negmax[:, :], col[:, :], -1.0, None,
                                       ALU.mult))
        se = em.s(lambda s: s.activation(eS[:, :], ssb[:, :], AF.Exp,
                                         bias=negmax[:, 0:1]),
                  waits=[em.now("cv")])
        em.v(lambda v: v.reduce_sum(col[:, :], eS[:, :], axis=AX.X), waits=[se])
        em.v(lambda v: v.reciprocal(rden[:, :], col[:, :]))
        sm_done = em.now("cv")

        def tr_e(t):
            t.transpose(psT[:, 0:128], eS[:, 0:128], ident[:, :])
            return t.transpose(psT[:, 128:256], eS[:, 128:256], ident[:, :])
        tre = em.t(tr_e, waits=[sm_done])
        em.v(lambda v: v.tensor_copy(eST[:, :], psT[:, 0:256]), waits=[tre])
        est_done = em.now("cv")
        for half in range(2):
            def mm_a(t, half=half):
                t.matmul(ps_b[:, :], eST[:, 0:128],
                         vP[:, half * 512:half * 512 + 512],
                         start=True, stop=False)
                return t.matmul(ps_b[:, :], eST[:, 128:256],
                                v0[:, half * 512:half * 512 + 512],
                                start=False, stop=True)
            mma = em.t(mm_a, waits=[est_done, em.now("cv")])
            em.v(lambda v, half=half: v.tensor_scalar(
                attn[:, half * 512:half * 512 + 512], ps_b[:, :],
                rden[:, 0:1], None, ALU.mult), waits=[mma])
        at_done = em.now("cv")
        tra = emit_transpose8(em, psT, attn[:, :], ident,
                              [at_done, em.now("ct")])
        em.v(lambda v: v.tensor_copy(attnT[:, :], psT[:, :]), waits=[tra])
        aT_done = em.now("cv")
        for half in range(2):
            def mm_p(t, half=half):
                inst = None
                for dc in range(8):
                    inst = t.matmul(ps_b[:, :],
                                    attnT[:, dc * 128:(dc + 1) * 128],
                                    woutb[:, dc * D + half * 512:
                                          dc * D + half * 512 + 512],
                                    start=(dc == 0), stop=(dc == 7))
                return inst
            mmp = em.t(mm_p, waits=[aT_done, em.now("cv")])
            em.v(lambda v, half=half: v.tensor_add(
                x4t[:, half * 512:half * 512 + 512], ps_b[:, :],
                x4t[:, half * 512:half * 512 + 512]), waits=[mmp])
        em.dma("sx40", "sync", lambda s: s.dma_start(
            out=o_x4[:, :], in_=x4t[:, :]), waits=[em.now("cv")])
        em.barrier_all()
        em.build(block)
    es.close()
    return nc


_BUILD_CACHE = {}


def _prog(name, fn, **kw):
    key = (name, tuple(sorted(kw.items())))
    if key not in _BUILD_CACHE:
        _BUILD_CACHE[key] = fn(**kw)
    return _BUILD_CACHE[key]


def _run(nc, maps):
    return run_bass_kernel_spmd(nc, maps, core_ids=list(range(NCORES)))


def _moe_round(x_flat, gate_weight, e_bias, shared_gate, shared_up, shared_down,
               ffn_w1, ffn_w2, ffn_w3):
    import ml_dtypes
    ncA = _prog("A", build_launch_A)
    mapsA = launch_A_inputs(x_flat, gate_weight, e_bias, shared_gate,
                            shared_up, shared_down)
    resA = _run(ncA, mapsA).results
    xln_list = [r["xln"] for r in resA]
    counts_list = [r["counts"] for r in resA]
    contrib_list = [r["contrib"] for r in resA]
    dest = [r["dest"] for r in resA]
    shared = [r["shared"] for r in resA]

    ncB = _prog("B", build_launch_B)
    mapsB = [launch_B_inputs(c, xln_list, counts_list, contrib_list,
                             ffn_w1, ffn_w2, ffn_w3) for c in range(NCORES)]
    resB = _run(ncB, mapsB).results
    ye = [r["ye"] for r in resB]
    zero_row = np.zeros((1, D), ml_dtypes.bfloat16)
    yeR = []
    for j in range(NCORES):
        parts = [ye[i][j * EPC * CAP:(j + 1) * EPC * CAP] for i in range(NCORES)]
        yeR.append(np.concatenate(parts + [zero_row], 0))
    return dest, shared, yeR


def _c_consts():
    return {
        "tri": np.triu(np.ones((128, 128), np.float32)),
        "bdm": np.kron(np.eye(2, dtype=np.float32),
                       np.ones((64, 64), np.float32)),
        "hmask": np.concatenate(
            [np.repeat([[1.0], [0.0]], 64, 0),
             np.repeat([[0.0], [1.0]], 64, 0)], 1).astype(np.float32),
    }


def kernel(**inputs):
    import ml_dtypes
    g = lambda k: np.asarray(inputs[k], dtype=np.float32)
    x0 = g("L0_attn_query").reshape(NCORES * TC, D)

    dest1, shared1, yeR1 = _moe_round(
        x0, g("L0_ffn_gate_weight"), g("L0_ffn_e_bias"),
        g("L0_ffn_shared_gate"), g("L0_ffn_shared_up"),
        g("L0_ffn_shared_down"),
        g("L0_ffn_w1"), g("L0_ffn_w2"), g("L0_ffn_w3"))

    ncC = _prog("C", build_launch_C)
    cc = _c_consts()
    Wq, Wk, Wv = g("L0_attn_Wq"), g("L0_attn_Wk"), g("L0_attn_Wv")
    bq, bk, bv = g("L0_attn_bq"), g("L0_attn_bk"), g("L0_attn_bv")
    mapsC = []
    for c in range(NCORES):
        mapsC.append({
            "x": np.ascontiguousarray(x0[c * TC:(c + 1) * TC]),
            "shared": shared1[c], "dest": dest1[c], "yeR": yeR1[c],
            "WqT": np.ascontiguousarray(Wq.T),
            "WkT": np.ascontiguousarray(Wk.T),
            "WvT": np.ascontiguousarray(Wv.T),
            "bqc": bq.reshape(8, 128).T.copy(),
            "bkc": bk.reshape(8, 128).T.copy(),
            "bv": bv.reshape(1, D), **cc,
        })
    resC = _run(ncC, mapsC).results

    ncD = _prog("D", build_launch_D)
    zst = np.zeros((128, 8 * 128), np.float32)
    mapsD = []
    for c in range(NCORES):
        sp = resC[c - 1]["state"] if c % 2 == 1 else zst
        mapsD.append({
            "x1": resC[c]["x1"], "opart": resC[c]["opart"],
            "qfm": resC[c]["qfm"], "Spart": np.asarray(sp, np.float32),
            "ln_g": g("L0_attn_ln_g").reshape(8, 128).T.copy(),
            "ln_b": g("L0_attn_ln_b").reshape(8, 128).T.copy(),
            "WoT": np.ascontiguousarray(g("L0_attn_Wo").T),
            "bo": g("L0_attn_bo").reshape(1, D),
        })
    resD = _run(ncD, mapsD).results
    x2 = np.concatenate([r["x2"] for r in resD], 0)

    dest2, shared2, yeR2 = _moe_round(
        x2, g("L0_ffn_gate_weight"), g("L0_ffn_e_bias"),
        g("L0_ffn_shared_gate"), g("L0_ffn_shared_up"),
        g("L0_ffn_shared_down"),
        g("L0_ffn_w1"), g("L0_ffn_w2"), g("L0_ffn_w3"))

    ncC2 = _prog("Cc", build_launch_C, upto="combine")
    mapsC2 = []
    for c in range(NCORES):
        m = dict(mapsC[c])
        m["x"] = np.ascontiguousarray(x2[c * TC:(c + 1) * TC])
        m["shared"] = shared2[c]
        m["dest"] = dest2[c]
        m["yeR"] = yeR2[c]
        mapsC2.append(m)
    resC2 = _run(ncC2, mapsC2).results
    x3moe = [r["x1"] for r in resC2]

    ncF = _prog("FFN", build_launch_FFN)
    w1, w2 = g("L1_ffn_w1"), g("L1_ffn_w2")
    ffnw = {
        "w1": w1, "w1T": np.ascontiguousarray(w1.T),
        "w2": w2, "w2T": np.ascontiguousarray(w2.T),
        "b1c": g("L1_ffn_b1").reshape(HF // 128, 128).T.copy(),
        "b2c": g("L1_ffn_b2").reshape(8, 128).T.copy(),
    }
    mapsF = [{"x": np.asarray(x3moe[c], np.float32), **ffnw}
             for c in range(NCORES)]
    resF = _run(ncF, mapsF).results
    x3 = [r["xo"] for r in resF]

    ncW = _prog("WATT", build_launch_WATT)
    wmask = np.where(np.triu(np.ones((128, 256), bool), 129), -3.4e38,
                     0).astype(np.float32)
    wq_nat = g("L1_attn_w_qkv")
    wo_nat = g("L1_attn_w_out")
    mapsW = [{"x": np.asarray(x3[c], np.float32), "w_qkv": wq_nat,
              "w_out": wo_nat, "b_out": g("L1_attn_b_out").reshape(1, D),
              "wmask": wmask} for c in range(NCORES)]
    resW = _run(ncW, mapsW).results

    ncW0 = _prog("W0", build_launch_W0)
    zbf = np.zeros((128, 1024), ml_dtypes.bfloat16)
    maps0 = []
    for c in range(NCORES):
        kP = resW[c - 1]["kLfm"] if c % 2 == 1 else zbf
        vP = resW[c - 1]["vLtm"] if c % 2 == 1 else zbf
        maps0.append({
            "x4p0": np.ascontiguousarray(resW[c]["x4p"][0:128]),
            "q0fm": resW[c]["q0fm"], "k0fm": resW[c]["k0fm"],
            "v0tm": resW[c]["v0tm"], "kPfm": np.asarray(kP),
            "vPtm": np.asarray(vP), "w_out": wo_nat, "wmask": wmask,
        })
    res0 = _run(ncW0, maps0).results

    mapsF2 = []
    for c in range(NCORES):
        x4 = np.array(resW[c]["x4p"], np.float32)
        x4[0:128] = res0[c]["x4t0"]
        mapsF2.append({"x": x4, **ffnw})
    resF2 = _run(ncF, mapsF2).results

    out = np.concatenate([r["xo"] for r in resF2], 0)
    return out.reshape(4, 8192, D).astype(np.float32)
